# version 02
# baseline (speedup 1.0000x reference)
import contextlib
import math

import numpy as np
import ml_dtypes

import concourse.bass as bass
import concourse.mybir as mybir
from concourse.bass_utils import run_bass_kernel_spmd

F32 = mybir.dt.float32
BF16 = mybir.dt.bfloat16
AF = mybir.ActivationFunctionType
ALU = mybir.AluOpType
AX = mybir.AxisListType

T = 4096
D = 1024
NT = T // 128
NCH = D // 128
PAD = 16

NEG = -1.0e30
BF = ml_dtypes.bfloat16
DBG = False
BISECT = set()


class Sem:
    __slots__ = ("h", "n")

    def __init__(self, h):
        self.h = h
        self.n = 0

    def inc(self, k=1):
        self.n += k
        return self.n


class Eng:

    def __init__(self, p, name):
        self.p = p
        self.name = name
        self.h = getattr(p.nc, name)
        self.sem = p.sem(f"{name}sf")

    def wait(self, sem, n):
        if n > 0:
            self.h.wait_ge(sem.h, n)

    def __call__(self, method, *a, chain=True, **kw):
        if chain and self.sem.n > 0:
            self.h.wait_ge(self.sem.h, self.sem.n)
        inst = getattr(self.h, method)(*a, **kw)
        t = self.sem.inc()
        inst.then_inc(self.sem.h, 1)
        return t


class Prog:

    def __init__(self):
        self.nc = bass.Bass("TRN2", target_bir_lowering=False)
        self.es = contextlib.ExitStack()
        self._semi = 0
        self.input_names = []
        self._cmap = None
        self.es_stack = [self.es]
        self.V = Eng(self, "vector")
        self._pre_v = None
        self.S = Eng(self, "scalar")
        self.G = Eng(self, "gpsimd")
        self.Tn = Eng(self, "tensor")

    def inp(self, name, shape, dtype):
        self.input_names.append(name)
        return self.nc.dram_tensor(name, list(shape), dtype, kind="ExternalInput").ap()

    def out(self, name, shape, dtype):
        return self.nc.dram_tensor(name, list(shape), dtype, kind="ExternalOutput").ap()

    def dram(self, name, shape, dtype):
        return self.nc.dram_tensor(name, list(shape), dtype, kind="Internal").ap()

    def sem(self, name=None):
        self._semi += 1
        h = self.es.enter_context(self.nc.semaphore(name or f"s{self._semi}"))
        return Sem(h)

    def sb(self, name, shape, dtype, es=None):
        return (es or self.es_stack[-1]).enter_context(
            self.nc.sbuf_tensor(name, list(shape), dtype))

    def ps(self, name, shape, dtype, es=None):
        return (es or self.es_stack[-1]).enter_context(
            self.nc.psum_tensor(name, list(shape), dtype))

    def dma(self, q, out, in_, sem, transpose=False):
        eng = getattr(self.nc, q)
        if transpose:
            inst = eng.dma_start_transpose(out, in_)
        else:
            inst = eng.dma_start(out=out, in_=in_)
        inst.then_inc(sem.h, 16)
        return sem.inc(16)

    def qwait(self, q, sem, n):
        if n > 0:
            getattr(self.nc, q).wait_ge(sem.h, n)

    def prealloc_cbuf(self):
        if self._cmap is None:
            self._cbuf = self.sb("cbuf", [128, 16], F32, es=self.es)
            self._cmap = {}
            self._cn = 0

    def const_col(self, val, width=1):
        if self._cmap is None:
            self.prealloc_cbuf()
        key = (float(val), width)
        if key in self._cmap:
            return self._cmap[key]
        k = self._cn
        self._cn += width
        assert self._cn <= 16
        ap = self._cbuf[:, k:k + width]
        t = self.V("memset", ap, float(val))
        self._cmap[key] = (ap, t)
        return (ap, t)



def build_constants():
    c = {}
    c["c_ident16"] = np.eye(128, dtype=np.float32).astype(BF)
    c["c_ident32"] = np.eye(128, dtype=np.float32)
    c["c_ones16"] = np.ones((128, 512), dtype=np.float32).astype(BF)
    c["c_ones32"] = np.ones((128, 512), dtype=np.float32)
    return c


CONST_SPECS = {
    "c_ident16": ([128, 128], BF16),
    "c_ident32": ([128, 128], F32),
    "c_ones16": ([128, 512], BF16),
    "c_ones32": ([128, 512], F32),
}


def declare_consts(p, names):
    return {n: p.inp(n, *CONST_SPECS[n]) for n in names}



class LnPass:

    def __init__(self, p, tag, es=None, need16=True, need32=True, needadd=True):
        self.p = p
        self.xt = [p.sb(f"{tag}_xt{i}", [128, D], F32, es) for i in range(2)]
        self.at = [p.sb(f"{tag}_at{i}", [128, D], F32, es) for i in range(2)] if needadd else None
        self.o16 = [p.sb(f"{tag}_o16_{i}", [128, D], BF16, es) for i in range(2)] if need16 else None
        self.o32 = [p.sb(f"{tag}_o32_{i}", [128, D], F32, es) for i in range(2)] if need32 else None
        self.sq = [p.sb(f"{tag}_sq{i}", [128, D], BF16, es) for i in range(2)]
        self.ssq = [p.sb(f"{tag}_ssq{i}", [128, 1], F32, es) for i in range(2)]
        self.mu = [p.sb(f"{tag}_mu{i}", [128, 1], F32, es) for i in range(2)]
        self.var = [p.sb(f"{tag}_var{i}", [128, 1], F32, es) for i in range(2)]
        self.rstd = [p.sb(f"{tag}_rstd{i}", [128, 1], F32, es) for i in range(2)]
        self.v_done = {}
        self.s_done = {}
        self.g_done = {}

    def emit(self, ldx, sg, src, out16=None, out32=None, add=None, rms2=False,
             eps=1e-5, tiles=None):
        p = self.p
        nc = p.nc
        V, S = p.V, p.S
        if tiles is None:
            tiles = range(NT)
        for j in tiles:
            i = j % 2
            if j - 2 in self.v_done:
                nc.sync.wait_ge(V.sem.h, self.v_done[j - 2])
                nc.sync.wait_ge(S.sem.h, self.s_done[j - 2])
            t_ld = p.dma("sync", self.xt[i][:, :], src[j * 128:(j + 1) * 128, :], ldx)
            if add is not None:
                t_ld = p.dma("sync", self.at[i][:, :], add[j * 128:(j + 1) * 128, :], ldx)

            V.wait(ldx, t_ld)
            V("reduce_sum", self.mu[i][:, :], self.xt[i][:, :], axis=AX.X)
            V("tensor_scalar_mul", self.mu[i][:, :], self.mu[i][:, :], 1.0 / D)
            V("tensor_mul", self.var[i][:, :], self.mu[i][:, :], self.mu[i][:, :])
            S.wait(ldx, t_ld)
            t_sq = S("activation", self.sq[i][:, :], self.xt[i][:, :], AF.Square,
                     accum_out=self.ssq[i][:, :])
            V.wait(S.sem, t_sq)
            t_var = V("scalar_tensor_tensor", self.var[i][:, :], self.ssq[i][:, :],
                      1.0 / D, self.var[i][:, :], ALU.mult, ALU.subtract)
            scale_ln, bias_ln = (1.0 + 1e-6, 1e-6 * eps) if rms2 else (1.0, eps)
            bias_ap, bias_t = p.const_col(bias_ln)
            S.wait(V.sem, max(t_var, bias_t))
            S("activation", self.rstd[i][:, :], self.var[i][:, :], AF.Ln,
              bias=bias_ap, scale=float(scale_ln))
            t_rstd = S("activation", self.rstd[i][:, :], self.rstd[i][:, :], AF.Exp,
                       scale=-0.5)

            V.wait(S.sem, t_rstd)
            if j - 2 in self.g_done:
                V.wait(sg, self.g_done[j - 2])
            outs = []
            t_norm = None
            if out16 is not None:
                t_norm = V("tensor_scalar", self.o16[i][:, :], self.xt[i][:, :],
                           self.mu[i][:, :], self.rstd[i][:, :], ALU.subtract, ALU.mult)
                outs.append((self.o16[i], out16))
            if out32 is not None:
                t_norm = V("tensor_scalar", self.o32[i][:, :], self.xt[i][:, :],
                           self.mu[i][:, :], self.rstd[i][:, :], ALU.subtract, ALU.mult)
                if add is not None:
                    t_norm = V("tensor_add", self.o32[i][:, :], self.o32[i][:, :],
                               self.at[i][:, :])
                outs.append((self.o32[i], out32))
            self.v_done[j] = t_norm
            self.s_done[j] = t_sq

            p.qwait("gpsimd", V.sem, t_norm)
            for buf, dst in outs:
                p.dma("gpsimd", dst[j * 128:(j + 1) * 128, :], buf[:, :], sg)
            self.g_done[j] = sg.n
        return sg.n



def emit_fm_load(p, src16_tm, fm_chunks, sem, col0=PAD, rows=None):
    if rows is None:
        rows = len(fm_chunks) * 128
    for c in range(rows // 128):
        p.dma("sync", fm_chunks[c][:, col0:col0 + T],
              src16_tm[:, c * 128:(c + 1) * 128], sem, transpose=True)
    return sem.n



def build_stage_identity():
    p = Prog()
    x_in = p.inp("x", [T, D], F32)
    x_out = p.out("x_out", [T, D], F32)
    ln16_tm = p.dram("ln16_tm", [T, D], BF16)
    sub_tm = p.dram("sub_tm", [T, D], F32)

    ldx, sg = p.sem("ldx"), p.sem("sg")
    ln = LnPass(p, "pre")
    t_pre = ln.emit(ldx, sg, x_in, out16=ln16_tm)

    p.qwait("gpsimd", sg, t_pre)
    scast = p.sem("scast")
    for j in range(NT):
        p.dma("gpsimd", sub_tm[j * 128:(j + 1) * 128, :],
              ln16_tm[j * 128:(j + 1) * 128, :], scast)
    p.qwait("sync", scast, scast.n)

    ln2 = LnPass(p, "post")
    t_post = ln2.emit(ldx, sg, sub_tm, out32=x_out, add=x_in)
    p.qwait("sync", sg, t_post)
    return p



def load_weight16(p, dram_ap, sbuf, sem, rows, cols):
    p.qwait("gpsimd", p.Tn.sem, p.Tn.sem.n)
    chunks = []
    for c in range(rows // 128):
        ap = sbuf[:, c * cols:(c + 1) * cols]
        p.dma("gpsimd", ap, dram_ap[c * 128:(c + 1) * 128, :], sem)
        chunks.append(ap)
    return chunks, sem.n


def mm_group(p, psum_ap, pairs):
    nc = p.nc
    last = None
    n = len(pairs)
    for i, (l, r) in enumerate(pairs):
        last = nc.tensor.matmul(psum_ap, l, r, start=(i == 0), stop=(i == n - 1))
    t = p.Tn.sem.inc()
    last.then_inc(p.Tn.sem.h, 1)
    return t


class PsumPool:

    def __init__(self, p, tag, n, shape, dtype=F32, es=None):
        self.tiles = [p.ps(f"{tag}{i}", shape, dtype, es) for i in range(n)]
        self.evac = [None] * n
        self.k = 0
        self.n = n

    def get(self):
        i = self.k % self.n
        self.k += 1
        return self.tiles[i], self.evac[i]

    def done(self, sem, target):
        i = (self.k - 1) % self.n
        self.evac[i] = (sem, target)


def proj_fm(p, w_chunks, src_fm, out_chunks, pool, waits=(),
            bias_cols=None, nslice=512, col0=PAD, src_col0=PAD, evac=None):
    nc = p.nc
    for sem, n in waits:
        p.Tn.wait(sem, n)
    t_last = 0
    for oc in range(len(out_chunks) if out_chunks else (w_chunks[0].shape[1] // 128)):
        for ts in range(T // nslice):
            pk, war = pool.get()
            if war is not None:
                p.Tn.wait(war[0], war[1])
            t_mm = mm_group(p, pk[:, :], [
                (w_chunks[ic][:, oc * 128:(oc + 1) * 128],
                 src_fm[ic][:, src_col0 + ts * nslice:src_col0 + (ts + 1) * nslice])
                for ic in range(len(w_chunks))])
            p.V.wait(p.Tn.sem, t_mm)
            if evac is not None:
                t_last = evac(oc, ts, pk)
            elif bias_cols is not None:
                t_last = p.V("tensor_scalar_add",
                             out_chunks[oc][:, col0 + ts * nslice:col0 + (ts + 1) * nslice],
                             pk[:, :], bias_cols[:, oc:oc + 1])
            else:
                t_last = p.V("tensor_copy",
                             out_chunks[oc][:, col0 + ts * nslice:col0 + (ts + 1) * nslice],
                             pk[:, :])
            pool.done(p.V.sem, t_last)
    return t_last



class Dance:

    def __init__(self, p, tag, maxw=32, es=None):
        self.p = p
        self.ps = p.ps(f"{tag}_ps", [128, 128], F32, es)
        self.st = p.sb(f"{tag}_st", [32, 128], F32, es)
        self.dr = p.dram(f"{tag}_dr", [32 * 128], F32)
        self.sem = p.sem(f"{tag}_dma")

    def rows_from_cols(self, cols_ap, out_row, w, ident32, waits=()):
        p = self.p
        for sem, n in waits:
            p.Tn.wait(sem, n)
        t_tr = p.Tn.sem.inc()
        p.nc.tensor.transpose(self.ps[0:w, 0:128], cols_ap, ident32).then_inc(p.Tn.sem.h, 1)
        p.V.wait(p.Tn.sem, t_tr)
        t_e = p.V("tensor_copy", self.st[0:w, :], self.ps[0:w, 0:128])
        p.qwait("gpsimd", p.V.sem, t_e)
        t1 = p.dma("gpsimd", self.dr[0:w * 128].rearrange("(a b) -> a b", b=128),
                   self.st[0:w, :], self.sem)
        p.qwait("sync", self.sem, t1)
        t2 = p.dma("sync", out_row, self.dr[0:w * 128].rearrange("(o a) -> o a", o=1), self.sem)
        return t2

    def cols_from_row(self, row_ap, out_cols, w, ident32, waits=(), row_sem=None):
        p = self.p
        for sem, n in waits:
            p.qwait("gpsimd", sem, n)
        t1 = p.dma("gpsimd", self.dr[0:w * 128].rearrange("(o a) -> o a", o=1), row_ap, self.sem)
        p.qwait("sync", self.sem, t1)
        t2 = p.dma("sync", self.st[0:w, :],
                   self.dr[0:w * 128].rearrange("(a b) -> a b", b=128), self.sem)
        p.Tn.wait(self.sem, t2)
        t_tr = p.Tn.sem.inc()
        p.nc.tensor.transpose(self.ps[0:128, 0:w], self.st[0:w, 0:128],
                              ident32[0:w, 0:w]).then_inc(p.Tn.sem.h, 1)
        p.V.wait(p.Tn.sem, t_tr)
        t_e = p.V("tensor_copy", out_cols, self.ps[0:128, 0:w])
        return t_e



def barrier(p, *sts):
    for st_ in sts:
        sem, n = st_
        if n <= 0:
            continue
        for eng in (p.V, p.S, p.G, p.Tn):
            eng.wait(sem, n)
        p.qwait("sync", sem, n)


def full_barrier(p):
    barrier(p, (p.V.sem, p.V.sem.n), (p.S.sem, p.S.sem.n), (p.Tn.sem, p.Tn.sem.n))


def emit_L1_attn(p, io, weights, consts):
    nc = p.nc
    V, S, G, Tn = p.V, p.S, p.G, p.Tn
    ident32 = consts["ident32_sb"]
    ident16 = consts["ident16_sb"]
    ones16 = consts["ones16_sb"]
    ones32 = consts["ones32_sb"]
    CW = PAD + T

    qn_tm_d = p.dram("l1a_qn_tm", [T, D], BF16)
    kn_tm_d = p.dram("l1a_kn_tm", [T, D], BF16)

    scr_fm = [p.sb(f"l1a_scr{c}", [128, CW], BF16) for c in range(8)]
    srow = p.sb("l1a_srow", [1, D], F32)
    vs16 = p.sb("l1a_vs16", [1, T], BF16)
    tgrow = p.sb("l1a_tgrow", [1, D], F32)
    tg16 = p.sb("l1a_tg16", [1, D], BF16)
    tb = p.sb("l1a_tb", [128, D], F32)
    kse = p.sb("l1a_kse", [128, NT], BF16)

    sw = p.sem("l1a_sw")
    sld = p.sem("l1a_sld")
    sst = p.sem("l1a_sst")
    sfm = p.sem("l1a_sfm")

    es1 = contextlib.ExitStack()
    ln_fm = [p.sb(f"l1a_lnfm{c}", [128, CW], BF16, es1) for c in range(8)]
    w16 = p.sb("l1a_w16", [128, 8 * D], BF16, es1)
    bcols = p.sb("l1a_bcols_sb", [128, 24], F32, es1)
    gam = p.sb("l1a_gam_sb", [1, 1], F32, es1)
    sqs = p.sb("l1a_sqs", [128, T], BF16, es1)
    nrm = p.sb("l1a_nrm", [128, 8], F32, es1)
    ksf = p.sb("l1a_ksf", [128, NT], F32, es1)
    tmt = [p.sb(f"l1a_tmt{i}", [128, D], BF16, es1) for i in range(2)]
    scols = p.sb("l1a_scols", [128, 8], F32, es1)
    gcol = p.sb("l1a_gcol", [128, 1], F32, es1)
    pool = PsumPool(p, "l1a_pp", 2, [128, 512], es=es1)
    trp = PsumPool(p, "l1a_tr", 2, [128, 128], BF16, es=es1)
    ps_s = p.ps("l1a_ps_s", [1, 1024], F32, es1)
    dance = Dance(p, "l1a_dance", es=es1)

    p.qwait("sync", io["pre_done"][0], io["pre_done"][1])
    t_fm = emit_fm_load(p, io["ln16_tm"], ln_fm, sfm)
    t_b = p.dma("sync", bcols[:, :], weights["l1a_bcols"][:, :], sld)
    t_g = p.dma("sync", gam[:, :], weights["l1a_gamma"][:, :], sld)
    V.wait(sld, t_b)

    def l2norm_phase(which, bc0, out_tiles_d, s_accum):
        wch, t_w = load_weight16(p, weights[which], w16, sw, D, D)
        t_proj = proj_fm(p, wch, ln_fm, scr_fm, pool,
                         waits=[(sw, t_w), (sfm, t_fm)],
                         bias_cols=bcols[:, bc0:bc0 + 8])
        for c in range(8):
            S.wait(V.sem, t_proj)
            S("activation", sqs[:, :], scr_fm[c][:, PAD:PAD + T], AF.Square,
              accum_out=nrm[:, c:c + 1])
        S("activation", nrm[:, 0:8], nrm[:, 0:8], AF.Ln)
        t_rs = S("activation", nrm[:, 0:8], nrm[:, 0:8], AF.Exp, scale=-0.5)
        for c in range(8):
            V.wait(S.sem, t_rs)
            V("tensor_scalar_mul", scr_fm[c][:, PAD:PAD + T], scr_fm[c][:, PAD:PAD + T],
              nrm[:, c:c + 1])
        t_nrm = V.sem.n

        tile_st = {}
        s_last = None
        for j in range(NT):
            i = j % 2
            if j - 2 in tile_st:
                V.wait(sst, tile_st[j - 2])
            for c in range(8):
                pk, war = trp.get()
                if war is not None:
                    Tn.wait(war[0], war[1])
                Tn.wait(V.sem, t_nrm)
                t_tr = Tn.sem.inc()
                nc.tensor.transpose(pk[:, :],
                                    scr_fm[c][:, PAD + j * 128:PAD + (j + 1) * 128],
                                    ident16).then_inc(Tn.sem.h, 1)
                V.wait(Tn.sem, t_tr)
                t_e = V("tensor_copy", tmt[i][:, c * 128:(c + 1) * 128], pk[:, :])
                trp.done(V.sem, t_e)
            t_tile = V.sem.n
            if which == "l1a_wkT":
                V("reduce_sum", ksf[:, j:j + 1], tmt[i][:, :], axis=AX.X)
                V("tensor_scalar_add", kse[:, j:j + 1], ksf[:, j:j + 1], 1e-6)
            if s_accum:
                Tn.wait(V.sem, t_tile)
                for h2 in range(2):
                    s_last = nc.tensor.matmul(
                        ps_s[0:1, h2 * 512:(h2 + 1) * 512], kse[:, j:j + 1],
                        tmt[i][:, h2 * 512:(h2 + 1) * 512],
                        start=(j == 0), stop=(j == NT - 1), skip_group_check=True)
            p.qwait("gpsimd", V.sem, t_tile)
            p.dma("gpsimd", out_tiles_d[j * 128:(j + 1) * 128, :], tmt[i][:, :], sst)
            tile_st[j] = sst.n
        if s_accum:
            t_s = Tn.sem.inc()
            s_last.then_inc(Tn.sem.h, 1)
            return t_s
        return None

    l2norm_phase("l1a_wkT", 8, kn_tm_d, s_accum=False)
    t_sdone = l2norm_phase("l1a_wqT", 0, qn_tm_d, s_accum=True)
    V.wait(Tn.sem, t_sdone)
    t_srow = V("tensor_copy", srow[0:1, :], ps_s[0:1, :])

    wch, t_w = load_weight16(p, weights["l1a_wvT"], w16, sw, D, D)
    t_vproj = proj_fm(p, wch, ln_fm, scr_fm, pool, waits=[(sw, t_w)],
                      bias_cols=bcols[:, 16:24])
    v_fm = scr_fm
    for ts in range(8):
        pk, war = pool.get()
        if war is not None:
            Tn.wait(war[0], war[1])
        Tn.wait(V.sem, t_vproj)
        t_mm = mm_group(p, pk[0:1, :], [
            (ones16[:, 0:1], v_fm[c][:, PAD + ts * 512:PAD + (ts + 1) * 512])
            for c in range(8)])
        V.wait(Tn.sem, t_mm)
        t_e = V("tensor_copy", vs16[0:1, ts * 512:(ts + 1) * 512], pk[0:1, :])
        pool.done(V.sem, t_e)
    t_vs = V.sem.n

    dance.cols_from_row(srow[0:1, :], scols[:, :], 8, ident32,
                        waits=[(V.sem, t_srow)])
    V("tensor_scalar_add", scols[:, :], scols[:, :], 1024.0)
    V("reciprocal", scols[:, :], scols[:, :])
    pk, war = pool.get()
    if war is not None:
        Tn.wait(war[0], war[1])
    Tn.wait(sld, t_g)
    t_mm = mm_group(p, pk[0:128, 0:1], [(ones32[0:1, 0:128], gam[0:1, 0:1])])
    V.wait(Tn.sem, t_mm)
    t_e = V("tensor_copy", gcol[:, :], pk[0:128, 0:1])
    pool.done(V.sem, t_e)
    t_tg = V("tensor_scalar_mul", scols[:, :], scols[:, :], gcol[:, 0:1])
    t_tgrow = dance.rows_from_cols(scols[:, :], tgrow[0:1, :], 8, ident32,
                                   waits=[(V.sem, t_tg)])
    V.wait(dance.sem, t_tgrow)
    t_tg16 = V("tensor_copy", tg16[0:1, :], tgrow[0:1, :])
    for h2 in range(2):
        pk, war = pool.get()
        if war is not None:
            Tn.wait(war[0], war[1])
        Tn.wait(V.sem, t_tg16)
        t_mm = mm_group(p, pk[:, :], [(ones32[0:1, 0:128],
                                       tgrow[0:1, h2 * 512:(h2 + 1) * 512])])
        V.wait(Tn.sem, t_mm)
        t_e = V("tensor_copy", tb[:, h2 * 512:(h2 + 1) * 512], pk[:, :])
        pool.done(V.sem, t_e)
    t_tb = V.sem.n

    full_barrier(p)
    es1.close()

    a16 = [p.sb(f"l1a_a16_{c}", [128, D], BF16) for c in range(8)]
    es2 = contextlib.ExitStack()
    apool = [p.ps(f"l1a_ap{i}", [128, 512], F32, es2) for i in range(8)]
    qt = [p.sb(f"l1a_qt{i}", [128, 512], BF16, es2) for i in range(2)]
    kt = [p.sb(f"l1a_kt{i}", [128, D], BF16, es2) for i in range(2)]
    qk_ld = p.sem("l1a_qkld")
    t_aev = 0
    for half in range(2):
        amm = {}
        for j in range(NT):
            i = j % 2
            if j - 2 in amm:
                nc.sync.wait_ge(Tn.sem.h, amm[j - 2])
            if j == 0 and half == 1:
                Tn.wait(V.sem, t_aev)
            t_q = p.dma("sync", kt[i][:, :], kn_tm_d[j * 128:(j + 1) * 128, :], qk_ld)
            t_q = p.dma("sync", qt[i][:, :],
                        qn_tm_d[j * 128:(j + 1) * 128, half * 512:(half + 1) * 512], qk_ld)
            Tn.wait(qk_ld, t_q)
            last = None
            for fb in range(8):
                last = nc.tensor.matmul(apool[fb][:, :], kt[i][:, fb * 128:(fb + 1) * 128],
                                        qt[i][:, :], start=(j == 0), stop=(j == NT - 1),
                                        skip_group_check=True)
            amm[j] = Tn.sem.inc()
            last.then_inc(Tn.sem.h, 1)
        V.wait(Tn.sem, amm[NT - 1])
        for fb in range(8):
            V("tensor_mul", a16[fb][:, half * 512:(half + 1) * 512],
              apool[fb][:, :], tb[:, half * 512:(half + 1) * 512])
        t_aev = V.sem.n
    full_barrier(p)
    es2.close()

    es3 = contextlib.ExitStack()
    opool = PsumPool(p, "l1a_op", 4, [128, 512], es=es3)
    lnt = [p.sb(f"l1a_lnt{i}", [128, D], F32, es3) for i in range(2)]
    sub = [p.sb(f"l1a_sub{i}", [128, D], F32, es3) for i in range(2)]
    sub_done = {}
    st_done = {}
    for j in range(NT):
        i = j % 2
        if j - 2 in st_done:
            V.wait(sst, st_done[j - 2])
        if j - 2 in sub_done:
            p.qwait("sync", V.sem, sub_done[j - 2])
        t_ln = p.dma("sync", lnt[i][:, :], io["ln32_tm"][j * 128:(j + 1) * 128, :], sld)
        t_e = None
        for h2 in range(2):
            pk, war = opool.get()
            if war is not None:
                Tn.wait(war[0], war[1])
            pairs = [(v_fm[fb][:, PAD + j * 128:PAD + (j + 1) * 128],
                      a16[fb][:, h2 * 512:(h2 + 1) * 512]) for fb in range(8)]
            pairs.append((vs16[0:1, j * 128:(j + 1) * 128],
                          tg16[0:1, h2 * 512:(h2 + 1) * 512]))
            t_mm = mm_group(p, pk[:, :], pairs)
            V.wait(Tn.sem, t_mm)
            V.wait(sld, t_ln)
            t_e = V("scalar_tensor_tensor", sub[i][:, h2 * 512:(h2 + 1) * 512],
                    pk[:, :], 1.0, lnt[i][:, h2 * 512:(h2 + 1) * 512],
                    ALU.bypass, ALU.add)
            opool.done(V.sem, t_e)
        sub_done[j] = t_e
        p.qwait("gpsimd", V.sem, t_e)
        p.dma("gpsimd", io["sub_tm"][j * 128:(j + 1) * 128, :], sub[i][:, :], sst)
        st_done[j] = sst.n
    full_barrier(p)
    es3.close()
    io["sub_done"] = (sst, sst.n)


def host_prep_L1_attn(inputs):
    w = {}
    w["l1a_wqT"] = np.ascontiguousarray(inputs["L1_attn_Wq"].T)
    w["l1a_wkT"] = np.ascontiguousarray(inputs["L1_attn_Wk"].T)
    w["l1a_wvT"] = np.ascontiguousarray(inputs["L1_attn_Wv"].T)
    b = np.stack([inputs["L1_attn_bq"], inputs["L1_attn_bk"], inputs["L1_attn_bv"]])
    w["l1a_bcols"] = np.ascontiguousarray(
        b.reshape(3, 8, 128).transpose(2, 0, 1).reshape(128, 24)[:, [0, 1, 2, 3, 4, 5, 6, 7,
                                                                     8, 9, 10, 11, 12, 13, 14, 15,
                                                                     16, 17, 18, 19, 20, 21, 22, 23]])
    w["l1a_gamma"] = inputs["L1_attn_gamma"].reshape(1, 1)
    return w


L1A_WEIGHT_SPECS = {
    "l1a_wqT": ([D, D], F32),
    "l1a_wkT": ([D, D], F32),
    "l1a_wvT": ([D, D], F32),
    "l1a_bcols": ([128, 24], F32),
    "l1a_gamma": ([1, 1], F32),
}


def build_stage(sub_emit, weight_specs, rms2=False, need_ln32=False):
    p = Prog()
    x_in = p.inp("x", [T, D], F32)
    x_out = p.out("x_out", [T, D], F32)
    weights = {n: p.inp(n, sh, dt) for n, (sh, dt) in weight_specs.items()}
    p.prealloc_cbuf()
    consts = {}
    for n, (sh, dt) in CONST_SPECS.items():
        ap = p.inp(n, sh, dt)
        sb = p.sb(n + "_sb", sh, dt)
        consts[n + "_dram"] = ap
        consts[n.replace("c_", "") + "_sb"] = sb[:, :]
    io = {}
    io["ln16_tm"] = p.dram("ln16_tm", [T, D], BF16)
    io["ln32_tm"] = (p.out if DBG else p.dram)("ln32_tm", [T, D], F32)
    io["sub_tm"] = (p.out if DBG else p.dram)("sub_tm", [T, D], F32)

    sc = p.sem("consts")
    for n in CONST_SPECS:
        p.dma("sync", consts[n.replace("c_", "") + "_sb"][:, :], consts[n + "_dram"][:, :], sc)
    p.qwait("sync", sc, sc.n)

    ldx, sg = p.sem("ldx"), p.sem("sg")
    es_pre = contextlib.ExitStack()
    ln = LnPass(p, "pre", es=es_pre, need32=need_ln32, needadd=False)
    t_pre = ln.emit(ldx, sg, x_in, out16=io["ln16_tm"],
                    out32=(io["ln32_tm"] if need_ln32 else None), rms2=rms2)
    io["pre_done"] = (sg, t_pre)
    barrier(p, io["pre_done"])
    es_pre.close()

    sub_emit(p, io, weights, consts)

    barrier(p, io["sub_done"])
    es_post = contextlib.ExitStack()
    ln2 = LnPass(p, "post", es=es_post, need16=False)
    t_post = ln2.emit(ldx, sg, io["sub_tm"], out32=x_out, add=x_in)
    p.qwait("sync", sg, t_post)
    es_post.close()
    return p


def build_stage_L1_attn():
    return build_stage(emit_L1_attn, L1A_WEIGHT_SPECS, need_ln32=True)



def attn_core_head(p, Kaug, Qaug, vslice, o32, pexp, psB, psO, n_aug,
                   kt_lists, pre_exp=None, exp_scale=1.0, exp_bias_ap=None):
    nc = p.nc
    V, S, Tn = p.V, p.S, p.Tn
    pexp_war = [0, 0]
    t_o = 0
    for qc in range(8):
        po, warO = psO.get()
        if warO is not None:
            Tn.wait(warO[0], warO[1])
        kts = kt_lists[qc]
        t_av = 0
        for idx, (kt, mask_ap) in enumerate(kts):
            pb, warB = psB.get()
            if warB is not None:
                Tn.wait(warB[0], warB[1])
            last = nc.tensor.matmul(pb[:, :], Kaug[0:n_aug, kt * 128:(kt + 1) * 128],
                                    Qaug[0:n_aug, qc * 512:(qc + 1) * 512],
                                    start=True, stop=True)
            t_b = Tn.sem.inc()
            last.then_inc(Tn.sem.h, 1)
            t_fix = None
            if pre_exp is not None:
                V.wait(Tn.sem, t_b)
                t_fix = pre_exp(pb, qc, kt)
            if mask_ap is not None:
                if t_fix is None:
                    V.wait(Tn.sem, t_b)
                t_fix = V("tensor_add", pb[:, :], pb[:, :], mask_ap)
            b = idx % 2
            S.wait(Tn.sem, t_b)
            if t_fix is not None:
                S.wait(V.sem, t_fix)
            if pexp_war[b]:
                S.wait(Tn.sem, pexp_war[b])
            if exp_bias_ap is not None:
                t_e = S("activation", pexp[b][:, :], pb[:, :], AF.Exp,
                        scale=float(exp_scale), bias=exp_bias_ap)
            else:
                t_e = S("activation", pexp[b][:, :], pb[:, :], AF.Exp,
                        scale=float(exp_scale))
            psB.done(p.S.sem, t_e)
            Tn.wait(S.sem, t_e)
            last = nc.tensor.matmul(po[:, :], vslice(kt), pexp[b][:, :],
                                    start=(idx == 0), stop=(idx == len(kts) - 1),
                                    skip_group_check=True)
            t_av = Tn.sem.inc()
            last.then_inc(Tn.sem.h, 1)
            pexp_war[b] = t_av
        V.wait(Tn.sem, t_av)
        t_o = V("tensor_copy", o32[0:65, qc * 512:(qc + 1) * 512], po[:, :])
        psO.done(p.V.sem, t_o)
    return t_o


def attn_divide_head(p, o32, o16h, dance, lrow, lrec, lcols, pool, ident32,
                     ones32, t_o):
    V, Tn = p.V, p.Tn
    t_l = V("tensor_copy", lrow[0:1, :], o32[64:65, :])
    dance.cols_from_row(lrow[0:1, :], lcols[:, :], 32, ident32,
                        waits=[(V.sem, t_l)])
    V("reciprocal", lcols[:, :], lcols[:, :])
    t_r = p.V.sem.n
    t_row = dance.rows_from_cols(lcols[:, :], lrec[0:1, :], 32, ident32,
                                 waits=[(V.sem, t_r)])
    t_e = 0
    for ts in range(8):
        pk, war = pool.get()
        if war is not None:
            Tn.wait(war[0], war[1])
        Tn.wait(dance.sem, t_row)
        t_mm = mm_group(p, pk[0:64, :], [(ones32[0:1, 0:64],
                                          lrec[0:1, ts * 512:(ts + 1) * 512])])
        V.wait(Tn.sem, t_mm)
        t_e = V("tensor_mul", o16h[0:64, ts * 512:(ts + 1) * 512],
                o32[0:64, ts * 512:(ts + 1) * 512], pk[0:64, :])
        pool.done(p.V.sem, t_e)
    return t_e


def attn_out_proj(p, io, o_fm_d, wo_dram, bias_row_dram, sld, sst, n_of=8, tag="aop"):
    nc = p.nc
    V, Tn = p.V, p.Tn
    es = contextlib.ExitStack()
    ofm = [p.sb(f"{tag}_ofm{c}", [128, T], BF16, es) for c in range(n_of)]
    wo16 = p.sb(f"{tag}_wo16", [128, n_of * D], BF16, es)
    brow = p.sb(f"{tag}_brow", [1, D], F32, es)
    brow16 = p.sb(f"{tag}_brow16", [1, D], BF16, es)
    sub = [p.sb(f"{tag}_sub{i}", [128, D], F32, es) for i in range(2)]
    opool = PsumPool(p, f"{tag}_op", 4, [128, 512], es=es)
    sw = p.sem(f"{tag}_sw")
    for c in range(n_of):
        p.dma("sync", ofm[c][:, :], o_fm_d[c * 128:(c + 1) * 128, :], sld)
    t_of = sld.n
    wch, t_w = load_weight16(p, wo_dram, wo16, sw, n_of * 128, D)
    if bias_row_dram is not None:
        t_b = p.dma("sync", brow[0:1, :], bias_row_dram[:, :], sld)
        V.wait(sld, t_b)
        t_b16 = V("tensor_copy", brow16[0:1, :], brow[0:1, :])
    sub_done = {}
    st_done = {}
    for j in range(NT):
        i = j % 2
        if j - 2 in st_done:
            V.wait(sst, st_done[j - 2])
        t_e = None
        for ns in range(2):
            pk, war = opool.get()
            if war is not None:
                Tn.wait(war[0], war[1])
            Tn.wait(sld, t_of)
            Tn.wait(sw, t_w)
            pairs = [(ofm[c][:, j * 128:(j + 1) * 128],
                      wo16[:, (c * 2 + ns) * 512:(c * 2 + ns + 1) * 512])
                     for c in range(n_of)]
            if bias_row_dram is not None:
                Tn.wait(V.sem, t_b16)
                pairs.append((p._ones16_row[0:1, 0:128],
                              brow16[0:1, ns * 512:(ns + 1) * 512]))
            t_mm = mm_group(p, pk[:, :], pairs)
            V.wait(Tn.sem, t_mm)
            t_e = V("tensor_copy", sub[i][:, ns * 512:(ns + 1) * 512], pk[:, :])
            opool.done(p.V.sem, t_e)
        sub_done[j] = t_e
        p.qwait("gpsimd", V.sem, t_e)
        p.dma("gpsimd", io["sub_tm"][j * 128:(j + 1) * 128, :], sub[i][:, :], sst)
        st_done[j] = sst.n
    full_barrier(p)
    es.close()
    return sst.n



def emit_L3_attn(p, io, weights, consts):
    nc = p.nc
    V, S, G, Tn = p.V, p.S, p.G, p.Tn
    ident16 = consts["ident16_sb"]
    ident32 = consts["ident32_sb"]
    ones16 = consts["ones16_sb"]
    ones32 = consts["ones32_sb"]
    p._ones16_row = ones16
    CW = PAD + T

    o_fm_d = (p.out if DBG else p.dram)("l3a_ofm", [D, T], BF16)
    if DBG:
        dbg_qaug = p.out("dbg_qaug", [65, T], BF16)
        dbg_kaug = p.out("dbg_kaug", [65, T], BF16)
        dbg_o32 = p.out("dbg_o32", [65, T], F32)
        dbg_vt = p.out("dbg_vt", [128, NT * 130], BF16)
        dbg_nrow = p.out("dbg_nrow", [1, T], F32)
        dbg_kmax = p.out("dbg_kmax", [1, 1], F32)
    sld = p.sem("l3a_sld")
    sst = p.sem("l3a_sst")
    sfm = p.sem("l3a_sfm")
    sw = p.sem("l3a_sw")

    es1 = contextlib.ExitStack()
    ln_fm = [p.sb(f"l3a_lnfm{c}", [128, CW], BF16, es1) for c in range(8)]
    w16 = p.sb("l3a_w16", [128, 3 * 8 * 128], BF16, es1)
    nscol = p.sb("l3a_ns_sb", [128, 8], F32, es1)
    qp = p.sb("l3a_qp", [128, T], BF16, es1)
    Qaug = [p.sb(f"l3a_qaug{i}", [65, T], BF16, es1) for i in range(2)]
    Kaug = [p.sb(f"l3a_kaug{i}", [65, T], BF16, es1) for i in range(2)]
    Vt = p.sb("l3a_vt", [128, NT * 130], BF16, es1)
    sqs = p.sb("l3a_sqs", [64, T], BF16, es1)
    nrow = p.sb("l3a_nrow", [1, T], F32, es1)
    kmaxc = p.sb("l3a_kmaxc", [1, 1], F32, es1)
    o32 = p.sb("l3a_o32", [65, T], F32, es1)
    o16h = p.sb("l3a_o16h", [64, T], BF16, es1)
    lrow = p.sb("l3a_lrow", [1, T], F32, es1)
    lrec = p.sb("l3a_lrec", [1, T], F32, es1)
    lcols = p.sb("l3a_lcols", [128, 32], F32, es1)
    pexp = [p.sb(f"l3a_pexp{i}", [128, 512], BF16, es1) for i in range(2)]

    pool = PsumPool(p, "l3a_pp", 2, [128, 512], es=es1)
    trp = PsumPool(p, "l3a_tr", 2, [128, 128], BF16, es=es1)
    psB = PsumPool(p, "l3a_psB", 2, [128, 512], es=es1)
    psO = PsumPool(p, "l3a_psO", 1, [65, 512], es=es1)
    dance = Dance(p, "l3a_dance", es=es1)

    p.qwait("sync", io["pre_done"][0], io["pre_done"][1])
    t_fm = emit_fm_load(p, io["ln16_tm"], ln_fm, sfm)
    t_ns = p.dma("sync", nscol[:, :], weights["l3a_ns"][:, :], sld)
    V.wait(sld, t_ns)

    kt_lists = {qc: [(kt, None) for kt in range(NT)] for qc in range(8)}
    WNAMES = ("l3a_wq", "l3a_wk", "l3a_wv")

    for pair in range(8):
        p.qwait("gpsimd", Tn.sem, Tn.sem.n)
        wch = {}
        for wi, wname in enumerate(WNAMES):
            for c in range(8):
                ap = w16[:, (wi * 8 + c) * 128:(wi * 8 + c + 1) * 128]
                p.dma("gpsimd", ap,
                      weights[wname][c * 128:(c + 1) * 128,
                                     pair * 128:(pair + 1) * 128], sw)
            wch[wname] = [w16[:, (wi * 8 + c) * 128:(wi * 8 + c + 1) * 128]
                          for c in range(8)]
        t_w = sw.n
        V.wait(sw, t_w)
        t_scale = 0
        for wi, wname in enumerate(WNAMES):
            qscale = 0.125 if wname == "l3a_wq" else 1.0
            for c in range(8):
                t_scale = V("tensor_scalar", wch[wname][c], wch[wname][c],
                            nscol[:, c:c + 1], qscale, ALU.mult, ALU.mult)

        def pair_proj(wname, split_dests):
            t_last = 0
            for ts in range(8):
                pk, war = pool.get()
                if war is not None:
                    Tn.wait(war[0], war[1])
                Tn.wait(V.sem, t_scale)
                Tn.wait(sfm, t_fm)
                t_mm = mm_group(p, pk[:, :], [
                    (wch[wname][ic], ln_fm[ic][:, PAD + ts * 512:PAD + (ts + 1) * 512])
                    for ic in range(8)])
                V.wait(Tn.sem, t_mm)
                if split_dests[1] is None:
                    t_last = V("tensor_copy", split_dests[0][:, ts * 512:(ts + 1) * 512],
                               pk[:, :])
                else:
                    V("tensor_copy", split_dests[0][0:64, ts * 512:(ts + 1) * 512],
                      pk[0:64, :])
                    t_last = V("tensor_copy",
                               split_dests[1][0:64, ts * 512:(ts + 1) * 512],
                               pk[64:128, :])
                pool.done(p.V.sem, t_last)
            return t_last

        pair_proj("l3a_wq", (Qaug[0], Qaug[1]))
        pair_proj("l3a_wk", (Kaug[0], Kaug[1]))
        t_v = pair_proj("l3a_wv", (qp, None))

        if pair == 0:
            V("memset", Vt[:, :], 1.0)
        for j in range(NT):
            pk, war = trp.get()
            if war is not None:
                Tn.wait(war[0], war[1])
            Tn.wait(V.sem, t_v)
            t_tr = Tn.sem.inc()
            nc.tensor.transpose(pk[:, :], qp[:, j * 128:(j + 1) * 128],
                                ident16).then_inc(Tn.sem.h, 1)
            V.wait(Tn.sem, t_tr)
            V("tensor_copy", Vt[:, j * 130:j * 130 + 64], pk[:, 0:64])
            t_e = V("tensor_copy", Vt[:, j * 130 + 65:j * 130 + 129], pk[:, 64:128])
            trp.done(p.V.sem, t_e)

        for hh in range(2):
            QA, KA = Qaug[hh], Kaug[hh]
            V("memset", KA[64:65, :], 1.0)

            def norm_row(A):
                S.wait(V.sem, V.sem.n)
                t_sq = S("activation", sqs[0:64, :], A[0:64, :], AF.Square)
                for ts in range(8):
                    pk, war = pool.get()
                    if war is not None:
                        Tn.wait(war[0], war[1])
                    Tn.wait(S.sem, t_sq)
                    t_mm = mm_group(p, pk[0:1, :], [
                        (ones16[0:64, 0:1], sqs[0:64, ts * 512:(ts + 1) * 512])])
                    V.wait(Tn.sem, t_mm)
                    t_l = V("tensor_copy", nrow[0:1, ts * 512:(ts + 1) * 512],
                            pk[0:1, :])
                    pool.done(p.V.sem, t_l)
                S.wait(V.sem, t_l)
                return S("activation", nrow[0:1, :], nrow[0:1, :], AF.Sqrt)

            t_kn = norm_row(KA)
            V.wait(S.sem, t_kn)
            t_km = V("reduce_max", kmaxc[0:1, 0:1], nrow[0:1, :], axis=AX.X)
            t_qn = norm_row(QA)
            V.wait(S.sem, t_qn)
            t_bnd = V("tensor_scalar", QA[64:65, :], nrow[0:1, :], kmaxc[0:1, 0:1],
                      -1.0, ALU.mult, ALU.mult)
            full_barrier(p)
            h = pair * 2 + hh
            attn_core_head(p, KA, QA,
                           lambda kt, hh=hh: Vt[:, kt * 130 + hh * 65:kt * 130 + hh * 65 + 65],
                           o32, pexp, psB, psO, 65, kt_lists)
            if DBG and h == 0:
                full_barrier(p)
                p.dma("gpsimd", dbg_nrow[:, :], nrow[0:1, :], sst)
                p.dma("gpsimd", dbg_kmax[:, :], kmaxc[0:1, :], sst)
                p.dma("gpsimd", dbg_qaug[:, :], QA[0:65, :], sst)
                p.dma("gpsimd", dbg_kaug[:, :], KA[0:65, :], sst)
                p.dma("gpsimd", dbg_o32[:, :], o32[0:65, :], sst)
                p.dma("gpsimd", dbg_vt[:, :], Vt[:, :], sst)
            V.wait(sst, sst.n)
            t_d = attn_divide_head(p, o32, o16h, dance, lrow, lrec, lcols, pool,
                                   ident32, ones32, 0)
            p.qwait("gpsimd", V.sem, t_d)
            p.dma("gpsimd", o_fm_d[h * 64:(h + 1) * 64, :], o16h[0:64, :], sst)

    full_barrier(p)
    es1.close()

    p.qwait("sync", sst, sst.n)
    attn_out_proj(p, io, o_fm_d, weights["l3a_wo"], weights["l3a_bout"], sld, sst, tag="l3aop")
    io["sub_done"] = (sst, sst.n)


def host_prep_L3_attn(inputs):
    w = {}
    w["l3a_wq"] = np.ascontiguousarray(inputs["L3_attn_wq"].reshape(D, D))
    w["l3a_wk"] = np.ascontiguousarray(inputs["L3_attn_wk"].reshape(D, D))
    w["l3a_wv"] = np.ascontiguousarray(inputs["L3_attn_wv"].reshape(D, D))
    w["l3a_wo"] = np.ascontiguousarray(inputs["L3_attn_w_out"].reshape(D, D))
    w["l3a_ns"] = np.ascontiguousarray(inputs["L3_attn_norm_scale"].reshape(8, 128).T)
    w["l3a_bout"] = np.ascontiguousarray(inputs["L3_attn_b_out"].reshape(1, D))
    return w


L3A_WEIGHT_SPECS = {
    "l3a_wq": ([D, D], F32),
    "l3a_wk": ([D, D], F32),
    "l3a_wv": ([D, D], F32),
    "l3a_wo": ([D, D], F32),
    "l3a_ns": ([128, 8], F32),
    "l3a_bout": ([1, D], F32),
}


def build_stage_L3_attn():
    return build_stage(emit_L3_attn, L3A_WEIGHT_SPECS, rms2=True)



def emit_L0_attn(p, io, weights, consts):
    nc = p.nc
    V, S, G, Tn = p.V, p.S, p.G, p.Tn
    ident16 = consts["ident16_sb"]
    ident32 = consts["ident32_sb"]
    ones16 = consts["ones16_sb"]
    ones32 = consts["ones32_sb"]
    p._ones16_row = ones16
    CW = PAD + T

    o_fm_d = p.dram("l0a_ofm", [D, T], BF16)
    sld = p.sem("l0a_sld")
    sst = p.sem("l0a_sst")
    sfm = p.sem("l0a_sfm")
    sw = p.sem("l0a_sw")

    es1 = contextlib.ExitStack()
    ln_fm = [p.sb(f"l0a_lnfm{c}", [128, CW], BF16, es1) for c in range(8)]
    w16 = p.sb("l0a_w16", [128, 3 * 8 * 128], BF16, es1)
    qp = p.sb("l0a_qp", [128, T], BF16, es1)
    Qaug = [p.sb(f"l0a_qaug{i}", [66, T], BF16, es1) for i in range(2)]
    Kaug = [p.sb(f"l0a_kaug{i}", [66, T], BF16, es1) for i in range(2)]
    Vt = p.sb("l0a_vt", [128, NT * 130], BF16, es1)
    o32 = p.sb("l0a_o32", [65, T], F32, es1)
    o16h = p.sb("l0a_o16h", [64, T], BF16, es1)
    lrec = p.sb("l0a_lrec", [1, T], F32, es1)
    lcols = p.sb("l0a_lcols", [128, 32], F32, es1)
    pexp = [p.sb(f"l0a_pexp{i}", [128, 512], BF16, es1) for i in range(2)]
    mcols = p.sb("l0a_mcols", [128, NT], F32, es1)
    mtmp = p.sb("l0a_mtmp", [128, 1], F32, es1)
    mrow = lrow = p.sb("l0a_lrow2", [1, T], F32, es1)
    dirs16 = p.sb("l0a_dirs16", [8, 16], BF16, es1)
    hs8 = p.sb("l0a_hs8", [1, 16], F32, es1)
    maskAw = p.sb("l0a_maskA_sb", [128, 896], F32, es1)
    maskBw = p.sb("l0a_maskB_sb", [128, 896], F32, es1)
    row16 = p.sb("l0a_row16", [1, T], BF16, es1)
    srow = p.sem("l0a_srow")

    def maskA(idx):
        return maskAw[:, 384 - 128 * idx:896 - 128 * idx]

    def maskB(idx):
        return maskBw[:, 384 - 128 * idx:896 - 128 * idx]

    pool = PsumPool(p, "l0a_pp", 2, [128, 512], es=es1)
    trp = PsumPool(p, "l0a_tr", 2, [128, 128], BF16, es=es1)
    psB = PsumPool(p, "l0a_psB", 2, [128, 512], es=es1)
    psO = PsumPool(p, "l0a_psO", 1, [65, 512], es=es1)
    dance = Dance(p, "l0a_dance", es=es1)

    p.qwait("sync", io["pre_done"][0], io["pre_done"][1])
    t_fm = emit_fm_load(p, io["ln16_tm"], ln_fm, sfm)
    t_c = p.dma("sync", maskAw[:, :], weights["l0a_maskA"][:, :], sld)
    t_c = p.dma("sync", maskBw[:, :], weights["l0a_maskB"][:, :], sld)
    t_c = p.dma("gpsimd", dirs16[:, :], weights["l0a_dirsT"][:, :], sld)
    t_c = p.dma("sync", hs8[:, :], weights["l0a_hs"][:, :], sld)
    V.wait(sld, t_c)
    t_hs8 = V("tensor_scalar_mul", hs8[0:1, :], hs8[0:1, :], 8.0)

    t_r16 = V("memset", row16[0:1, :], 1.0)
    p.qwait("gpsimd", V.sem, t_r16)
    p.dma("gpsimd", Kaug[0][65:66, :], row16[0:1, :], srow)
    p.dma("gpsimd", Kaug[1][65:66, :], row16[0:1, :], srow)

    kt_lists = {}
    for qc in range(8):
        lst = []
        for kt in range(4 * (qc + 1)):
            if kt >= 4 * qc:
                lst.append((kt, maskB(kt - 4 * qc)))
            else:
                lst.append((kt, None))
        kt_lists[qc] = lst

    for pair in range(8):
        p.qwait("gpsimd", Tn.sem, Tn.sem.n)
        wch = {}
        for wi in range(3):
            for c in range(8):
                ap = w16[:, (wi * 8 + c) * 128:(wi * 8 + c + 1) * 128]
                p.dma("gpsimd", ap,
                      weights["l0a_wqkv"][c * 128:(c + 1) * 128,
                                          wi * 1024 + pair * 128:wi * 1024 + (pair + 1) * 128],
                      sw)
            wch[wi] = [w16[:, (wi * 8 + c) * 128:(wi * 8 + c + 1) * 128]
                       for c in range(8)]
        t_w = sw.n
        V.wait(sw, t_w)
        t_scale = 0
        for c in range(8):
            t_scale = V("tensor_scalar_mul", wch[1][c], wch[1][c], 0.125)

        def pair_proj(wi, split_dests, wait_t):
            t_last = 0
            for ts in range(8):
                pk, war = pool.get()
                if war is not None:
                    Tn.wait(war[0], war[1])
                Tn.wait(V.sem, wait_t)
                Tn.wait(sfm, t_fm)
                t_mm = mm_group(p, pk[:, :], [
                    (wch[wi][ic], ln_fm[ic][:, PAD + ts * 512:PAD + (ts + 1) * 512])
                    for ic in range(8)])
                V.wait(Tn.sem, t_mm)
                if split_dests[1] is None:
                    t_last = V("tensor_copy", split_dests[0][:, ts * 512:(ts + 1) * 512],
                               pk[:, :])
                else:
                    V("tensor_copy", split_dests[0][0:64, ts * 512:(ts + 1) * 512],
                      pk[0:64, :])
                    t_last = V("tensor_copy",
                               split_dests[1][0:64, ts * 512:(ts + 1) * 512],
                               pk[64:128, :])
                pool.done(p.V.sem, t_last)
            return t_last

        pair_proj(0, (Qaug[0], Qaug[1]), t_scale)
        pair_proj(1, (Kaug[0], Kaug[1]), t_scale)
        t_v = pair_proj(2, (qp, None), t_scale)

        if pair == 0:
            V("memset", Vt[:, :], 1.0)
        for j in range(NT):
            pk, war = trp.get()
            if war is not None:
                Tn.wait(war[0], war[1])
            Tn.wait(V.sem, t_v)
            t_tr = Tn.sem.inc()
            nc.tensor.transpose(pk[:, :], qp[:, j * 128:(j + 1) * 128],
                                ident16).then_inc(Tn.sem.h, 1)
            V.wait(Tn.sem, t_tr)
            V("tensor_copy", Vt[:, j * 130:j * 130 + 64], pk[:, 0:64])
            t_e = V("tensor_copy", Vt[:, j * 130 + 65:j * 130 + 129], pk[:, 64:128])
            trp.done(p.V.sem, t_e)

        for hh in range(2):
            h = 2 * pair + hh
            QA, KA = Qaug[hh], Kaug[hh]
            for (A, dst_mul) in ((QA, None), (KA, None)):
                pass
            for ts in range(8):
                pk, war = pool.get()
                if war is not None:
                    Tn.wait(war[0], war[1])
                Tn.wait(V.sem, max(t_hs8, p.V.sem.n if False else t_hs8))
                t_mm = mm_group(p, pk[0:1, :], [
                    (dirs16[0:8, h:h + 1], QA[0:8, ts * 512:(ts + 1) * 512])])
                V.wait(Tn.sem, t_mm)
                t_q64 = V("tensor_scalar_mul", QA[64:65, ts * 512:(ts + 1) * 512],
                          pk[0:1, :], hs8[0:1, h:h + 1])
                pool.done(p.V.sem, t_q64)
            for ts in range(8):
                pk, war = pool.get()
                if war is not None:
                    Tn.wait(war[0], war[1])
                t_mm = mm_group(p, pk[0:1, :], [
                    (dirs16[0:8, h:h + 1], KA[0:8, ts * 512:(ts + 1) * 512])])
                V.wait(Tn.sem, t_mm)
                t_k64 = V("tensor_copy", KA[64:65, ts * 512:(ts + 1) * 512], pk[0:1, :])
                pool.done(p.V.sem, t_k64)

            t_mc = V("memset", mcols[:, :], NEG)
            full_barrier(p)
            for j in range(NT):
                for c in range(j // 4 + 1):
                    pb, war = psB.get()
                    if war is not None:
                        Tn.wait(war[0], war[1])
                    last = nc.tensor.matmul(pb[:, :], QA[0:65, j * 128:(j + 1) * 128],
                                            KA[0:65, c * 512:(c + 1) * 512],
                                            start=True, stop=True)
                    t_b = Tn.sem.inc()
                    last.then_inc(Tn.sem.h, 1)
                    V.wait(Tn.sem, t_b)
                    if c == j // 4:
                        V("tensor_add", pb[:, :], pb[:, :], maskA(j % 4))
                    V("reduce_max", mtmp[:, :], pb[:, :], axis=AX.X)
                    t_m = V("tensor_max", mcols[:, j:j + 1], mcols[:, j:j + 1],
                            mtmp[:, :])
                    psB.done(p.V.sem, t_m)
            dance.rows_from_cols(mcols[:, :], mrow[0:1, :], 32, ident32,
                                 waits=[(V.sem, p.V.sem.n)])
            V.wait(dance.sem, dance.sem.n)
            V.wait(srow, srow.n)
            t_m65 = V("tensor_scalar_mul", row16[0:1, :], mrow[0:1, :], -1.0)
            p.qwait("gpsimd", V.sem, t_m65)
            p.dma("gpsimd", QA[65:66, :], row16[0:1, :], srow)

            V.wait(sst, sst.n)
            full_barrier(p)
            Tn.wait(srow, srow.n)
            attn_core_head(p, KA, QA,
                           lambda kt, hh=hh: Vt[:, kt * 130 + hh * 65:kt * 130 + hh * 65 + 65],
                           o32, pexp, psB, psO, 66, kt_lists)
            t_d = attn_divide_head(p, o32, o16h, dance, lrow, lrec, lcols, pool,
                                   ident32, ones32, 0)
            p.qwait("gpsimd", V.sem, t_d)
            p.dma("gpsimd", o_fm_d[h * 64:(h + 1) * 64, :], o16h[0:64, :], sst)

    full_barrier(p)
    es1.close()
    p.qwait("sync", sst, sst.n)
    attn_out_proj(p, io, o_fm_d, weights["l0a_wo"], None, sld, sst, tag="l0aop")
    io["sub_done"] = (sst, sst.n)


def host_prep_L0_attn(inputs):
    w = {}
    w["l0a_wqkv"] = np.ascontiguousarray(inputs["L0_attn_w_qkv"])
    w["l0a_wo"] = np.ascontiguousarray(inputs["L0_attn_w_out"])
    w["l0a_dirsT"] = np.ascontiguousarray(inputs["L0_attn_head_dirs"].T)
    w["l0a_hs"] = np.ascontiguousarray(inputs["L0_attn_head_scales"].reshape(1, 16))
    qp = np.arange(128)[:, None]
    wv = np.arange(896)[None, :]
    w["l0a_maskA"] = np.where(wv > qp + 384, NEG, 0.0).astype(np.float32)
    kp = np.arange(128)[:, None]
    w["l0a_maskB"] = np.where(wv < kp + 384, NEG, 0.0).astype(np.float32)
    return w


L0A_WEIGHT_SPECS = {
    "l0a_wqkv": ([D, 3 * D], F32),
    "l0a_wo": ([D, D], F32),
    "l0a_dirsT": ([8, 16], F32),
    "l0a_hs": ([1, 16], F32),
    "l0a_maskA": ([128, 896], F32),
    "l0a_maskB": ([128, 896], F32),
}


def build_stage_L0_attn():
    return build_stage(emit_L0_attn, L0A_WEIGHT_SPECS)



def make_causal_masks():
    qp = np.arange(128)[:, None]
    wv = np.arange(896)[None, :]
    mA = np.where(wv > qp + 384, NEG, 0.0).astype(np.float32)
    kp = np.arange(128)[:, None]
    mB = np.where(wv < kp + 384, NEG, 0.0).astype(np.float32)
    return mA, mB


def causal_kt_lists(maskB):
    kt_lists = {}
    for qc in range(8):
        lst = []
        for kt in range(4 * (qc + 1)):
            lst.append((kt, maskB(kt - 4 * qc) if kt >= 4 * qc else None))
        kt_lists[qc] = lst
    return kt_lists


def emit_L2_attn(p, io, weights, consts):
    nc = p.nc
    V, S, G, Tn = p.V, p.S, p.G, p.Tn
    ident16 = consts["ident16_sb"]
    ident32 = consts["ident32_sb"]
    ones16 = consts["ones16_sb"]
    ones32 = consts["ones32_sb"]
    p._ones16_row = ones16
    CW = PAD + T
    SC96 = 1.0 / math.sqrt(96.0)

    o_fm_d = (p.out if DBG else p.dram)("l2a_ofm", [D, T], BF16)
    if DBG:
        dbg_qaug = p.out("dbg_qaug", [96, T], BF16)
        dbg_kaug = p.out("dbg_kaug", [96, T], BF16)
        dbg_o32 = p.out("dbg_o32", [65, T], F32)
    sld = p.sem("l2a_sld")
    sst = p.sem("l2a_sst")
    sfm = p.sem("l2a_sfm")
    sw = p.sem("l2a_sw")

    esA = contextlib.ExitStack()
    cq = [p.sb(f"l2a_cq{c}", [128, T], BF16, esA) for c in range(4)]
    ckv = [p.sb(f"l2a_ckv{c}", [128, T], BF16, esA) for c in range(4)]
    kr = p.sb("l2a_kr_sb", [32, T], BF16, esA)
    ctab = p.sb("l2a_ctab_sb", [128, T], BF16, esA)
    stab = p.sb("l2a_stab_sb", [128, T], BF16, esA)
    rotscr = p.sb("l2a_rot", [128, 512], BF16, esA)
    qp = p.sb("l2a_qp", [128, T], BF16, esA)
    maskBw = p.sb("l2a_maskB_sb", [128, 896], F32, esA)
    perm4 = p.sb("l2a_perm4_sb", [128, 128], BF16, esA)
    w16 = p.sb("l2a_w16", [128, 4096], BF16, esA)

    def maskB(idx):
        return maskBw[:, 384 - 128 * idx:896 - 128 * idx]

    pool = PsumPool(p, "l2a_pp", 2, [128, 512], es=esA)
    trp = PsumPool(p, "l2a_tr", 2, [128, 128], BF16, es=esA)
    psB = PsumPool(p, "l2a_psB", 2, [128, 512], es=esA)
    psO = PsumPool(p, "l2a_psO", 1, [65, 512], es=esA)
    dance = Dance(p, "l2a_dance", es=esA)

    bias_ap, t_bias = p.const_col(-80.0 * SC96)

    es0 = contextlib.ExitStack()
    ln_fm = [p.sb(f"l2a_lnfm{c}", [128, CW], BF16, es0) for c in range(8)]
    w0 = p.sb("l2a_w0", [128, 8 * 512], BF16, es0)

    p.qwait("sync", io["pre_done"][0], io["pre_done"][1])
    t_fm = emit_fm_load(p, io["ln16_tm"], ln_fm, sfm)
    t_c = p.dma("sync", maskBw[:, :], weights["l2a_maskB"][:, :], sld)
    t_c = p.dma("sync", ctab[:, :], weights["l2a_ctab"][:, :], sld)
    t_c = p.dma("sync", stab[:, :], weights["l2a_stab"][:, :], sld)
    t_c = p.dma("sync", perm4[:, :], weights["l2a_perm4"][:, :], sld)
    p.qwait("sync", sld, t_c)

    def rope32(dst_ap, src_ap, nrows, tab_off):
        t_last = 0
        for ts in range(8):
            sl = slice(ts * 512, (ts + 1) * 512)
            pk, war = psB.get()
            if war is not None:
                Tn.wait(war[0], war[1])
            Tn.wait(V.sem, V.sem.n)
            t_mm = mm_group(p, pk[0:nrows, :], [
                (perm4[0:nrows, 0:nrows], src_ap[0:nrows, sl])])
            V.wait(Tn.sem, t_mm)
            V("tensor_mul", rotscr[0:nrows, :], pk[0:nrows, :],
              stab[tab_off:tab_off + nrows, sl])
            t_ps = p.V.sem.n
            psB.done(p.V.sem, t_ps)
            V("tensor_mul", dst_ap[0:nrows, sl], src_ap[0:nrows, sl],
              ctab[tab_off:tab_off + nrows, sl])
            t_last = V("tensor_add", dst_ap[0:nrows, sl], dst_ap[0:nrows, sl],
                       rotscr[0:nrows, 0:512])
        return t_last

    for wname, dst in (("l2a_wdqT", cq), ("l2a_wdkvT", ckv)):
        wch, t_w = load_weight16(p, weights[wname], w0, sw, D, 512)
        proj_fm(p, wch, ln_fm, dst, pool,
                waits=[(sw, t_w), (sfm, t_fm)], col0=0)
    wch, t_w = load_weight16(p, weights["l2a_wkrT"], w16, sw, D, 32)
    t_krp = 0
    for ts in range(8):
        pk, war = pool.get()
        if war is not None:
            Tn.wait(war[0], war[1])
        Tn.wait(sw, t_w)
        t_mm = mm_group(p, pk[0:32, :], [
            (wch[ic], ln_fm[ic][:, PAD + ts * 512:PAD + (ts + 1) * 512])
            for ic in range(8)])
        V.wait(Tn.sem, t_mm)
        t_krp = V("tensor_copy", qp[0:32, ts * 512:(ts + 1) * 512], pk[0:32, :])
        pool.done(p.V.sem, t_krp)
    rope32(kr, qp, 32, 0)

    full_barrier(p)
    es0.close()

    qr16 = p.sb("l2a_qr16_sb", [128, T], BF16, esA)
    Qaug = [p.sb(f"l2a_qaug{i}", [96, T], BF16, esA) for i in range(2)]
    Kaug = [p.sb(f"l2a_kaug{i}", [96, T], BF16, esA) for i in range(2)]
    Vt = p.sb("l2a_vt", [128, NT * 130], BF16, esA)
    o32 = p.sb("l2a_o32", [65, T], F32, esA)
    o16h = p.sb("l2a_o16h", [64, T], BF16, esA)
    lrow = p.sb("l2a_lrow", [1, T], F32, esA)
    lrec = lrow
    lcols = p.sb("l2a_lcols", [128, 32], F32, esA)
    pexp = [p.sb(f"l2a_pexp{i}", [128, 512], BF16, esA) for i in range(2)]

    kt_lists = causal_kt_lists(maskB)
    for pair in range(8):
        p.qwait("gpsimd", Tn.sem, Tn.sem.n)
        wch = {}
        for wi, wname in enumerate(("l2a_wuqT", "l2a_wukT", "l2a_wuvT")):
            for c in range(4):
                ap = w16[:, (wi * 4 + c) * 128:(wi * 4 + c + 1) * 128]
                p.dma("gpsimd", ap,
                      weights[wname][c * 128:(c + 1) * 128,
                                     pair * 128:(pair + 1) * 128], sw)
            wch[wi] = [w16[:, (wi * 4 + c) * 128:(wi * 4 + c + 1) * 128]
                       for c in range(4)]
        t_w = sw.n

        def pair_proj(wi, src, split_dests):
            t_last = 0
            for ts in range(8):
                pk, war = pool.get()
                if war is not None:
                    Tn.wait(war[0], war[1])
                Tn.wait(sw, t_w)
                Tn.wait(V.sem, V.sem.n)
                t_mm = mm_group(p, pk[:, :], [
                    (wch[wi][ic], src[ic][:, ts * 512:(ts + 1) * 512])
                    for ic in range(4)])
                V.wait(Tn.sem, t_mm)
                if split_dests[1] is None:
                    t_last = V("tensor_copy", split_dests[0][:, ts * 512:(ts + 1) * 512],
                               pk[:, :])
                else:
                    V("tensor_copy", split_dests[0][0:64, ts * 512:(ts + 1) * 512],
                      pk[0:64, :])
                    t_last = V("tensor_copy",
                               split_dests[1][0:64, ts * 512:(ts + 1) * 512],
                               pk[64:128, :])
                pool.done(p.V.sem, t_last)
            return t_last

        pair_proj(0, cq, (Qaug[0], Qaug[1]))
        pair_proj(1, ckv, (Kaug[0], Kaug[1]))
        t_v = pair_proj(2, ckv, (qp, None))

        if pair == 0:
            V("memset", Vt[:, :], 1.0)
        for j in range(NT):
            pk, war = trp.get()
            if war is not None:
                Tn.wait(war[0], war[1])
            Tn.wait(V.sem, t_v)
            t_tr = Tn.sem.inc()
            nc.tensor.transpose(pk[:, :], qp[:, j * 128:(j + 1) * 128],
                                ident16).then_inc(Tn.sem.h, 1)
            V.wait(Tn.sem, t_tr)
            V("tensor_copy", Vt[:, j * 130:j * 130 + 64], pk[:, 0:64])
            t_e = V("tensor_copy", Vt[:, j * 130 + 65:j * 130 + 129], pk[:, 64:128])
            trp.done(p.V.sem, t_e)

        if pair % 2 == 0:
            ch = pair // 2
            wchq = load_weight16(p, weights["l2a_wqrT"][:, ch * 128:(ch + 1) * 128]
                                 if False else weights["l2a_wqrT"], None, None, 0, 0) \
                if False else None
            for c in range(4):
                ap = w16[:, (12 + c) * 128:(13 + c) * 128]
                p.dma("gpsimd", ap,
                      weights["l2a_wqrT"][c * 128:(c + 1) * 128,
                                          ch * 128:(ch + 1) * 128], sw)
            t_wq = sw.n
            wq = [w16[:, (12 + c) * 128:(13 + c) * 128] for c in range(4)]
            t_last = 0
            for ts in range(8):
                pk, war = pool.get()
                if war is not None:
                    Tn.wait(war[0], war[1])
                Tn.wait(sw, t_wq)
                t_mm = mm_group(p, pk[:, :], [
                    (wq[ic], cq[ic][:, ts * 512:(ts + 1) * 512]) for ic in range(4)])
                V.wait(Tn.sem, t_mm)
                t_last = V("tensor_copy", qp[:, ts * 512:(ts + 1) * 512], pk[:, :])
                pool.done(p.V.sem, t_last)
            rope32(qr16, qp, 128, 0)

        for hh in range(2):
            h = 2 * pair + hh
            r0 = 32 * (h % 4)
            V("tensor_copy", Qaug[hh][64:96, :], qr16[r0:r0 + 32, :])
            V("tensor_copy", Kaug[hh][64:96, :], kr[0:32, :])

        for hh in range(2):
            h = 2 * pair + hh
            QA, KA = Qaug[hh], Kaug[hh]
            V.wait(sst, sst.n)
            full_barrier(p)

            def clip(pb, qc, kt):
                return V("tensor_scalar", pb[:, :], pb[:, :], 80.0, -80.0,
                         ALU.min, ALU.max)

            attn_core_head(p, KA, QA,
                           lambda kt, hh=hh: Vt[:, kt * 130 + hh * 65:kt * 130 + hh * 65 + 65],
                           o32, pexp, psB, psO, 96, kt_lists, pre_exp=clip,
                           exp_scale=SC96, exp_bias_ap=bias_ap)
            if DBG and h == 0:
                full_barrier(p)
                p.dma("gpsimd", dbg_qaug[:, :], QA[0:96, :], sst)
                p.dma("gpsimd", dbg_kaug[:, :], KA[0:96, :], sst)
                p.dma("gpsimd", dbg_o32[:, :], o32[0:65, :], sst)
            t_d = attn_divide_head(p, o32, o16h, dance, lrow, lrec, lcols, pool,
                                   ident32, ones32, 0)
            p.qwait("gpsimd", V.sem, t_d)
            p.dma("gpsimd", o_fm_d[h * 64:(h + 1) * 64, :], o16h[0:64, :], sst)

    full_barrier(p)
    esA.close()
    p.qwait("sync", sst, sst.n)
    attn_out_proj(p, io, o_fm_d, weights["l2a_woT"], None, sld, sst, tag="l2aop")
    io["sub_done"] = (sst, sst.n)


def host_prep_L2_attn(inputs):
    w = {}
    w["l2a_wdqT"] = np.ascontiguousarray(inputs["L2_attn_W_DQ"].T)
    w["l2a_wuqT"] = np.ascontiguousarray(inputs["L2_attn_W_UQ"].T)
    w["l2a_wqrT"] = np.ascontiguousarray(inputs["L2_attn_W_QR"].T)
    w["l2a_wdkvT"] = np.ascontiguousarray(inputs["L2_attn_W_DKV"].T)
    w["l2a_wukT"] = np.ascontiguousarray(inputs["L2_attn_W_UK"].T)
    w["l2a_wuvT"] = np.ascontiguousarray(inputs["L2_attn_W_UV"].T)
    w["l2a_wkrT"] = np.ascontiguousarray(inputs["L2_attn_W_KR"].T)
    w["l2a_woT"] = np.ascontiguousarray(inputs["L2_attn_W_O"].T)
    _, mB = make_causal_masks()
    w["l2a_maskB"] = mB
    d_hr = 32
    inv = 10000.0 ** (-np.arange(0, d_hr, 2, dtype=np.float64) / d_hr)
    ang = np.arange(T, dtype=np.float64)[:, None] * inv[None, :]
    C = np.cos(ang)[:, np.repeat(np.arange(16), 2)].T
    Sn = np.sin(ang)[:, np.repeat(np.arange(16), 2)].T
    w["l2a_ctab"] = np.tile(C, (4, 1)).astype(BF)
    w["l2a_stab"] = np.tile(Sn, (4, 1)).astype(BF)
    P32 = np.zeros((32, 32), np.float32)
    for i in range(16):
        P32[2 * i + 1, 2 * i] = -1.0
        P32[2 * i, 2 * i + 1] = 1.0
    w["l2a_perm4"] = np.kron(np.eye(4, dtype=np.float32), P32).astype(BF)
    return w


L2A_WEIGHT_SPECS = {
    "l2a_wdqT": ([D, 512], F32),
    "l2a_wuqT": ([512, D], F32),
    "l2a_wqrT": ([512, 512], F32),
    "l2a_wdkvT": ([D, 512], F32),
    "l2a_wukT": ([512, D], F32),
    "l2a_wuvT": ([512, D], F32),
    "l2a_wkrT": ([D, 32], F32),
    "l2a_woT": ([D, D], F32),
    "l2a_maskB": ([128, 896], F32),
    "l2a_ctab": ([128, T], BF16),
    "l2a_stab": ([128, T], BF16),
    "l2a_perm4": ([128, 128], BF16),
}


def build_stage_L2_attn():
    return build_stage(emit_L2_attn, L2A_WEIGHT_SPECS)



def emit_L2_ffn(p, io, weights, consts):
    nc = p.nc
    V, S, G, Tn = p.V, p.S, p.G, p.Tn
    ones16 = consts["ones16_sb"]
    CW = PAD + T
    NE = 16

    sld = p.sem("l2f_sld")
    sst = p.sem("l2f_sst")
    sfm = p.sem("l2f_sfm")
    sw = p.sem("l2f_sw")

    esA = contextlib.ExitStack()
    ln_fm = [p.sb(f"l2f_lnfm{c}", [128, CW], BF16, esA) for c in range(8)]
    acc = p.sb("l2f_acc", [128, NT * D], BF16, esA)
    Hfm = [p.sb(f"l2f_h{c}", [128, T], BF16, esA) for c in range(4)]
    hg = [p.sb(f"l2f_hg{i}", [128, 512], BF16, esA) for i in range(2)]
    wbuf = p.sb("l2f_wbuf", [128, 8 * 512], BF16, esA)
    wdbuf = p.sb("l2f_wdbuf", [128, 4 * D], BF16, esA)
    rwT = p.sb("l2f_rwT", [128, 8 * 16], BF16, esA)
    rb16 = p.sb("l2f_rb16", [1, 16], BF16, esA)
    rbf = p.sb("l2f_rbf", [1, 16], F32, esA)
    comb = p.sb("l2f_comb", [128, NT * 16], F32, esA)
    bi = p.sb("l2f_bi", [128, 16], F32, esA)
    m1 = p.sb("l2f_m1", [128, 16], F32, esA)
    bi2 = p.sb("l2f_bi2", [128, 16], F32, esA)
    t1c = p.sb("l2f_t1", [128, 1], F32, esA)
    t2c = p.sb("l2f_t2", [128, 1], F32, esA)
    sgc = p.sb("l2f_sg", [128, 2], F32, esA)
    sub = [p.sb(f"l2f_sub{i}", [128, D], F32, esA) for i in range(2)]
    acc32 = p.sb("l2f_acc32", [128, 512], F32, esA)

    pool = PsumPool(p, "l2f_pp", 2, [128, 512], es=esA)
    pup = PsumPool(p, "l2f_pu", 2, [128, 512], es=esA)
    pr = PsumPool(p, "l2f_pr", 2, [128, 16], es=esA)

    p.qwait("sync", io["pre_done"][0], io["pre_done"][1])
    t_fm = emit_fm_load(p, io["ln16_tm"], ln_fm, sfm)

    rch, t_rw = load_weight16(p, weights["l2f_routerT"], rwT, sw, D, 16)
    t_rb = p.dma("sync", rbf[0:1, :], weights["l2f_rbias"][:, :], sld)
    V.wait(sld, t_rb)
    t_rb16 = V("tensor_copy", rb16[0:1, :], rbf[0:1, :])
    t_acc0 = V("memset", acc[:, :], 0.0)
    for j in range(NT):
        pk, war = pr.get()
        if war is not None:
            Tn.wait(war[0], war[1])
        Tn.wait(sw, t_rw)
        Tn.wait(sfm, t_fm)
        Tn.wait(V.sem, t_rb16)
        pairs = [(ln_fm[ic][:, PAD + j * 128:PAD + (j + 1) * 128], rch[ic])
                 for ic in range(8)]
        pairs.append((ones16[0:1, 0:128], rb16[0:1, 0:16]))
        t_mm = mm_group(p, pk[:, :], pairs)
        V.wait(Tn.sem, t_mm)
        V("tensor_copy", bi[:, :], pk[:, 0:16])
        pr.done(p.V.sem, p.V.sem.n)
        V("reduce_max", t1c[:, :], bi[:, :], axis=AX.X)
        V("tensor_scalar", m1[:, :], bi[:, :], t1c[:, 0:1], None, ALU.is_ge)
        V("scalar_tensor_tensor", bi2[:, :], m1[:, :], -1e30, bi[:, :],
          ALU.mult, ALU.add)
        V("reduce_max", t2c[:, :], bi2[:, :], axis=AX.X)
        t_d = V("tensor_sub", sgc[:, 0:1], t1c[:, :], t2c[:, :])
        S.wait(V.sem, t_d)
        t_sig = S("activation", sgc[:, 0:1], sgc[:, 0:1], AF.Sigmoid)
        V.wait(S.sem, t_sig)
        V("tensor_scalar", sgc[:, 1:2], sgc[:, 0:1], -1.0, 1.0, ALU.mult, ALU.add)
        V("tensor_scalar", sgc[:, 0:1], sgc[:, 0:1], 2.0, -1.0, ALU.mult, ALU.add)
        V("tensor_scalar", bi2[:, :], bi[:, :], t2c[:, 0:1], None, ALU.is_ge)
        V("tensor_scalar", m1[:, :], m1[:, :], sgc[:, 0:1], None, ALU.mult)
        V("scalar_tensor_tensor", comb[:, j * 16:(j + 1) * 16], bi2[:, :],
          sgc[:, 1:2], m1[:, :], ALU.mult, ALU.add)
    t_comb = p.V.sem.n

    for e in range(NE):
        gch, t_g = load_weight16(p, weights["l2f_gate"][e], wbuf, sw, D, 512)
        uch, t_u = load_weight16(p, weights["l2f_up"][e], wdbuf, sw, D, 512)
        V.wait(Tn.sem, Tn.sem.n)
        hg_rd = {}
        k = 0
        for oc in range(4):
            for ts in range(8):
                pg, warg = pool.get()
                if warg is not None:
                    Tn.wait(warg[0], warg[1])
                Tn.wait(sw, max(t_g, t_u))
                t_mm = mm_group(p, pg[:, :], [
                    (gch[ic][:, oc * 128:(oc + 1) * 128],
                     ln_fm[ic][:, PAD + ts * 512:PAD + (ts + 1) * 512])
                    for ic in range(8)])
                S.wait(Tn.sem, t_mm)
                if k - 2 in hg_rd:
                    S.wait(V.sem, hg_rd[k - 2])
                t_sil = S("activation", hg[k % 2][:, :], pg[:, :], AF.Silu)
                pool.done(p.S.sem, t_sil)
                pu, waru = pup.get()
                if waru is not None:
                    Tn.wait(waru[0], waru[1])
                t_mm2 = mm_group(p, pu[:, :], [
                    (uch[ic][:, oc * 128:(oc + 1) * 128],
                     ln_fm[ic][:, PAD + ts * 512:PAD + (ts + 1) * 512])
                    for ic in range(8)])
                V.wait(Tn.sem, t_mm2)
                V.wait(S.sem, t_sil)
                t_h = V("tensor_mul", Hfm[oc][:, ts * 512:(ts + 1) * 512],
                        pu[:, :], hg[k % 2][:, :])
                hg_rd[k] = t_h
                pup.done(p.V.sem, t_h)
                k += 1
        t_hdone = p.V.sem.n
        dch, t_d = load_weight16(p, weights["l2f_down"][e], wdbuf, sw, 512, D)
        for j in range(NT):
            for ns in range(2):
                pk, war = pool.get()
                if war is not None:
                    Tn.wait(war[0], war[1])
                Tn.wait(sw, t_d)
                Tn.wait(V.sem, t_hdone)
                t_mm = mm_group(p, pk[:, :], [
                    (Hfm[c][:, j * 128:(j + 1) * 128],
                     dch[c][:, ns * 512:(ns + 1) * 512]) for c in range(4)])
                V.wait(Tn.sem, t_mm)
                V.wait(V.sem, t_comb)
                t_e2 = V("scalar_tensor_tensor",
                         acc[:, j * D + ns * 512:j * D + (ns + 1) * 512],
                         pk[:, :], comb[:, j * 16 + e:j * 16 + e + 1],
                         acc[:, j * D + ns * 512:j * D + (ns + 1) * 512],
                         ALU.mult, ALU.add)
                pool.done(p.V.sem, t_e2)

    sub_done = {}
    for half in range(2):
        gch, t_g = load_weight16(
            p, weights["l2f_shgT"][:, half * 512:(half + 1) * 512], wbuf, sw, D, 512)
        uch, t_u = load_weight16(
            p, weights["l2f_shuT"][:, half * 512:(half + 1) * 512], wdbuf, sw, D, 512)
        V.wait(Tn.sem, Tn.sem.n)
        hg_rd = {}
        k = 0
        for oc in range(4):
            for ts in range(8):
                pg, warg = pool.get()
                if warg is not None:
                    Tn.wait(warg[0], warg[1])
                Tn.wait(sw, max(t_g, t_u))
                t_mm = mm_group(p, pg[:, :], [
                    (gch[ic][:, oc * 128:(oc + 1) * 128],
                     ln_fm[ic][:, PAD + ts * 512:PAD + (ts + 1) * 512])
                    for ic in range(8)])
                S.wait(Tn.sem, t_mm)
                if k - 2 in hg_rd:
                    S.wait(V.sem, hg_rd[k - 2])
                t_sil = S("activation", hg[k % 2][:, :], pg[:, :], AF.Silu)
                pool.done(p.S.sem, t_sil)
                pu, waru = pup.get()
                if waru is not None:
                    Tn.wait(waru[0], waru[1])
                t_mm2 = mm_group(p, pu[:, :], [
                    (uch[ic][:, oc * 128:(oc + 1) * 128],
                     ln_fm[ic][:, PAD + ts * 512:PAD + (ts + 1) * 512])
                    for ic in range(8)])
                V.wait(Tn.sem, t_mm2)
                V.wait(S.sem, t_sil)
                t_h = V("tensor_mul", Hfm[oc][:, ts * 512:(ts + 1) * 512],
                        pu[:, :], hg[k % 2][:, :])
                hg_rd[k] = t_h
                pup.done(p.V.sem, t_h)
                k += 1
        t_hs = p.V.sem.n
        dch, t_d = load_weight16(
            p, weights["l2f_shdT"][half * 512:(half + 1) * 512, :], wdbuf, sw, 512, D)
        for j in range(NT):
            i = j % 2
            if half == 1 and j - 2 in sub_done:
                V.wait(sst, sub_done[j - 2])
            t_e2 = None
            for ns in range(2):
                pk, war = pool.get()
                if war is not None:
                    Tn.wait(war[0], war[1])
                Tn.wait(sw, t_d)
                Tn.wait(V.sem, t_hs)
                t_mm = mm_group(p, pk[:, :], [
                    (Hfm[c][:, j * 128:(j + 1) * 128],
                     dch[c][:, ns * 512:(ns + 1) * 512]) for c in range(4)])
                V.wait(Tn.sem, t_mm)
                if half == 0:
                    t_e2 = V("scalar_tensor_tensor",
                             acc[:, j * D + ns * 512:j * D + (ns + 1) * 512],
                             pk[:, :], 1.0,
                             acc[:, j * D + ns * 512:j * D + (ns + 1) * 512],
                             ALU.mult, ALU.add)
                else:
                    V("tensor_copy", acc32[:, :],
                      acc[:, j * D + ns * 512:j * D + (ns + 1) * 512])
                    t_e2 = V("tensor_add", sub[i][:, ns * 512:(ns + 1) * 512],
                             pk[:, :], acc32[:, :])
                pool.done(p.V.sem, t_e2)
            if half == 1:
                p.qwait("gpsimd", V.sem, t_e2)
                p.dma("gpsimd", io["sub_tm"][j * 128:(j + 1) * 128, :], sub[i][:, :], sst)
                sub_done[j] = sst.n
    full_barrier(p)
    esA.close()
    io["sub_done"] = (sst, sst.n)


def host_prep_L2_ffn(inputs):
    w = {}
    w["l2f_routerT"] = np.ascontiguousarray(inputs["L2_ffn_router_w"].T)
    w["l2f_rbias"] = np.ascontiguousarray(inputs["L2_ffn_router_bias"].reshape(1, 16))
    w["l2f_gate"] = np.ascontiguousarray(inputs["L2_ffn_gate_proj"])
    w["l2f_up"] = np.ascontiguousarray(inputs["L2_ffn_up_proj"])
    w["l2f_down"] = np.ascontiguousarray(inputs["L2_ffn_down_proj"])
    w["l2f_shgT"] = np.ascontiguousarray(inputs["L2_ffn_sh_gate"].T)
    w["l2f_shuT"] = np.ascontiguousarray(inputs["L2_ffn_sh_up"].T)
    w["l2f_shdT"] = np.ascontiguousarray(inputs["L2_ffn_sh_down"].T)
    return w


L2F_WEIGHT_SPECS = {
    "l2f_routerT": ([D, 16], F32),
    "l2f_rbias": ([1, 16], F32),
    "l2f_gate": ([16, D, 512], F32),
    "l2f_up": ([16, D, 512], F32),
    "l2f_down": ([16, 512, D], F32),
    "l2f_shgT": ([D, D], F32),
    "l2f_shuT": ([D, D], F32),
    "l2f_shdT": ([D, D], F32),
}


def build_stage_L2_ffn():
    return build_stage(emit_L2_ffn, L2F_WEIGHT_SPECS)



def emit_L1_ffn(p, io, weights, consts):
    nc = p.nc
    V, S, G, Tn = p.V, p.S, p.G, p.Tn
    ones16 = consts["ones16_sb"]
    ones32 = consts["ones32_sb"]
    CW = PAD + T
    NE = 8
    NH = 11

    sld = p.sem("l1f_sld")
    sst = p.sem("l1f_sst")
    sfm = p.sem("l1f_sfm")
    sw = p.sem("l1f_sw")

    esA = contextlib.ExitStack()
    ln_fm = [p.sb(f"l1f_lnfm{c}", [128, CW], BF16, esA) for c in range(8)]
    acc = p.sb("l1f_acc", [128, NT * D], BF16, esA)
    Hfm = [p.sb(f"l1f_h{c}", [128, T], BF16, esA) for c in range(4)]
    hrel = [p.sb(f"l1f_hrel{i}", [128, 512], BF16, esA) for i in range(2)]
    wbuf = p.sb("l1f_wbuf", [128, 8 * 512], BF16, esA)
    wdbuf = p.sb("l1f_wdbuf", [128, 4 * D], BF16, esA)
    gwT = p.sb("l1f_gwT", [128, 8 * 8], BF16, esA)
    ebrow = p.sb("l1f_ebrow", [1, 8], F32, esA)
    ebb = p.sb("l1f_ebb", [128, 8], F32, esA)
    comb = p.sb("l1f_comb", [128, NT * 8], F32, esA)
    sc = p.sb("l1f_sc", [128, 8], F32, esA)
    bi = p.sb("l1f_bi", [128, 8], F32, esA)
    m1 = p.sb("l1f_m1", [128, 8], F32, esA)
    bi2 = p.sb("l1f_bi2", [128, 8], F32, esA)
    t1c = p.sb("l1f_t1", [128, 1], F32, esA)
    t2c = p.sb("l1f_t2", [128, 1], F32, esA)
    sub = [p.sb(f"l1f_sub{i}", [128, D], F32, esA) for i in range(2)]
    acc32 = p.sb("l1f_acc32", [128, 512], F32, esA)

    pool = PsumPool(p, "l1f_pp", 2, [128, 512], es=esA)
    pr = PsumPool(p, "l1f_pr", 2, [128, 8], es=esA)

    if DBG:
        dbg_comb = p.out("dbg_comb", [128, NT * 8], F32)
        dbg_acc = p.out("dbg_acc", [128, NT * D], BF16)
        dbg_sc = p.out("dbg_sc", [128, 8], F32)
        dbg_bi = p.out("dbg_bi", [128, 8], F32)
        dbg_ebb = p.out("dbg_ebb", [128, 8], F32)
    p.qwait("sync", io["pre_done"][0], io["pre_done"][1])
    t_fm = emit_fm_load(p, io["ln16_tm"], ln_fm, sfm)

    gch, t_rw = load_weight16(p, weights["l1f_gateT"], gwT, sw, D, 8)
    t_rb = p.dma("sync", ebrow[0:1, :], weights["l1f_ebias"][:, :], sld)
    pk, war = pr.get()
    Tn.wait(sld, t_rb)
    t_mm = mm_group(p, pk[:, :], [(ones32[0:1, 0:128], ebrow[0:1, 0:8])])
    V.wait(Tn.sem, t_mm)
    t_eb = V("tensor_copy", ebb[:, :], pk[:, 0:8])
    pr.done(p.V.sem, t_eb)
    t_acc0 = V("memset", acc[:, :], 0.0)
    t_scrd = 0
    for j in range(NT):
        pk, war = pr.get()
        if war is not None:
            Tn.wait(war[0], war[1])
        Tn.wait(sw, t_rw)
        Tn.wait(sfm, t_fm)
        pairs = [(ln_fm[ic][:, PAD + j * 128:PAD + (j + 1) * 128], gch[ic])
                 for ic in range(8)]
        t_mm = mm_group(p, pk[:, :], pairs)
        S.wait(Tn.sem, t_mm)
        S.wait(V.sem, t_scrd)
        t_sig = S("activation", sc[:, :], pk[:, 0:8], AF.Sigmoid)
        pr.done(p.S.sem, t_sig)
        V.wait(S.sem, t_sig)
        t_scrd = V("tensor_add", bi[:, :], sc[:, :], ebb[:, :])
        V("reduce_max", t1c[:, :], bi[:, :], axis=AX.X)
        V("tensor_scalar", m1[:, :], bi[:, :], t1c[:, 0:1], None, ALU.is_ge)
        V("scalar_tensor_tensor", bi2[:, :], m1[:, :], -1e30, bi[:, :],
          ALU.mult, ALU.add)
        V("reduce_max", t2c[:, :], bi2[:, :], axis=AX.X)
        V("tensor_scalar", bi2[:, :], bi[:, :], t2c[:, 0:1], None, ALU.is_ge)
        t_scrd = V("tensor_mul", m1[:, :], sc[:, :], sc[:, :])
        t_cj = V("tensor_mul", comb[:, j * 8:(j + 1) * 8], bi2[:, :], m1[:, :])
        if DBG and j == 0:
            p.qwait("gpsimd", V.sem, t_cj)
            p.dma("gpsimd", dbg_sc[:, :], sc[:, :], sst)
            p.dma("gpsimd", dbg_bi[:, :], bi[:, :], sst)
            p.dma("gpsimd", dbg_ebb[:, :], ebb[:, :], sst)
            p.qwait("sync", sst, sst.n)
            V.wait(sst, sst.n)
    t_comb = p.V.sem.n

    def up_relu2(upT_dram, n_h):
        uch, t_u = load_weight16(p, upT_dram, wbuf, sw, D, n_h * 128)
        assert n_h <= 4
        S.wait(Tn.sem, Tn.sem.n)
        sq_done = {}
        k = 0
        for oc in range(n_h):
            for ts in range(8):
                pg, warg = pool.get()
                if warg is not None:
                    Tn.wait(warg[0], warg[1])
                Tn.wait(sw, t_u)
                t_mm = mm_group(p, pg[:, :], [
                    (uch[ic][:, oc * 128:(oc + 1) * 128],
                     ln_fm[ic][:, PAD + ts * 512:PAD + (ts + 1) * 512])
                    for ic in range(8)])
                V.wait(Tn.sem, t_mm)
                if k - 2 in sq_done:
                    V.wait(S.sem, sq_done[k - 2])
                t_r = V("tensor_scalar_max", hrel[k % 2][:, :], pg[:, :], 0.0)
                pool.done(p.V.sem, t_r)
                S.wait(V.sem, t_r)
                sq_done[k] = S("activation", Hfm[oc][:, ts * 512:(ts + 1) * 512],
                               hrel[k % 2][:, :], AF.Square)
                k += 1
        return p.S.sem.n

    def down_acc(downT_dram, n_h, comb_col_fn, to_sub=False):
        dch, t_d = load_weight16(p, downT_dram, wdbuf, sw, n_h * 128, D)
        t_hdone = p.S.sem.n
        sub_done = {}
        for j in range(NT):
            i = j % 2
            if to_sub and j - 2 in sub_done:
                V.wait(sst, sub_done[j - 2])
            t_e2 = None
            for ns in range(2):
                pk, war = pool.get()
                if war is not None:
                    Tn.wait(war[0], war[1])
                Tn.wait(sw, t_d)
                Tn.wait(S.sem, t_hdone)
                t_mm = mm_group(p, pk[:, :], [
                    (Hfm[c][:, j * 128:(j + 1) * 128],
                     dch[c][:, ns * 512:(ns + 1) * 512]) for c in range(n_h)])
                V.wait(Tn.sem, t_mm)
                ccol = comb_col_fn(j)
                if to_sub:
                    V("tensor_copy", acc32[:, :],
                      acc[:, j * D + ns * 512:j * D + (ns + 1) * 512])
                    t_e2 = V("tensor_add", sub[i][:, ns * 512:(ns + 1) * 512],
                             pk[:, :], acc32[:, :])
                elif ccol is None:
                    t_e2 = V("scalar_tensor_tensor",
                             acc[:, j * D + ns * 512:j * D + (ns + 1) * 512],
                             pk[:, :], 1.0,
                             acc[:, j * D + ns * 512:j * D + (ns + 1) * 512],
                             ALU.mult, ALU.add)
                else:
                    V.wait(V.sem, t_comb)
                    t_e2 = V("scalar_tensor_tensor",
                             acc[:, j * D + ns * 512:j * D + (ns + 1) * 512],
                             pk[:, :], ccol,
                             acc[:, j * D + ns * 512:j * D + (ns + 1) * 512],
                             ALU.mult, ALU.add)
                pool.done(p.V.sem, t_e2)
            if to_sub:
                p.qwait("gpsimd", V.sem, t_e2)
                p.dma("gpsimd", io["sub_tm"][j * 128:(j + 1) * 128, :], sub[i][:, :], sst)
                sub_done[j] = sst.n

    HALVES = ((0, 4), (4, 4), (8, 3))
    for e in range(NE):
        for (h0, nh) in HALVES:
            up_relu2(weights["l1f_upT"][e][:, h0 * 128:(h0 + nh) * 128], nh)
            down_acc(weights["l1f_downT"][e][h0 * 128:(h0 + nh) * 128, :], nh,
                     lambda j, e=e: comb[:, j * 8 + e:j * 8 + e + 1])

    if DBG:
        full_barrier(p)
        p.dma("gpsimd", dbg_comb[:, :], comb[:, :], sst)
        p.dma("gpsimd", dbg_acc[:, :], acc[:, :], sst)
    for (h0, nh) in HALVES[:-1]:
        up_relu2(weights["l1f_shupT"][:, h0 * 128:(h0 + nh) * 128], nh)
        down_acc(weights["l1f_shdownT"][h0 * 128:(h0 + nh) * 128, :], nh, lambda j: None)
    h0, nh = HALVES[-1]
    up_relu2(weights["l1f_shupT"][:, h0 * 128:(h0 + nh) * 128], nh)
    down_acc(weights["l1f_shdownT"][h0 * 128:(h0 + nh) * 128, :], nh, lambda j: None,
             to_sub=True)

    full_barrier(p)
    esA.close()
    io["sub_done"] = (sst, sst.n)


def host_prep_L1_ffn(inputs):
    w = {}
    w["l1f_gateT"] = np.ascontiguousarray(inputs["L1_ffn_gate_w"].T)
    w["l1f_ebias"] = np.ascontiguousarray(inputs["L1_ffn_expert_bias"].reshape(1, 8))
    w["l1f_upT"] = np.ascontiguousarray(inputs["L1_ffn_w_up"].transpose(0, 2, 1))
    w["l1f_downT"] = np.ascontiguousarray(inputs["L1_ffn_w_down"].transpose(0, 2, 1))
    w["l1f_shupT"] = np.ascontiguousarray(inputs["L1_ffn_shared_w_up"].T)
    w["l1f_shdownT"] = np.ascontiguousarray(inputs["L1_ffn_shared_w_down"].T)
    return w


L1F_WEIGHT_SPECS = {
    "l1f_gateT": ([D, 8], F32),
    "l1f_ebias": ([1, 8], F32),
    "l1f_upT": ([8, D, 1408], F32),
    "l1f_downT": ([8, 1408, D], F32),
    "l1f_shupT": ([D, 1408], F32),
    "l1f_shdownT": ([1408, D], F32),
}


def build_stage_L1_ffn():
    return build_stage(emit_L1_ffn, L1F_WEIGHT_SPECS)



def emit_local_stack(p, io, weights, consts, cfg):
    nc = p.nc
    V, S, G, Tn = p.V, p.S, p.G, p.Tn
    ident16 = consts["ident16_sb"]
    ident32 = consts["ident32_sb"]
    ones16 = consts["ones16_sb"]
    ones32 = consts["ones32_sb"]
    tag = cfg["tag"]
    DEPTH, H, W, INNER = cfg["depth"], cfg["H"], cfg["W"], cfg["inner"]
    NP = H // 2
    NIC = INNER // 128
    NHG = 22
    CW = PAD + T
    KW = W + T

    x_f = [p.dram(f"{tag}_xf{i}", [T, D], F32) for i in range(2)]
    x_h = [p.dram(f"{tag}_xh{i}", [T, D], BF16) for i in range(2)]
    o_fm_d = (p.out if DBG else p.dram)(f"{tag}_ofm", [INNER, T], BF16)
    if DBG:
        dbg_q = p.out("dbg_q", [128, T], BF16)
        dbg_k = p.out("dbg_k", [128, W + T], BF16)
        dbg_o32 = p.out("dbg_o32", [65, T], F32)
        dbg_nm = p.out("dbg_nm", [1, T], BF16)
    xr_d = p.dram(f"{tag}_xrd", [D, T], BF16)

    sld = p.sem(f"{tag}_sld")
    sst = p.sem(f"{tag}_sst")
    sfm = p.sem(f"{tag}_sfm")
    sw = p.sem(f"{tag}_sw")
    sxr = p.sem(f"{tag}_sxr")

    ctabS = p.sb(f"{tag}_ctabSs", [128, T], BF16)
    stabS = p.sb(f"{tag}_stabSs", [128, T], BF16)
    ctabI = p.sb(f"{tag}_ctabIs", [128, T], BF16)
    stabI = p.sb(f"{tag}_stabIs", [128, T], BF16)
    blkS = p.sb(f"{tag}_blkSs", [128, 128], BF16)
    perm = p.sb(f"{tag}_perms", [128, 128], BF16)
    qkSf = p.sb(f"{tag}_qkSf", [128, 2 * DEPTH], F32)

    maskA = p.sb(f"{tag}_maskAs", [128, W + 128], F32)
    maskB = p.sb(f"{tag}_maskBs", [128, W + 128], F32)
    lnbf = p.sb(f"{tag}_lnbf", [128, 16 * DEPTH], F32)
    lnb16 = p.sb(f"{tag}_lnb16", [128, 16 * DEPTH], BF16)

    rowf = p.sb(f"{tag}_rowf", [97, T], F32)
    rowh = p.sb(f"{tag}_rowh", [97, T], BF16)

    mcols = p.sb(f"{tag}_mcols", [128, NT], F32)
    mtmp = p.sb(f"{tag}_mtmp", [128, 1], F32)
    lcols = p.sb(f"{tag}_lcols", [128, 32], F32)
    rows = rowf[0:1, :]
    srow = rowf[32:33, :]
    ssqrow = rowf[64:65, :]
    lrow = rowf[0:1, :]
    nmrow = rowh[0:1, :]
    murrow = rowh[0:1, :]



    t_c = 0
    for nm, buf in (("ctabS", ctabS), ("stabS", stabS), ("ctabI", ctabI),
                    ("stabI", stabI), ("blkS", blkS), ("perm", perm),
                    ("qkS", qkSf), ("maskA", maskA), ("maskB", maskB),
                    ("lnb", lnbf)):
        t_c = p.dma("sync", buf[:, :], weights[f"{tag}_{nm}"][:, :], sld)
    V.wait(sld, t_c)
    V("tensor_scalar_mul", qkSf[:, 0:DEPTH], qkSf[:, 0:DEPTH], 8.0)
    V("tensor_copy", lnb16[:, :], lnbf[:, :])

    dance = Dance(p, f"{tag}_dance")
    eps24_ap, _ = p.const_col(1e-24)
    eps5_ap, _ = p.const_col(1e-5)

    def sh(fmc, c, t0, n):
        off = PAD - 1 if c >= 4 else PAD
        return fmc[c][:, off + t0:off + t0 + n]

    blists = {}
    for qc in range(8):
        q0, q1 = qc * 512, qc * 512 + 512
        lst = []
        for kt in range(NT):
            klo = kt * 128
            lo = max(q0, klo)
            hi = min(q1, klo + 128 + W)
            if lo < hi:
                lst.append((kt, lo, hi - lo, lo - klo))
        blists[qc] = lst

    for layer in range(DEPTH):
        lsrc_f = io["ln32_tm"] if layer == 0 else x_f[0]
        lsrc_h = io["ln16_tm"] if layer == 0 else x_h[0]
        qs_col = qkSf[:, layer:layer + 1]
        ks_col = qkSf[:, DEPTH + layer:DEPTH + layer + 1]
        g_cols = [lnbf[:, 16 * layer + c:16 * layer + c + 1] for c in range(8)]
        b_cols = [lnb16[:, 16 * layer + 8 + c:16 * layer + 8 + c + 1] for c in range(8)]

        esA = contextlib.ExitStack()
        hfm = [p.sb(f"{tag}_hfm{layer}_{c}", [128, CW], BF16, esA) for c in range(8)]
        qrp = p.sb(f"{tag}_qrp{layer}", [128, T], BF16, esA)
        krp = p.sb(f"{tag}_krp{layer}", [128, KW], BF16, esA)
        vsl = [p.sb(f"{tag}_vsl{layer}_{i}", [128, 512], BF16, esA) for i in range(2)]
        rsl = [p.sb(f"{tag}_rsl{layer}_{i}", [128, 512], BF16, esA) for i in range(2)]
        sq2 = [p.sb(f"{tag}_sq2_{layer}_{i}", [128, 512], BF16, esA) for i in range(2)]
        rot = p.sb(f"{tag}_rot{layer}", [128, 512], BF16, esA)
        Vt = p.sb(f"{tag}_vt{layer}", [128, NT * 130], BF16, esA)
        o32 = p.sb(f"{tag}_o32_{layer}", [65, T], F32, esA)
        o16h = p.sb(f"{tag}_o16h{layer}", [64, T], BF16, esA)
        pexp = [p.sb(f"{tag}_pexp{layer}_{i}", [128, 512], BF16, esA) for i in range(2)]
        w16 = p.sb(f"{tag}_w16a{layer}", [128, 3 * 8 * 128], BF16, esA)
        pool = PsumPool(p, f"{tag}_pp{layer}", 2, [128, 512], es=esA)
        trp = PsumPool(p, f"{tag}_tr{layer}", 2, [128, 128], BF16, es=esA)
        psB = PsumPool(p, f"{tag}_psB{layer}", 2, [128, 512], es=esA)
        psO = PsumPool(p, f"{tag}_psO{layer}", 1, [65, 512], es=esA)

        for c in range(8):
            V("memset", hfm[c][:, 0:PAD], 0.0)
        if layer == 0:
            p.qwait("sync", io["pre_done"][0], io["pre_done"][1])
        else:
            p.qwait("sync", sst, sst.n)
        t_fm = emit_fm_load(p, lsrc_h, hfm, sfm)

        for pair in range(NP):
            p.qwait("gpsimd", Tn.sem, Tn.sem.n)
            vsl_rd = {}
            wch = {}
            for wi in range(3):
                for c in range(8):
                    ap = w16[:, (wi * 8 + c) * 128:(wi * 8 + c + 1) * 128]
                    p.dma("gpsimd", ap,
                          weights[f"{tag}_wqkv"][layer][c * 128:(c + 1) * 128,
                                                        wi * INNER + pair * 128:
                                                        wi * INNER + (pair + 1) * 128],
                          sw)
                wch[wi] = [w16[:, (wi * 8 + c) * 128:(wi * 8 + c + 1) * 128]
                           for c in range(8)]
            t_w = sw.n

            def pproj(wi, dst, col0):
                t_last = 0
                for ts in range(8):
                    pk, war = pool.get()
                    if war is not None:
                        Tn.wait(war[0], war[1])
                    Tn.wait(sw, t_w)
                    Tn.wait(sfm, t_fm)
                    Tn.wait(V.sem, V.sem.n)
                    t_mm = mm_group(p, pk[:, :], [
                        (wch[wi][ic], sh(hfm, ic, ts * 512, 512))
                        for ic in range(8)])
                    V.wait(Tn.sem, t_mm)
                    t_last = V("tensor_copy",
                               dst[:, col0 + ts * 512:col0 + (ts + 1) * 512], pk[:, :])
                    pool.done(p.V.sem, t_last)
                return t_last

            def l2norm_rot(buf, col0, scol, ctab, stab):
                sq_rd = {}
                rsl_rd = {}
                for ts in range(8):
                    b = ts % 2
                    bslc = slice(col0 + ts * 512, col0 + (ts + 1) * 512)
                    if ts - 2 in sq_rd:
                        S.wait(Tn.sem, sq_rd[ts - 2])
                    S.wait(V.sem, V.sem.n)
                    t_sq = S("activation", sq2[b][:, :], buf[:, bslc], AF.Square)
                    pk, war = pool.get()
                    if war is not None:
                        Tn.wait(war[0], war[1])
                    Tn.wait(S.sem, t_sq)
                    last = nc.tensor.matmul(pk[:, :], blkS[:, :], sq2[b][:, :],
                                            start=True, stop=True)
                    t_mm = Tn.sem.inc()
                    last.then_inc(Tn.sem.h, 1)
                    sq_rd[ts] = t_mm
                    S.wait(Tn.sem, t_mm)
                    if ts - 2 in rsl_rd:
                        S.wait(V.sem, rsl_rd[ts - 2])
                    S("activation", rsl[b][:, :], pk[:, :], AF.Ln, bias=eps24_ap)
                    t_ex = S("activation", rsl[b][:, :], rsl[b][:, :], AF.Exp,
                             scale=-0.5)
                    pool.done(p.S.sem, t_ex)
                    V.wait(S.sem, t_ex)
                    t_n = V("scalar_tensor_tensor", buf[:, bslc], buf[:, bslc],
                            scol, rsl[b][:, :], ALU.mult, ALU.mult)
                    pk2, war2 = psB.get()
                    if war2 is not None:
                        Tn.wait(war2[0], war2[1])
                    Tn.wait(V.sem, t_n)
                    t_mm2 = mm_group(p, pk2[:, :], [(perm[:, :], buf[:, bslc])])
                    V.wait(Tn.sem, t_mm2)
                    V("tensor_mul", rot[:, :], pk2[:, :],
                      stab[:, ts * 512:(ts + 1) * 512])
                    psB.done(p.V.sem, p.V.sem.n)
                    V("tensor_mul", buf[:, bslc], buf[:, bslc],
                      ctab[:, ts * 512:(ts + 1) * 512])
                    rsl_rd[ts] = V("tensor_add", buf[:, bslc], buf[:, bslc],
                                   rot[:, 0:512])
                return p.V.sem.n

            pproj(0, qrp, 0)
            l2norm_rot(qrp, 0, qs_col, ctabS, stabS)
            V("memset", krp[:, 0:W], 0.0)
            pproj(1, krp, W)
            l2norm_rot(krp, W, ks_col, ctabI, stabI)

            if pair == 0:
                V("memset", Vt[:, :], 1.0)
            for ts in range(8):
                b = ts % 2
                pk, war = pool.get()
                if war is not None:
                    Tn.wait(war[0], war[1])
                Tn.wait(sw, t_w)
                Tn.wait(sfm, t_fm)
                t_mm = mm_group(p, pk[:, :], [
                    (wch[2][ic], sh(hfm, ic, ts * 512, 512)) for ic in range(8)])
                V.wait(Tn.sem, t_mm)
                if ts >= 2:
                    V.wait(Tn.sem, vsl_rd[ts - 2])
                t_v = V("tensor_copy", vsl[b][:, :], pk[:, :])
                pool.done(p.V.sem, t_v)
                for sub in range(4):
                    j = ts * 4 + sub
                    pk2, war2 = trp.get()
                    if war2 is not None:
                        Tn.wait(war2[0], war2[1])
                    Tn.wait(V.sem, t_v)
                    t_tr = Tn.sem.inc()
                    nc.tensor.transpose(pk2[:, :], vsl[b][:, sub * 128:(sub + 1) * 128],
                                        ident16).then_inc(Tn.sem.h, 1)
                    if sub == 3:
                        vsl_rd[ts] = t_tr
                    V.wait(Tn.sem, t_tr)
                    V("tensor_copy", Vt[:, j * 130:j * 130 + 64], pk2[:, 0:64])
                    t_e = V("tensor_copy", Vt[:, j * 130 + 65:j * 130 + 129],
                            pk2[:, 64:128])
                    trp.done(p.V.sem, t_e)

            for hh in range(2):
                h = 2 * pair + hh
                rb0 = hh * 64
                V("memset", mcols[:, :], NEG)
                full_barrier(p)
                for j in range(NT):
                    kst = max(0, j * 128 - W)
                    kn = j * 128 + 128 - kst
                    moff = kst - (j * 128 - W)
                    nsub = [kn] if kn <= 512 else [512, kn - 512]
                    off = 0
                    for nn2 in nsub:
                        pb, war = psB.get()
                        if war is not None:
                            Tn.wait(war[0], war[1])
                        last = nc.tensor.matmul(
                            pb[:, 0:nn2],
                            qrp[rb0:rb0 + 64, j * 128:(j + 1) * 128],
                            krp[rb0:rb0 + 64, W + kst + off:W + kst + off + nn2],
                            start=True, stop=True)
                        t_b = Tn.sem.inc()
                        last.then_inc(Tn.sem.h, 1)
                        V.wait(Tn.sem, t_b)
                        V("tensor_add", pb[:, 0:nn2], pb[:, 0:nn2],
                          maskA[:, moff + off:moff + off + nn2])
                        V("reduce_max", mtmp[:, :], pb[:, 0:nn2], axis=AX.X)
                        t_m = V("tensor_max", mcols[:, j:j + 1], mcols[:, j:j + 1],
                                mtmp[:, :])
                        psB.done(p.V.sem, t_m)
                        off += nn2
                dance.rows_from_cols(mcols[:, :], rows, 32, ident32,
                                     waits=[(V.sem, p.V.sem.n)])
                V.wait(dance.sem, dance.sem.n)
                t_nm = V("tensor_scalar_mul", nmrow, rows, -1.0)

                pexp_war = [0, 0]
                t_av = 0
                for qc in range(8):
                    po, warO = psO.get()
                    if warO is not None:
                        Tn.wait(warO[0], warO[1])
                    lst = blists[qc]
                    for idx, (kt, qlo, nn2, moff) in enumerate(lst):
                        pb, warB = psB.get()
                        if warB is not None:
                            Tn.wait(warB[0], warB[1])
                        Tn.wait(V.sem, t_nm)
                        nc.tensor.matmul(
                            pb[:, 0:nn2],
                            krp[rb0:rb0 + 64, W + kt * 128:W + kt * 128 + 128],
                            qrp[rb0:rb0 + 64, qlo:qlo + nn2],
                            start=True, stop=False)
                        last = nc.tensor.matmul(
                            pb[:, 0:nn2], ones16[0:1, 0:128],
                            nmrow[0:1, qlo:qlo + nn2], start=False, stop=True)
                        t_b = Tn.sem.inc()
                        last.then_inc(Tn.sem.h, 1)
                        V.wait(Tn.sem, t_b)
                        t_fix = V("tensor_add", pb[:, 0:nn2], pb[:, 0:nn2],
                                  maskB[:, moff:moff + nn2])
                        b = idx % 2
                        S.wait(Tn.sem, t_b)
                        S.wait(V.sem, t_fix)
                        if pexp_war[b]:
                            S.wait(Tn.sem, pexp_war[b])
                        t_e = S("activation", pexp[b][:, 0:nn2], pb[:, 0:nn2], AF.Exp)
                        psB.done(p.S.sem, t_e)
                        Tn.wait(S.sem, t_e)
                        last = nc.tensor.matmul(
                            po[0:65, qlo - qc * 512:qlo - qc * 512 + nn2],
                            Vt[:, kt * 130 + hh * 65:kt * 130 + hh * 65 + 65],
                            pexp[b][:, 0:nn2],
                            start=(idx == 0), stop=(idx == len(lst) - 1),
                            skip_group_check=True)
                        t_av = Tn.sem.inc()
                        last.then_inc(Tn.sem.h, 1)
                        pexp_war[b] = t_av
                    V.wait(Tn.sem, t_av)
                    t_o = V("tensor_copy", o32[0:65, qc * 512:(qc + 1) * 512],
                            po[:, :])
                    psO.done(p.V.sem, t_o)

                if DBG and h == 0 and layer == 0:
                    full_barrier(p)
                    p.dma("gpsimd", dbg_q[:, :], qrp[:, :], sst)
                    p.dma("gpsimd", dbg_k[:, :], krp[:, :], sst)
                    p.dma("gpsimd", dbg_o32[:, :], o32[0:65, :], sst)
                    p.dma("gpsimd", dbg_nm[:, :], nmrow, sst)
                V.wait(sst, sst.n)
                t_d = attn_divide_head(p, o32, o16h, dance, lrow, lrow, lcols,
                                       pool, ident32, ones32, 0)
                p.qwait("gpsimd", V.sem, t_d)
                p.dma("gpsimd", o_fm_d[h * 64:(h + 1) * 64, :], o16h[0:64, :], sst)

        full_barrier(p)
        esA.close()

        stop_attn = cfg.get("stop") == "attn" and layer == DEPTH - 1
        dst_f = io["sub_tm"] if stop_attn else x_f[1]
        dst_h = x_h[1]
        esP = contextlib.ExitStack()
        ofm = [p.sb(f"{tag}_pofm{layer}_{c}", [128, T], BF16, esP) for c in range(NIC)]
        wo16 = p.sb(f"{tag}_pwo{layer}", [128, NIC * D], BF16, esP)
        padd = [p.sb(f"{tag}_padd{layer}_{i}", [128, D], F32, esP) for i in range(2)]
        psub = [p.sb(f"{tag}_psub{layer}_{i}", [128, D], F32, esP) for i in range(2)]
        opool = PsumPool(p, f"{tag}_pop{layer}", 4, [128, 512], es=esP)
        p.qwait("sync", sst, sst.n)
        for c in range(NIC):
            p.dma("sync", ofm[c][:, :], o_fm_d[c * 128:(c + 1) * 128, :], sld)
        t_of = sld.n
        wch, t_w = load_weight16(p, weights[f"{tag}_wo"][layer], wo16, sw, INNER, D)
        st_done = {}
        padd_rd = {}
        for j in range(NT):
            i = j % 2
            if j - 2 in st_done:
                V.wait(sst, st_done[j - 2])
            if j - 2 in padd_rd:
                p.qwait("sync", V.sem, padd_rd[j - 2])
            t_a = p.dma("sync", padd[i][:, :], lsrc_f[j * 128:(j + 1) * 128, :], sld)
            t_e = None
            for ns in range(2):
                pk, war = opool.get()
                if war is not None:
                    Tn.wait(war[0], war[1])
                Tn.wait(sld, t_of)
                Tn.wait(sw, t_w)
                pairs = [(ofm[c][:, j * 128:(j + 1) * 128],
                          wo16[:, (c * 2 + ns) * 512:(c * 2 + ns + 1) * 512])
                         for c in range(NIC)]
                t_mm = mm_group(p, pk[:, :], pairs)
                V.wait(Tn.sem, t_mm)
                V.wait(sld, t_a)
                t_e = V("tensor_add", psub[i][:, ns * 512:(ns + 1) * 512], pk[:, :],
                        padd[i][:, ns * 512:(ns + 1) * 512])
                opool.done(p.V.sem, t_e)
            padd_rd[j] = t_e
            p.qwait("gpsimd", V.sem, t_e)
            p.dma("gpsimd", dst_f[j * 128:(j + 1) * 128, :], psub[i][:, :], sst)
            p.dma("gpsimd", dst_h[j * 128:(j + 1) * 128, :], psub[i][:, :], sst)
            st_done[j] = sst.n
        full_barrier(p)
        esP.close()
        if stop_attn:
            io["sub_done"] = (sst, sst.n)
            continue

        esB = contextlib.ExitStack()
        rowb1 = p.sb(f"{tag}_rowb1_{layer}", [1, 5460], BF16, esB)
        rowu = p.sb(f"{tag}_rowu_{layer}", [1, 5460], BF16, esB)
        rowb2 = p.sb(f"{tag}_rowb2_{layer}", [1, D], BF16, esB)
        b1row = rowb1[0:1, :]
        urow = rowu[0:1, :]
        b2row = rowb2[0:1, :]
        esF1 = contextlib.ExitStack()
        rowfB = p.sb(f"{tag}_rowfB{layer}", [33, T], F32, esF1)
        hfm2 = [p.sb(f"{tag}_h2fm{layer}_{c}", [128, CW], BF16, esF1) for c in range(8)]
        rb = p.sb(f"{tag}_rbs{layer}", [128, T], BF16, esF1)
        xrb = [p.sb(f"{tag}_xrb{layer}_{i}", [128, T], BF16, esF1) for i in range(2)]
        sqf = p.sb(f"{tag}_sqf{layer}", [128, 512], BF16, esF1)
        strow = PsumPool(p, f"{tag}_strow{layer}", 2, [1, 512], es=esF1)
        spool = PsumPool(p, f"{tag}_spool{layer}", 2, [128, 512], es=esF1)
        for c in range(8):
            V("memset", hfm2[c][:, 0:PAD], 0.0)
        p.qwait("sync", sst, sst.n)
        t_fm2 = emit_fm_load(p, dst_h, hfm2, sfm)
        p.qwait("gpsimd", V.sem, V.sem.n)
        t_b1raw = p.dma("gpsimd", b1row, weights[f"{tag}_b1"][layer][:, :], sw)
        t_b2 = p.dma("sync", rowf[96:97, 0:D], weights[f"{tag}_b2"][layer][:, :], sld)
        V.wait(sld, t_b2)
        t_b2r = V("tensor_copy", b2row, rowf[96:97, 0:D])

        for which in (0, 1):
            outrow = rows if which == 0 else rowfB[0:1, :]
            for ts in range(8):
                pk, war = strow.get()
                if war is not None:
                    Tn.wait(war[0], war[1])
                if which == 1:
                    last = None
                    for ic in range(8):
                        S.wait(Tn.sem, Tn.sem.n)
                        S.wait(sfm, t_fm2)
                        t_sq = S("activation", sqf[:, :],
                                 sh(hfm2, ic, ts * 512, 512), AF.Square)
                        Tn.wait(S.sem, t_sq)
                        last = nc.tensor.matmul(pk[0:1, :], ones16[:, 0:1],
                                                sqf[:, :], start=(ic == 0),
                                                stop=(ic == 7),
                                                skip_group_check=True)
                    t_mm = Tn.sem.inc()
                    last.then_inc(Tn.sem.h, 1)
                else:
                    Tn.wait(sfm, t_fm2)
                    t_mm = mm_group(p, pk[0:1, :], [
                        (ones16[:, 0:1], sh(hfm2, ic, ts * 512, 512))
                        for ic in range(8)])
                V.wait(Tn.sem, t_mm)
                t_r = V("tensor_copy", outrow[0:1, ts * 512:(ts + 1) * 512],
                        pk[0:1, :])
                strow.done(p.V.sem, t_r)
        B0 = rowfB[0:1, :]
        B32 = rowfB[32:33, :]
        V("tensor_scalar_mul", rows, rows, 1.0 / D)
        V("tensor_scalar_mul", srow, rows, -1.0)
        V("tensor_mul", rows, rows, rows)
        V("tensor_scalar_mul", B0, B0, 1.0 / D)
        t_var = V("tensor_sub", B0, B0, rows)
        S.wait(V.sem, t_var)
        S("activation", B0, B0, AF.Ln, bias=eps5_ap[0:1, 0:1])
        t_rst = S("activation", B0, B0, AF.Exp, scale=-0.5)
        V.wait(S.sem, t_rst)
        V("tensor_copy", B32, B0)
        t_mur = V("tensor_mul", murrow, srow, B32)
        for ts in range(8):
            pk, war = spool.get()
            if war is not None:
                Tn.wait(war[0], war[1])
            Tn.wait(V.sem, t_mur)
            t_mm = mm_group(p, pk[:, :], [(ones32[0:1, 0:128],
                                           B0[0:1, ts * 512:(ts + 1) * 512])])
            V.wait(Tn.sem, t_mm)
            t_rb = V("tensor_copy", rb[:, ts * 512:(ts + 1) * 512], pk[:, :])
            spool.done(p.V.sem, t_rb)
        xst = {}
        for c in range(8):
            if c - 2 in xst:
                V.wait(sxr, xst[c - 2])
            t_x = V("tensor_mul", xrb[c % 2][:, :], sh(hfm2, c, 0, T), rb[:, :])
            p.qwait("gpsimd", V.sem, t_x)
            p.dma("gpsimd", xr_d[c * 128:(c + 1) * 128, :], xrb[c % 2][:, :], sxr)
            xst[c] = sxr.n
        full_barrier(p)
        esF1.close()
        if cfg.get("stop") == "xr" and layer == DEPTH - 1:
            scp = p.sem(f"{tag}_scp{layer}")
            p.qwait("gpsimd", sst, sst.n)
            for j in range(NT):
                p.dma("gpsimd", io["sub_tm"][j * 128:(j + 1) * 128, :],
                      dst_f[j * 128:(j + 1) * 128, :], scp)
            io["sub_done"] = (scp, scp.n)
            continue

        out_f = io["sub_tm"] if layer == DEPTH - 1 else x_f[0]
        out_h = x_h[0]
        esF2 = contextlib.ExitStack()
        Hp = [p.sb(f"{tag}_Hp{layer}_{c}", [128, T], BF16, esF2) for c in range(2)]
        acc = p.sb(f"{tag}_facc{layer}", [128, NT * D], BF16, esF2)
        xsl = p.sb(f"{tag}_xsl{layer}", [128, 8 * 512], BF16, esF2)
        wA = p.sb(f"{tag}_wA{layer}", [128, 8 * 256], BF16, esF2)
        wG = p.sb(f"{tag}_wG{layer}", [128, 8 * 256], BF16, esF2)
        w2b = p.sb(f"{tag}_w2b{layer}", [128, 2 * D], BF16, esF2)
        gg = [p.sb(f"{tag}_gg{layer}_{i}", [128, 512], BF16, esF2) for i in range(2)]
        fadd = [p.sb(f"{tag}_fadd{layer}", [128, D], F32, esF2)]
        fsub = [p.sb(f"{tag}_fsub{layer}", [128, D], F32, esF2)]
        gpool = PsumPool(p, f"{tag}_gp{layer}", 2, [128, 512], es=esF2)
        apool = PsumPool(p, f"{tag}_ap{layer}", 2, [128, 512], es=esF2)
        dpool = PsumPool(p, f"{tag}_dp{layer}", 2, [128, 512], es=esF2)
        rpool = PsumPool(p, f"{tag}_rp{layer}", 1, [1, 512], es=esF2)

        PARTS = [(k * 2, min(2, NHG - k * 2)) for k in range(11)]
        for pi, (c0, nch) in enumerate(PARTS):
            col_lo = c0 * 128
            width = min(2730, col_lo + nch * 128) - col_lo
            p.qwait("gpsimd", Tn.sem, Tn.sem.n)
            for blk, wdst, base in ((0, wA, col_lo), (1, wG, 2730 + col_lo)):
                for c in range(8):
                    p.dma("gpsimd", wdst[:, c * 256:c * 256 + width],
                          weights[f"{tag}_w1"][layer][c * 128:(c + 1) * 128,
                                                      base:base + width], sw)
            t_w1 = sw.n
            V.wait(sw, t_b1raw)
            for blk, wdst, base in ((0, wA, col_lo), (1, wG, 2730 + col_lo)):
                pk, war = rpool.get()
                if war is not None:
                    Tn.wait(war[0], war[1])
                Tn.wait(sw, t_w1)
                Tn.wait(V.sem, V.sem.n)
                last = None
                for c in range(8):
                    last = nc.tensor.matmul(pk[0:1, 0:width], b_cols[c],
                                            wdst[:, c * 256:c * 256 + width],
                                            start=(c == 0), stop=(c == 7))
                t_mm = Tn.sem.inc()
                last.then_inc(Tn.sem.h, 1)
                V.wait(Tn.sem, t_mm)
                t_b1 = V("scalar_tensor_tensor", b1row[0:1, base:base + width],
                         pk[0:1, 0:width], 1.0, b1row[0:1, base:base + width],
                         ALU.mult, ALU.add)
                rpool.done(p.V.sem, t_b1)
            t_gs = 0
            for blk, wdst in ((0, wA), (1, wG)):
                for c in range(8):
                    t_gs = V("tensor_scalar_mul", wdst[:, c * 256:c * 256 + width],
                             wdst[:, c * 256:c * 256 + width], g_cols[c])
            for blk, wdst, base in ((0, wA, col_lo), (1, wG, 2730 + col_lo)):
                pk, war = rpool.get()
                if war is not None:
                    Tn.wait(war[0], war[1])
                Tn.wait(V.sem, t_gs)
                last = None
                for c in range(8):
                    last = nc.tensor.matmul(pk[0:1, 0:width], ones16[:, 0:1],
                                            wdst[:, c * 256:c * 256 + width],
                                            start=(c == 0), stop=(c == 7))
                t_mm = Tn.sem.inc()
                last.then_inc(Tn.sem.h, 1)
                V.wait(Tn.sem, t_mm)
                t_u = V("tensor_copy", urow[0:1, base:base + width],
                        pk[0:1, 0:width])
                rpool.done(p.V.sem, t_u)
            t_prep = p.V.sem.n

            gg_rd = {}
            k = 0
            for ts in range(8):
                p.qwait("sync", Tn.sem, Tn.sem.n)
                for ic in range(8):
                    p.dma("sync", xsl[:, ic * 512:(ic + 1) * 512],
                          xr_d[ic * 128:(ic + 1) * 128, ts * 512:(ts + 1) * 512],
                          sld)
                t_xs = sld.n
                for cc in range(nch):
                    cw = min(128, 2730 - (c0 + cc) * 128)
                    gbase = 2730 + (c0 + cc) * 128
                    abase = (c0 + cc) * 128
                    pg, warg = gpool.get()
                    if warg is not None:
                        Tn.wait(warg[0], warg[1])
                    Tn.wait(V.sem, max(t_prep, t_b2r))
                    Tn.wait(sld, t_xs)
                    pairs = [(wG[:, ic * 256 + cc * 128:ic * 256 + cc * 128 + cw],
                              xsl[:, ic * 512:(ic + 1) * 512]) for ic in range(8)]
                    if "norank1" not in BISECT:
                        pairs.append((b1row[0:1, gbase:gbase + cw], ones16[0:1, 0:512]))
                        pairs.append((urow[0:1, gbase:gbase + cw],
                                      murrow[0:1, ts * 512:(ts + 1) * 512]))
                    t_mm = mm_group(p, pg[0:cw, :], pairs)
                    b = k % 2
                    S.wait(Tn.sem, t_mm)
                    if k - 2 in gg_rd:
                        S.wait(V.sem, gg_rd[k - 2])
                    t_gel = S("activation", gg[b][0:cw, :], pg[0:cw, :],
                              AF.Relu if "nogelu" in BISECT else AF.Gelu)
                    gpool.done(p.S.sem, t_gel)
                    pa, wara = apool.get()
                    if wara is not None:
                        Tn.wait(wara[0], wara[1])
                    pairs = [(wA[:, ic * 256 + cc * 128:ic * 256 + cc * 128 + cw],
                              xsl[:, ic * 512:(ic + 1) * 512]) for ic in range(8)]
                    if "norank1" not in BISECT:
                        pairs.append((b1row[0:1, abase:abase + cw], ones16[0:1, 0:512]))
                        pairs.append((urow[0:1, abase:abase + cw],
                                      murrow[0:1, ts * 512:(ts + 1) * 512]))
                    t_mm2 = mm_group(p, pa[0:cw, :], pairs)
                    V.wait(Tn.sem, t_mm2)
                    V.wait(S.sem, t_gel)
                    t_h = V("tensor_mul", Hp[cc][0:cw, ts * 512:(ts + 1) * 512],
                            pa[0:cw, :], gg[b][0:cw, :])
                    gg_rd[k] = t_h
                    apool.done(p.V.sem, t_h)
                    k += 1
            t_hp = p.V.sem.n
            if cfg.get("stop") == "h":
                continue

            p.qwait("gpsimd", Tn.sem, Tn.sem.n)
            for cc in range(nch):
                cw = min(128, 2730 - (c0 + cc) * 128)
                p.dma("gpsimd", w2b[0:cw, cc * D:(cc + 1) * D],
                      weights[f"{tag}_w2"][layer][(c0 + cc) * 128:(c0 + cc) * 128 + cw, :],
                      sw)
            t_w2 = sw.n
            st_done = {}
            final = (pi == len(PARTS) - 1)
            for j in range(NT):
                i = 0
                if final and j - 1 in st_done:
                    V.wait(sst, st_done[j - 1])
                if final:
                    p.qwait("sync", V.sem, V.sem.n)
                    t_a2 = p.dma("sync", fadd[i][:, :],
                                 dst_f[j * 128:(j + 1) * 128, :], sld)
                t_e2 = None
                for ns in range(2):
                    pk, war = dpool.get()
                    if war is not None:
                        Tn.wait(war[0], war[1])
                    Tn.wait(sw, t_w2)
                    Tn.wait(V.sem, t_hp)
                    pairs = [(Hp[cc][0:min(128, 2730 - (c0 + cc) * 128),
                                     j * 128:(j + 1) * 128],
                              w2b[0:min(128, 2730 - (c0 + cc) * 128),
                                  cc * D + ns * 512:cc * D + ns * 512 + 512])
                             for cc in range(nch)]
                    if final:
                        pairs.append((ones16[0:1, 0:128],
                                      b2row[0:1, ns * 512:(ns + 1) * 512]))
                    t_mm = mm_group(p, pk[:, :], pairs)
                    V.wait(Tn.sem, t_mm)
                    if pi == 0:
                        t_e2 = V("tensor_copy",
                                 acc[:, j * D + ns * 512:j * D + (ns + 1) * 512],
                                 pk[:, :])
                    elif not final:
                        t_e2 = V("scalar_tensor_tensor",
                                 acc[:, j * D + ns * 512:j * D + (ns + 1) * 512],
                                 pk[:, :], 1.0,
                                 acc[:, j * D + ns * 512:j * D + (ns + 1) * 512],
                                 ALU.mult, ALU.add)
                    else:
                        fslc = fsub[i][:, ns * 512:(ns + 1) * 512]
                        V("tensor_copy", fslc,
                          acc[:, j * D + ns * 512:j * D + (ns + 1) * 512])
                        V("tensor_add", fslc, fslc, pk[:, :])
                        V.wait(sld, t_a2)
                        t_e2 = V("tensor_add", fslc, fslc,
                                 fadd[i][:, ns * 512:(ns + 1) * 512])
                    dpool.done(p.V.sem, t_e2)
                if final:
                    p.qwait("gpsimd", V.sem, t_e2)
                    p.dma("gpsimd", out_f[j * 128:(j + 1) * 128, :], fsub[i][:, :], sst)
                    if layer < DEPTH - 1:
                        p.dma("gpsimd", out_h[j * 128:(j + 1) * 128, :],
                              fsub[i][:, :], sst)
                    st_done[j] = sst.n
        full_barrier(p)
        esF2.close()
        esB.close()
        if cfg.get("stop") == "h" and layer == DEPTH - 1:
            scp = p.sem(f"{tag}_scph{layer}")
            p.qwait("gpsimd", sst, sst.n)
            for j in range(NT):
                p.dma("gpsimd", io["sub_tm"][j * 128:(j + 1) * 128, :],
                      dst_f[j * 128:(j + 1) * 128, :], scp)
            io["sub_done"] = (scp, scp.n)

    if cfg.get("stop") == "h":
        return
    io["sub_done"] = (sst, sst.n)


def _xpos_tables(Wl):
    half = np.arange(0, 64, 2, dtype=np.float64)
    inv_freq = 1.0 / 10000.0 ** (half / 64)
    t = np.arange(T, dtype=np.float64)
    freqs = t[:, None] * inv_freq[None, :]
    freqs = np.concatenate([freqs, freqs], -1)
    base = (half + 0.4 * 64) / (1.4 * 64)
    power = (t - T // 2) / (Wl // 2)
    scale = base[None, :] ** power[:, None]
    scale = np.concatenate([scale, scale], -1)
    cos, sin = np.cos(freqs), np.sin(freqs)
    cS = (cos * scale).T
    sS = (sin * scale).T
    cI = (cos / scale).T
    sI = (sin / scale).T
    tile2 = lambda a: np.tile(a, (2, 1)).astype(BF)
    return tile2(cS), tile2(sS), tile2(cI), tile2(sI)


def _local_masks(Wl):
    qp = np.arange(128)[:, None]
    kp = np.arange(Wl + 128)[None, :]
    mA = np.where((kp < qp) | (kp > qp + Wl), NEG, 0.0).astype(np.float32)
    kp2 = np.arange(128)[:, None]
    qp2 = np.arange(Wl + 128)[None, :]
    mB = np.where((qp2 < kp2) | (qp2 > kp2 + Wl), NEG, 0.0).astype(np.float32)
    return mA, mB


def _rot_perm128():
    P = np.zeros((64, 64), np.float32)
    for i in range(32):
        P[i + 32, i] = -1.0
        P[i, i + 32] = 1.0
    return np.kron(np.eye(2, dtype=np.float32), P).astype(BF)


def _local_host_prep(inputs, tag, pre, depth, Wl):
    w = {}
    w[f"{tag}_wqkv"] = np.ascontiguousarray(inputs[f"{pre}_wqkv" if tag == "l3f" else f"{pre}_Wqkv"])
    w[f"{tag}_wo"] = np.ascontiguousarray(inputs[f"{pre}_wo" if tag == "l3f" else f"{pre}_Wo"])
    w[f"{tag}_w1"] = np.ascontiguousarray(inputs[f"{pre}_w1" if tag == "l3f" else f"{pre}_W1"])
    w[f"{tag}_w2"] = np.ascontiguousarray(inputs[f"{pre}_w2" if tag == "l3f" else f"{pre}_W2"])
    w[f"{tag}_b1"] = np.ascontiguousarray(inputs[f"{pre}_b1"].reshape(depth, 1, 5460))
    w[f"{tag}_b2"] = np.ascontiguousarray(inputs[f"{pre}_b2"].reshape(depth, 1, D))
    qs = inputs[f"{pre}_q_scale"]
    ks = inputs[f"{pre}_k_scale"]
    qk = np.zeros((128, 2 * depth), np.float32)
    for l in range(depth):
        qk[:, l] = np.tile(qs[l], 2)
        qk[:, depth + l] = np.tile(ks[l], 2)
    w[f"{tag}_qkS"] = qk
    lg = inputs[f"{pre}_ln_g"]
    lb = inputs[f"{pre}_ln_b"]
    lnb = np.zeros((128, 16 * depth), np.float32)
    for l in range(depth):
        lnb[:, 16 * l:16 * l + 8] = lg[l].reshape(8, 128).T
        lnb[:, 16 * l + 8:16 * l + 16] = lb[l].reshape(8, 128).T
    w[f"{tag}_lnb"] = lnb
    cS, sS, cI, sI = _xpos_tables(Wl)
    w[f"{tag}_ctabS"] = cS
    w[f"{tag}_stabS"] = sS
    w[f"{tag}_ctabI"] = cI
    w[f"{tag}_stabI"] = sI
    mA, mB = _local_masks(Wl)
    w[f"{tag}_maskA"] = mA
    w[f"{tag}_maskB"] = mB
    w[f"{tag}_blkS"] = np.kron(np.eye(2, dtype=np.float32),
                               np.ones((64, 64), np.float32)).astype(BF)
    w[f"{tag}_perm"] = _rot_perm128()
    return w


def host_prep_L3_ffn(inputs):
    return _local_host_prep(inputs, "l3f", "L3_ffn", 2, 512)


def host_prep_L0_ffn(inputs):
    return _local_host_prep(inputs, "l0f", "L0_ffn", 6, 256)


def _local_specs(tag, depth, Wl, inner):
    return {
        f"{tag}_wqkv": ([depth, D, 3 * inner], F32),
        f"{tag}_wo": ([depth, inner, D], F32),
        f"{tag}_w1": ([depth, D, 5460], F32),
        f"{tag}_w2": ([depth, 2730, D], F32),
        f"{tag}_b1": ([depth, 1, 5460], F32),
        f"{tag}_b2": ([depth, 1, D], F32),
        f"{tag}_qkS": ([128, 2 * depth], F32),
        f"{tag}_lnb": ([128, 16 * depth], F32),
        f"{tag}_ctabS": ([128, T], BF16),
        f"{tag}_stabS": ([128, T], BF16),
        f"{tag}_ctabI": ([128, T], BF16),
        f"{tag}_stabI": ([128, T], BF16),
        f"{tag}_maskA": ([128, Wl + 128], F32),
        f"{tag}_maskB": ([128, Wl + 128], F32),
        f"{tag}_blkS": ([128, 128], BF16),
        f"{tag}_perm": ([128, 128], BF16),
    }


L3F_WEIGHT_SPECS = _local_specs("l3f", 2, 512, 512)
L0F_WEIGHT_SPECS = _local_specs("l0f", 6, 256, 1024)


def emit_L3_ffn(p, io, weights, consts):
    emit_local_stack(p, io, weights, consts,
                     dict(tag="l3f", depth=2, H=8, W=512, inner=512))


def emit_L0_ffn(p, io, weights, consts):
    emit_local_stack(p, io, weights, consts,
                     dict(tag="l0f", depth=6, H=16, W=256, inner=1024))


def build_stage_L3_ffn():
    return build_stage(emit_L3_ffn, L3F_WEIGHT_SPECS, need_ln32=True)


def build_stage_L0_ffn():
    return build_stage(emit_L0_ffn, L0F_WEIGHT_SPECS, need_ln32=True)



FUSED_STAGES = [
    ("L0_attn", emit_L0_attn, L0A_WEIGHT_SPECS, host_prep_L0_attn, False, False),
    ("L0_ffn", emit_L0_ffn, L0F_WEIGHT_SPECS, host_prep_L0_ffn, False, True),
    ("L1_attn", emit_L1_attn, L1A_WEIGHT_SPECS, host_prep_L1_attn, False, True),
    ("L1_ffn", emit_L1_ffn, L1F_WEIGHT_SPECS, host_prep_L1_ffn, False, False),
    ("L2_attn", emit_L2_attn, L2A_WEIGHT_SPECS, host_prep_L2_attn, False, False),
    ("L2_ffn", emit_L2_ffn, L2F_WEIGHT_SPECS, host_prep_L2_ffn, False, False),
    ("L3_attn", emit_L3_attn, L3A_WEIGHT_SPECS, host_prep_L3_attn, True, False),
    ("L3_ffn", emit_L3_ffn, L3F_WEIGHT_SPECS, host_prep_L3_ffn, False, True),
]


def build_fused():
    p = Prog()
    x_in = p.inp("x", [T, D], F32)
    x_out = p.out("x_out", [T, D], F32)
    weights = {}
    for _, _, spec, _, _, _ in FUSED_STAGES:
        for n, (sh, dt) in spec.items():
            weights[n] = p.inp(n, sh, dt)
    p.prealloc_cbuf()
    consts = {}
    for n, (sh, dt) in CONST_SPECS.items():
        ap = p.inp(n, sh, dt)
        sbuf = p.sb(n + "_sb", sh, dt, es=p.es)
        consts[n + "_dram"] = ap
        consts[n.replace("c_", "") + "_sb"] = sbuf[:, :]
    sc = p.sem("consts")
    for n in CONST_SPECS:
        p.dma("sync", consts[n.replace("c_", "") + "_sb"][:, :],
              consts[n + "_dram"][:, :], sc)
    p.qwait("sync", sc, sc.n)

    xa = p.dram("x_ping", [T, D], F32)
    xb = p.dram("x_pong", [T, D], F32)
    io = {}
    io["ln16_tm"] = p.dram("ln16_tm", [T, D], BF16)
    io["ln32_tm"] = p.dram("ln32_tm", [T, D], F32)
    io["sub_tm"] = p.dram("sub_tm", [T, D], F32)
    ldx, sg = p.sem("ldx"), p.sem("sg")

    for i, (name, emit_fn, spec, _, rms2, need32) in enumerate(FUSED_STAGES):
        src = x_in if i == 0 else (xa if i % 2 == 1 else xb)
        dst = x_out if i == len(FUSED_STAGES) - 1 else (xa if i % 2 == 0 else xb)
        es_stage = contextlib.ExitStack()
        p.es_stack.append(es_stage)

        es_pre = contextlib.ExitStack()
        p.es_stack.append(es_pre)
        p.qwait("sync", sg, sg.n)
        full_barrier(p)
        ln = LnPass(p, f"pre{i}", needadd=False, need32=need32)
        t_pre = ln.emit(ldx, sg, src, out16=io["ln16_tm"],
                        out32=(io["ln32_tm"] if need32 else None), rms2=rms2)
        io["pre_done"] = (sg, t_pre)
        barrier(p, io["pre_done"])
        p.es_stack.pop()
        es_pre.close()

        emit_fn(p, io, weights, consts)

        barrier(p, io["sub_done"])
        es_post = contextlib.ExitStack()
        p.es_stack.append(es_post)
        ln2 = LnPass(p, f"post{i}", need16=False)
        t_post = ln2.emit(ldx, sg, io["sub_tm"], out32=dst, add=src)
        p.qwait("sync", sg, t_post)
        p.es_stack.pop()
        es_post.close()
        full_barrier(p)
        p.es_stack.pop()
        es_stage.close()
    return p


STAGES = [
    ("L0_attn", build_stage_L0_attn, host_prep_L0_attn),
    ("L0_ffn", build_stage_L0_ffn, host_prep_L0_ffn),
    ("L1_attn", build_stage_L1_attn, host_prep_L1_attn),
    ("L1_ffn", build_stage_L1_ffn, host_prep_L1_ffn),
    ("L2_attn", build_stage_L2_attn, host_prep_L2_attn),
    ("L2_ffn", build_stage_L2_ffn, host_prep_L2_ffn),
    ("L3_attn", build_stage_L3_attn, host_prep_L3_attn),
    ("L3_ffn", build_stage_L3_ffn, host_prep_L3_ffn),
]

_prog_cache = {}


def kernel(**inputs):
    consts = build_constants()
    x = np.ascontiguousarray(inputs["L0_attn_x"], dtype=np.float32)
    if "fused" not in _prog_cache:
        _prog_cache["fused"] = build_fused()
    p = _prog_cache["fused"]
    w = {}
    for _, _, _, prep_fn, _, _ in FUSED_STAGES:
        w.update(prep_fn(inputs))
    w = {k: np.ascontiguousarray(v) for k, v in w.items()}
    in_maps = []
    for c in range(8):
        m = dict(x=np.ascontiguousarray(x[c]))
        m.update(w)
        m.update(consts)
        in_maps.append(m)
    res = run_bass_kernel_spmd(p.nc, in_maps, core_ids=list(range(8)))
    return np.stack([np.asarray(res.results[c]["x_out"]) for c in range(8)])


# version 03
# speedup vs baseline: 1.2032x; 1.2032x over previous
import contextlib
import math

import numpy as np
import ml_dtypes

import concourse.bass as bass
import concourse.mybir as mybir
from concourse.bass_utils import run_bass_kernel_spmd

F32 = mybir.dt.float32
BF16 = mybir.dt.bfloat16
AF = mybir.ActivationFunctionType
ALU = mybir.AluOpType
AX = mybir.AxisListType

T = 4096
D = 1024
NT = T // 128
NCH = D // 128
PAD = 16

NEG = -1.0e30
BF = ml_dtypes.bfloat16
DBG = False
BISECT = set()


class Sem:
    __slots__ = ("h", "n")

    def __init__(self, h):
        self.h = h
        self.n = 0

    def inc(self, k=1):
        self.n += k
        return self.n


class Eng:

    def __init__(self, p, name):
        self.p = p
        self.name = name
        self.h = getattr(p.nc, name)
        self.sem = p.sem(f"{name}sf")

    def wait(self, sem, n):
        if n > 0:
            self.h.wait_ge(sem.h, n)

    def __call__(self, method, *a, chain=True, **kw):
        if chain and self.sem.n > 0:
            self.h.wait_ge(self.sem.h, self.sem.n)
        inst = getattr(self.h, method)(*a, **kw)
        t = self.sem.inc()
        inst.then_inc(self.sem.h, 1)
        return t


class Prog:

    def __init__(self):
        self.nc = bass.Bass("TRN2", target_bir_lowering=False)
        self.es = contextlib.ExitStack()
        self._semi = 0
        self.input_names = []
        self._cmap = None
        self.es_stack = [self.es]
        self.V = Eng(self, "vector")
        self._pre_v = None
        self.S = Eng(self, "scalar")
        self.G = Eng(self, "gpsimd")
        self.Tn = Eng(self, "tensor")

    def inp(self, name, shape, dtype):
        self.input_names.append(name)
        return self.nc.dram_tensor(name, list(shape), dtype, kind="ExternalInput").ap()

    def out(self, name, shape, dtype):
        return self.nc.dram_tensor(name, list(shape), dtype, kind="ExternalOutput").ap()

    def dram(self, name, shape, dtype):
        return self.nc.dram_tensor(name, list(shape), dtype, kind="Internal").ap()

    def sem(self, name=None):
        self._semi += 1
        h = self.es.enter_context(self.nc.semaphore(name or f"s{self._semi}"))
        return Sem(h)

    def sb(self, name, shape, dtype, es=None):
        return (es or self.es_stack[-1]).enter_context(
            self.nc.sbuf_tensor(name, list(shape), dtype))

    def ps(self, name, shape, dtype, es=None):
        return (es or self.es_stack[-1]).enter_context(
            self.nc.psum_tensor(name, list(shape), dtype))

    def dma(self, q, out, in_, sem, transpose=False):
        eng = getattr(self.nc, q)
        if transpose:
            inst = eng.dma_start_transpose(out, in_)
        else:
            inst = eng.dma_start(out=out, in_=in_)
        inst.then_inc(sem.h, 16)
        return sem.inc(16)

    def qwait(self, q, sem, n):
        if n > 0:
            getattr(self.nc, q).wait_ge(sem.h, n)

    def prealloc_cbuf(self):
        if self._cmap is None:
            self._cbuf = self.sb("cbuf", [128, 16], F32, es=self.es)
            self._cmap = {}
            self._cn = 0

    def const_col(self, val, width=1):
        if self._cmap is None:
            self.prealloc_cbuf()
        key = (float(val), width)
        if key in self._cmap:
            return self._cmap[key]
        k = self._cn
        self._cn += width
        assert self._cn <= 16
        ap = self._cbuf[:, k:k + width]
        t = self.V("memset", ap, float(val))
        self._cmap[key] = (ap, t)
        return (ap, t)



def build_constants():
    c = {}
    c["c_ident16"] = np.eye(128, dtype=np.float32).astype(BF)
    c["c_ident32"] = np.eye(128, dtype=np.float32)
    c["c_ones16"] = np.ones((128, 512), dtype=np.float32).astype(BF)
    c["c_ones32"] = np.ones((128, 512), dtype=np.float32)
    return c


CONST_SPECS = {
    "c_ident16": ([128, 128], BF16),
    "c_ident32": ([128, 128], F32),
    "c_ones16": ([128, 512], BF16),
    "c_ones32": ([128, 512], F32),
}


def declare_consts(p, names):
    return {n: p.inp(n, *CONST_SPECS[n]) for n in names}



class LnPass:

    def __init__(self, p, tag, es=None, need16=True, need32=True, needadd=True):
        self.p = p
        self.xt = [p.sb(f"{tag}_xt{i}", [128, D], F32, es) for i in range(2)]
        self.at = [p.sb(f"{tag}_at{i}", [128, D], F32, es) for i in range(2)] if needadd else None
        self.o16 = [p.sb(f"{tag}_o16_{i}", [128, D], BF16, es) for i in range(2)] if need16 else None
        self.o32 = [p.sb(f"{tag}_o32_{i}", [128, D], F32, es) for i in range(2)] if need32 else None
        self.sq = [p.sb(f"{tag}_sq{i}", [128, D], BF16, es) for i in range(2)]
        self.ssq = [p.sb(f"{tag}_ssq{i}", [128, 1], F32, es) for i in range(2)]
        self.mu = [p.sb(f"{tag}_mu{i}", [128, 1], F32, es) for i in range(2)]
        self.var = [p.sb(f"{tag}_var{i}", [128, 1], F32, es) for i in range(2)]
        self.rstd = [p.sb(f"{tag}_rstd{i}", [128, 1], F32, es) for i in range(2)]
        self.v_done = {}
        self.s_done = {}
        self.g_done = {}

    def emit(self, ldx, sg, src, out16=None, out32=None, add=None, rms2=False,
             eps=1e-5, tiles=None):
        p = self.p
        nc = p.nc
        V, S = p.V, p.S
        if tiles is None:
            tiles = range(NT)
        for j in tiles:
            i = j % 2
            if j - 2 in self.v_done:
                nc.sync.wait_ge(V.sem.h, self.v_done[j - 2])
                nc.sync.wait_ge(S.sem.h, self.s_done[j - 2])
            t_ld = p.dma("sync", self.xt[i][:, :], src[j * 128:(j + 1) * 128, :], ldx)
            if add is not None:
                t_ld = p.dma("sync", self.at[i][:, :], add[j * 128:(j + 1) * 128, :], ldx)

            V.wait(ldx, t_ld)
            V("reduce_sum", self.mu[i][:, :], self.xt[i][:, :], axis=AX.X)
            V("tensor_scalar_mul", self.mu[i][:, :], self.mu[i][:, :], 1.0 / D)
            V("tensor_mul", self.var[i][:, :], self.mu[i][:, :], self.mu[i][:, :])
            S.wait(ldx, t_ld)
            t_sq = S("activation", self.sq[i][:, :], self.xt[i][:, :], AF.Square,
                     accum_out=self.ssq[i][:, :])
            V.wait(S.sem, t_sq)
            t_var = V("scalar_tensor_tensor", self.var[i][:, :], self.ssq[i][:, :],
                      1.0 / D, self.var[i][:, :], ALU.mult, ALU.subtract)
            scale_ln, bias_ln = (1.0 + 1e-6, 1e-6 * eps) if rms2 else (1.0, eps)
            bias_ap, bias_t = p.const_col(bias_ln)
            S.wait(V.sem, max(t_var, bias_t))
            S("activation", self.rstd[i][:, :], self.var[i][:, :], AF.Ln,
              bias=bias_ap, scale=float(scale_ln))
            t_rstd = S("activation", self.rstd[i][:, :], self.rstd[i][:, :], AF.Exp,
                       scale=-0.5)

            V.wait(S.sem, t_rstd)
            if j - 2 in self.g_done:
                V.wait(sg, self.g_done[j - 2])
            outs = []
            t_norm = None
            if out16 is not None:
                t_norm = V("tensor_scalar", self.o16[i][:, :], self.xt[i][:, :],
                           self.mu[i][:, :], self.rstd[i][:, :], ALU.subtract, ALU.mult)
                outs.append((self.o16[i], out16))
            if out32 is not None:
                t_norm = V("tensor_scalar", self.o32[i][:, :], self.xt[i][:, :],
                           self.mu[i][:, :], self.rstd[i][:, :], ALU.subtract, ALU.mult)
                if add is not None:
                    t_norm = V("tensor_add", self.o32[i][:, :], self.o32[i][:, :],
                               self.at[i][:, :])
                outs.append((self.o32[i], out32))
            self.v_done[j] = t_norm
            self.s_done[j] = t_sq

            p.qwait("gpsimd", V.sem, t_norm)
            for buf, dst in outs:
                p.dma("gpsimd", dst[j * 128:(j + 1) * 128, :], buf[:, :], sg)
            self.g_done[j] = sg.n
        return sg.n



def emit_fm_load(p, src16_tm, fm_chunks, sem, col0=PAD, rows=None):
    if rows is None:
        rows = len(fm_chunks) * 128
    for c in range(rows // 128):
        p.dma("sync", fm_chunks[c][:, col0:col0 + T],
              src16_tm[:, c * 128:(c + 1) * 128], sem, transpose=True)
    return sem.n



def build_stage_identity():
    p = Prog()
    x_in = p.inp("x", [T, D], F32)
    x_out = p.out("x_out", [T, D], F32)
    ln16_tm = p.dram("ln16_tm", [T, D], BF16)
    sub_tm = p.dram("sub_tm", [T, D], F32)

    ldx, sg = p.sem("ldx"), p.sem("sg")
    ln = LnPass(p, "pre")
    t_pre = ln.emit(ldx, sg, x_in, out16=ln16_tm)

    p.qwait("gpsimd", sg, t_pre)
    scast = p.sem("scast")
    for j in range(NT):
        p.dma("gpsimd", sub_tm[j * 128:(j + 1) * 128, :],
              ln16_tm[j * 128:(j + 1) * 128, :], scast)
    p.qwait("sync", scast, scast.n)

    ln2 = LnPass(p, "post")
    t_post = ln2.emit(ldx, sg, sub_tm, out32=x_out, add=x_in)
    p.qwait("sync", sg, t_post)
    return p



def load_weight16(p, dram_ap, sbuf, sem, rows, cols):
    p.qwait("gpsimd", p.Tn.sem, p.Tn.sem.n)
    chunks = []
    for c in range(rows // 128):
        ap = sbuf[:, c * cols:(c + 1) * cols]
        p.dma("gpsimd", ap, dram_ap[c * 128:(c + 1) * 128, :], sem)
        chunks.append(ap)
    return chunks, sem.n


def mm_group(p, psum_ap, pairs):
    nc = p.nc
    last = None
    n = len(pairs)
    for i, (l, r) in enumerate(pairs):
        last = nc.tensor.matmul(psum_ap, l, r, start=(i == 0), stop=(i == n - 1))
    t = p.Tn.sem.inc()
    last.then_inc(p.Tn.sem.h, 1)
    return t


class PsumPool:

    def __init__(self, p, tag, n, shape, dtype=F32, es=None):
        self.tiles = [p.ps(f"{tag}{i}", shape, dtype, es) for i in range(n)]
        self.evac = [None] * n
        self.k = 0
        self.n = n

    def get(self):
        i = self.k % self.n
        self.k += 1
        return self.tiles[i], self.evac[i]

    def done(self, sem, target):
        i = (self.k - 1) % self.n
        self.evac[i] = (sem, target)


def proj_fm(p, w_chunks, src_fm, out_chunks, pool, waits=(),
            bias_cols=None, nslice=512, col0=PAD, src_col0=PAD, evac=None):
    nc = p.nc
    for sem, n in waits:
        p.Tn.wait(sem, n)
    t_last = 0
    for oc in range(len(out_chunks) if out_chunks else (w_chunks[0].shape[1] // 128)):
        for ts in range(T // nslice):
            pk, war = pool.get()
            if war is not None:
                p.Tn.wait(war[0], war[1])
            t_mm = mm_group(p, pk[:, :], [
                (w_chunks[ic][:, oc * 128:(oc + 1) * 128],
                 src_fm[ic][:, src_col0 + ts * nslice:src_col0 + (ts + 1) * nslice])
                for ic in range(len(w_chunks))])
            p.V.wait(p.Tn.sem, t_mm)
            if evac is not None:
                t_last = evac(oc, ts, pk)
            elif bias_cols is not None:
                t_last = p.V("tensor_scalar_add",
                             out_chunks[oc][:, col0 + ts * nslice:col0 + (ts + 1) * nslice],
                             pk[:, :], bias_cols[:, oc:oc + 1])
            else:
                t_last = p.V("tensor_copy",
                             out_chunks[oc][:, col0 + ts * nslice:col0 + (ts + 1) * nslice],
                             pk[:, :])
            pool.done(p.V.sem, t_last)
    return t_last



class Dance:

    def __init__(self, p, tag, maxw=32, es=None):
        self.p = p
        self.ps = p.ps(f"{tag}_ps", [128, 128], F32, es)
        self.st = p.sb(f"{tag}_st", [32, 128], F32, es)
        self.dr = p.dram(f"{tag}_dr", [32 * 128], F32)
        self.sem = p.sem(f"{tag}_dma")

    def rows_from_cols(self, cols_ap, out_row, w, ident32, waits=()):
        p = self.p
        for sem, n in waits:
            p.Tn.wait(sem, n)
        t_tr = p.Tn.sem.inc()
        p.nc.tensor.transpose(self.ps[0:w, 0:128], cols_ap, ident32).then_inc(p.Tn.sem.h, 1)
        p.V.wait(p.Tn.sem, t_tr)
        t_e = p.V("tensor_copy", self.st[0:w, :], self.ps[0:w, 0:128])
        p.qwait("gpsimd", p.V.sem, t_e)
        t1 = p.dma("gpsimd", self.dr[0:w * 128].rearrange("(a b) -> a b", b=128),
                   self.st[0:w, :], self.sem)
        p.qwait("sync", self.sem, t1)
        t2 = p.dma("sync", out_row, self.dr[0:w * 128].rearrange("(o a) -> o a", o=1), self.sem)
        return t2

    def cols_from_row(self, row_ap, out_cols, w, ident32, waits=(), row_sem=None):
        p = self.p
        for sem, n in waits:
            p.qwait("gpsimd", sem, n)
        t1 = p.dma("gpsimd", self.dr[0:w * 128].rearrange("(o a) -> o a", o=1), row_ap, self.sem)
        p.qwait("sync", self.sem, t1)
        t2 = p.dma("sync", self.st[0:w, :],
                   self.dr[0:w * 128].rearrange("(a b) -> a b", b=128), self.sem)
        p.Tn.wait(self.sem, t2)
        t_tr = p.Tn.sem.inc()
        p.nc.tensor.transpose(self.ps[0:128, 0:w], self.st[0:w, 0:128],
                              ident32[0:w, 0:w]).then_inc(p.Tn.sem.h, 1)
        p.V.wait(p.Tn.sem, t_tr)
        t_e = p.V("tensor_copy", out_cols, self.ps[0:128, 0:w])
        return t_e



def barrier(p, *sts):
    for st_ in sts:
        sem, n = st_
        if n <= 0:
            continue
        for eng in (p.V, p.S, p.G, p.Tn):
            eng.wait(sem, n)
        p.qwait("sync", sem, n)


def full_barrier(p):
    barrier(p, (p.V.sem, p.V.sem.n), (p.S.sem, p.S.sem.n), (p.Tn.sem, p.Tn.sem.n))


def emit_L1_attn(p, io, weights, consts):
    nc = p.nc
    V, S, G, Tn = p.V, p.S, p.G, p.Tn
    ident32 = consts["ident32_sb"]
    ident16 = consts["ident16_sb"]
    ones16 = consts["ones16_sb"]
    ones32 = consts["ones32_sb"]
    CW = PAD + T

    qn_tm_d = p.dram("l1a_qn_tm", [T, D], BF16)
    kn_tm_d = p.dram("l1a_kn_tm", [T, D], BF16)

    scr_fm = [p.sb(f"l1a_scr{c}", [128, CW], BF16) for c in range(8)]
    srow = p.sb("l1a_srow", [1, D], F32)
    vs16 = p.sb("l1a_vs16", [1, T], BF16)
    tgrow = p.sb("l1a_tgrow", [1, D], F32)
    tg16 = p.sb("l1a_tg16", [1, D], BF16)
    tb = p.sb("l1a_tb", [128, D], F32)
    kse = p.sb("l1a_kse", [128, NT], BF16)

    sw = p.sem("l1a_sw")
    sld = p.sem("l1a_sld")
    sst = p.sem("l1a_sst")
    sfm = p.sem("l1a_sfm")

    es1 = contextlib.ExitStack()
    ln_fm = [p.sb(f"l1a_lnfm{c}", [128, CW], BF16, es1) for c in range(8)]
    w16 = p.sb("l1a_w16", [128, 8 * D], BF16, es1)
    bcols = p.sb("l1a_bcols_sb", [128, 24], F32, es1)
    gam = p.sb("l1a_gam_sb", [1, 1], F32, es1)
    sqs = p.sb("l1a_sqs", [128, T], BF16, es1)
    nrm = p.sb("l1a_nrm", [128, 8], F32, es1)
    ksf = p.sb("l1a_ksf", [128, NT], F32, es1)
    tmt = [p.sb(f"l1a_tmt{i}", [128, D], BF16, es1) for i in range(2)]
    scols = p.sb("l1a_scols", [128, 8], F32, es1)
    gcol = p.sb("l1a_gcol", [128, 1], F32, es1)
    pool = PsumPool(p, "l1a_pp", 2, [128, 512], es=es1)
    trp = PsumPool(p, "l1a_tr", 2, [128, 128], BF16, es=es1)
    ps_s = p.ps("l1a_ps_s", [1, 1024], F32, es1)
    dance = Dance(p, "l1a_dance", es=es1)

    p.qwait("sync", io["pre_done"][0], io["pre_done"][1])
    t_fm = emit_fm_load(p, io["ln16_tm"], ln_fm, sfm)
    t_b = p.dma("sync", bcols[:, :], weights["l1a_bcols"][:, :], sld)
    t_g = p.dma("sync", gam[:, :], weights["l1a_gamma"][:, :], sld)
    V.wait(sld, t_b)

    def l2norm_phase(which, bc0, out_tiles_d, s_accum):
        wch, t_w = load_weight16(p, weights[which], w16, sw, D, D)
        t_proj = proj_fm(p, wch, ln_fm, scr_fm, pool,
                         waits=[(sw, t_w), (sfm, t_fm)],
                         bias_cols=bcols[:, bc0:bc0 + 8])
        for c in range(8):
            S.wait(V.sem, t_proj)
            S("activation", sqs[:, :], scr_fm[c][:, PAD:PAD + T], AF.Square,
              accum_out=nrm[:, c:c + 1])
        S("activation", nrm[:, 0:8], nrm[:, 0:8], AF.Ln)
        t_rs = S("activation", nrm[:, 0:8], nrm[:, 0:8], AF.Exp, scale=-0.5)
        for c in range(8):
            V.wait(S.sem, t_rs)
            V("tensor_scalar_mul", scr_fm[c][:, PAD:PAD + T], scr_fm[c][:, PAD:PAD + T],
              nrm[:, c:c + 1])
        t_nrm = V.sem.n

        tile_st = {}
        s_last = None
        for j in range(NT):
            i = j % 2
            if j - 2 in tile_st:
                V.wait(sst, tile_st[j - 2])
            for c in range(8):
                pk, war = trp.get()
                if war is not None:
                    Tn.wait(war[0], war[1])
                Tn.wait(V.sem, t_nrm)
                t_tr = Tn.sem.inc()
                nc.tensor.transpose(pk[:, :],
                                    scr_fm[c][:, PAD + j * 128:PAD + (j + 1) * 128],
                                    ident16).then_inc(Tn.sem.h, 1)
                V.wait(Tn.sem, t_tr)
                t_e = V("tensor_copy", tmt[i][:, c * 128:(c + 1) * 128], pk[:, :])
                trp.done(V.sem, t_e)
            t_tile = V.sem.n
            if which == "l1a_wkT":
                V("reduce_sum", ksf[:, j:j + 1], tmt[i][:, :], axis=AX.X)
                V("tensor_scalar_add", kse[:, j:j + 1], ksf[:, j:j + 1], 1e-6)
            if s_accum:
                Tn.wait(V.sem, t_tile)
                for h2 in range(2):
                    s_last = nc.tensor.matmul(
                        ps_s[0:1, h2 * 512:(h2 + 1) * 512], kse[:, j:j + 1],
                        tmt[i][:, h2 * 512:(h2 + 1) * 512],
                        start=(j == 0), stop=(j == NT - 1), skip_group_check=True)
            p.qwait("gpsimd", V.sem, t_tile)
            p.dma("gpsimd", out_tiles_d[j * 128:(j + 1) * 128, :], tmt[i][:, :], sst)
            tile_st[j] = sst.n
        if s_accum:
            t_s = Tn.sem.inc()
            s_last.then_inc(Tn.sem.h, 1)
            return t_s
        return None

    l2norm_phase("l1a_wkT", 8, kn_tm_d, s_accum=False)
    t_sdone = l2norm_phase("l1a_wqT", 0, qn_tm_d, s_accum=True)
    V.wait(Tn.sem, t_sdone)
    t_srow = V("tensor_copy", srow[0:1, :], ps_s[0:1, :])

    wch, t_w = load_weight16(p, weights["l1a_wvT"], w16, sw, D, D)
    t_vproj = proj_fm(p, wch, ln_fm, scr_fm, pool, waits=[(sw, t_w)],
                      bias_cols=bcols[:, 16:24])
    v_fm = scr_fm
    for ts in range(8):
        pk, war = pool.get()
        if war is not None:
            Tn.wait(war[0], war[1])
        Tn.wait(V.sem, t_vproj)
        t_mm = mm_group(p, pk[0:1, :], [
            (ones16[:, 0:1], v_fm[c][:, PAD + ts * 512:PAD + (ts + 1) * 512])
            for c in range(8)])
        V.wait(Tn.sem, t_mm)
        t_e = V("tensor_copy", vs16[0:1, ts * 512:(ts + 1) * 512], pk[0:1, :])
        pool.done(V.sem, t_e)
    t_vs = V.sem.n

    dance.cols_from_row(srow[0:1, :], scols[:, :], 8, ident32,
                        waits=[(V.sem, t_srow)])
    V("tensor_scalar_add", scols[:, :], scols[:, :], 1024.0)
    V("reciprocal", scols[:, :], scols[:, :])
    pk, war = pool.get()
    if war is not None:
        Tn.wait(war[0], war[1])
    Tn.wait(sld, t_g)
    t_mm = mm_group(p, pk[0:128, 0:1], [(ones32[0:1, 0:128], gam[0:1, 0:1])])
    V.wait(Tn.sem, t_mm)
    t_e = V("tensor_copy", gcol[:, :], pk[0:128, 0:1])
    pool.done(V.sem, t_e)
    t_tg = V("tensor_scalar_mul", scols[:, :], scols[:, :], gcol[:, 0:1])
    t_tgrow = dance.rows_from_cols(scols[:, :], tgrow[0:1, :], 8, ident32,
                                   waits=[(V.sem, t_tg)])
    V.wait(dance.sem, t_tgrow)
    t_tg16 = V("tensor_copy", tg16[0:1, :], tgrow[0:1, :])
    for h2 in range(2):
        pk, war = pool.get()
        if war is not None:
            Tn.wait(war[0], war[1])
        Tn.wait(V.sem, t_tg16)
        t_mm = mm_group(p, pk[:, :], [(ones32[0:1, 0:128],
                                       tgrow[0:1, h2 * 512:(h2 + 1) * 512])])
        V.wait(Tn.sem, t_mm)
        t_e = V("tensor_copy", tb[:, h2 * 512:(h2 + 1) * 512], pk[:, :])
        pool.done(V.sem, t_e)
    t_tb = V.sem.n

    full_barrier(p)
    es1.close()

    a16 = [p.sb(f"l1a_a16_{c}", [128, D], BF16) for c in range(8)]
    es2 = contextlib.ExitStack()
    apool = [p.ps(f"l1a_ap{i}", [128, 512], F32, es2) for i in range(8)]
    qt = [p.sb(f"l1a_qt{i}", [128, 512], BF16, es2) for i in range(2)]
    kt = [p.sb(f"l1a_kt{i}", [128, D], BF16, es2) for i in range(2)]
    qk_ld = p.sem("l1a_qkld")
    t_aev = 0
    for half in range(2):
        amm = {}
        for j in range(NT):
            i = j % 2
            if j - 2 in amm:
                nc.sync.wait_ge(Tn.sem.h, amm[j - 2])
            if j == 0 and half == 1:
                Tn.wait(V.sem, t_aev)
            t_q = p.dma("sync", kt[i][:, :], kn_tm_d[j * 128:(j + 1) * 128, :], qk_ld)
            t_q = p.dma("sync", qt[i][:, :],
                        qn_tm_d[j * 128:(j + 1) * 128, half * 512:(half + 1) * 512], qk_ld)
            Tn.wait(qk_ld, t_q)
            last = None
            for fb in range(8):
                last = nc.tensor.matmul(apool[fb][:, :], kt[i][:, fb * 128:(fb + 1) * 128],
                                        qt[i][:, :], start=(j == 0), stop=(j == NT - 1),
                                        skip_group_check=True)
            amm[j] = Tn.sem.inc()
            last.then_inc(Tn.sem.h, 1)
        V.wait(Tn.sem, amm[NT - 1])
        for fb in range(8):
            V("tensor_mul", a16[fb][:, half * 512:(half + 1) * 512],
              apool[fb][:, :], tb[:, half * 512:(half + 1) * 512])
        t_aev = V.sem.n
    full_barrier(p)
    es2.close()

    es3 = contextlib.ExitStack()
    opool = PsumPool(p, "l1a_op", 4, [128, 512], es=es3)
    lnt = [p.sb(f"l1a_lnt{i}", [128, D], F32, es3) for i in range(2)]
    sub = [p.sb(f"l1a_sub{i}", [128, D], F32, es3) for i in range(2)]
    sub_done = {}
    st_done = {}
    for j in range(NT):
        i = j % 2
        if j - 2 in st_done:
            V.wait(sst, st_done[j - 2])
        if j - 2 in sub_done:
            p.qwait("sync", V.sem, sub_done[j - 2])
        t_ln = p.dma("sync", lnt[i][:, :], io["ln32_tm"][j * 128:(j + 1) * 128, :], sld)
        t_e = None
        for h2 in range(2):
            pk, war = opool.get()
            if war is not None:
                Tn.wait(war[0], war[1])
            pairs = [(v_fm[fb][:, PAD + j * 128:PAD + (j + 1) * 128],
                      a16[fb][:, h2 * 512:(h2 + 1) * 512]) for fb in range(8)]
            pairs.append((vs16[0:1, j * 128:(j + 1) * 128],
                          tg16[0:1, h2 * 512:(h2 + 1) * 512]))
            t_mm = mm_group(p, pk[:, :], pairs)
            V.wait(Tn.sem, t_mm)
            V.wait(sld, t_ln)
            t_e = V("scalar_tensor_tensor", sub[i][:, h2 * 512:(h2 + 1) * 512],
                    pk[:, :], 1.0, lnt[i][:, h2 * 512:(h2 + 1) * 512],
                    ALU.bypass, ALU.add)
            opool.done(V.sem, t_e)
        sub_done[j] = t_e
        p.qwait("gpsimd", V.sem, t_e)
        p.dma("gpsimd", io["sub_tm"][j * 128:(j + 1) * 128, :], sub[i][:, :], sst)
        st_done[j] = sst.n
    full_barrier(p)
    es3.close()
    io["sub_done"] = (sst, sst.n)


def host_prep_L1_attn(inputs):
    w = {}
    w["l1a_wqT"] = np.ascontiguousarray(inputs["L1_attn_Wq"].T)
    w["l1a_wkT"] = np.ascontiguousarray(inputs["L1_attn_Wk"].T)
    w["l1a_wvT"] = np.ascontiguousarray(inputs["L1_attn_Wv"].T)
    b = np.stack([inputs["L1_attn_bq"], inputs["L1_attn_bk"], inputs["L1_attn_bv"]])
    w["l1a_bcols"] = np.ascontiguousarray(
        b.reshape(3, 8, 128).transpose(2, 0, 1).reshape(128, 24)[:, [0, 1, 2, 3, 4, 5, 6, 7,
                                                                     8, 9, 10, 11, 12, 13, 14, 15,
                                                                     16, 17, 18, 19, 20, 21, 22, 23]])
    w["l1a_gamma"] = inputs["L1_attn_gamma"].reshape(1, 1)
    return w


L1A_WEIGHT_SPECS = {
    "l1a_wqT": ([D, D], F32),
    "l1a_wkT": ([D, D], F32),
    "l1a_wvT": ([D, D], F32),
    "l1a_bcols": ([128, 24], F32),
    "l1a_gamma": ([1, 1], F32),
}


def build_stage(sub_emit, weight_specs, rms2=False, need_ln32=False):
    p = Prog()
    x_in = p.inp("x", [T, D], F32)
    x_out = p.out("x_out", [T, D], F32)
    weights = {n: p.inp(n, sh, dt) for n, (sh, dt) in weight_specs.items()}
    p.prealloc_cbuf()
    consts = {}
    for n, (sh, dt) in CONST_SPECS.items():
        ap = p.inp(n, sh, dt)
        sb = p.sb(n + "_sb", sh, dt)
        consts[n + "_dram"] = ap
        consts[n.replace("c_", "") + "_sb"] = sb[:, :]
    io = {}
    io["ln16_tm"] = p.dram("ln16_tm", [T, D], BF16)
    io["ln32_tm"] = (p.out if DBG else p.dram)("ln32_tm", [T, D], F32)
    io["sub_tm"] = (p.out if DBG else p.dram)("sub_tm", [T, D], F32)

    sc = p.sem("consts")
    for n in CONST_SPECS:
        p.dma("sync", consts[n.replace("c_", "") + "_sb"][:, :], consts[n + "_dram"][:, :], sc)
    p.qwait("sync", sc, sc.n)

    ldx, sg = p.sem("ldx"), p.sem("sg")
    es_pre = contextlib.ExitStack()
    ln = LnPass(p, "pre", es=es_pre, need32=need_ln32, needadd=False)
    t_pre = ln.emit(ldx, sg, x_in, out16=io["ln16_tm"],
                    out32=(io["ln32_tm"] if need_ln32 else None), rms2=rms2)
    io["pre_done"] = (sg, t_pre)
    barrier(p, io["pre_done"])
    es_pre.close()

    sub_emit(p, io, weights, consts)

    barrier(p, io["sub_done"])
    es_post = contextlib.ExitStack()
    ln2 = LnPass(p, "post", es=es_post, need16=False)
    t_post = ln2.emit(ldx, sg, io["sub_tm"], out32=x_out, add=x_in)
    p.qwait("sync", sg, t_post)
    es_post.close()
    return p


def build_stage_L1_attn():
    return build_stage(emit_L1_attn, L1A_WEIGHT_SPECS, need_ln32=True)



def attn_core_head(p, Kaug, Qaug, vslice, o32, pexp, psB, psO, n_aug,
                   kt_lists, pre_exp=None, exp_scale=1.0, exp_bias_ap=None):
    nc = p.nc
    V, S, Tn = p.V, p.S, p.Tn
    NP = len(pexp)
    entries = []
    for qc in range(8):
        for idx, e in enumerate(kt_lists[qc]):
            entries.append((qc, idx, e, idx == 0, idx == len(kt_lists[qc]) - 1))
    n = len(entries)
    exp_done = {}
    av_of_pexp = {}
    po = None
    t_o = 0
    for i in range(n + 1):
        if i < n:
            qc, idx, (kt, mask_ap), first, lastk = entries[i]
            pb, warB = psB.get()
            if warB is not None:
                Tn.wait(warB[0], warB[1])
            last = nc.tensor.matmul(pb[:, :], Kaug[0:n_aug, kt * 128:(kt + 1) * 128],
                                    Qaug[0:n_aug, qc * 512:(qc + 1) * 512],
                                    start=True, stop=True)
            t_b = Tn.sem.inc()
            last.then_inc(Tn.sem.h, 1)
            t_fix = None
            if pre_exp is not None:
                V.wait(Tn.sem, t_b)
                t_fix = pre_exp(pb, qc, kt)
            if mask_ap is not None:
                if t_fix is None:
                    V.wait(Tn.sem, t_b)
                t_fix = V("tensor_add", pb[:, :], pb[:, :], mask_ap)
            b = i % NP
            S.wait(Tn.sem, t_b)
            if t_fix is not None:
                S.wait(V.sem, t_fix)
            if b in av_of_pexp:
                S.wait(Tn.sem, av_of_pexp[b])
            if exp_bias_ap is not None:
                t_e = S("activation", pexp[b][:, :], pb[:, :], AF.Exp,
                        scale=float(exp_scale), bias=exp_bias_ap)
            else:
                t_e = S("activation", pexp[b][:, :], pb[:, :], AF.Exp,
                        scale=float(exp_scale))
            psB.done(p.S.sem, t_e)
            exp_done[i] = t_e
        if i >= 1:
            qc, idx, (kt, _), first, lastk = entries[i - 1]
            if first:
                po, warO = psO.get()
                if warO is not None:
                    Tn.wait(warO[0], warO[1])
            b1 = (i - 1) % NP
            Tn.wait(S.sem, exp_done[i - 1])
            last = nc.tensor.matmul(po[0:65, :], vslice(kt), pexp[b1][:, :],
                                    start=first, stop=lastk,
                                    skip_group_check=True)
            t_av = Tn.sem.inc()
            last.then_inc(Tn.sem.h, 1)
            av_of_pexp[b1] = t_av
            if lastk:
                V.wait(Tn.sem, t_av)
                t_o = V("tensor_copy", o32[0:65, qc * 512:(qc + 1) * 512],
                        po[0:65, :])
                psO.done(p.V.sem, t_o)
    return t_o


def attn_divide_head(p, o32, o16h, dance, lrow, lrec, lcols, pool, ident32,
                     ones32, t_o):
    V, Tn = p.V, p.Tn
    t_l = V("tensor_copy", lrow[0:1, :], o32[64:65, :])
    dance.cols_from_row(lrow[0:1, :], lcols[:, :], 32, ident32,
                        waits=[(V.sem, t_l)])
    V("reciprocal", lcols[:, :], lcols[:, :])
    t_r = p.V.sem.n
    t_row = dance.rows_from_cols(lcols[:, :], lrec[0:1, :], 32, ident32,
                                 waits=[(V.sem, t_r)])
    t_e = 0
    for ts in range(8):
        pk, war = pool.get()
        if war is not None:
            Tn.wait(war[0], war[1])
        Tn.wait(dance.sem, t_row)
        t_mm = mm_group(p, pk[0:64, :], [(ones32[0:1, 0:64],
                                          lrec[0:1, ts * 512:(ts + 1) * 512])])
        V.wait(Tn.sem, t_mm)
        t_e = V("tensor_mul", o16h[0:64, ts * 512:(ts + 1) * 512],
                o32[0:64, ts * 512:(ts + 1) * 512], pk[0:64, :])
        pool.done(p.V.sem, t_e)
    return t_e


def attn_out_proj(p, io, o_fm_d, wo_dram, bias_row_dram, sld, sst, n_of=8, tag="aop"):
    nc = p.nc
    V, Tn = p.V, p.Tn
    es = contextlib.ExitStack()
    ofm = [p.sb(f"{tag}_ofm{c}", [128, T], BF16, es) for c in range(n_of)]
    wo16 = p.sb(f"{tag}_wo16", [128, n_of * D], BF16, es)
    brow = p.sb(f"{tag}_brow", [1, D], F32, es)
    brow16 = p.sb(f"{tag}_brow16", [1, D], BF16, es)
    sub = [p.sb(f"{tag}_sub{i}", [128, D], F32, es) for i in range(2)]
    opool = PsumPool(p, f"{tag}_op", 4, [128, 512], es=es)
    sw = p.sem(f"{tag}_sw")
    for c in range(n_of):
        p.dma("sync", ofm[c][:, :], o_fm_d[c * 128:(c + 1) * 128, :], sld)
    t_of = sld.n
    wch, t_w = load_weight16(p, wo_dram, wo16, sw, n_of * 128, D)
    if bias_row_dram is not None:
        t_b = p.dma("sync", brow[0:1, :], bias_row_dram[:, :], sld)
        V.wait(sld, t_b)
        t_b16 = V("tensor_copy", brow16[0:1, :], brow[0:1, :])
    sub_done = {}
    st_done = {}
    for j in range(NT):
        i = j % 2
        if j - 2 in st_done:
            V.wait(sst, st_done[j - 2])
        t_e = None
        for ns in range(2):
            pk, war = opool.get()
            if war is not None:
                Tn.wait(war[0], war[1])
            Tn.wait(sld, t_of)
            Tn.wait(sw, t_w)
            pairs = [(ofm[c][:, j * 128:(j + 1) * 128],
                      wo16[:, (c * 2 + ns) * 512:(c * 2 + ns + 1) * 512])
                     for c in range(n_of)]
            if bias_row_dram is not None:
                Tn.wait(V.sem, t_b16)
                pairs.append((p._ones16_row[0:1, 0:128],
                              brow16[0:1, ns * 512:(ns + 1) * 512]))
            t_mm = mm_group(p, pk[:, :], pairs)
            V.wait(Tn.sem, t_mm)
            t_e = V("tensor_copy", sub[i][:, ns * 512:(ns + 1) * 512], pk[:, :])
            opool.done(p.V.sem, t_e)
        sub_done[j] = t_e
        p.qwait("gpsimd", V.sem, t_e)
        p.dma("gpsimd", io["sub_tm"][j * 128:(j + 1) * 128, :], sub[i][:, :], sst)
        st_done[j] = sst.n
    full_barrier(p)
    es.close()
    return sst.n



def emit_L3_attn(p, io, weights, consts):
    nc = p.nc
    V, S, G, Tn = p.V, p.S, p.G, p.Tn
    ident16 = consts["ident16_sb"]
    ident32 = consts["ident32_sb"]
    ones16 = consts["ones16_sb"]
    ones32 = consts["ones32_sb"]
    p._ones16_row = ones16
    CW = PAD + T

    o_fm_d = (p.out if DBG else p.dram)("l3a_ofm", [D, T], BF16)
    if DBG:
        dbg_qaug = p.out("dbg_qaug", [65, T], BF16)
        dbg_kaug = p.out("dbg_kaug", [65, T], BF16)
        dbg_o32 = p.out("dbg_o32", [65, T], F32)
        dbg_vt = p.out("dbg_vt", [128, NT * 130], BF16)
        dbg_nrow = p.out("dbg_nrow", [1, T], F32)
        dbg_kmax = p.out("dbg_kmax", [1, 1], F32)
    sld = p.sem("l3a_sld")
    sst = p.sem("l3a_sst")
    sfm = p.sem("l3a_sfm")
    sw = p.sem("l3a_sw")

    es1 = contextlib.ExitStack()
    ln_fm = [p.sb(f"l3a_lnfm{c}", [128, CW], BF16, es1) for c in range(8)]
    w16 = p.sb("l3a_w16", [128, 3 * 8 * 128], BF16, es1)
    nscol = p.sb("l3a_ns_sb", [128, 8], F32, es1)
    qp = p.sb("l3a_qp", [128, T], BF16, es1)
    Qaug = [p.sb(f"l3a_qaug{i}", [65, T], BF16, es1) for i in range(2)]
    Kaug = [p.sb(f"l3a_kaug{i}", [65, T], BF16, es1) for i in range(2)]
    Vt = p.sb("l3a_vt", [128, NT * 130], BF16, es1)
    sqs = p.sb("l3a_sqs", [64, T], BF16, es1)
    nrow = p.sb("l3a_nrow", [1, T], F32, es1)
    kmaxc = p.sb("l3a_kmaxc", [1, 1], F32, es1)
    o32 = p.sb("l3a_o32", [65, T], F32, es1)
    o16h = p.sb("l3a_o16h", [64, T], BF16, es1)
    lrow = p.sb("l3a_lrow", [1, T], F32, es1)
    lrec = p.sb("l3a_lrec", [1, T], F32, es1)
    lcols = p.sb("l3a_lcols", [128, 32], F32, es1)
    pexp = [p.sb(f"l3a_pexp{i}", [128, 512], BF16, es1) for i in range(3)]

    pool = PsumPool(p, "l3a_pp", 2, [128, 512], es=es1)
    trp = PsumPool(p, "l3a_tr", 2, [128, 128], BF16, es=es1)
    psB = PsumPool(p, "l3a_psB", 3, [128, 512], es=es1)
    psO = pool
    dance = Dance(p, "l3a_dance", es=es1)

    p.qwait("sync", io["pre_done"][0], io["pre_done"][1])
    t_fm = emit_fm_load(p, io["ln16_tm"], ln_fm, sfm)
    t_ns = p.dma("sync", nscol[:, :], weights["l3a_ns"][:, :], sld)
    V.wait(sld, t_ns)

    kt_lists = {qc: [(kt, None) for kt in range(NT)] for qc in range(8)}
    WNAMES = ("l3a_wq", "l3a_wk", "l3a_wv")

    for pair in range(8):
        p.qwait("gpsimd", Tn.sem, Tn.sem.n)
        wch = {}
        for wi, wname in enumerate(WNAMES):
            for c in range(8):
                ap = w16[:, (wi * 8 + c) * 128:(wi * 8 + c + 1) * 128]
                p.dma("gpsimd", ap,
                      weights[wname][c * 128:(c + 1) * 128,
                                     pair * 128:(pair + 1) * 128], sw)
            wch[wname] = [w16[:, (wi * 8 + c) * 128:(wi * 8 + c + 1) * 128]
                          for c in range(8)]
        t_w = sw.n
        V.wait(sw, t_w)
        t_scale = 0
        for wi, wname in enumerate(WNAMES):
            qscale = 0.125 if wname == "l3a_wq" else 1.0
            for c in range(8):
                t_scale = V("tensor_scalar", wch[wname][c], wch[wname][c],
                            nscol[:, c:c + 1], qscale, ALU.mult, ALU.mult)

        def pair_proj(wname, split_dests):
            t_last = 0
            for ts in range(8):
                pk, war = pool.get()
                if war is not None:
                    Tn.wait(war[0], war[1])
                Tn.wait(V.sem, t_scale)
                Tn.wait(sfm, t_fm)
                t_mm = mm_group(p, pk[:, :], [
                    (wch[wname][ic], ln_fm[ic][:, PAD + ts * 512:PAD + (ts + 1) * 512])
                    for ic in range(8)])
                V.wait(Tn.sem, t_mm)
                if split_dests[1] is None:
                    t_last = V("tensor_copy", split_dests[0][:, ts * 512:(ts + 1) * 512],
                               pk[:, :])
                else:
                    V("tensor_copy", split_dests[0][0:64, ts * 512:(ts + 1) * 512],
                      pk[0:64, :])
                    t_last = V("tensor_copy",
                               split_dests[1][0:64, ts * 512:(ts + 1) * 512],
                               pk[64:128, :])
                pool.done(p.V.sem, t_last)
            return t_last

        pair_proj("l3a_wq", (Qaug[0], Qaug[1]))
        pair_proj("l3a_wk", (Kaug[0], Kaug[1]))
        t_v = pair_proj("l3a_wv", (qp, None))

        if pair == 0:
            V("memset", Vt[:, :], 1.0)
        for j in range(NT):
            pk, war = trp.get()
            if war is not None:
                Tn.wait(war[0], war[1])
            Tn.wait(V.sem, t_v)
            t_tr = Tn.sem.inc()
            nc.tensor.transpose(pk[:, :], qp[:, j * 128:(j + 1) * 128],
                                ident16).then_inc(Tn.sem.h, 1)
            V.wait(Tn.sem, t_tr)
            V("tensor_copy", Vt[:, j * 130:j * 130 + 64], pk[:, 0:64])
            t_e = V("tensor_copy", Vt[:, j * 130 + 65:j * 130 + 129], pk[:, 64:128])
            trp.done(p.V.sem, t_e)

        for hh in range(2):
            QA, KA = Qaug[hh], Kaug[hh]
            V("memset", KA[64:65, :], 1.0)

            def norm_row(A):
                S.wait(V.sem, V.sem.n)
                t_sq = S("activation", sqs[0:64, :], A[0:64, :], AF.Square)
                for ts in range(8):
                    pk, war = pool.get()
                    if war is not None:
                        Tn.wait(war[0], war[1])
                    Tn.wait(S.sem, t_sq)
                    t_mm = mm_group(p, pk[0:1, :], [
                        (ones16[0:64, 0:1], sqs[0:64, ts * 512:(ts + 1) * 512])])
                    V.wait(Tn.sem, t_mm)
                    t_l = V("tensor_copy", nrow[0:1, ts * 512:(ts + 1) * 512],
                            pk[0:1, :])
                    pool.done(p.V.sem, t_l)
                S.wait(V.sem, t_l)
                return S("activation", nrow[0:1, :], nrow[0:1, :], AF.Sqrt)

            t_kn = norm_row(KA)
            V.wait(S.sem, t_kn)
            t_km = V("reduce_max", kmaxc[0:1, 0:1], nrow[0:1, :], axis=AX.X)
            t_qn = norm_row(QA)
            V.wait(S.sem, t_qn)
            t_bnd = V("tensor_scalar", QA[64:65, :], nrow[0:1, :], kmaxc[0:1, 0:1],
                      -1.0, ALU.mult, ALU.mult)
            full_barrier(p)
            h = pair * 2 + hh
            attn_core_head(p, KA, QA,
                           lambda kt, hh=hh: Vt[:, kt * 130 + hh * 65:kt * 130 + hh * 65 + 65],
                           o32, pexp, psB, psO, 65, kt_lists)
            if DBG and h == 0:
                full_barrier(p)
                p.dma("gpsimd", dbg_nrow[:, :], nrow[0:1, :], sst)
                p.dma("gpsimd", dbg_kmax[:, :], kmaxc[0:1, :], sst)
                p.dma("gpsimd", dbg_qaug[:, :], QA[0:65, :], sst)
                p.dma("gpsimd", dbg_kaug[:, :], KA[0:65, :], sst)
                p.dma("gpsimd", dbg_o32[:, :], o32[0:65, :], sst)
                p.dma("gpsimd", dbg_vt[:, :], Vt[:, :], sst)
            V.wait(sst, sst.n)
            t_d = attn_divide_head(p, o32, o16h, dance, lrow, lrec, lcols, pool,
                                   ident32, ones32, 0)
            p.qwait("gpsimd", V.sem, t_d)
            p.dma("gpsimd", o_fm_d[h * 64:(h + 1) * 64, :], o16h[0:64, :], sst)

    full_barrier(p)
    es1.close()

    p.qwait("sync", sst, sst.n)
    attn_out_proj(p, io, o_fm_d, weights["l3a_wo"], weights["l3a_bout"], sld, sst, tag="l3aop")
    io["sub_done"] = (sst, sst.n)


def host_prep_L3_attn(inputs):
    w = {}
    w["l3a_wq"] = np.ascontiguousarray(inputs["L3_attn_wq"].reshape(D, D))
    w["l3a_wk"] = np.ascontiguousarray(inputs["L3_attn_wk"].reshape(D, D))
    w["l3a_wv"] = np.ascontiguousarray(inputs["L3_attn_wv"].reshape(D, D))
    w["l3a_wo"] = np.ascontiguousarray(inputs["L3_attn_w_out"].reshape(D, D))
    w["l3a_ns"] = np.ascontiguousarray(inputs["L3_attn_norm_scale"].reshape(8, 128).T)
    w["l3a_bout"] = np.ascontiguousarray(inputs["L3_attn_b_out"].reshape(1, D))
    return w


L3A_WEIGHT_SPECS = {
    "l3a_wq": ([D, D], F32),
    "l3a_wk": ([D, D], F32),
    "l3a_wv": ([D, D], F32),
    "l3a_wo": ([D, D], F32),
    "l3a_ns": ([128, 8], F32),
    "l3a_bout": ([1, D], F32),
}


def build_stage_L3_attn():
    return build_stage(emit_L3_attn, L3A_WEIGHT_SPECS, rms2=True)



def emit_L0_attn(p, io, weights, consts):
    nc = p.nc
    V, S, G, Tn = p.V, p.S, p.G, p.Tn
    ident16 = consts["ident16_sb"]
    ident32 = consts["ident32_sb"]
    ones16 = consts["ones16_sb"]
    ones32 = consts["ones32_sb"]
    p._ones16_row = ones16
    CW = PAD + T

    o_fm_d = p.dram("l0a_ofm", [D, T], BF16)
    sld = p.sem("l0a_sld")
    sst = p.sem("l0a_sst")
    sfm = p.sem("l0a_sfm")
    sw = p.sem("l0a_sw")

    es1 = contextlib.ExitStack()
    ln_fm = [p.sb(f"l0a_lnfm{c}", [128, CW], BF16, es1) for c in range(8)]
    w16 = p.sb("l0a_w16", [128, 3 * 8 * 128], BF16, es1)
    qp = p.sb("l0a_qp", [128, T], BF16, es1)
    Qaug = [p.sb(f"l0a_qaug{i}", [66, T], BF16, es1) for i in range(2)]
    Kaug = [p.sb(f"l0a_kaug{i}", [66, T], BF16, es1) for i in range(2)]
    Vt = p.sb("l0a_vt", [128, NT * 130], BF16, es1)
    o32 = p.sb("l0a_o32", [65, T], F32, es1)
    o16h = p.sb("l0a_o16h", [64, T], BF16, es1)
    lrec = p.sb("l0a_lrec", [1, T], F32, es1)
    lcols = p.sb("l0a_lcols", [128, 32], F32, es1)
    pexp = [p.sb(f"l0a_pexp{i}", [128, 512], BF16, es1) for i in range(3)]
    mcols = p.sb("l0a_mcols", [128, NT], F32, es1)
    mtmp = p.sb("l0a_mtmp", [128, 1], F32, es1)
    mrow = lrow = p.sb("l0a_lrow2", [1, T], F32, es1)
    dirs16 = p.sb("l0a_dirs16", [8, 16], BF16, es1)
    hs8 = p.sb("l0a_hs8", [1, 16], F32, es1)
    maskAw = p.sb("l0a_maskA_sb", [128, 896], F32, es1)
    maskBw = p.sb("l0a_maskB_sb", [128, 896], F32, es1)
    row16 = p.sb("l0a_row16", [1, T], BF16, es1)
    srow = p.sem("l0a_srow")

    def maskA(idx):
        return maskAw[:, 384 - 128 * idx:896 - 128 * idx]

    def maskB(idx):
        return maskBw[:, 384 - 128 * idx:896 - 128 * idx]

    pool = PsumPool(p, "l0a_pp", 2, [128, 512], es=es1)
    trp = PsumPool(p, "l0a_tr", 2, [128, 128], BF16, es=es1)
    psB = PsumPool(p, "l0a_psB", 3, [128, 512], es=es1)
    psO = pool
    dance = Dance(p, "l0a_dance", es=es1)

    p.qwait("sync", io["pre_done"][0], io["pre_done"][1])
    t_fm = emit_fm_load(p, io["ln16_tm"], ln_fm, sfm)
    t_c = p.dma("sync", maskAw[:, :], weights["l0a_maskA"][:, :], sld)
    t_c = p.dma("sync", maskBw[:, :], weights["l0a_maskB"][:, :], sld)
    t_c = p.dma("gpsimd", dirs16[:, :], weights["l0a_dirsT"][:, :], sld)
    t_c = p.dma("sync", hs8[:, :], weights["l0a_hs"][:, :], sld)
    V.wait(sld, t_c)
    t_hs8 = V("tensor_scalar_mul", hs8[0:1, :], hs8[0:1, :], 8.0)

    t_r16 = V("memset", row16[0:1, :], 1.0)
    p.qwait("gpsimd", V.sem, t_r16)
    p.dma("gpsimd", Kaug[0][65:66, :], row16[0:1, :], srow)
    p.dma("gpsimd", Kaug[1][65:66, :], row16[0:1, :], srow)

    kt_lists = {}
    for qc in range(8):
        lst = []
        for kt in range(4 * (qc + 1)):
            if kt >= 4 * qc:
                lst.append((kt, maskB(kt - 4 * qc)))
            else:
                lst.append((kt, None))
        kt_lists[qc] = lst

    for pair in range(8):
        p.qwait("gpsimd", Tn.sem, Tn.sem.n)
        wch = {}
        for wi in range(3):
            for c in range(8):
                ap = w16[:, (wi * 8 + c) * 128:(wi * 8 + c + 1) * 128]
                p.dma("gpsimd", ap,
                      weights["l0a_wqkv"][c * 128:(c + 1) * 128,
                                          wi * 1024 + pair * 128:wi * 1024 + (pair + 1) * 128],
                      sw)
            wch[wi] = [w16[:, (wi * 8 + c) * 128:(wi * 8 + c + 1) * 128]
                       for c in range(8)]
        t_w = sw.n
        V.wait(sw, t_w)
        t_scale = 0
        for c in range(8):
            t_scale = V("tensor_scalar_mul", wch[1][c], wch[1][c], 0.125)

        def pair_proj(wi, split_dests, wait_t):
            t_last = 0
            for ts in range(8):
                pk, war = pool.get()
                if war is not None:
                    Tn.wait(war[0], war[1])
                Tn.wait(V.sem, wait_t)
                Tn.wait(sfm, t_fm)
                t_mm = mm_group(p, pk[:, :], [
                    (wch[wi][ic], ln_fm[ic][:, PAD + ts * 512:PAD + (ts + 1) * 512])
                    for ic in range(8)])
                V.wait(Tn.sem, t_mm)
                if split_dests[1] is None:
                    t_last = V("tensor_copy", split_dests[0][:, ts * 512:(ts + 1) * 512],
                               pk[:, :])
                else:
                    V("tensor_copy", split_dests[0][0:64, ts * 512:(ts + 1) * 512],
                      pk[0:64, :])
                    t_last = V("tensor_copy",
                               split_dests[1][0:64, ts * 512:(ts + 1) * 512],
                               pk[64:128, :])
                pool.done(p.V.sem, t_last)
            return t_last

        pair_proj(0, (Qaug[0], Qaug[1]), t_scale)
        pair_proj(1, (Kaug[0], Kaug[1]), t_scale)
        t_v = pair_proj(2, (qp, None), t_scale)

        if pair == 0:
            V("memset", Vt[:, :], 1.0)
        for j in range(NT):
            pk, war = trp.get()
            if war is not None:
                Tn.wait(war[0], war[1])
            Tn.wait(V.sem, t_v)
            t_tr = Tn.sem.inc()
            nc.tensor.transpose(pk[:, :], qp[:, j * 128:(j + 1) * 128],
                                ident16).then_inc(Tn.sem.h, 1)
            V.wait(Tn.sem, t_tr)
            V("tensor_copy", Vt[:, j * 130:j * 130 + 64], pk[:, 0:64])
            t_e = V("tensor_copy", Vt[:, j * 130 + 65:j * 130 + 129], pk[:, 64:128])
            trp.done(p.V.sem, t_e)

        for hh in range(2):
            h = 2 * pair + hh
            QA, KA = Qaug[hh], Kaug[hh]
            for (A, dst_mul) in ((QA, None), (KA, None)):
                pass
            for ts in range(8):
                pk, war = pool.get()
                if war is not None:
                    Tn.wait(war[0], war[1])
                Tn.wait(V.sem, max(t_hs8, p.V.sem.n if False else t_hs8))
                t_mm = mm_group(p, pk[0:1, :], [
                    (dirs16[0:8, h:h + 1], QA[0:8, ts * 512:(ts + 1) * 512])])
                V.wait(Tn.sem, t_mm)
                t_q64 = V("tensor_scalar_mul", QA[64:65, ts * 512:(ts + 1) * 512],
                          pk[0:1, :], hs8[0:1, h:h + 1])
                pool.done(p.V.sem, t_q64)
            for ts in range(8):
                pk, war = pool.get()
                if war is not None:
                    Tn.wait(war[0], war[1])
                t_mm = mm_group(p, pk[0:1, :], [
                    (dirs16[0:8, h:h + 1], KA[0:8, ts * 512:(ts + 1) * 512])])
                V.wait(Tn.sem, t_mm)
                t_k64 = V("tensor_copy", KA[64:65, ts * 512:(ts + 1) * 512], pk[0:1, :])
                pool.done(p.V.sem, t_k64)

            t_mc = V("memset", mcols[:, :], NEG)
            full_barrier(p)
            for j in range(NT):
                for c in range(j // 4 + 1):
                    pb, war = psB.get()
                    if war is not None:
                        Tn.wait(war[0], war[1])
                    last = nc.tensor.matmul(pb[:, :], QA[0:65, j * 128:(j + 1) * 128],
                                            KA[0:65, c * 512:(c + 1) * 512],
                                            start=True, stop=True)
                    t_b = Tn.sem.inc()
                    last.then_inc(Tn.sem.h, 1)
                    V.wait(Tn.sem, t_b)
                    if c == j // 4:
                        V("tensor_add", pb[:, :], pb[:, :], maskA(j % 4))
                    V("reduce_max", mtmp[:, :], pb[:, :], axis=AX.X)
                    t_m = V("tensor_max", mcols[:, j:j + 1], mcols[:, j:j + 1],
                            mtmp[:, :])
                    psB.done(p.V.sem, t_m)
            dance.rows_from_cols(mcols[:, :], mrow[0:1, :], 32, ident32,
                                 waits=[(V.sem, p.V.sem.n)])
            V.wait(dance.sem, dance.sem.n)
            V.wait(srow, srow.n)
            t_m65 = V("tensor_scalar_mul", row16[0:1, :], mrow[0:1, :], -1.0)
            p.qwait("gpsimd", V.sem, t_m65)
            p.dma("gpsimd", QA[65:66, :], row16[0:1, :], srow)

            V.wait(sst, sst.n)
            full_barrier(p)
            Tn.wait(srow, srow.n)
            attn_core_head(p, KA, QA,
                           lambda kt, hh=hh: Vt[:, kt * 130 + hh * 65:kt * 130 + hh * 65 + 65],
                           o32, pexp, psB, psO, 66, kt_lists)
            t_d = attn_divide_head(p, o32, o16h, dance, lrow, lrec, lcols, pool,
                                   ident32, ones32, 0)
            p.qwait("gpsimd", V.sem, t_d)
            p.dma("gpsimd", o_fm_d[h * 64:(h + 1) * 64, :], o16h[0:64, :], sst)

    full_barrier(p)
    es1.close()
    p.qwait("sync", sst, sst.n)
    attn_out_proj(p, io, o_fm_d, weights["l0a_wo"], None, sld, sst, tag="l0aop")
    io["sub_done"] = (sst, sst.n)


def host_prep_L0_attn(inputs):
    w = {}
    w["l0a_wqkv"] = np.ascontiguousarray(inputs["L0_attn_w_qkv"])
    w["l0a_wo"] = np.ascontiguousarray(inputs["L0_attn_w_out"])
    w["l0a_dirsT"] = np.ascontiguousarray(inputs["L0_attn_head_dirs"].T)
    w["l0a_hs"] = np.ascontiguousarray(inputs["L0_attn_head_scales"].reshape(1, 16))
    qp = np.arange(128)[:, None]
    wv = np.arange(896)[None, :]
    w["l0a_maskA"] = np.where(wv > qp + 384, NEG, 0.0).astype(np.float32)
    kp = np.arange(128)[:, None]
    w["l0a_maskB"] = np.where(wv < kp + 384, NEG, 0.0).astype(np.float32)
    return w


L0A_WEIGHT_SPECS = {
    "l0a_wqkv": ([D, 3 * D], F32),
    "l0a_wo": ([D, D], F32),
    "l0a_dirsT": ([8, 16], F32),
    "l0a_hs": ([1, 16], F32),
    "l0a_maskA": ([128, 896], F32),
    "l0a_maskB": ([128, 896], F32),
}


def build_stage_L0_attn():
    return build_stage(emit_L0_attn, L0A_WEIGHT_SPECS)



def make_causal_masks():
    qp = np.arange(128)[:, None]
    wv = np.arange(896)[None, :]
    mA = np.where(wv > qp + 384, NEG, 0.0).astype(np.float32)
    kp = np.arange(128)[:, None]
    mB = np.where(wv < kp + 384, NEG, 0.0).astype(np.float32)
    return mA, mB


def causal_kt_lists(maskB):
    kt_lists = {}
    for qc in range(8):
        lst = []
        for kt in range(4 * (qc + 1)):
            lst.append((kt, maskB(kt - 4 * qc) if kt >= 4 * qc else None))
        kt_lists[qc] = lst
    return kt_lists


def emit_L2_attn(p, io, weights, consts):
    nc = p.nc
    V, S, G, Tn = p.V, p.S, p.G, p.Tn
    ident16 = consts["ident16_sb"]
    ident32 = consts["ident32_sb"]
    ones16 = consts["ones16_sb"]
    ones32 = consts["ones32_sb"]
    p._ones16_row = ones16
    CW = PAD + T
    SC96 = 1.0 / math.sqrt(96.0)

    o_fm_d = (p.out if DBG else p.dram)("l2a_ofm", [D, T], BF16)
    if DBG:
        dbg_qaug = p.out("dbg_qaug", [96, T], BF16)
        dbg_kaug = p.out("dbg_kaug", [96, T], BF16)
        dbg_o32 = p.out("dbg_o32", [65, T], F32)
    sld = p.sem("l2a_sld")
    sst = p.sem("l2a_sst")
    sfm = p.sem("l2a_sfm")
    sw = p.sem("l2a_sw")

    esA = contextlib.ExitStack()
    cq = [p.sb(f"l2a_cq{c}", [128, T], BF16, esA) for c in range(4)]
    ckv = [p.sb(f"l2a_ckv{c}", [128, T], BF16, esA) for c in range(4)]
    kr = p.sb("l2a_kr_sb", [32, T], BF16, esA)
    ctab = p.sb("l2a_ctab_sb", [128, T], BF16, esA)
    stab = p.sb("l2a_stab_sb", [128, T], BF16, esA)
    rotscr = p.sb("l2a_rot", [128, 512], BF16, esA)
    qp = p.sb("l2a_qp", [128, T], BF16, esA)
    maskBw = p.sb("l2a_maskB_sb", [128, 896], F32, esA)
    perm4 = p.sb("l2a_perm4_sb", [128, 128], BF16, esA)
    w16 = p.sb("l2a_w16", [128, 4096], BF16, esA)

    def maskB(idx):
        return maskBw[:, 384 - 128 * idx:896 - 128 * idx]

    pool = PsumPool(p, "l2a_pp", 2, [128, 512], es=esA)
    trp = PsumPool(p, "l2a_tr", 2, [128, 128], BF16, es=esA)
    psB = PsumPool(p, "l2a_psB", 3, [128, 512], es=esA)
    psO = pool
    dance = Dance(p, "l2a_dance", es=esA)

    bias_ap, t_bias = p.const_col(-80.0 * SC96)

    es0 = contextlib.ExitStack()
    ln_fm = [p.sb(f"l2a_lnfm{c}", [128, CW], BF16, es0) for c in range(8)]
    w0 = p.sb("l2a_w0", [128, 8 * 512], BF16, es0)

    p.qwait("sync", io["pre_done"][0], io["pre_done"][1])
    t_fm = emit_fm_load(p, io["ln16_tm"], ln_fm, sfm)
    t_c = p.dma("sync", maskBw[:, :], weights["l2a_maskB"][:, :], sld)
    t_c = p.dma("sync", ctab[:, :], weights["l2a_ctab"][:, :], sld)
    t_c = p.dma("sync", stab[:, :], weights["l2a_stab"][:, :], sld)
    t_c = p.dma("sync", perm4[:, :], weights["l2a_perm4"][:, :], sld)
    p.qwait("sync", sld, t_c)

    def rope32(dst_ap, src_ap, nrows, tab_off):
        t_last = 0
        for ts in range(8):
            sl = slice(ts * 512, (ts + 1) * 512)
            pk, war = psB.get()
            if war is not None:
                Tn.wait(war[0], war[1])
            Tn.wait(V.sem, V.sem.n)
            t_mm = mm_group(p, pk[0:nrows, :], [
                (perm4[0:nrows, 0:nrows], src_ap[0:nrows, sl])])
            V.wait(Tn.sem, t_mm)
            V("tensor_mul", rotscr[0:nrows, :], pk[0:nrows, :],
              stab[tab_off:tab_off + nrows, sl])
            t_ps = p.V.sem.n
            psB.done(p.V.sem, t_ps)
            V("tensor_mul", dst_ap[0:nrows, sl], src_ap[0:nrows, sl],
              ctab[tab_off:tab_off + nrows, sl])
            t_last = V("tensor_add", dst_ap[0:nrows, sl], dst_ap[0:nrows, sl],
                       rotscr[0:nrows, 0:512])
        return t_last

    for wname, dst in (("l2a_wdqT", cq), ("l2a_wdkvT", ckv)):
        wch, t_w = load_weight16(p, weights[wname], w0, sw, D, 512)
        proj_fm(p, wch, ln_fm, dst, pool,
                waits=[(sw, t_w), (sfm, t_fm)], col0=0)
    wch, t_w = load_weight16(p, weights["l2a_wkrT"], w16, sw, D, 32)
    t_krp = 0
    for ts in range(8):
        pk, war = pool.get()
        if war is not None:
            Tn.wait(war[0], war[1])
        Tn.wait(sw, t_w)
        t_mm = mm_group(p, pk[0:32, :], [
            (wch[ic], ln_fm[ic][:, PAD + ts * 512:PAD + (ts + 1) * 512])
            for ic in range(8)])
        V.wait(Tn.sem, t_mm)
        t_krp = V("tensor_copy", qp[0:32, ts * 512:(ts + 1) * 512], pk[0:32, :])
        pool.done(p.V.sem, t_krp)
    rope32(kr, qp, 32, 0)

    full_barrier(p)
    es0.close()

    qr16 = p.sb("l2a_qr16_sb", [128, T], BF16, esA)
    Qaug = [p.sb(f"l2a_qaug{i}", [96, T], BF16, esA) for i in range(2)]
    Kaug = [p.sb(f"l2a_kaug{i}", [96, T], BF16, esA) for i in range(2)]
    Vt = p.sb("l2a_vt", [128, NT * 130], BF16, esA)
    o32 = p.sb("l2a_o32", [65, T], F32, esA)
    o16h = p.sb("l2a_o16h", [64, T], BF16, esA)
    lrow = p.sb("l2a_lrow", [1, T], F32, esA)
    lrec = lrow
    lcols = p.sb("l2a_lcols", [128, 32], F32, esA)
    pexp = [p.sb(f"l2a_pexp{i}", [128, 512], BF16, esA) for i in range(3)]

    kt_lists = causal_kt_lists(maskB)
    for pair in range(8):
        p.qwait("gpsimd", Tn.sem, Tn.sem.n)
        wch = {}
        for wi, wname in enumerate(("l2a_wuqT", "l2a_wukT", "l2a_wuvT")):
            for c in range(4):
                ap = w16[:, (wi * 4 + c) * 128:(wi * 4 + c + 1) * 128]
                p.dma("gpsimd", ap,
                      weights[wname][c * 128:(c + 1) * 128,
                                     pair * 128:(pair + 1) * 128], sw)
            wch[wi] = [w16[:, (wi * 4 + c) * 128:(wi * 4 + c + 1) * 128]
                       for c in range(4)]
        t_w = sw.n

        def pair_proj(wi, src, split_dests):
            t_last = 0
            for ts in range(8):
                pk, war = pool.get()
                if war is not None:
                    Tn.wait(war[0], war[1])
                Tn.wait(sw, t_w)
                Tn.wait(V.sem, V.sem.n)
                t_mm = mm_group(p, pk[:, :], [
                    (wch[wi][ic], src[ic][:, ts * 512:(ts + 1) * 512])
                    for ic in range(4)])
                V.wait(Tn.sem, t_mm)
                if split_dests[1] is None:
                    t_last = V("tensor_copy", split_dests[0][:, ts * 512:(ts + 1) * 512],
                               pk[:, :])
                else:
                    V("tensor_copy", split_dests[0][0:64, ts * 512:(ts + 1) * 512],
                      pk[0:64, :])
                    t_last = V("tensor_copy",
                               split_dests[1][0:64, ts * 512:(ts + 1) * 512],
                               pk[64:128, :])
                pool.done(p.V.sem, t_last)
            return t_last

        pair_proj(0, cq, (Qaug[0], Qaug[1]))
        pair_proj(1, ckv, (Kaug[0], Kaug[1]))
        t_v = pair_proj(2, ckv, (qp, None))

        if pair == 0:
            V("memset", Vt[:, :], 1.0)
        for j in range(NT):
            pk, war = trp.get()
            if war is not None:
                Tn.wait(war[0], war[1])
            Tn.wait(V.sem, t_v)
            t_tr = Tn.sem.inc()
            nc.tensor.transpose(pk[:, :], qp[:, j * 128:(j + 1) * 128],
                                ident16).then_inc(Tn.sem.h, 1)
            V.wait(Tn.sem, t_tr)
            V("tensor_copy", Vt[:, j * 130:j * 130 + 64], pk[:, 0:64])
            t_e = V("tensor_copy", Vt[:, j * 130 + 65:j * 130 + 129], pk[:, 64:128])
            trp.done(p.V.sem, t_e)

        if pair % 2 == 0:
            ch = pair // 2
            wchq = load_weight16(p, weights["l2a_wqrT"][:, ch * 128:(ch + 1) * 128]
                                 if False else weights["l2a_wqrT"], None, None, 0, 0) \
                if False else None
            for c in range(4):
                ap = w16[:, (12 + c) * 128:(13 + c) * 128]
                p.dma("gpsimd", ap,
                      weights["l2a_wqrT"][c * 128:(c + 1) * 128,
                                          ch * 128:(ch + 1) * 128], sw)
            t_wq = sw.n
            wq = [w16[:, (12 + c) * 128:(13 + c) * 128] for c in range(4)]
            t_last = 0
            for ts in range(8):
                pk, war = pool.get()
                if war is not None:
                    Tn.wait(war[0], war[1])
                Tn.wait(sw, t_wq)
                t_mm = mm_group(p, pk[:, :], [
                    (wq[ic], cq[ic][:, ts * 512:(ts + 1) * 512]) for ic in range(4)])
                V.wait(Tn.sem, t_mm)
                t_last = V("tensor_copy", qp[:, ts * 512:(ts + 1) * 512], pk[:, :])
                pool.done(p.V.sem, t_last)
            rope32(qr16, qp, 128, 0)

        for hh in range(2):
            h = 2 * pair + hh
            r0 = 32 * (h % 4)
            V("tensor_copy", Qaug[hh][64:96, :], qr16[r0:r0 + 32, :])
            V("tensor_copy", Kaug[hh][64:96, :], kr[0:32, :])

        for hh in range(2):
            h = 2 * pair + hh
            QA, KA = Qaug[hh], Kaug[hh]
            V.wait(sst, sst.n)
            full_barrier(p)

            def clip(pb, qc, kt):
                return V("tensor_scalar", pb[:, :], pb[:, :], 80.0, -80.0,
                         ALU.min, ALU.max)

            attn_core_head(p, KA, QA,
                           lambda kt, hh=hh: Vt[:, kt * 130 + hh * 65:kt * 130 + hh * 65 + 65],
                           o32, pexp, psB, psO, 96, kt_lists, pre_exp=clip,
                           exp_scale=SC96, exp_bias_ap=bias_ap)
            if DBG and h == 0:
                full_barrier(p)
                p.dma("gpsimd", dbg_qaug[:, :], QA[0:96, :], sst)
                p.dma("gpsimd", dbg_kaug[:, :], KA[0:96, :], sst)
                p.dma("gpsimd", dbg_o32[:, :], o32[0:65, :], sst)
            t_d = attn_divide_head(p, o32, o16h, dance, lrow, lrec, lcols, pool,
                                   ident32, ones32, 0)
            p.qwait("gpsimd", V.sem, t_d)
            p.dma("gpsimd", o_fm_d[h * 64:(h + 1) * 64, :], o16h[0:64, :], sst)

    full_barrier(p)
    esA.close()
    p.qwait("sync", sst, sst.n)
    attn_out_proj(p, io, o_fm_d, weights["l2a_woT"], None, sld, sst, tag="l2aop")
    io["sub_done"] = (sst, sst.n)


def host_prep_L2_attn(inputs):
    w = {}
    w["l2a_wdqT"] = np.ascontiguousarray(inputs["L2_attn_W_DQ"].T)
    w["l2a_wuqT"] = np.ascontiguousarray(inputs["L2_attn_W_UQ"].T)
    w["l2a_wqrT"] = np.ascontiguousarray(inputs["L2_attn_W_QR"].T)
    w["l2a_wdkvT"] = np.ascontiguousarray(inputs["L2_attn_W_DKV"].T)
    w["l2a_wukT"] = np.ascontiguousarray(inputs["L2_attn_W_UK"].T)
    w["l2a_wuvT"] = np.ascontiguousarray(inputs["L2_attn_W_UV"].T)
    w["l2a_wkrT"] = np.ascontiguousarray(inputs["L2_attn_W_KR"].T)
    w["l2a_woT"] = np.ascontiguousarray(inputs["L2_attn_W_O"].T)
    _, mB = make_causal_masks()
    w["l2a_maskB"] = mB
    d_hr = 32
    inv = 10000.0 ** (-np.arange(0, d_hr, 2, dtype=np.float64) / d_hr)
    ang = np.arange(T, dtype=np.float64)[:, None] * inv[None, :]
    C = np.cos(ang)[:, np.repeat(np.arange(16), 2)].T
    Sn = np.sin(ang)[:, np.repeat(np.arange(16), 2)].T
    w["l2a_ctab"] = np.tile(C, (4, 1)).astype(BF)
    w["l2a_stab"] = np.tile(Sn, (4, 1)).astype(BF)
    P32 = np.zeros((32, 32), np.float32)
    for i in range(16):
        P32[2 * i + 1, 2 * i] = -1.0
        P32[2 * i, 2 * i + 1] = 1.0
    w["l2a_perm4"] = np.kron(np.eye(4, dtype=np.float32), P32).astype(BF)
    return w


L2A_WEIGHT_SPECS = {
    "l2a_wdqT": ([D, 512], F32),
    "l2a_wuqT": ([512, D], F32),
    "l2a_wqrT": ([512, 512], F32),
    "l2a_wdkvT": ([D, 512], F32),
    "l2a_wukT": ([512, D], F32),
    "l2a_wuvT": ([512, D], F32),
    "l2a_wkrT": ([D, 32], F32),
    "l2a_woT": ([D, D], F32),
    "l2a_maskB": ([128, 896], F32),
    "l2a_ctab": ([128, T], BF16),
    "l2a_stab": ([128, T], BF16),
    "l2a_perm4": ([128, 128], BF16),
}


def build_stage_L2_attn():
    return build_stage(emit_L2_attn, L2A_WEIGHT_SPECS)



def emit_L2_ffn(p, io, weights, consts):
    nc = p.nc
    V, S, G, Tn = p.V, p.S, p.G, p.Tn
    ones16 = consts["ones16_sb"]
    CW = PAD + T
    NE = 16

    sld = p.sem("l2f_sld")
    sst = p.sem("l2f_sst")
    sfm = p.sem("l2f_sfm")
    sw = p.sem("l2f_sw")

    esA = contextlib.ExitStack()
    ln_fm = [p.sb(f"l2f_lnfm{c}", [128, CW], BF16, esA) for c in range(8)]
    acc = p.sb("l2f_acc", [128, NT * D], BF16, esA)
    Hfm = [p.sb(f"l2f_h{c}", [128, T], BF16, esA) for c in range(4)]
    hg = [p.sb(f"l2f_hg{i}", [128, 512], BF16, esA) for i in range(2)]
    wbuf = p.sb("l2f_wbuf", [128, 8 * 512], BF16, esA)
    wdbuf = p.sb("l2f_wdbuf", [128, 4 * D], BF16, esA)
    rwT = p.sb("l2f_rwT", [128, 8 * 16], BF16, esA)
    rb16 = p.sb("l2f_rb16", [1, 16], BF16, esA)
    rbf = p.sb("l2f_rbf", [1, 16], F32, esA)
    comb = p.sb("l2f_comb", [128, NT * 16], F32, esA)
    bi = p.sb("l2f_bi", [128, 16], F32, esA)
    m1 = p.sb("l2f_m1", [128, 16], F32, esA)
    bi2 = p.sb("l2f_bi2", [128, 16], F32, esA)
    t1c = p.sb("l2f_t1", [128, 1], F32, esA)
    t2c = p.sb("l2f_t2", [128, 1], F32, esA)
    sgc = p.sb("l2f_sg", [128, 2], F32, esA)
    sub = [p.sb(f"l2f_sub{i}", [128, D], F32, esA) for i in range(2)]
    acc32 = p.sb("l2f_acc32", [128, 512], F32, esA)

    pool = PsumPool(p, "l2f_pp", 2, [128, 512], es=esA)
    pup = PsumPool(p, "l2f_pu", 2, [128, 512], es=esA)
    pr = PsumPool(p, "l2f_pr", 2, [128, 16], es=esA)

    p.qwait("sync", io["pre_done"][0], io["pre_done"][1])
    t_fm = emit_fm_load(p, io["ln16_tm"], ln_fm, sfm)

    rch, t_rw = load_weight16(p, weights["l2f_routerT"], rwT, sw, D, 16)
    t_rb = p.dma("sync", rbf[0:1, :], weights["l2f_rbias"][:, :], sld)
    V.wait(sld, t_rb)
    t_rb16 = V("tensor_copy", rb16[0:1, :], rbf[0:1, :])
    t_acc0 = V("memset", acc[:, :], 0.0)
    for j in range(NT):
        pk, war = pr.get()
        if war is not None:
            Tn.wait(war[0], war[1])
        Tn.wait(sw, t_rw)
        Tn.wait(sfm, t_fm)
        Tn.wait(V.sem, t_rb16)
        pairs = [(ln_fm[ic][:, PAD + j * 128:PAD + (j + 1) * 128], rch[ic])
                 for ic in range(8)]
        pairs.append((ones16[0:1, 0:128], rb16[0:1, 0:16]))
        t_mm = mm_group(p, pk[:, :], pairs)
        V.wait(Tn.sem, t_mm)
        V("tensor_copy", bi[:, :], pk[:, 0:16])
        pr.done(p.V.sem, p.V.sem.n)
        V("reduce_max", t1c[:, :], bi[:, :], axis=AX.X)
        V("tensor_scalar", m1[:, :], bi[:, :], t1c[:, 0:1], None, ALU.is_ge)
        V("scalar_tensor_tensor", bi2[:, :], m1[:, :], -1e30, bi[:, :],
          ALU.mult, ALU.add)
        V("reduce_max", t2c[:, :], bi2[:, :], axis=AX.X)
        t_d = V("tensor_sub", sgc[:, 0:1], t1c[:, :], t2c[:, :])
        S.wait(V.sem, t_d)
        t_sig = S("activation", sgc[:, 0:1], sgc[:, 0:1], AF.Sigmoid)
        V.wait(S.sem, t_sig)
        V("tensor_scalar", sgc[:, 1:2], sgc[:, 0:1], -1.0, 1.0, ALU.mult, ALU.add)
        V("tensor_scalar", sgc[:, 0:1], sgc[:, 0:1], 2.0, -1.0, ALU.mult, ALU.add)
        V("tensor_scalar", bi2[:, :], bi[:, :], t2c[:, 0:1], None, ALU.is_ge)
        V("tensor_scalar", m1[:, :], m1[:, :], sgc[:, 0:1], None, ALU.mult)
        V("scalar_tensor_tensor", comb[:, j * 16:(j + 1) * 16], bi2[:, :],
          sgc[:, 1:2], m1[:, :], ALU.mult, ALU.add)
    t_comb = p.V.sem.n

    for e in range(NE):
        gch, t_g = load_weight16(p, weights["l2f_gate"][e], wbuf, sw, D, 512)
        uch, t_u = load_weight16(p, weights["l2f_up"][e], wdbuf, sw, D, 512)
        V.wait(Tn.sem, Tn.sem.n)
        hg_rd = {}
        k = 0
        for oc in range(4):
            for ts in range(8):
                pg, warg = pool.get()
                if warg is not None:
                    Tn.wait(warg[0], warg[1])
                Tn.wait(sw, max(t_g, t_u))
                t_mm = mm_group(p, pg[:, :], [
                    (gch[ic][:, oc * 128:(oc + 1) * 128],
                     ln_fm[ic][:, PAD + ts * 512:PAD + (ts + 1) * 512])
                    for ic in range(8)])
                S.wait(Tn.sem, t_mm)
                if k - 2 in hg_rd:
                    S.wait(V.sem, hg_rd[k - 2])
                t_sil = S("activation", hg[k % 2][:, :], pg[:, :], AF.Silu)
                pool.done(p.S.sem, t_sil)
                pu, waru = pup.get()
                if waru is not None:
                    Tn.wait(waru[0], waru[1])
                t_mm2 = mm_group(p, pu[:, :], [
                    (uch[ic][:, oc * 128:(oc + 1) * 128],
                     ln_fm[ic][:, PAD + ts * 512:PAD + (ts + 1) * 512])
                    for ic in range(8)])
                V.wait(Tn.sem, t_mm2)
                V.wait(S.sem, t_sil)
                t_h = V("tensor_mul", Hfm[oc][:, ts * 512:(ts + 1) * 512],
                        pu[:, :], hg[k % 2][:, :])
                hg_rd[k] = t_h
                pup.done(p.V.sem, t_h)
                k += 1
        t_hdone = p.V.sem.n
        dch, t_d = load_weight16(p, weights["l2f_down"][e], wdbuf, sw, 512, D)
        for j in range(NT):
            for ns in range(2):
                pk, war = pool.get()
                if war is not None:
                    Tn.wait(war[0], war[1])
                Tn.wait(sw, t_d)
                Tn.wait(V.sem, t_hdone)
                t_mm = mm_group(p, pk[:, :], [
                    (Hfm[c][:, j * 128:(j + 1) * 128],
                     dch[c][:, ns * 512:(ns + 1) * 512]) for c in range(4)])
                V.wait(Tn.sem, t_mm)
                V.wait(V.sem, t_comb)
                t_e2 = V("scalar_tensor_tensor",
                         acc[:, j * D + ns * 512:j * D + (ns + 1) * 512],
                         pk[:, :], comb[:, j * 16 + e:j * 16 + e + 1],
                         acc[:, j * D + ns * 512:j * D + (ns + 1) * 512],
                         ALU.mult, ALU.add)
                pool.done(p.V.sem, t_e2)

    sub_done = {}
    for half in range(2):
        gch, t_g = load_weight16(
            p, weights["l2f_shgT"][:, half * 512:(half + 1) * 512], wbuf, sw, D, 512)
        uch, t_u = load_weight16(
            p, weights["l2f_shuT"][:, half * 512:(half + 1) * 512], wdbuf, sw, D, 512)
        V.wait(Tn.sem, Tn.sem.n)
        hg_rd = {}
        k = 0
        for oc in range(4):
            for ts in range(8):
                pg, warg = pool.get()
                if warg is not None:
                    Tn.wait(warg[0], warg[1])
                Tn.wait(sw, max(t_g, t_u))
                t_mm = mm_group(p, pg[:, :], [
                    (gch[ic][:, oc * 128:(oc + 1) * 128],
                     ln_fm[ic][:, PAD + ts * 512:PAD + (ts + 1) * 512])
                    for ic in range(8)])
                S.wait(Tn.sem, t_mm)
                if k - 2 in hg_rd:
                    S.wait(V.sem, hg_rd[k - 2])
                t_sil = S("activation", hg[k % 2][:, :], pg[:, :], AF.Silu)
                pool.done(p.S.sem, t_sil)
                pu, waru = pup.get()
                if waru is not None:
                    Tn.wait(waru[0], waru[1])
                t_mm2 = mm_group(p, pu[:, :], [
                    (uch[ic][:, oc * 128:(oc + 1) * 128],
                     ln_fm[ic][:, PAD + ts * 512:PAD + (ts + 1) * 512])
                    for ic in range(8)])
                V.wait(Tn.sem, t_mm2)
                V.wait(S.sem, t_sil)
                t_h = V("tensor_mul", Hfm[oc][:, ts * 512:(ts + 1) * 512],
                        pu[:, :], hg[k % 2][:, :])
                hg_rd[k] = t_h
                pup.done(p.V.sem, t_h)
                k += 1
        t_hs = p.V.sem.n
        dch, t_d = load_weight16(
            p, weights["l2f_shdT"][half * 512:(half + 1) * 512, :], wdbuf, sw, 512, D)
        for j in range(NT):
            i = j % 2
            if half == 1 and j - 2 in sub_done:
                V.wait(sst, sub_done[j - 2])
            t_e2 = None
            for ns in range(2):
                pk, war = pool.get()
                if war is not None:
                    Tn.wait(war[0], war[1])
                Tn.wait(sw, t_d)
                Tn.wait(V.sem, t_hs)
                t_mm = mm_group(p, pk[:, :], [
                    (Hfm[c][:, j * 128:(j + 1) * 128],
                     dch[c][:, ns * 512:(ns + 1) * 512]) for c in range(4)])
                V.wait(Tn.sem, t_mm)
                if half == 0:
                    t_e2 = V("scalar_tensor_tensor",
                             acc[:, j * D + ns * 512:j * D + (ns + 1) * 512],
                             pk[:, :], 1.0,
                             acc[:, j * D + ns * 512:j * D + (ns + 1) * 512],
                             ALU.mult, ALU.add)
                else:
                    V("tensor_copy", acc32[:, :],
                      acc[:, j * D + ns * 512:j * D + (ns + 1) * 512])
                    t_e2 = V("tensor_add", sub[i][:, ns * 512:(ns + 1) * 512],
                             pk[:, :], acc32[:, :])
                pool.done(p.V.sem, t_e2)
            if half == 1:
                p.qwait("gpsimd", V.sem, t_e2)
                p.dma("gpsimd", io["sub_tm"][j * 128:(j + 1) * 128, :], sub[i][:, :], sst)
                sub_done[j] = sst.n
    full_barrier(p)
    esA.close()
    io["sub_done"] = (sst, sst.n)


def host_prep_L2_ffn(inputs):
    w = {}
    w["l2f_routerT"] = np.ascontiguousarray(inputs["L2_ffn_router_w"].T)
    w["l2f_rbias"] = np.ascontiguousarray(inputs["L2_ffn_router_bias"].reshape(1, 16))
    w["l2f_gate"] = np.ascontiguousarray(inputs["L2_ffn_gate_proj"])
    w["l2f_up"] = np.ascontiguousarray(inputs["L2_ffn_up_proj"])
    w["l2f_down"] = np.ascontiguousarray(inputs["L2_ffn_down_proj"])
    w["l2f_shgT"] = np.ascontiguousarray(inputs["L2_ffn_sh_gate"].T)
    w["l2f_shuT"] = np.ascontiguousarray(inputs["L2_ffn_sh_up"].T)
    w["l2f_shdT"] = np.ascontiguousarray(inputs["L2_ffn_sh_down"].T)
    return w


L2F_WEIGHT_SPECS = {
    "l2f_routerT": ([D, 16], F32),
    "l2f_rbias": ([1, 16], F32),
    "l2f_gate": ([16, D, 512], F32),
    "l2f_up": ([16, D, 512], F32),
    "l2f_down": ([16, 512, D], F32),
    "l2f_shgT": ([D, D], F32),
    "l2f_shuT": ([D, D], F32),
    "l2f_shdT": ([D, D], F32),
}


def build_stage_L2_ffn():
    return build_stage(emit_L2_ffn, L2F_WEIGHT_SPECS)



def emit_L1_ffn(p, io, weights, consts):
    nc = p.nc
    V, S, G, Tn = p.V, p.S, p.G, p.Tn
    ones16 = consts["ones16_sb"]
    ones32 = consts["ones32_sb"]
    CW = PAD + T
    NE = 8
    NH = 11

    sld = p.sem("l1f_sld")
    sst = p.sem("l1f_sst")
    sfm = p.sem("l1f_sfm")
    sw = p.sem("l1f_sw")

    esA = contextlib.ExitStack()
    ln_fm = [p.sb(f"l1f_lnfm{c}", [128, CW], BF16, esA) for c in range(8)]
    acc = p.sb("l1f_acc", [128, NT * D], BF16, esA)
    Hfm = [p.sb(f"l1f_h{c}", [128, T], BF16, esA) for c in range(4)]
    hrel = [p.sb(f"l1f_hrel{i}", [128, 512], BF16, esA) for i in range(2)]
    wbuf = p.sb("l1f_wbuf", [128, 8 * 512], BF16, esA)
    wdbuf = p.sb("l1f_wdbuf", [128, 4 * D], BF16, esA)
    gwT = p.sb("l1f_gwT", [128, 8 * 8], BF16, esA)
    ebrow = p.sb("l1f_ebrow", [1, 8], F32, esA)
    ebb = p.sb("l1f_ebb", [128, 8], F32, esA)
    comb = p.sb("l1f_comb", [128, NT * 8], F32, esA)
    sc = p.sb("l1f_sc", [128, 8], F32, esA)
    bi = p.sb("l1f_bi", [128, 8], F32, esA)
    m1 = p.sb("l1f_m1", [128, 8], F32, esA)
    bi2 = p.sb("l1f_bi2", [128, 8], F32, esA)
    t1c = p.sb("l1f_t1", [128, 1], F32, esA)
    t2c = p.sb("l1f_t2", [128, 1], F32, esA)
    sub = [p.sb(f"l1f_sub{i}", [128, D], F32, esA) for i in range(2)]
    acc32 = p.sb("l1f_acc32", [128, 512], F32, esA)

    pool = PsumPool(p, "l1f_pp", 2, [128, 512], es=esA)
    pr = PsumPool(p, "l1f_pr", 2, [128, 8], es=esA)

    if DBG:
        dbg_comb = p.out("dbg_comb", [128, NT * 8], F32)
        dbg_acc = p.out("dbg_acc", [128, NT * D], BF16)
        dbg_sc = p.out("dbg_sc", [128, 8], F32)
        dbg_bi = p.out("dbg_bi", [128, 8], F32)
        dbg_ebb = p.out("dbg_ebb", [128, 8], F32)
    p.qwait("sync", io["pre_done"][0], io["pre_done"][1])
    t_fm = emit_fm_load(p, io["ln16_tm"], ln_fm, sfm)

    gch, t_rw = load_weight16(p, weights["l1f_gateT"], gwT, sw, D, 8)
    t_rb = p.dma("sync", ebrow[0:1, :], weights["l1f_ebias"][:, :], sld)
    pk, war = pr.get()
    Tn.wait(sld, t_rb)
    t_mm = mm_group(p, pk[:, :], [(ones32[0:1, 0:128], ebrow[0:1, 0:8])])
    V.wait(Tn.sem, t_mm)
    t_eb = V("tensor_copy", ebb[:, :], pk[:, 0:8])
    pr.done(p.V.sem, t_eb)
    t_acc0 = V("memset", acc[:, :], 0.0)
    t_scrd = 0
    for j in range(NT):
        pk, war = pr.get()
        if war is not None:
            Tn.wait(war[0], war[1])
        Tn.wait(sw, t_rw)
        Tn.wait(sfm, t_fm)
        pairs = [(ln_fm[ic][:, PAD + j * 128:PAD + (j + 1) * 128], gch[ic])
                 for ic in range(8)]
        t_mm = mm_group(p, pk[:, :], pairs)
        S.wait(Tn.sem, t_mm)
        S.wait(V.sem, t_scrd)
        t_sig = S("activation", sc[:, :], pk[:, 0:8], AF.Sigmoid)
        pr.done(p.S.sem, t_sig)
        V.wait(S.sem, t_sig)
        t_scrd = V("tensor_add", bi[:, :], sc[:, :], ebb[:, :])
        V("reduce_max", t1c[:, :], bi[:, :], axis=AX.X)
        V("tensor_scalar", m1[:, :], bi[:, :], t1c[:, 0:1], None, ALU.is_ge)
        V("scalar_tensor_tensor", bi2[:, :], m1[:, :], -1e30, bi[:, :],
          ALU.mult, ALU.add)
        V("reduce_max", t2c[:, :], bi2[:, :], axis=AX.X)
        V("tensor_scalar", bi2[:, :], bi[:, :], t2c[:, 0:1], None, ALU.is_ge)
        t_scrd = V("tensor_mul", m1[:, :], sc[:, :], sc[:, :])
        t_cj = V("tensor_mul", comb[:, j * 8:(j + 1) * 8], bi2[:, :], m1[:, :])
        if DBG and j == 0:
            p.qwait("gpsimd", V.sem, t_cj)
            p.dma("gpsimd", dbg_sc[:, :], sc[:, :], sst)
            p.dma("gpsimd", dbg_bi[:, :], bi[:, :], sst)
            p.dma("gpsimd", dbg_ebb[:, :], ebb[:, :], sst)
            p.qwait("sync", sst, sst.n)
            V.wait(sst, sst.n)
    t_comb = p.V.sem.n

    def up_relu2(upT_dram, n_h):
        uch, t_u = load_weight16(p, upT_dram, wbuf, sw, D, n_h * 128)
        assert n_h <= 4
        S.wait(Tn.sem, Tn.sem.n)
        sq_done = {}
        k = 0
        for oc in range(n_h):
            for ts in range(8):
                pg, warg = pool.get()
                if warg is not None:
                    Tn.wait(warg[0], warg[1])
                Tn.wait(sw, t_u)
                t_mm = mm_group(p, pg[:, :], [
                    (uch[ic][:, oc * 128:(oc + 1) * 128],
                     ln_fm[ic][:, PAD + ts * 512:PAD + (ts + 1) * 512])
                    for ic in range(8)])
                V.wait(Tn.sem, t_mm)
                if k - 2 in sq_done:
                    V.wait(S.sem, sq_done[k - 2])
                t_r = V("tensor_scalar_max", hrel[k % 2][:, :], pg[:, :], 0.0)
                pool.done(p.V.sem, t_r)
                S.wait(V.sem, t_r)
                sq_done[k] = S("activation", Hfm[oc][:, ts * 512:(ts + 1) * 512],
                               hrel[k % 2][:, :], AF.Square)
                k += 1
        return p.S.sem.n

    def down_acc(downT_dram, n_h, comb_col_fn, to_sub=False):
        dch, t_d = load_weight16(p, downT_dram, wdbuf, sw, n_h * 128, D)
        t_hdone = p.S.sem.n
        sub_done = {}
        for j in range(NT):
            i = j % 2
            if to_sub and j - 2 in sub_done:
                V.wait(sst, sub_done[j - 2])
            t_e2 = None
            for ns in range(2):
                pk, war = pool.get()
                if war is not None:
                    Tn.wait(war[0], war[1])
                Tn.wait(sw, t_d)
                Tn.wait(S.sem, t_hdone)
                t_mm = mm_group(p, pk[:, :], [
                    (Hfm[c][:, j * 128:(j + 1) * 128],
                     dch[c][:, ns * 512:(ns + 1) * 512]) for c in range(n_h)])
                V.wait(Tn.sem, t_mm)
                ccol = comb_col_fn(j)
                if to_sub:
                    V("tensor_copy", acc32[:, :],
                      acc[:, j * D + ns * 512:j * D + (ns + 1) * 512])
                    t_e2 = V("tensor_add", sub[i][:, ns * 512:(ns + 1) * 512],
                             pk[:, :], acc32[:, :])
                elif ccol is None:
                    t_e2 = V("scalar_tensor_tensor",
                             acc[:, j * D + ns * 512:j * D + (ns + 1) * 512],
                             pk[:, :], 1.0,
                             acc[:, j * D + ns * 512:j * D + (ns + 1) * 512],
                             ALU.mult, ALU.add)
                else:
                    V.wait(V.sem, t_comb)
                    t_e2 = V("scalar_tensor_tensor",
                             acc[:, j * D + ns * 512:j * D + (ns + 1) * 512],
                             pk[:, :], ccol,
                             acc[:, j * D + ns * 512:j * D + (ns + 1) * 512],
                             ALU.mult, ALU.add)
                pool.done(p.V.sem, t_e2)
            if to_sub:
                p.qwait("gpsimd", V.sem, t_e2)
                p.dma("gpsimd", io["sub_tm"][j * 128:(j + 1) * 128, :], sub[i][:, :], sst)
                sub_done[j] = sst.n

    HALVES = ((0, 4), (4, 4), (8, 3))
    for e in range(NE):
        for (h0, nh) in HALVES:
            up_relu2(weights["l1f_upT"][e][:, h0 * 128:(h0 + nh) * 128], nh)
            down_acc(weights["l1f_downT"][e][h0 * 128:(h0 + nh) * 128, :], nh,
                     lambda j, e=e: comb[:, j * 8 + e:j * 8 + e + 1])

    if DBG:
        full_barrier(p)
        p.dma("gpsimd", dbg_comb[:, :], comb[:, :], sst)
        p.dma("gpsimd", dbg_acc[:, :], acc[:, :], sst)
    for (h0, nh) in HALVES[:-1]:
        up_relu2(weights["l1f_shupT"][:, h0 * 128:(h0 + nh) * 128], nh)
        down_acc(weights["l1f_shdownT"][h0 * 128:(h0 + nh) * 128, :], nh, lambda j: None)
    h0, nh = HALVES[-1]
    up_relu2(weights["l1f_shupT"][:, h0 * 128:(h0 + nh) * 128], nh)
    down_acc(weights["l1f_shdownT"][h0 * 128:(h0 + nh) * 128, :], nh, lambda j: None,
             to_sub=True)

    full_barrier(p)
    esA.close()
    io["sub_done"] = (sst, sst.n)


def host_prep_L1_ffn(inputs):
    w = {}
    w["l1f_gateT"] = np.ascontiguousarray(inputs["L1_ffn_gate_w"].T)
    w["l1f_ebias"] = np.ascontiguousarray(inputs["L1_ffn_expert_bias"].reshape(1, 8))
    w["l1f_upT"] = np.ascontiguousarray(inputs["L1_ffn_w_up"].transpose(0, 2, 1))
    w["l1f_downT"] = np.ascontiguousarray(inputs["L1_ffn_w_down"].transpose(0, 2, 1))
    w["l1f_shupT"] = np.ascontiguousarray(inputs["L1_ffn_shared_w_up"].T)
    w["l1f_shdownT"] = np.ascontiguousarray(inputs["L1_ffn_shared_w_down"].T)
    return w


L1F_WEIGHT_SPECS = {
    "l1f_gateT": ([D, 8], F32),
    "l1f_ebias": ([1, 8], F32),
    "l1f_upT": ([8, D, 1408], F32),
    "l1f_downT": ([8, 1408, D], F32),
    "l1f_shupT": ([D, 1408], F32),
    "l1f_shdownT": ([1408, D], F32),
}


def build_stage_L1_ffn():
    return build_stage(emit_L1_ffn, L1F_WEIGHT_SPECS)



def emit_local_stack(p, io, weights, consts, cfg):
    nc = p.nc
    V, S, G, Tn = p.V, p.S, p.G, p.Tn
    ident16 = consts["ident16_sb"]
    ident32 = consts["ident32_sb"]
    ones16 = consts["ones16_sb"]
    ones32 = consts["ones32_sb"]
    tag = cfg["tag"]
    DEPTH, H, W, INNER = cfg["depth"], cfg["H"], cfg["W"], cfg["inner"]
    NP = H // 2
    NIC = INNER // 128
    NHG = 22
    CW = PAD + T
    KW = W + T

    x_f = [p.dram(f"{tag}_xf{i}", [T, D], F32) for i in range(2)]
    x_h = [p.dram(f"{tag}_xh{i}", [T, D], BF16) for i in range(2)]
    o_fm_d = (p.out if DBG else p.dram)(f"{tag}_ofm", [INNER, T], BF16)
    if DBG:
        dbg_q = p.out("dbg_q", [128, T], BF16)
        dbg_k = p.out("dbg_k", [128, W + T], BF16)
        dbg_o32 = p.out("dbg_o32", [65, T], F32)
        dbg_nm = p.out("dbg_nm", [1, T], BF16)
    xr_d = p.dram(f"{tag}_xrd", [D, T], BF16)

    sld = p.sem(f"{tag}_sld")
    sst = p.sem(f"{tag}_sst")
    sfm = p.sem(f"{tag}_sfm")
    sw = p.sem(f"{tag}_sw")
    sxr = p.sem(f"{tag}_sxr")

    ctabS = p.sb(f"{tag}_ctabSs", [128, T], BF16)
    stabS = p.sb(f"{tag}_stabSs", [128, T], BF16)
    ctabI = p.sb(f"{tag}_ctabIs", [128, T], BF16)
    stabI = p.sb(f"{tag}_stabIs", [128, T], BF16)
    blkS = p.sb(f"{tag}_blkSs", [128, 128], BF16)
    perm = p.sb(f"{tag}_perms", [128, 128], BF16)
    qkSf = p.sb(f"{tag}_qkSf", [128, 2 * DEPTH], F32)

    maskA = p.sb(f"{tag}_maskAs", [128, W + 128], F32)
    maskB = p.sb(f"{tag}_maskBs", [128, W + 128], F32)
    lnbf = p.sb(f"{tag}_lnbf", [128, 16 * DEPTH], F32)
    lnb16 = p.sb(f"{tag}_lnb16", [128, 16 * DEPTH], BF16)

    rowf = p.sb(f"{tag}_rowf", [97, T], F32)
    rowh = p.sb(f"{tag}_rowh", [97, T], BF16)

    mcols = p.sb(f"{tag}_mcols", [128, NT], F32)
    mtmp = p.sb(f"{tag}_mtmp", [128, 1], F32)
    lcols = p.sb(f"{tag}_lcols", [128, 32], F32)
    rows = rowf[0:1, :]
    srow = rowf[32:33, :]
    ssqrow = rowf[64:65, :]
    lrow = rowf[0:1, :]
    nmrow = rowh[0:1, :]
    murrow = rowh[0:1, :]



    t_c = 0
    for nm, buf in (("ctabS", ctabS), ("stabS", stabS), ("ctabI", ctabI),
                    ("stabI", stabI), ("blkS", blkS), ("perm", perm),
                    ("qkS", qkSf), ("maskA", maskA), ("maskB", maskB),
                    ("lnb", lnbf)):
        t_c = p.dma("sync", buf[:, :], weights[f"{tag}_{nm}"][:, :], sld)
    V.wait(sld, t_c)
    V("tensor_scalar_mul", qkSf[:, 0:DEPTH], qkSf[:, 0:DEPTH], 8.0)
    V("tensor_copy", lnb16[:, :], lnbf[:, :])

    dance = Dance(p, f"{tag}_dance")
    eps24_ap, _ = p.const_col(1e-24)
    eps5_ap, _ = p.const_col(1e-5)

    def sh(fmc, c, t0, n):
        off = PAD - 1 if c >= 4 else PAD
        return fmc[c][:, off + t0:off + t0 + n]

    blists = {}
    for qc in range(8):
        q0, q1 = qc * 512, qc * 512 + 512
        lst = []
        for kt in range(NT):
            klo = kt * 128
            lo = max(q0, klo)
            hi = min(q1, klo + 128 + W)
            if lo < hi:
                lst.append((kt, lo, hi - lo, lo - klo))
        blists[qc] = lst

    for layer in range(DEPTH):
        lsrc_f = io["ln32_tm"] if layer == 0 else x_f[0]
        lsrc_h = io["ln16_tm"] if layer == 0 else x_h[0]
        qs_col = qkSf[:, layer:layer + 1]
        ks_col = qkSf[:, DEPTH + layer:DEPTH + layer + 1]
        g_cols = [lnbf[:, 16 * layer + c:16 * layer + c + 1] for c in range(8)]
        b_cols = [lnb16[:, 16 * layer + 8 + c:16 * layer + 8 + c + 1] for c in range(8)]

        esA = contextlib.ExitStack()
        hfm = [p.sb(f"{tag}_hfm{layer}_{c}", [128, CW], BF16, esA) for c in range(8)]
        qrp = p.sb(f"{tag}_qrp{layer}", [128, T], BF16, esA)
        krp = p.sb(f"{tag}_krp{layer}", [128, KW], BF16, esA)
        vsl = [p.sb(f"{tag}_vsl{layer}_{i}", [128, 512], BF16, esA) for i in range(2)]
        rsl = [p.sb(f"{tag}_rsl{layer}_{i}", [128, 512], BF16, esA) for i in range(2)]
        sq2 = [p.sb(f"{tag}_sq2_{layer}_{i}", [128, 512], BF16, esA) for i in range(2)]
        rot = p.sb(f"{tag}_rot{layer}", [128, 512], BF16, esA)
        Vt = p.sb(f"{tag}_vt{layer}", [128, NT * 130], BF16, esA)
        o32 = p.sb(f"{tag}_o32_{layer}", [65, T], F32, esA)
        o16h = p.sb(f"{tag}_o16h{layer}", [64, T], BF16, esA)
        pexp = [p.sb(f"{tag}_pexp{layer}_{i}", [128, 512], BF16, esA) for i in range(3)]
        w16 = p.sb(f"{tag}_w16a{layer}", [128, 3 * 8 * 128], BF16, esA)
        pool = PsumPool(p, f"{tag}_pp{layer}", 2, [128, 512], es=esA)
        trp = PsumPool(p, f"{tag}_tr{layer}", 2, [128, 128], BF16, es=esA)
        psB = PsumPool(p, f"{tag}_psB{layer}", 3, [128, 512], es=esA)
        psO = pool

        for c in range(8):
            V("memset", hfm[c][:, 0:PAD], 0.0)
        if layer == 0:
            p.qwait("sync", io["pre_done"][0], io["pre_done"][1])
        else:
            p.qwait("sync", sst, sst.n)
        t_fm = emit_fm_load(p, lsrc_h, hfm, sfm)

        for pair in range(NP):
            p.qwait("gpsimd", Tn.sem, Tn.sem.n)
            vsl_rd = {}
            wch = {}
            for wi in range(3):
                for c in range(8):
                    ap = w16[:, (wi * 8 + c) * 128:(wi * 8 + c + 1) * 128]
                    p.dma("gpsimd", ap,
                          weights[f"{tag}_wqkv"][layer][c * 128:(c + 1) * 128,
                                                        wi * INNER + pair * 128:
                                                        wi * INNER + (pair + 1) * 128],
                          sw)
                wch[wi] = [w16[:, (wi * 8 + c) * 128:(wi * 8 + c + 1) * 128]
                           for c in range(8)]
            t_w = sw.n

            def pproj(wi, dst, col0):
                t_last = 0
                for ts in range(8):
                    pk, war = pool.get()
                    if war is not None:
                        Tn.wait(war[0], war[1])
                    Tn.wait(sw, t_w)
                    Tn.wait(sfm, t_fm)
                    Tn.wait(V.sem, V.sem.n)
                    t_mm = mm_group(p, pk[:, :], [
                        (wch[wi][ic], sh(hfm, ic, ts * 512, 512))
                        for ic in range(8)])
                    V.wait(Tn.sem, t_mm)
                    t_last = V("tensor_copy",
                               dst[:, col0 + ts * 512:col0 + (ts + 1) * 512], pk[:, :])
                    pool.done(p.V.sem, t_last)
                return t_last

            def l2norm_rot(buf, col0, scol, ctab, stab):
                sq_rd = {}
                rsl_rd = {}
                for ts in range(8):
                    b = ts % 2
                    bslc = slice(col0 + ts * 512, col0 + (ts + 1) * 512)
                    if ts - 2 in sq_rd:
                        S.wait(Tn.sem, sq_rd[ts - 2])
                    S.wait(V.sem, V.sem.n)
                    t_sq = S("activation", sq2[b][:, :], buf[:, bslc], AF.Square)
                    pk, war = pool.get()
                    if war is not None:
                        Tn.wait(war[0], war[1])
                    Tn.wait(S.sem, t_sq)
                    last = nc.tensor.matmul(pk[:, :], blkS[:, :], sq2[b][:, :],
                                            start=True, stop=True)
                    t_mm = Tn.sem.inc()
                    last.then_inc(Tn.sem.h, 1)
                    sq_rd[ts] = t_mm
                    S.wait(Tn.sem, t_mm)
                    if ts - 2 in rsl_rd:
                        S.wait(V.sem, rsl_rd[ts - 2])
                    S("activation", rsl[b][:, :], pk[:, :], AF.Ln, bias=eps24_ap)
                    t_ex = S("activation", rsl[b][:, :], rsl[b][:, :], AF.Exp,
                             scale=-0.5)
                    pool.done(p.S.sem, t_ex)
                    V.wait(S.sem, t_ex)
                    t_n = V("scalar_tensor_tensor", buf[:, bslc], buf[:, bslc],
                            scol, rsl[b][:, :], ALU.mult, ALU.mult)
                    pk2, war2 = psB.get()
                    if war2 is not None:
                        Tn.wait(war2[0], war2[1])
                    Tn.wait(V.sem, t_n)
                    t_mm2 = mm_group(p, pk2[:, :], [(perm[:, :], buf[:, bslc])])
                    V.wait(Tn.sem, t_mm2)
                    V("tensor_mul", rot[:, :], pk2[:, :],
                      stab[:, ts * 512:(ts + 1) * 512])
                    psB.done(p.V.sem, p.V.sem.n)
                    V("tensor_mul", buf[:, bslc], buf[:, bslc],
                      ctab[:, ts * 512:(ts + 1) * 512])
                    rsl_rd[ts] = V("tensor_add", buf[:, bslc], buf[:, bslc],
                                   rot[:, 0:512])
                return p.V.sem.n

            pproj(0, qrp, 0)
            l2norm_rot(qrp, 0, qs_col, ctabS, stabS)
            V("memset", krp[:, 0:W], 0.0)
            pproj(1, krp, W)
            l2norm_rot(krp, W, ks_col, ctabI, stabI)

            if pair == 0:
                V("memset", Vt[:, :], 1.0)
            for ts in range(8):
                b = ts % 2
                pk, war = pool.get()
                if war is not None:
                    Tn.wait(war[0], war[1])
                Tn.wait(sw, t_w)
                Tn.wait(sfm, t_fm)
                t_mm = mm_group(p, pk[:, :], [
                    (wch[2][ic], sh(hfm, ic, ts * 512, 512)) for ic in range(8)])
                V.wait(Tn.sem, t_mm)
                if ts >= 2:
                    V.wait(Tn.sem, vsl_rd[ts - 2])
                t_v = V("tensor_copy", vsl[b][:, :], pk[:, :])
                pool.done(p.V.sem, t_v)
                for sub in range(4):
                    j = ts * 4 + sub
                    pk2, war2 = trp.get()
                    if war2 is not None:
                        Tn.wait(war2[0], war2[1])
                    Tn.wait(V.sem, t_v)
                    t_tr = Tn.sem.inc()
                    nc.tensor.transpose(pk2[:, :], vsl[b][:, sub * 128:(sub + 1) * 128],
                                        ident16).then_inc(Tn.sem.h, 1)
                    if sub == 3:
                        vsl_rd[ts] = t_tr
                    V.wait(Tn.sem, t_tr)
                    V("tensor_copy", Vt[:, j * 130:j * 130 + 64], pk2[:, 0:64])
                    t_e = V("tensor_copy", Vt[:, j * 130 + 65:j * 130 + 129],
                            pk2[:, 64:128])
                    trp.done(p.V.sem, t_e)

            for hh in range(2):
                h = 2 * pair + hh
                rb0 = hh * 64
                V("memset", mcols[:, :], NEG)
                full_barrier(p)
                for j in range(NT):
                    kst = max(0, j * 128 - W)
                    kn = j * 128 + 128 - kst
                    moff = kst - (j * 128 - W)
                    nsub = [kn] if kn <= 512 else [512, kn - 512]
                    off = 0
                    for nn2 in nsub:
                        pb, war = psB.get()
                        if war is not None:
                            Tn.wait(war[0], war[1])
                        last = nc.tensor.matmul(
                            pb[:, 0:nn2],
                            qrp[rb0:rb0 + 64, j * 128:(j + 1) * 128],
                            krp[rb0:rb0 + 64, W + kst + off:W + kst + off + nn2],
                            start=True, stop=True)
                        t_b = Tn.sem.inc()
                        last.then_inc(Tn.sem.h, 1)
                        V.wait(Tn.sem, t_b)
                        V("tensor_add", pb[:, 0:nn2], pb[:, 0:nn2],
                          maskA[:, moff + off:moff + off + nn2])
                        V("reduce_max", mtmp[:, :], pb[:, 0:nn2], axis=AX.X)
                        t_m = V("tensor_max", mcols[:, j:j + 1], mcols[:, j:j + 1],
                                mtmp[:, :])
                        psB.done(p.V.sem, t_m)
                        off += nn2
                dance.rows_from_cols(mcols[:, :], rows, 32, ident32,
                                     waits=[(V.sem, p.V.sem.n)])
                V.wait(dance.sem, dance.sem.n)
                t_nm = V("tensor_scalar_mul", nmrow, rows, -1.0)

                entries = []
                for qc in range(8):
                    lst = blists[qc]
                    for idx, e in enumerate(lst):
                        entries.append((qc, e, idx == 0, idx == len(lst) - 1))
                nE = len(entries)
                exp_done = {}
                av_of_pexp = {}
                po = None
                for i2 in range(nE + 1):
                    if i2 < nE:
                        qc, (kt, qlo, nn2, moff), first, lastk = entries[i2]
                        pb, warB = psB.get()
                        if warB is not None:
                            Tn.wait(warB[0], warB[1])
                        Tn.wait(V.sem, t_nm)
                        nc.tensor.matmul(
                            pb[:, 0:nn2],
                            krp[rb0:rb0 + 64, W + kt * 128:W + kt * 128 + 128],
                            qrp[rb0:rb0 + 64, qlo:qlo + nn2],
                            start=True, stop=False)
                        last = nc.tensor.matmul(
                            pb[:, 0:nn2], ones16[0:1, 0:128],
                            nmrow[0:1, qlo:qlo + nn2], start=False, stop=True)
                        t_b = Tn.sem.inc()
                        last.then_inc(Tn.sem.h, 1)
                        V.wait(Tn.sem, t_b)
                        t_fix = V("tensor_add", pb[:, 0:nn2], pb[:, 0:nn2],
                                  maskB[:, moff:moff + nn2])
                        b = i2 % 3
                        S.wait(Tn.sem, t_b)
                        S.wait(V.sem, t_fix)
                        if b in av_of_pexp:
                            S.wait(Tn.sem, av_of_pexp[b])
                        t_e = S("activation", pexp[b][:, 0:nn2], pb[:, 0:nn2], AF.Exp)
                        psB.done(p.S.sem, t_e)
                        exp_done[i2] = t_e
                    if i2 >= 1:
                        qc, (kt, qlo, nn2, moff), first, lastk = entries[i2 - 1]
                        if first:
                            po, warO = psO.get()
                            if warO is not None:
                                Tn.wait(warO[0], warO[1])
                        b1 = (i2 - 1) % 3
                        Tn.wait(S.sem, exp_done[i2 - 1])
                        last = nc.tensor.matmul(
                            po[0:65, qlo - qc * 512:qlo - qc * 512 + nn2],
                            Vt[:, kt * 130 + hh * 65:kt * 130 + hh * 65 + 65],
                            pexp[b1][:, 0:nn2],
                            start=first, stop=lastk,
                            skip_group_check=True)
                        t_av = Tn.sem.inc()
                        last.then_inc(Tn.sem.h, 1)
                        av_of_pexp[b1] = t_av
                        if lastk:
                            V.wait(Tn.sem, t_av)
                            t_o = V("tensor_copy",
                                    o32[0:65, qc * 512:(qc + 1) * 512], po[0:65, :])
                            psO.done(p.V.sem, t_o)

                if DBG and h == 0 and layer == 0:
                    full_barrier(p)
                    p.dma("gpsimd", dbg_q[:, :], qrp[:, :], sst)
                    p.dma("gpsimd", dbg_k[:, :], krp[:, :], sst)
                    p.dma("gpsimd", dbg_o32[:, :], o32[0:65, :], sst)
                    p.dma("gpsimd", dbg_nm[:, :], nmrow, sst)
                V.wait(sst, sst.n)
                t_d = attn_divide_head(p, o32, o16h, dance, lrow, lrow, lcols,
                                       pool, ident32, ones32, 0)
                p.qwait("gpsimd", V.sem, t_d)
                p.dma("gpsimd", o_fm_d[h * 64:(h + 1) * 64, :], o16h[0:64, :], sst)

        full_barrier(p)
        esA.close()

        stop_attn = cfg.get("stop") == "attn" and layer == DEPTH - 1
        dst_f = io["sub_tm"] if stop_attn else x_f[1]
        dst_h = x_h[1]
        esP = contextlib.ExitStack()
        ofm = [p.sb(f"{tag}_pofm{layer}_{c}", [128, T], BF16, esP) for c in range(NIC)]
        wo16 = p.sb(f"{tag}_pwo{layer}", [128, NIC * D], BF16, esP)
        padd = [p.sb(f"{tag}_padd{layer}_{i}", [128, D], F32, esP) for i in range(2)]
        psub = [p.sb(f"{tag}_psub{layer}_{i}", [128, D], F32, esP) for i in range(2)]
        opool = PsumPool(p, f"{tag}_pop{layer}", 4, [128, 512], es=esP)
        p.qwait("sync", sst, sst.n)
        for c in range(NIC):
            p.dma("sync", ofm[c][:, :], o_fm_d[c * 128:(c + 1) * 128, :], sld)
        t_of = sld.n
        wch, t_w = load_weight16(p, weights[f"{tag}_wo"][layer], wo16, sw, INNER, D)
        st_done = {}
        padd_rd = {}
        for j in range(NT):
            i = j % 2
            if j - 2 in st_done:
                V.wait(sst, st_done[j - 2])
            if j - 2 in padd_rd:
                p.qwait("sync", V.sem, padd_rd[j - 2])
            t_a = p.dma("sync", padd[i][:, :], lsrc_f[j * 128:(j + 1) * 128, :], sld)
            t_e = None
            for ns in range(2):
                pk, war = opool.get()
                if war is not None:
                    Tn.wait(war[0], war[1])
                Tn.wait(sld, t_of)
                Tn.wait(sw, t_w)
                pairs = [(ofm[c][:, j * 128:(j + 1) * 128],
                          wo16[:, (c * 2 + ns) * 512:(c * 2 + ns + 1) * 512])
                         for c in range(NIC)]
                t_mm = mm_group(p, pk[:, :], pairs)
                V.wait(Tn.sem, t_mm)
                V.wait(sld, t_a)
                t_e = V("tensor_add", psub[i][:, ns * 512:(ns + 1) * 512], pk[:, :],
                        padd[i][:, ns * 512:(ns + 1) * 512])
                opool.done(p.V.sem, t_e)
            padd_rd[j] = t_e
            p.qwait("gpsimd", V.sem, t_e)
            p.dma("gpsimd", dst_f[j * 128:(j + 1) * 128, :], psub[i][:, :], sst)
            p.dma("gpsimd", dst_h[j * 128:(j + 1) * 128, :], psub[i][:, :], sst)
            st_done[j] = sst.n
        full_barrier(p)
        esP.close()
        if stop_attn:
            io["sub_done"] = (sst, sst.n)
            continue

        esB = contextlib.ExitStack()
        rowb1 = p.sb(f"{tag}_rowb1_{layer}", [1, 5460], BF16, esB)
        rowu = p.sb(f"{tag}_rowu_{layer}", [1, 5460], BF16, esB)
        rowb2 = p.sb(f"{tag}_rowb2_{layer}", [1, D], BF16, esB)
        b1row = rowb1[0:1, :]
        urow = rowu[0:1, :]
        b2row = rowb2[0:1, :]
        esF1 = contextlib.ExitStack()
        rowfB = p.sb(f"{tag}_rowfB{layer}", [33, T], F32, esF1)
        hfm2 = [p.sb(f"{tag}_h2fm{layer}_{c}", [128, CW], BF16, esF1) for c in range(8)]
        rb = p.sb(f"{tag}_rbs{layer}", [128, T], BF16, esF1)
        xrb = [p.sb(f"{tag}_xrb{layer}_{i}", [128, T], BF16, esF1) for i in range(2)]
        sqf = p.sb(f"{tag}_sqf{layer}", [128, 512], BF16, esF1)
        strow = PsumPool(p, f"{tag}_strow{layer}", 2, [1, 512], es=esF1)
        spool = PsumPool(p, f"{tag}_spool{layer}", 2, [128, 512], es=esF1)
        for c in range(8):
            V("memset", hfm2[c][:, 0:PAD], 0.0)
        p.qwait("sync", sst, sst.n)
        t_fm2 = emit_fm_load(p, dst_h, hfm2, sfm)
        p.qwait("gpsimd", V.sem, V.sem.n)
        t_b1raw = p.dma("gpsimd", b1row, weights[f"{tag}_b1"][layer][:, :], sw)
        t_b2 = p.dma("sync", rowf[96:97, 0:D], weights[f"{tag}_b2"][layer][:, :], sld)
        V.wait(sld, t_b2)
        t_b2r = V("tensor_copy", b2row, rowf[96:97, 0:D])

        for which in (0, 1):
            outrow = rows if which == 0 else rowfB[0:1, :]
            for ts in range(8):
                pk, war = strow.get()
                if war is not None:
                    Tn.wait(war[0], war[1])
                if which == 1:
                    last = None
                    for ic in range(8):
                        S.wait(Tn.sem, Tn.sem.n)
                        S.wait(sfm, t_fm2)
                        t_sq = S("activation", sqf[:, :],
                                 sh(hfm2, ic, ts * 512, 512), AF.Square)
                        Tn.wait(S.sem, t_sq)
                        last = nc.tensor.matmul(pk[0:1, :], ones16[:, 0:1],
                                                sqf[:, :], start=(ic == 0),
                                                stop=(ic == 7),
                                                skip_group_check=True)
                    t_mm = Tn.sem.inc()
                    last.then_inc(Tn.sem.h, 1)
                else:
                    Tn.wait(sfm, t_fm2)
                    t_mm = mm_group(p, pk[0:1, :], [
                        (ones16[:, 0:1], sh(hfm2, ic, ts * 512, 512))
                        for ic in range(8)])
                V.wait(Tn.sem, t_mm)
                t_r = V("tensor_copy", outrow[0:1, ts * 512:(ts + 1) * 512],
                        pk[0:1, :])
                strow.done(p.V.sem, t_r)
        B0 = rowfB[0:1, :]
        B32 = rowfB[32:33, :]
        V("tensor_scalar_mul", rows, rows, 1.0 / D)
        V("tensor_scalar_mul", srow, rows, -1.0)
        V("tensor_mul", rows, rows, rows)
        V("tensor_scalar_mul", B0, B0, 1.0 / D)
        t_var = V("tensor_sub", B0, B0, rows)
        S.wait(V.sem, t_var)
        S("activation", B0, B0, AF.Ln, bias=eps5_ap[0:1, 0:1])
        t_rst = S("activation", B0, B0, AF.Exp, scale=-0.5)
        V.wait(S.sem, t_rst)
        V("tensor_copy", B32, B0)
        t_mur = V("tensor_mul", murrow, srow, B32)
        for ts in range(8):
            pk, war = spool.get()
            if war is not None:
                Tn.wait(war[0], war[1])
            Tn.wait(V.sem, t_mur)
            t_mm = mm_group(p, pk[:, :], [(ones32[0:1, 0:128],
                                           B0[0:1, ts * 512:(ts + 1) * 512])])
            V.wait(Tn.sem, t_mm)
            t_rb = V("tensor_copy", rb[:, ts * 512:(ts + 1) * 512], pk[:, :])
            spool.done(p.V.sem, t_rb)
        xst = {}
        for c in range(8):
            if c - 2 in xst:
                V.wait(sxr, xst[c - 2])
            t_x = V("tensor_mul", xrb[c % 2][:, :], sh(hfm2, c, 0, T), rb[:, :])
            p.qwait("gpsimd", V.sem, t_x)
            p.dma("gpsimd", xr_d[c * 128:(c + 1) * 128, :], xrb[c % 2][:, :], sxr)
            xst[c] = sxr.n
        full_barrier(p)
        esF1.close()
        if cfg.get("stop") == "xr" and layer == DEPTH - 1:
            scp = p.sem(f"{tag}_scp{layer}")
            p.qwait("gpsimd", sst, sst.n)
            for j in range(NT):
                p.dma("gpsimd", io["sub_tm"][j * 128:(j + 1) * 128, :],
                      dst_f[j * 128:(j + 1) * 128, :], scp)
            io["sub_done"] = (scp, scp.n)
            continue

        out_f = io["sub_tm"] if layer == DEPTH - 1 else x_f[0]
        out_h = x_h[0]
        esF2 = contextlib.ExitStack()
        Hp = [p.sb(f"{tag}_Hp{layer}_{c}", [128, T], BF16, esF2) for c in range(2)]
        acc = p.sb(f"{tag}_facc{layer}", [128, NT * D], BF16, esF2)
        xsl = p.sb(f"{tag}_xsl{layer}", [128, 8 * 512], BF16, esF2)
        wA = p.sb(f"{tag}_wA{layer}", [128, 8 * 256], BF16, esF2)
        wG = p.sb(f"{tag}_wG{layer}", [128, 8 * 256], BF16, esF2)
        w2b = p.sb(f"{tag}_w2b{layer}", [128, 2 * D], BF16, esF2)
        gg = [p.sb(f"{tag}_gg{layer}_{i}", [128, 512], BF16, esF2) for i in range(2)]
        fadd = [p.sb(f"{tag}_fadd{layer}", [128, D], F32, esF2)]
        fsub = [p.sb(f"{tag}_fsub{layer}", [128, D], F32, esF2)]
        gpool = PsumPool(p, f"{tag}_gp{layer}", 2, [128, 512], es=esF2)
        apool = PsumPool(p, f"{tag}_ap{layer}", 2, [128, 512], es=esF2)
        dpool = PsumPool(p, f"{tag}_dp{layer}", 2, [128, 512], es=esF2)
        rpool = PsumPool(p, f"{tag}_rp{layer}", 1, [1, 512], es=esF2)

        PARTS = [(k * 2, min(2, NHG - k * 2)) for k in range(11)]
        for pi, (c0, nch) in enumerate(PARTS):
            col_lo = c0 * 128
            width = min(2730, col_lo + nch * 128) - col_lo
            p.qwait("gpsimd", Tn.sem, Tn.sem.n)
            for blk, wdst, base in ((0, wA, col_lo), (1, wG, 2730 + col_lo)):
                for c in range(8):
                    p.dma("gpsimd", wdst[:, c * 256:c * 256 + width],
                          weights[f"{tag}_w1"][layer][c * 128:(c + 1) * 128,
                                                      base:base + width], sw)
            t_w1 = sw.n
            V.wait(sw, t_b1raw)
            for blk, wdst, base in ((0, wA, col_lo), (1, wG, 2730 + col_lo)):
                pk, war = rpool.get()
                if war is not None:
                    Tn.wait(war[0], war[1])
                Tn.wait(sw, t_w1)
                Tn.wait(V.sem, V.sem.n)
                last = None
                for c in range(8):
                    last = nc.tensor.matmul(pk[0:1, 0:width], b_cols[c],
                                            wdst[:, c * 256:c * 256 + width],
                                            start=(c == 0), stop=(c == 7))
                t_mm = Tn.sem.inc()
                last.then_inc(Tn.sem.h, 1)
                V.wait(Tn.sem, t_mm)
                t_b1 = V("scalar_tensor_tensor", b1row[0:1, base:base + width],
                         pk[0:1, 0:width], 1.0, b1row[0:1, base:base + width],
                         ALU.mult, ALU.add)
                rpool.done(p.V.sem, t_b1)
            t_gs = 0
            for blk, wdst in ((0, wA), (1, wG)):
                for c in range(8):
                    t_gs = V("tensor_scalar_mul", wdst[:, c * 256:c * 256 + width],
                             wdst[:, c * 256:c * 256 + width], g_cols[c])
            for blk, wdst, base in ((0, wA, col_lo), (1, wG, 2730 + col_lo)):
                pk, war = rpool.get()
                if war is not None:
                    Tn.wait(war[0], war[1])
                Tn.wait(V.sem, t_gs)
                last = None
                for c in range(8):
                    last = nc.tensor.matmul(pk[0:1, 0:width], ones16[:, 0:1],
                                            wdst[:, c * 256:c * 256 + width],
                                            start=(c == 0), stop=(c == 7))
                t_mm = Tn.sem.inc()
                last.then_inc(Tn.sem.h, 1)
                V.wait(Tn.sem, t_mm)
                t_u = V("tensor_copy", urow[0:1, base:base + width],
                        pk[0:1, 0:width])
                rpool.done(p.V.sem, t_u)
            t_prep = p.V.sem.n

            gg_rd = {}
            k = 0
            for ts in range(8):
                p.qwait("sync", Tn.sem, Tn.sem.n)
                for ic in range(8):
                    p.dma("sync", xsl[:, ic * 512:(ic + 1) * 512],
                          xr_d[ic * 128:(ic + 1) * 128, ts * 512:(ts + 1) * 512],
                          sld)
                t_xs = sld.n
                for cc in range(nch):
                    cw = min(128, 2730 - (c0 + cc) * 128)
                    gbase = 2730 + (c0 + cc) * 128
                    abase = (c0 + cc) * 128
                    pg, warg = gpool.get()
                    if warg is not None:
                        Tn.wait(warg[0], warg[1])
                    Tn.wait(V.sem, max(t_prep, t_b2r))
                    Tn.wait(sld, t_xs)
                    pairs = [(wG[:, ic * 256 + cc * 128:ic * 256 + cc * 128 + cw],
                              xsl[:, ic * 512:(ic + 1) * 512]) for ic in range(8)]
                    if "norank1" not in BISECT:
                        pairs.append((b1row[0:1, gbase:gbase + cw], ones16[0:1, 0:512]))
                        pairs.append((urow[0:1, gbase:gbase + cw],
                                      murrow[0:1, ts * 512:(ts + 1) * 512]))
                    t_mm = mm_group(p, pg[0:cw, :], pairs)
                    b = k % 2
                    S.wait(Tn.sem, t_mm)
                    if k - 2 in gg_rd:
                        S.wait(V.sem, gg_rd[k - 2])
                    t_gel = S("activation", gg[b][0:cw, :], pg[0:cw, :],
                              AF.Relu if "nogelu" in BISECT else AF.Gelu)
                    gpool.done(p.S.sem, t_gel)
                    pa, wara = apool.get()
                    if wara is not None:
                        Tn.wait(wara[0], wara[1])
                    pairs = [(wA[:, ic * 256 + cc * 128:ic * 256 + cc * 128 + cw],
                              xsl[:, ic * 512:(ic + 1) * 512]) for ic in range(8)]
                    if "norank1" not in BISECT:
                        pairs.append((b1row[0:1, abase:abase + cw], ones16[0:1, 0:512]))
                        pairs.append((urow[0:1, abase:abase + cw],
                                      murrow[0:1, ts * 512:(ts + 1) * 512]))
                    t_mm2 = mm_group(p, pa[0:cw, :], pairs)
                    V.wait(Tn.sem, t_mm2)
                    V.wait(S.sem, t_gel)
                    t_h = V("tensor_mul", Hp[cc][0:cw, ts * 512:(ts + 1) * 512],
                            pa[0:cw, :], gg[b][0:cw, :])
                    gg_rd[k] = t_h
                    apool.done(p.V.sem, t_h)
                    k += 1
            t_hp = p.V.sem.n
            if cfg.get("stop") == "h":
                continue

            p.qwait("gpsimd", Tn.sem, Tn.sem.n)
            for cc in range(nch):
                cw = min(128, 2730 - (c0 + cc) * 128)
                p.dma("gpsimd", w2b[0:cw, cc * D:(cc + 1) * D],
                      weights[f"{tag}_w2"][layer][(c0 + cc) * 128:(c0 + cc) * 128 + cw, :],
                      sw)
            t_w2 = sw.n
            st_done = {}
            final = (pi == len(PARTS) - 1)
            for j in range(NT):
                i = 0
                if final and j - 1 in st_done:
                    V.wait(sst, st_done[j - 1])
                if final:
                    p.qwait("sync", V.sem, V.sem.n)
                    t_a2 = p.dma("sync", fadd[i][:, :],
                                 dst_f[j * 128:(j + 1) * 128, :], sld)
                t_e2 = None
                for ns in range(2):
                    pk, war = dpool.get()
                    if war is not None:
                        Tn.wait(war[0], war[1])
                    Tn.wait(sw, t_w2)
                    Tn.wait(V.sem, t_hp)
                    pairs = [(Hp[cc][0:min(128, 2730 - (c0 + cc) * 128),
                                     j * 128:(j + 1) * 128],
                              w2b[0:min(128, 2730 - (c0 + cc) * 128),
                                  cc * D + ns * 512:cc * D + ns * 512 + 512])
                             for cc in range(nch)]
                    if final:
                        pairs.append((ones16[0:1, 0:128],
                                      b2row[0:1, ns * 512:(ns + 1) * 512]))
                    t_mm = mm_group(p, pk[:, :], pairs)
                    V.wait(Tn.sem, t_mm)
                    if pi == 0:
                        t_e2 = V("tensor_copy",
                                 acc[:, j * D + ns * 512:j * D + (ns + 1) * 512],
                                 pk[:, :])
                    elif not final:
                        t_e2 = V("scalar_tensor_tensor",
                                 acc[:, j * D + ns * 512:j * D + (ns + 1) * 512],
                                 pk[:, :], 1.0,
                                 acc[:, j * D + ns * 512:j * D + (ns + 1) * 512],
                                 ALU.mult, ALU.add)
                    else:
                        fslc = fsub[i][:, ns * 512:(ns + 1) * 512]
                        V("tensor_copy", fslc,
                          acc[:, j * D + ns * 512:j * D + (ns + 1) * 512])
                        V("tensor_add", fslc, fslc, pk[:, :])
                        V.wait(sld, t_a2)
                        t_e2 = V("tensor_add", fslc, fslc,
                                 fadd[i][:, ns * 512:(ns + 1) * 512])
                    dpool.done(p.V.sem, t_e2)
                if final:
                    p.qwait("gpsimd", V.sem, t_e2)
                    p.dma("gpsimd", out_f[j * 128:(j + 1) * 128, :], fsub[i][:, :], sst)
                    if layer < DEPTH - 1:
                        p.dma("gpsimd", out_h[j * 128:(j + 1) * 128, :],
                              fsub[i][:, :], sst)
                    st_done[j] = sst.n
        full_barrier(p)
        esF2.close()
        esB.close()
        if cfg.get("stop") == "h" and layer == DEPTH - 1:
            scp = p.sem(f"{tag}_scph{layer}")
            p.qwait("gpsimd", sst, sst.n)
            for j in range(NT):
                p.dma("gpsimd", io["sub_tm"][j * 128:(j + 1) * 128, :],
                      dst_f[j * 128:(j + 1) * 128, :], scp)
            io["sub_done"] = (scp, scp.n)

    if cfg.get("stop") == "h":
        return
    io["sub_done"] = (sst, sst.n)


def _xpos_tables(Wl):
    half = np.arange(0, 64, 2, dtype=np.float64)
    inv_freq = 1.0 / 10000.0 ** (half / 64)
    t = np.arange(T, dtype=np.float64)
    freqs = t[:, None] * inv_freq[None, :]
    freqs = np.concatenate([freqs, freqs], -1)
    base = (half + 0.4 * 64) / (1.4 * 64)
    power = (t - T // 2) / (Wl // 2)
    scale = base[None, :] ** power[:, None]
    scale = np.concatenate([scale, scale], -1)
    cos, sin = np.cos(freqs), np.sin(freqs)
    cS = (cos * scale).T
    sS = (sin * scale).T
    cI = (cos / scale).T
    sI = (sin / scale).T
    tile2 = lambda a: np.tile(a, (2, 1)).astype(BF)
    return tile2(cS), tile2(sS), tile2(cI), tile2(sI)


def _local_masks(Wl):
    qp = np.arange(128)[:, None]
    kp = np.arange(Wl + 128)[None, :]
    mA = np.where((kp < qp) | (kp > qp + Wl), NEG, 0.0).astype(np.float32)
    kp2 = np.arange(128)[:, None]
    qp2 = np.arange(Wl + 128)[None, :]
    mB = np.where((qp2 < kp2) | (qp2 > kp2 + Wl), NEG, 0.0).astype(np.float32)
    return mA, mB


def _rot_perm128():
    P = np.zeros((64, 64), np.float32)
    for i in range(32):
        P[i + 32, i] = -1.0
        P[i, i + 32] = 1.0
    return np.kron(np.eye(2, dtype=np.float32), P).astype(BF)


def _local_host_prep(inputs, tag, pre, depth, Wl):
    w = {}
    w[f"{tag}_wqkv"] = np.ascontiguousarray(inputs[f"{pre}_wqkv" if tag == "l3f" else f"{pre}_Wqkv"])
    w[f"{tag}_wo"] = np.ascontiguousarray(inputs[f"{pre}_wo" if tag == "l3f" else f"{pre}_Wo"])
    w[f"{tag}_w1"] = np.ascontiguousarray(inputs[f"{pre}_w1" if tag == "l3f" else f"{pre}_W1"])
    w[f"{tag}_w2"] = np.ascontiguousarray(inputs[f"{pre}_w2" if tag == "l3f" else f"{pre}_W2"])
    w[f"{tag}_b1"] = np.ascontiguousarray(inputs[f"{pre}_b1"].reshape(depth, 1, 5460))
    w[f"{tag}_b2"] = np.ascontiguousarray(inputs[f"{pre}_b2"].reshape(depth, 1, D))
    qs = inputs[f"{pre}_q_scale"]
    ks = inputs[f"{pre}_k_scale"]
    qk = np.zeros((128, 2 * depth), np.float32)
    for l in range(depth):
        qk[:, l] = np.tile(qs[l], 2)
        qk[:, depth + l] = np.tile(ks[l], 2)
    w[f"{tag}_qkS"] = qk
    lg = inputs[f"{pre}_ln_g"]
    lb = inputs[f"{pre}_ln_b"]
    lnb = np.zeros((128, 16 * depth), np.float32)
    for l in range(depth):
        lnb[:, 16 * l:16 * l + 8] = lg[l].reshape(8, 128).T
        lnb[:, 16 * l + 8:16 * l + 16] = lb[l].reshape(8, 128).T
    w[f"{tag}_lnb"] = lnb
    cS, sS, cI, sI = _xpos_tables(Wl)
    w[f"{tag}_ctabS"] = cS
    w[f"{tag}_stabS"] = sS
    w[f"{tag}_ctabI"] = cI
    w[f"{tag}_stabI"] = sI
    mA, mB = _local_masks(Wl)
    w[f"{tag}_maskA"] = mA
    w[f"{tag}_maskB"] = mB
    w[f"{tag}_blkS"] = np.kron(np.eye(2, dtype=np.float32),
                               np.ones((64, 64), np.float32)).astype(BF)
    w[f"{tag}_perm"] = _rot_perm128()
    return w


def host_prep_L3_ffn(inputs):
    return _local_host_prep(inputs, "l3f", "L3_ffn", 2, 512)


def host_prep_L0_ffn(inputs):
    return _local_host_prep(inputs, "l0f", "L0_ffn", 6, 256)


def _local_specs(tag, depth, Wl, inner):
    return {
        f"{tag}_wqkv": ([depth, D, 3 * inner], F32),
        f"{tag}_wo": ([depth, inner, D], F32),
        f"{tag}_w1": ([depth, D, 5460], F32),
        f"{tag}_w2": ([depth, 2730, D], F32),
        f"{tag}_b1": ([depth, 1, 5460], F32),
        f"{tag}_b2": ([depth, 1, D], F32),
        f"{tag}_qkS": ([128, 2 * depth], F32),
        f"{tag}_lnb": ([128, 16 * depth], F32),
        f"{tag}_ctabS": ([128, T], BF16),
        f"{tag}_stabS": ([128, T], BF16),
        f"{tag}_ctabI": ([128, T], BF16),
        f"{tag}_stabI": ([128, T], BF16),
        f"{tag}_maskA": ([128, Wl + 128], F32),
        f"{tag}_maskB": ([128, Wl + 128], F32),
        f"{tag}_blkS": ([128, 128], BF16),
        f"{tag}_perm": ([128, 128], BF16),
    }


L3F_WEIGHT_SPECS = _local_specs("l3f", 2, 512, 512)
L0F_WEIGHT_SPECS = _local_specs("l0f", 6, 256, 1024)


def emit_L3_ffn(p, io, weights, consts):
    emit_local_stack(p, io, weights, consts,
                     dict(tag="l3f", depth=2, H=8, W=512, inner=512))


def emit_L0_ffn(p, io, weights, consts):
    emit_local_stack(p, io, weights, consts,
                     dict(tag="l0f", depth=6, H=16, W=256, inner=1024))


def build_stage_L3_ffn():
    return build_stage(emit_L3_ffn, L3F_WEIGHT_SPECS, need_ln32=True)


def build_stage_L0_ffn():
    return build_stage(emit_L0_ffn, L0F_WEIGHT_SPECS, need_ln32=True)



FUSED_STAGES = [
    ("L0_attn", emit_L0_attn, L0A_WEIGHT_SPECS, host_prep_L0_attn, False, False),
    ("L0_ffn", emit_L0_ffn, L0F_WEIGHT_SPECS, host_prep_L0_ffn, False, True),
    ("L1_attn", emit_L1_attn, L1A_WEIGHT_SPECS, host_prep_L1_attn, False, True),
    ("L1_ffn", emit_L1_ffn, L1F_WEIGHT_SPECS, host_prep_L1_ffn, False, False),
    ("L2_attn", emit_L2_attn, L2A_WEIGHT_SPECS, host_prep_L2_attn, False, False),
    ("L2_ffn", emit_L2_ffn, L2F_WEIGHT_SPECS, host_prep_L2_ffn, False, False),
    ("L3_attn", emit_L3_attn, L3A_WEIGHT_SPECS, host_prep_L3_attn, True, False),
    ("L3_ffn", emit_L3_ffn, L3F_WEIGHT_SPECS, host_prep_L3_ffn, False, True),
]


def build_fused():
    p = Prog()
    x_in = p.inp("x", [T, D], F32)
    x_out = p.out("x_out", [T, D], F32)
    weights = {}
    for _, _, spec, _, _, _ in FUSED_STAGES:
        for n, (sh, dt) in spec.items():
            weights[n] = p.inp(n, sh, dt)
    p.prealloc_cbuf()
    consts = {}
    for n, (sh, dt) in CONST_SPECS.items():
        ap = p.inp(n, sh, dt)
        sbuf = p.sb(n + "_sb", sh, dt, es=p.es)
        consts[n + "_dram"] = ap
        consts[n.replace("c_", "") + "_sb"] = sbuf[:, :]
    sc = p.sem("consts")
    for n in CONST_SPECS:
        p.dma("sync", consts[n.replace("c_", "") + "_sb"][:, :],
              consts[n + "_dram"][:, :], sc)
    p.qwait("sync", sc, sc.n)

    xa = p.dram("x_ping", [T, D], F32)
    xb = p.dram("x_pong", [T, D], F32)
    io = {}
    io["ln16_tm"] = p.dram("ln16_tm", [T, D], BF16)
    io["ln32_tm"] = p.dram("ln32_tm", [T, D], F32)
    io["sub_tm"] = p.dram("sub_tm", [T, D], F32)
    ldx, sg = p.sem("ldx"), p.sem("sg")

    for i, (name, emit_fn, spec, _, rms2, need32) in enumerate(FUSED_STAGES):
        src = x_in if i == 0 else (xa if i % 2 == 1 else xb)
        dst = x_out if i == len(FUSED_STAGES) - 1 else (xa if i % 2 == 0 else xb)
        es_stage = contextlib.ExitStack()
        p.es_stack.append(es_stage)

        es_pre = contextlib.ExitStack()
        p.es_stack.append(es_pre)
        p.qwait("sync", sg, sg.n)
        full_barrier(p)
        ln = LnPass(p, f"pre{i}", needadd=False, need32=need32)
        t_pre = ln.emit(ldx, sg, src, out16=io["ln16_tm"],
                        out32=(io["ln32_tm"] if need32 else None), rms2=rms2)
        io["pre_done"] = (sg, t_pre)
        barrier(p, io["pre_done"])
        p.es_stack.pop()
        es_pre.close()

        emit_fn(p, io, weights, consts)

        barrier(p, io["sub_done"])
        es_post = contextlib.ExitStack()
        p.es_stack.append(es_post)
        ln2 = LnPass(p, f"post{i}", need16=False)
        t_post = ln2.emit(ldx, sg, io["sub_tm"], out32=dst, add=src)
        p.qwait("sync", sg, t_post)
        p.es_stack.pop()
        es_post.close()
        full_barrier(p)
        p.es_stack.pop()
        es_stage.close()
    return p


STAGES = [
    ("L0_attn", build_stage_L0_attn, host_prep_L0_attn),
    ("L0_ffn", build_stage_L0_ffn, host_prep_L0_ffn),
    ("L1_attn", build_stage_L1_attn, host_prep_L1_attn),
    ("L1_ffn", build_stage_L1_ffn, host_prep_L1_ffn),
    ("L2_attn", build_stage_L2_attn, host_prep_L2_attn),
    ("L2_ffn", build_stage_L2_ffn, host_prep_L2_ffn),
    ("L3_attn", build_stage_L3_attn, host_prep_L3_attn),
    ("L3_ffn", build_stage_L3_ffn, host_prep_L3_ffn),
]

_prog_cache = {}


def kernel(**inputs):
    consts = build_constants()
    x = np.ascontiguousarray(inputs["L0_attn_x"], dtype=np.float32)
    if "fused" not in _prog_cache:
        _prog_cache["fused"] = build_fused()
    p = _prog_cache["fused"]
    w = {}
    for _, _, _, prep_fn, _, _ in FUSED_STAGES:
        w.update(prep_fn(inputs))
    w = {k: np.ascontiguousarray(v) for k, v in w.items()}
    in_maps = []
    for c in range(8):
        m = dict(x=np.ascontiguousarray(x[c]))
        m.update(w)
        m.update(consts)
        in_maps.append(m)
    res = run_bass_kernel_spmd(p.nc, in_maps, core_ids=list(range(8)))
    return np.stack([np.asarray(res.results[c]["x_out"]) for c in range(8)])


# version 04
# speedup vs baseline: 1.2228x; 1.0163x over previous
import contextlib
import math

import numpy as np
import ml_dtypes

import concourse.bass as bass
import concourse.mybir as mybir
from concourse.bass_utils import run_bass_kernel_spmd

F32 = mybir.dt.float32
BF16 = mybir.dt.bfloat16
AF = mybir.ActivationFunctionType
ALU = mybir.AluOpType
AX = mybir.AxisListType

T = 4096
D = 1024
NT = T // 128
NCH = D // 128
PAD = 16

NEG = -1.0e30
BF = ml_dtypes.bfloat16
DBG = False
BISECT = set()


class Sem:
    __slots__ = ("h", "n")

    def __init__(self, h):
        self.h = h
        self.n = 0

    def inc(self, k=1):
        self.n += k
        return self.n


class Eng:

    def __init__(self, p, name):
        self.p = p
        self.name = name
        self.h = getattr(p.nc, name)
        self.sem = p.sem(f"{name}sf")

    def wait(self, sem, n):
        if n > 0:
            self.h.wait_ge(sem.h, n)

    def __call__(self, method, *a, chain=True, **kw):
        if chain and self.sem.n > 0:
            self.h.wait_ge(self.sem.h, self.sem.n)
        inst = getattr(self.h, method)(*a, **kw)
        t = self.sem.inc()
        inst.then_inc(self.sem.h, 1)
        return t


class Prog:

    def __init__(self):
        self.nc = bass.Bass("TRN2", target_bir_lowering=False)
        self.es = contextlib.ExitStack()
        self._semi = 0
        self.input_names = []
        self._cmap = None
        self.es_stack = [self.es]
        self.V = Eng(self, "vector")
        self._pre_v = None
        self.S = Eng(self, "scalar")
        self.G = Eng(self, "gpsimd")
        self.Tn = Eng(self, "tensor")

    def inp(self, name, shape, dtype):
        self.input_names.append(name)
        return self.nc.dram_tensor(name, list(shape), dtype, kind="ExternalInput").ap()

    def out(self, name, shape, dtype):
        return self.nc.dram_tensor(name, list(shape), dtype, kind="ExternalOutput").ap()

    def dram(self, name, shape, dtype):
        return self.nc.dram_tensor(name, list(shape), dtype, kind="Internal").ap()

    def sem(self, name=None):
        self._semi += 1
        h = self.es.enter_context(self.nc.semaphore(name or f"s{self._semi}"))
        return Sem(h)

    def sb(self, name, shape, dtype, es=None):
        return (es or self.es_stack[-1]).enter_context(
            self.nc.sbuf_tensor(name, list(shape), dtype))

    def ps(self, name, shape, dtype, es=None):
        return (es or self.es_stack[-1]).enter_context(
            self.nc.psum_tensor(name, list(shape), dtype))

    def dma(self, q, out, in_, sem, transpose=False):
        eng = getattr(self.nc, q)
        if transpose:
            inst = eng.dma_start_transpose(out, in_)
        else:
            inst = eng.dma_start(out=out, in_=in_)
        inst.then_inc(sem.h, 16)
        return sem.inc(16)

    def qwait(self, q, sem, n):
        if n > 0:
            getattr(self.nc, q).wait_ge(sem.h, n)

    def prealloc_cbuf(self):
        if self._cmap is None:
            self._cbuf = self.sb("cbuf", [128, 16], F32, es=self.es)
            self._cmap = {}
            self._cn = 0

    def const_col(self, val, width=1):
        if self._cmap is None:
            self.prealloc_cbuf()
        key = (float(val), width)
        if key in self._cmap:
            return self._cmap[key]
        k = self._cn
        self._cn += width
        assert self._cn <= 16
        ap = self._cbuf[:, k:k + width]
        t = self.V("memset", ap, float(val))
        self._cmap[key] = (ap, t)
        return (ap, t)



def build_constants():
    c = {}
    c["c_ident16"] = np.eye(128, dtype=np.float32).astype(BF)
    c["c_ident32"] = np.eye(128, dtype=np.float32)
    c["c_ones16"] = np.ones((128, 512), dtype=np.float32).astype(BF)
    c["c_ones32"] = np.ones((128, 512), dtype=np.float32)
    return c


CONST_SPECS = {
    "c_ident16": ([128, 128], BF16),
    "c_ident32": ([128, 128], F32),
    "c_ones16": ([128, 512], BF16),
    "c_ones32": ([128, 512], F32),
}


def declare_consts(p, names):
    return {n: p.inp(n, *CONST_SPECS[n]) for n in names}



class LnPass:

    def __init__(self, p, tag, es=None, need16=True, need32=True, needadd=True):
        self.p = p
        self.xt = [p.sb(f"{tag}_xt{i}", [128, D], F32, es) for i in range(2)]
        self.at = [p.sb(f"{tag}_at{i}", [128, D], F32, es) for i in range(2)] if needadd else None
        self.o16 = [p.sb(f"{tag}_o16_{i}", [128, D], BF16, es) for i in range(2)] if need16 else None
        self.o32 = [p.sb(f"{tag}_o32_{i}", [128, D], F32, es) for i in range(2)] if need32 else None
        self.sq = [p.sb(f"{tag}_sq{i}", [128, D], BF16, es) for i in range(2)]
        self.ssq = [p.sb(f"{tag}_ssq{i}", [128, 1], F32, es) for i in range(2)]
        self.mu = [p.sb(f"{tag}_mu{i}", [128, 1], F32, es) for i in range(2)]
        self.var = [p.sb(f"{tag}_var{i}", [128, 1], F32, es) for i in range(2)]
        self.rstd = [p.sb(f"{tag}_rstd{i}", [128, 1], F32, es) for i in range(2)]
        self.v_done = {}
        self.s_done = {}
        self.g_done = {}

    def emit(self, ldx, sg, src, out16=None, out32=None, add=None, rms2=False,
             eps=1e-5, tiles=None):
        p = self.p
        nc = p.nc
        V, S = p.V, p.S
        if tiles is None:
            tiles = range(NT)
        for j in tiles:
            i = j % 2
            if j - 2 in self.v_done:
                nc.sync.wait_ge(V.sem.h, self.v_done[j - 2])
                nc.sync.wait_ge(S.sem.h, self.s_done[j - 2])
            t_ld = p.dma("sync", self.xt[i][:, :], src[j * 128:(j + 1) * 128, :], ldx)
            if add is not None:
                t_ld = p.dma("sync", self.at[i][:, :], add[j * 128:(j + 1) * 128, :], ldx)

            V.wait(ldx, t_ld)
            V("reduce_sum", self.mu[i][:, :], self.xt[i][:, :], axis=AX.X)
            V("tensor_scalar_mul", self.mu[i][:, :], self.mu[i][:, :], 1.0 / D)
            V("tensor_mul", self.var[i][:, :], self.mu[i][:, :], self.mu[i][:, :])
            S.wait(ldx, t_ld)
            t_sq = S("activation", self.sq[i][:, :], self.xt[i][:, :], AF.Square,
                     accum_out=self.ssq[i][:, :])
            V.wait(S.sem, t_sq)
            t_var = V("scalar_tensor_tensor", self.var[i][:, :], self.ssq[i][:, :],
                      1.0 / D, self.var[i][:, :], ALU.mult, ALU.subtract)
            scale_ln, bias_ln = (1.0 + 1e-6, 1e-6 * eps) if rms2 else (1.0, eps)
            bias_ap, bias_t = p.const_col(bias_ln)
            S.wait(V.sem, max(t_var, bias_t))
            S("activation", self.rstd[i][:, :], self.var[i][:, :], AF.Ln,
              bias=bias_ap, scale=float(scale_ln))
            t_rstd = S("activation", self.rstd[i][:, :], self.rstd[i][:, :], AF.Exp,
                       scale=-0.5)

            V.wait(S.sem, t_rstd)
            if j - 2 in self.g_done:
                V.wait(sg, self.g_done[j - 2])
            outs = []
            t_norm = None
            if out16 is not None:
                t_norm = V("tensor_scalar", self.o16[i][:, :], self.xt[i][:, :],
                           self.mu[i][:, :], self.rstd[i][:, :], ALU.subtract, ALU.mult)
                outs.append((self.o16[i], out16))
            if out32 is not None:
                t_norm = V("tensor_scalar", self.o32[i][:, :], self.xt[i][:, :],
                           self.mu[i][:, :], self.rstd[i][:, :], ALU.subtract, ALU.mult)
                if add is not None:
                    t_norm = V("tensor_add", self.o32[i][:, :], self.o32[i][:, :],
                               self.at[i][:, :])
                outs.append((self.o32[i], out32))
            self.v_done[j] = t_norm
            self.s_done[j] = t_sq

            p.qwait("gpsimd", V.sem, t_norm)
            for buf, dst in outs:
                p.dma("gpsimd", dst[j * 128:(j + 1) * 128, :], buf[:, :], sg)
            self.g_done[j] = sg.n
        return sg.n



def emit_fm_load(p, src16_tm, fm_chunks, sem, col0=PAD, rows=None):
    if rows is None:
        rows = len(fm_chunks) * 128
    for c in range(rows // 128):
        p.dma("sync", fm_chunks[c][:, col0:col0 + T],
              src16_tm[:, c * 128:(c + 1) * 128], sem, transpose=True)
    return sem.n



def build_stage_identity():
    p = Prog()
    x_in = p.inp("x", [T, D], F32)
    x_out = p.out("x_out", [T, D], F32)
    ln16_tm = p.dram("ln16_tm", [T, D], BF16)
    sub_tm = p.dram("sub_tm", [T, D], F32)

    ldx, sg = p.sem("ldx"), p.sem("sg")
    ln = LnPass(p, "pre")
    t_pre = ln.emit(ldx, sg, x_in, out16=ln16_tm)

    p.qwait("gpsimd", sg, t_pre)
    scast = p.sem("scast")
    for j in range(NT):
        p.dma("gpsimd", sub_tm[j * 128:(j + 1) * 128, :],
              ln16_tm[j * 128:(j + 1) * 128, :], scast)
    p.qwait("sync", scast, scast.n)

    ln2 = LnPass(p, "post")
    t_post = ln2.emit(ldx, sg, sub_tm, out32=x_out, add=x_in)
    p.qwait("sync", sg, t_post)
    return p



def load_weight16(p, dram_ap, sbuf, sem, rows, cols):
    p.qwait("gpsimd", p.Tn.sem, p.Tn.sem.n)
    chunks = []
    for c in range(rows // 128):
        ap = sbuf[:, c * cols:(c + 1) * cols]
        p.dma("gpsimd", ap, dram_ap[c * 128:(c + 1) * 128, :], sem)
        chunks.append(ap)
    return chunks, sem.n


def mm_group(p, psum_ap, pairs):
    nc = p.nc
    last = None
    n = len(pairs)
    for i, (l, r) in enumerate(pairs):
        last = nc.tensor.matmul(psum_ap, l, r, start=(i == 0), stop=(i == n - 1))
    t = p.Tn.sem.inc()
    last.then_inc(p.Tn.sem.h, 1)
    return t


class PsumPool:

    def __init__(self, p, tag, n, shape, dtype=F32, es=None):
        self.tiles = [p.ps(f"{tag}{i}", shape, dtype, es) for i in range(n)]
        self.evac = [None] * n
        self.k = 0
        self.n = n

    def get(self):
        i = self.k % self.n
        self.k += 1
        return self.tiles[i], self.evac[i]

    def done(self, sem, target):
        i = (self.k - 1) % self.n
        self.evac[i] = (sem, target)


def proj_fm(p, w_chunks, src_fm, out_chunks, pool, waits=(),
            bias_cols=None, nslice=512, col0=PAD, src_col0=PAD, evac=None):
    nc = p.nc
    for sem, n in waits:
        p.Tn.wait(sem, n)
    t_last = 0
    for oc in range(len(out_chunks) if out_chunks else (w_chunks[0].shape[1] // 128)):
        for ts in range(T // nslice):
            pk, war = pool.get()
            if war is not None:
                p.Tn.wait(war[0], war[1])
            t_mm = mm_group(p, pk[:, :], [
                (w_chunks[ic][:, oc * 128:(oc + 1) * 128],
                 src_fm[ic][:, src_col0 + ts * nslice:src_col0 + (ts + 1) * nslice])
                for ic in range(len(w_chunks))])
            p.V.wait(p.Tn.sem, t_mm)
            if evac is not None:
                t_last = evac(oc, ts, pk)
            elif bias_cols is not None:
                t_last = p.V("tensor_scalar_add",
                             out_chunks[oc][:, col0 + ts * nslice:col0 + (ts + 1) * nslice],
                             pk[:, :], bias_cols[:, oc:oc + 1])
            else:
                t_last = p.V("tensor_copy",
                             out_chunks[oc][:, col0 + ts * nslice:col0 + (ts + 1) * nslice],
                             pk[:, :])
            pool.done(p.V.sem, t_last)
    return t_last



class Dance:

    def __init__(self, p, tag, maxw=32, es=None):
        self.p = p
        self.ps = p.ps(f"{tag}_ps", [128, 128], F32, es)
        self.st = p.sb(f"{tag}_st", [32, 128], F32, es)
        self.dr = p.dram(f"{tag}_dr", [32 * 128], F32)
        self.sem = p.sem(f"{tag}_dma")

    def rows_from_cols(self, cols_ap, out_row, w, ident32, waits=()):
        p = self.p
        for sem, n in waits:
            p.Tn.wait(sem, n)
        t_tr = p.Tn.sem.inc()
        p.nc.tensor.transpose(self.ps[0:w, 0:128], cols_ap, ident32).then_inc(p.Tn.sem.h, 1)
        p.V.wait(p.Tn.sem, t_tr)
        t_e = p.V("tensor_copy", self.st[0:w, :], self.ps[0:w, 0:128])
        p.qwait("gpsimd", p.V.sem, t_e)
        t1 = p.dma("gpsimd", self.dr[0:w * 128].rearrange("(a b) -> a b", b=128),
                   self.st[0:w, :], self.sem)
        p.qwait("sync", self.sem, t1)
        t2 = p.dma("sync", out_row, self.dr[0:w * 128].rearrange("(o a) -> o a", o=1), self.sem)
        return t2

    def cols_from_row(self, row_ap, out_cols, w, ident32, waits=(), row_sem=None):
        p = self.p
        for sem, n in waits:
            p.qwait("gpsimd", sem, n)
        t1 = p.dma("gpsimd", self.dr[0:w * 128].rearrange("(o a) -> o a", o=1), row_ap, self.sem)
        p.qwait("sync", self.sem, t1)
        t2 = p.dma("sync", self.st[0:w, :],
                   self.dr[0:w * 128].rearrange("(a b) -> a b", b=128), self.sem)
        p.Tn.wait(self.sem, t2)
        t_tr = p.Tn.sem.inc()
        p.nc.tensor.transpose(self.ps[0:128, 0:w], self.st[0:w, 0:128],
                              ident32[0:w, 0:w]).then_inc(p.Tn.sem.h, 1)
        p.V.wait(p.Tn.sem, t_tr)
        t_e = p.V("tensor_copy", out_cols, self.ps[0:128, 0:w])
        return t_e



def barrier(p, *sts):
    for st_ in sts:
        sem, n = st_
        if n <= 0:
            continue
        for eng in (p.V, p.S, p.G, p.Tn):
            eng.wait(sem, n)
        p.qwait("sync", sem, n)


def tn_sees_v(p):
    p.Tn.wait(p.V.sem, p.V.sem.n)


def v_sees_tn(p):
    p.V.wait(p.Tn.sem, p.Tn.sem.n)


def full_barrier(p):
    barrier(p, (p.V.sem, p.V.sem.n), (p.S.sem, p.S.sem.n), (p.Tn.sem, p.Tn.sem.n))


def emit_L1_attn(p, io, weights, consts):
    nc = p.nc
    V, S, G, Tn = p.V, p.S, p.G, p.Tn
    ident32 = consts["ident32_sb"]
    ident16 = consts["ident16_sb"]
    ones16 = consts["ones16_sb"]
    ones32 = consts["ones32_sb"]
    CW = PAD + T

    qn_tm_d = p.dram("l1a_qn_tm", [T, D], BF16)
    kn_tm_d = p.dram("l1a_kn_tm", [T, D], BF16)

    scr_fm = [p.sb(f"l1a_scr{c}", [128, CW], BF16) for c in range(8)]
    srow = p.sb("l1a_srow", [1, D], F32)
    vs16 = p.sb("l1a_vs16", [1, T], BF16)
    tgrow = p.sb("l1a_tgrow", [1, D], F32)
    tg16 = p.sb("l1a_tg16", [1, D], BF16)
    tb = p.sb("l1a_tb", [128, D], F32)
    kse = p.sb("l1a_kse", [128, NT], BF16)

    sw = p.sem("l1a_sw")
    sld = p.sem("l1a_sld")
    sst = p.sem("l1a_sst")
    sfm = p.sem("l1a_sfm")

    es1 = contextlib.ExitStack()
    ln_fm = [p.sb(f"l1a_lnfm{c}", [128, CW], BF16, es1) for c in range(8)]
    w16 = p.sb("l1a_w16", [128, 8 * D], BF16, es1)
    bcols = p.sb("l1a_bcols_sb", [128, 24], F32, es1)
    gam = p.sb("l1a_gam_sb", [1, 1], F32, es1)
    sqs = p.sb("l1a_sqs", [128, T], BF16, es1)
    nrm = p.sb("l1a_nrm", [128, 8], F32, es1)
    ksf = p.sb("l1a_ksf", [128, NT], F32, es1)
    tmt = [p.sb(f"l1a_tmt{i}", [128, D], BF16, es1) for i in range(2)]
    scols = p.sb("l1a_scols", [128, 8], F32, es1)
    gcol = p.sb("l1a_gcol", [128, 1], F32, es1)
    pool = PsumPool(p, "l1a_pp", 2, [128, 512], es=es1)
    trp = PsumPool(p, "l1a_tr", 2, [128, 128], BF16, es=es1)
    ps_s = p.ps("l1a_ps_s", [1, 1024], F32, es1)
    dance = Dance(p, "l1a_dance", es=es1)

    p.qwait("sync", io["pre_done"][0], io["pre_done"][1])
    t_fm = emit_fm_load(p, io["ln16_tm"], ln_fm, sfm)
    t_b = p.dma("sync", bcols[:, :], weights["l1a_bcols"][:, :], sld)
    t_g = p.dma("sync", gam[:, :], weights["l1a_gamma"][:, :], sld)
    V.wait(sld, t_b)

    def l2norm_phase(which, bc0, out_tiles_d, s_accum):
        wch, t_w = load_weight16(p, weights[which], w16, sw, D, D)
        t_proj = proj_fm(p, wch, ln_fm, scr_fm, pool,
                         waits=[(sw, t_w), (sfm, t_fm)],
                         bias_cols=bcols[:, bc0:bc0 + 8])
        for c in range(8):
            S.wait(V.sem, t_proj)
            S("activation", sqs[:, :], scr_fm[c][:, PAD:PAD + T], AF.Square,
              accum_out=nrm[:, c:c + 1])
        S("activation", nrm[:, 0:8], nrm[:, 0:8], AF.Ln)
        t_rs = S("activation", nrm[:, 0:8], nrm[:, 0:8], AF.Exp, scale=-0.5)
        for c in range(8):
            V.wait(S.sem, t_rs)
            V("tensor_scalar_mul", scr_fm[c][:, PAD:PAD + T], scr_fm[c][:, PAD:PAD + T],
              nrm[:, c:c + 1])
        t_nrm = V.sem.n

        tile_st = {}
        s_last = None
        for j in range(NT):
            i = j % 2
            if j - 2 in tile_st:
                V.wait(sst, tile_st[j - 2])
            for c in range(8):
                pk, war = trp.get()
                if war is not None:
                    Tn.wait(war[0], war[1])
                Tn.wait(V.sem, t_nrm)
                t_tr = Tn.sem.inc()
                nc.tensor.transpose(pk[:, :],
                                    scr_fm[c][:, PAD + j * 128:PAD + (j + 1) * 128],
                                    ident16).then_inc(Tn.sem.h, 1)
                V.wait(Tn.sem, t_tr)
                t_e = V("tensor_copy", tmt[i][:, c * 128:(c + 1) * 128], pk[:, :])
                trp.done(V.sem, t_e)
            t_tile = V.sem.n
            if which == "l1a_wkT":
                V("reduce_sum", ksf[:, j:j + 1], tmt[i][:, :], axis=AX.X)
                V("tensor_scalar_add", kse[:, j:j + 1], ksf[:, j:j + 1], 1e-6)
            if s_accum:
                Tn.wait(V.sem, t_tile)
                for h2 in range(2):
                    s_last = nc.tensor.matmul(
                        ps_s[0:1, h2 * 512:(h2 + 1) * 512], kse[:, j:j + 1],
                        tmt[i][:, h2 * 512:(h2 + 1) * 512],
                        start=(j == 0), stop=(j == NT - 1), skip_group_check=True)
            p.qwait("gpsimd", V.sem, t_tile)
            p.dma("gpsimd", out_tiles_d[j * 128:(j + 1) * 128, :], tmt[i][:, :], sst)
            tile_st[j] = sst.n
        if s_accum:
            t_s = Tn.sem.inc()
            s_last.then_inc(Tn.sem.h, 1)
            return t_s
        return None

    l2norm_phase("l1a_wkT", 8, kn_tm_d, s_accum=False)
    t_sdone = l2norm_phase("l1a_wqT", 0, qn_tm_d, s_accum=True)
    V.wait(Tn.sem, t_sdone)
    t_srow = V("tensor_copy", srow[0:1, :], ps_s[0:1, :])

    wch, t_w = load_weight16(p, weights["l1a_wvT"], w16, sw, D, D)
    t_vproj = proj_fm(p, wch, ln_fm, scr_fm, pool, waits=[(sw, t_w)],
                      bias_cols=bcols[:, 16:24])
    v_fm = scr_fm
    for ts in range(8):
        pk, war = pool.get()
        if war is not None:
            Tn.wait(war[0], war[1])
        Tn.wait(V.sem, t_vproj)
        t_mm = mm_group(p, pk[0:1, :], [
            (ones16[:, 0:1], v_fm[c][:, PAD + ts * 512:PAD + (ts + 1) * 512])
            for c in range(8)])
        V.wait(Tn.sem, t_mm)
        t_e = V("tensor_copy", vs16[0:1, ts * 512:(ts + 1) * 512], pk[0:1, :])
        pool.done(V.sem, t_e)
    t_vs = V.sem.n

    dance.cols_from_row(srow[0:1, :], scols[:, :], 8, ident32,
                        waits=[(V.sem, t_srow)])
    V("tensor_scalar_add", scols[:, :], scols[:, :], 1024.0)
    V("reciprocal", scols[:, :], scols[:, :])
    pk, war = pool.get()
    if war is not None:
        Tn.wait(war[0], war[1])
    Tn.wait(sld, t_g)
    t_mm = mm_group(p, pk[0:128, 0:1], [(ones32[0:1, 0:128], gam[0:1, 0:1])])
    V.wait(Tn.sem, t_mm)
    t_e = V("tensor_copy", gcol[:, :], pk[0:128, 0:1])
    pool.done(V.sem, t_e)
    t_tg = V("tensor_scalar_mul", scols[:, :], scols[:, :], gcol[:, 0:1])
    t_tgrow = dance.rows_from_cols(scols[:, :], tgrow[0:1, :], 8, ident32,
                                   waits=[(V.sem, t_tg)])
    V.wait(dance.sem, t_tgrow)
    t_tg16 = V("tensor_copy", tg16[0:1, :], tgrow[0:1, :])
    for h2 in range(2):
        pk, war = pool.get()
        if war is not None:
            Tn.wait(war[0], war[1])
        Tn.wait(V.sem, t_tg16)
        t_mm = mm_group(p, pk[:, :], [(ones32[0:1, 0:128],
                                       tgrow[0:1, h2 * 512:(h2 + 1) * 512])])
        V.wait(Tn.sem, t_mm)
        t_e = V("tensor_copy", tb[:, h2 * 512:(h2 + 1) * 512], pk[:, :])
        pool.done(V.sem, t_e)
    t_tb = V.sem.n

    full_barrier(p)
    es1.close()

    a16 = [p.sb(f"l1a_a16_{c}", [128, D], BF16) for c in range(8)]
    es2 = contextlib.ExitStack()
    apool = [p.ps(f"l1a_ap{i}", [128, 512], F32, es2) for i in range(8)]
    qt = [p.sb(f"l1a_qt{i}", [128, 512], BF16, es2) for i in range(2)]
    kt = [p.sb(f"l1a_kt{i}", [128, D], BF16, es2) for i in range(2)]
    qk_ld = p.sem("l1a_qkld")
    t_aev = 0
    for half in range(2):
        amm = {}
        for j in range(NT):
            i = j % 2
            if j - 2 in amm:
                nc.sync.wait_ge(Tn.sem.h, amm[j - 2])
            if j == 0 and half == 1:
                Tn.wait(V.sem, t_aev)
            t_q = p.dma("sync", kt[i][:, :], kn_tm_d[j * 128:(j + 1) * 128, :], qk_ld)
            t_q = p.dma("sync", qt[i][:, :],
                        qn_tm_d[j * 128:(j + 1) * 128, half * 512:(half + 1) * 512], qk_ld)
            Tn.wait(qk_ld, t_q)
            last = None
            for fb in range(8):
                last = nc.tensor.matmul(apool[fb][:, :], kt[i][:, fb * 128:(fb + 1) * 128],
                                        qt[i][:, :], start=(j == 0), stop=(j == NT - 1),
                                        skip_group_check=True)
            amm[j] = Tn.sem.inc()
            last.then_inc(Tn.sem.h, 1)
        V.wait(Tn.sem, amm[NT - 1])
        for fb in range(8):
            V("tensor_mul", a16[fb][:, half * 512:(half + 1) * 512],
              apool[fb][:, :], tb[:, half * 512:(half + 1) * 512])
        t_aev = V.sem.n
    full_barrier(p)
    es2.close()

    es3 = contextlib.ExitStack()
    opool = PsumPool(p, "l1a_op", 4, [128, 512], es=es3)
    lnt = [p.sb(f"l1a_lnt{i}", [128, D], F32, es3) for i in range(2)]
    sub = [p.sb(f"l1a_sub{i}", [128, D], F32, es3) for i in range(2)]
    sub_done = {}
    st_done = {}
    for j in range(NT):
        i = j % 2
        if j - 2 in st_done:
            V.wait(sst, st_done[j - 2])
        if j - 2 in sub_done:
            p.qwait("sync", V.sem, sub_done[j - 2])
        t_ln = p.dma("sync", lnt[i][:, :], io["ln32_tm"][j * 128:(j + 1) * 128, :], sld)
        t_e = None
        for h2 in range(2):
            pk, war = opool.get()
            if war is not None:
                Tn.wait(war[0], war[1])
            pairs = [(v_fm[fb][:, PAD + j * 128:PAD + (j + 1) * 128],
                      a16[fb][:, h2 * 512:(h2 + 1) * 512]) for fb in range(8)]
            pairs.append((vs16[0:1, j * 128:(j + 1) * 128],
                          tg16[0:1, h2 * 512:(h2 + 1) * 512]))
            t_mm = mm_group(p, pk[:, :], pairs)
            V.wait(Tn.sem, t_mm)
            V.wait(sld, t_ln)
            t_e = V("scalar_tensor_tensor", sub[i][:, h2 * 512:(h2 + 1) * 512],
                    pk[:, :], 1.0, lnt[i][:, h2 * 512:(h2 + 1) * 512],
                    ALU.bypass, ALU.add)
            opool.done(V.sem, t_e)
        sub_done[j] = t_e
        p.qwait("gpsimd", V.sem, t_e)
        p.dma("gpsimd", io["sub_tm"][j * 128:(j + 1) * 128, :], sub[i][:, :], sst)
        st_done[j] = sst.n
    full_barrier(p)
    es3.close()
    io["sub_done"] = (sst, sst.n)


def host_prep_L1_attn(inputs):
    w = {}
    w["l1a_wqT"] = np.ascontiguousarray(inputs["L1_attn_Wq"].T)
    w["l1a_wkT"] = np.ascontiguousarray(inputs["L1_attn_Wk"].T)
    w["l1a_wvT"] = np.ascontiguousarray(inputs["L1_attn_Wv"].T)
    b = np.stack([inputs["L1_attn_bq"], inputs["L1_attn_bk"], inputs["L1_attn_bv"]])
    w["l1a_bcols"] = np.ascontiguousarray(
        b.reshape(3, 8, 128).transpose(2, 0, 1).reshape(128, 24)[:, [0, 1, 2, 3, 4, 5, 6, 7,
                                                                     8, 9, 10, 11, 12, 13, 14, 15,
                                                                     16, 17, 18, 19, 20, 21, 22, 23]])
    w["l1a_gamma"] = inputs["L1_attn_gamma"].reshape(1, 1)
    return w


L1A_WEIGHT_SPECS = {
    "l1a_wqT": ([D, D], F32),
    "l1a_wkT": ([D, D], F32),
    "l1a_wvT": ([D, D], F32),
    "l1a_bcols": ([128, 24], F32),
    "l1a_gamma": ([1, 1], F32),
}


def build_stage(sub_emit, weight_specs, rms2=False, need_ln32=False):
    p = Prog()
    x_in = p.inp("x", [T, D], F32)
    x_out = p.out("x_out", [T, D], F32)
    weights = {n: p.inp(n, sh, dt) for n, (sh, dt) in weight_specs.items()}
    p.prealloc_cbuf()
    consts = {}
    for n, (sh, dt) in CONST_SPECS.items():
        ap = p.inp(n, sh, dt)
        sb = p.sb(n + "_sb", sh, dt)
        consts[n + "_dram"] = ap
        consts[n.replace("c_", "") + "_sb"] = sb[:, :]
    io = {}
    io["ln16_tm"] = p.dram("ln16_tm", [T, D], BF16)
    io["ln32_tm"] = (p.out if DBG else p.dram)("ln32_tm", [T, D], F32)
    io["sub_tm"] = (p.out if DBG else p.dram)("sub_tm", [T, D], F32)

    sc = p.sem("consts")
    for n in CONST_SPECS:
        p.dma("sync", consts[n.replace("c_", "") + "_sb"][:, :], consts[n + "_dram"][:, :], sc)
    p.qwait("sync", sc, sc.n)

    ldx, sg = p.sem("ldx"), p.sem("sg")
    es_pre = contextlib.ExitStack()
    ln = LnPass(p, "pre", es=es_pre, need32=need_ln32, needadd=False)
    t_pre = ln.emit(ldx, sg, x_in, out16=io["ln16_tm"],
                    out32=(io["ln32_tm"] if need_ln32 else None), rms2=rms2)
    io["pre_done"] = (sg, t_pre)
    barrier(p, io["pre_done"])
    es_pre.close()

    sub_emit(p, io, weights, consts)

    barrier(p, io["sub_done"])
    es_post = contextlib.ExitStack()
    ln2 = LnPass(p, "post", es=es_post, need16=False)
    t_post = ln2.emit(ldx, sg, io["sub_tm"], out32=x_out, add=x_in)
    p.qwait("sync", sg, t_post)
    es_post.close()
    return p


def build_stage_L1_attn():
    return build_stage(emit_L1_attn, L1A_WEIGHT_SPECS, need_ln32=True)



def attn_core_head(p, Kaug, Qaug, vslice, o32, pexp, psB, psO, n_aug,
                   kt_lists, pre_exp=None, exp_scale=1.0, exp_bias_ap=None):
    nc = p.nc
    V, S, Tn = p.V, p.S, p.Tn
    NP = len(pexp)
    entries = []
    for qc in range(8):
        for idx, e in enumerate(kt_lists[qc]):
            entries.append((qc, idx, e, idx == 0, idx == len(kt_lists[qc]) - 1))
    n = len(entries)
    exp_done = {}
    av_of_pexp = {}
    po = None
    t_o = 0
    for i in range(n + 1):
        if i < n:
            qc, idx, (kt, mask_ap), first, lastk = entries[i]
            pb, warB = psB.get()
            if warB is not None:
                Tn.wait(warB[0], warB[1])
            last = nc.tensor.matmul(pb[:, :], Kaug[0:n_aug, kt * 128:(kt + 1) * 128],
                                    Qaug[0:n_aug, qc * 512:(qc + 1) * 512],
                                    start=True, stop=True)
            t_b = Tn.sem.inc()
            last.then_inc(Tn.sem.h, 1)
            t_fix = None
            if pre_exp is not None:
                V.wait(Tn.sem, t_b)
                t_fix = pre_exp(pb, qc, kt)
            if mask_ap is not None:
                if t_fix is None:
                    V.wait(Tn.sem, t_b)
                t_fix = V("tensor_add", pb[:, :], pb[:, :], mask_ap)
            b = i % NP
            S.wait(Tn.sem, t_b)
            if t_fix is not None:
                S.wait(V.sem, t_fix)
            if b in av_of_pexp:
                S.wait(Tn.sem, av_of_pexp[b])
            if exp_bias_ap is not None:
                t_e = S("activation", pexp[b][:, :], pb[:, :], AF.Exp,
                        scale=float(exp_scale), bias=exp_bias_ap)
            else:
                t_e = S("activation", pexp[b][:, :], pb[:, :], AF.Exp,
                        scale=float(exp_scale))
            psB.done(p.S.sem, t_e)
            exp_done[i] = t_e
        if i >= 1:
            qc, idx, (kt, _), first, lastk = entries[i - 1]
            if first:
                po, warO = psO.get()
                if warO is not None:
                    Tn.wait(warO[0], warO[1])
            b1 = (i - 1) % NP
            Tn.wait(S.sem, exp_done[i - 1])
            last = nc.tensor.matmul(po[0:65, :], vslice(kt), pexp[b1][:, :],
                                    start=first, stop=lastk,
                                    skip_group_check=True)
            t_av = Tn.sem.inc()
            last.then_inc(Tn.sem.h, 1)
            av_of_pexp[b1] = t_av
            if lastk:
                V.wait(Tn.sem, t_av)
                t_o = V("tensor_copy", o32[0:65, qc * 512:(qc + 1) * 512],
                        po[0:65, :])
                psO.done(p.V.sem, t_o)
    return t_o


def attn_divide_head(p, o32, o16h, dance, lrow, lrec, lcols, pool, ident32,
                     ones32, t_o):
    V, Tn = p.V, p.Tn
    t_l = V("tensor_copy", lrow[0:1, :], o32[64:65, :])
    dance.cols_from_row(lrow[0:1, :], lcols[:, :], 32, ident32,
                        waits=[(V.sem, t_l)])
    V("reciprocal", lcols[:, :], lcols[:, :])
    t_r = p.V.sem.n
    t_row = dance.rows_from_cols(lcols[:, :], lrec[0:1, :], 32, ident32,
                                 waits=[(V.sem, t_r)])
    t_e = 0
    for ts in range(8):
        pk, war = pool.get()
        if war is not None:
            Tn.wait(war[0], war[1])
        Tn.wait(dance.sem, t_row)
        t_mm = mm_group(p, pk[0:64, :], [(ones32[0:1, 0:64],
                                          lrec[0:1, ts * 512:(ts + 1) * 512])])
        V.wait(Tn.sem, t_mm)
        t_e = V("tensor_mul", o16h[0:64, ts * 512:(ts + 1) * 512],
                o32[0:64, ts * 512:(ts + 1) * 512], pk[0:64, :])
        pool.done(p.V.sem, t_e)
    return t_e


def attn_out_proj(p, io, o_fm_d, wo_dram, bias_row_dram, sld, sst, n_of=8, tag="aop"):
    nc = p.nc
    V, Tn = p.V, p.Tn
    es = contextlib.ExitStack()
    ofm = [p.sb(f"{tag}_ofm{c}", [128, T], BF16, es) for c in range(n_of)]
    wo16 = p.sb(f"{tag}_wo16", [128, n_of * D], BF16, es)
    brow = p.sb(f"{tag}_brow", [1, D], F32, es)
    brow16 = p.sb(f"{tag}_brow16", [1, D], BF16, es)
    sub = [p.sb(f"{tag}_sub{i}", [128, D], F32, es) for i in range(2)]
    opool = PsumPool(p, f"{tag}_op", 4, [128, 512], es=es)
    sw = p.sem(f"{tag}_sw")
    for c in range(n_of):
        p.dma("sync", ofm[c][:, :], o_fm_d[c * 128:(c + 1) * 128, :], sld)
    t_of = sld.n
    wch, t_w = load_weight16(p, wo_dram, wo16, sw, n_of * 128, D)
    if bias_row_dram is not None:
        t_b = p.dma("sync", brow[0:1, :], bias_row_dram[:, :], sld)
        V.wait(sld, t_b)
        t_b16 = V("tensor_copy", brow16[0:1, :], brow[0:1, :])
    sub_done = {}
    st_done = {}
    for j in range(NT):
        i = j % 2
        if j - 2 in st_done:
            V.wait(sst, st_done[j - 2])
        t_e = None
        for ns in range(2):
            pk, war = opool.get()
            if war is not None:
                Tn.wait(war[0], war[1])
            Tn.wait(sld, t_of)
            Tn.wait(sw, t_w)
            pairs = [(ofm[c][:, j * 128:(j + 1) * 128],
                      wo16[:, (c * 2 + ns) * 512:(c * 2 + ns + 1) * 512])
                     for c in range(n_of)]
            if bias_row_dram is not None:
                Tn.wait(V.sem, t_b16)
                pairs.append((p._ones16_row[0:1, 0:128],
                              brow16[0:1, ns * 512:(ns + 1) * 512]))
            t_mm = mm_group(p, pk[:, :], pairs)
            V.wait(Tn.sem, t_mm)
            t_e = V("tensor_copy", sub[i][:, ns * 512:(ns + 1) * 512], pk[:, :])
            opool.done(p.V.sem, t_e)
        sub_done[j] = t_e
        p.qwait("gpsimd", V.sem, t_e)
        p.dma("gpsimd", io["sub_tm"][j * 128:(j + 1) * 128, :], sub[i][:, :], sst)
        st_done[j] = sst.n
    full_barrier(p)
    es.close()
    return sst.n



def emit_L3_attn(p, io, weights, consts):
    nc = p.nc
    V, S, G, Tn = p.V, p.S, p.G, p.Tn
    ident16 = consts["ident16_sb"]
    ident32 = consts["ident32_sb"]
    ones16 = consts["ones16_sb"]
    ones32 = consts["ones32_sb"]
    p._ones16_row = ones16
    CW = PAD + T

    o_fm_d = (p.out if DBG else p.dram)("l3a_ofm", [D, T], BF16)
    if DBG:
        dbg_qaug = p.out("dbg_qaug", [65, T], BF16)
        dbg_kaug = p.out("dbg_kaug", [65, T], BF16)
        dbg_o32 = p.out("dbg_o32", [65, T], F32)
        dbg_vt = p.out("dbg_vt", [128, NT * 130], BF16)
        dbg_nrow = p.out("dbg_nrow", [1, T], F32)
        dbg_kmax = p.out("dbg_kmax", [1, 1], F32)
    sld = p.sem("l3a_sld")
    sst = p.sem("l3a_sst")
    sfm = p.sem("l3a_sfm")
    sw = p.sem("l3a_sw")

    es1 = contextlib.ExitStack()
    ln_fm = [p.sb(f"l3a_lnfm{c}", [128, CW], BF16, es1) for c in range(8)]
    w16 = p.sb("l3a_w16", [128, 3 * 8 * 128], BF16, es1)
    nscol = p.sb("l3a_ns_sb", [128, 8], F32, es1)
    qp = p.sb("l3a_qp", [128, T], BF16, es1)
    Qaug = [p.sb(f"l3a_qaug{i}", [65, T], BF16, es1) for i in range(2)]
    Kaug = [p.sb(f"l3a_kaug{i}", [65, T], BF16, es1) for i in range(2)]
    Vt = p.sb("l3a_vt", [128, NT * 130], BF16, es1)
    sqs = p.sb("l3a_sqs", [64, T], BF16, es1)
    nrow = p.sb("l3a_nrow", [1, T], F32, es1)
    kmaxc = p.sb("l3a_kmaxc", [1, 1], F32, es1)
    o32 = p.sb("l3a_o32", [65, T], F32, es1)
    o16h = p.sb("l3a_o16h", [64, T], BF16, es1)
    lrow = p.sb("l3a_lrow", [1, T], F32, es1)
    lrec = p.sb("l3a_lrec", [1, T], F32, es1)
    lcols = p.sb("l3a_lcols", [128, 32], F32, es1)
    pexp = [p.sb(f"l3a_pexp{i}", [128, 512], BF16, es1) for i in range(3)]

    pool = PsumPool(p, "l3a_pp", 2, [128, 512], es=es1)
    trp = PsumPool(p, "l3a_tr", 2, [128, 128], BF16, es=es1)
    psB = PsumPool(p, "l3a_psB", 3, [128, 512], es=es1)
    psO = pool
    dance = Dance(p, "l3a_dance", es=es1)

    p.qwait("sync", io["pre_done"][0], io["pre_done"][1])
    t_fm = emit_fm_load(p, io["ln16_tm"], ln_fm, sfm)
    t_ns = p.dma("sync", nscol[:, :], weights["l3a_ns"][:, :], sld)
    V.wait(sld, t_ns)

    kt_lists = {qc: [(kt, None) for kt in range(NT)] for qc in range(8)}
    WNAMES = ("l3a_wq", "l3a_wk", "l3a_wv")

    for pair in range(8):
        p.qwait("gpsimd", Tn.sem, Tn.sem.n)
        wch = {}
        for wi, wname in enumerate(WNAMES):
            for c in range(8):
                ap = w16[:, (wi * 8 + c) * 128:(wi * 8 + c + 1) * 128]
                p.dma("gpsimd", ap,
                      weights[wname][c * 128:(c + 1) * 128,
                                     pair * 128:(pair + 1) * 128], sw)
            wch[wname] = [w16[:, (wi * 8 + c) * 128:(wi * 8 + c + 1) * 128]
                          for c in range(8)]
        t_w = sw.n
        V.wait(sw, t_w)
        t_scale = 0
        for wi, wname in enumerate(WNAMES):
            qscale = 0.125 if wname == "l3a_wq" else 1.0
            for c in range(8):
                t_scale = V("tensor_scalar", wch[wname][c], wch[wname][c],
                            nscol[:, c:c + 1], qscale, ALU.mult, ALU.mult)

        def pair_proj(wname, split_dests):
            t_last = 0
            for ts in range(8):
                pk, war = pool.get()
                if war is not None:
                    Tn.wait(war[0], war[1])
                Tn.wait(V.sem, t_scale)
                Tn.wait(sfm, t_fm)
                t_mm = mm_group(p, pk[:, :], [
                    (wch[wname][ic], ln_fm[ic][:, PAD + ts * 512:PAD + (ts + 1) * 512])
                    for ic in range(8)])
                V.wait(Tn.sem, t_mm)
                if split_dests[1] is None:
                    t_last = V("tensor_copy", split_dests[0][:, ts * 512:(ts + 1) * 512],
                               pk[:, :])
                else:
                    V("tensor_copy", split_dests[0][0:64, ts * 512:(ts + 1) * 512],
                      pk[0:64, :])
                    t_last = V("tensor_copy",
                               split_dests[1][0:64, ts * 512:(ts + 1) * 512],
                               pk[64:128, :])
                pool.done(p.V.sem, t_last)
            return t_last

        pair_proj("l3a_wq", (Qaug[0], Qaug[1]))
        pair_proj("l3a_wk", (Kaug[0], Kaug[1]))
        t_v = pair_proj("l3a_wv", (qp, None))

        if pair == 0:
            V("memset", Vt[:, :], 1.0)
        for j in range(NT):
            pk, war = trp.get()
            if war is not None:
                Tn.wait(war[0], war[1])
            Tn.wait(V.sem, t_v)
            t_tr = Tn.sem.inc()
            nc.tensor.transpose(pk[:, :], qp[:, j * 128:(j + 1) * 128],
                                ident16).then_inc(Tn.sem.h, 1)
            V.wait(Tn.sem, t_tr)
            V("tensor_copy", Vt[:, j * 130:j * 130 + 64], pk[:, 0:64])
            t_e = V("tensor_copy", Vt[:, j * 130 + 65:j * 130 + 129], pk[:, 64:128])
            trp.done(p.V.sem, t_e)

        for hh in range(2):
            QA, KA = Qaug[hh], Kaug[hh]
            V("memset", KA[64:65, :], 1.0)

            def norm_row(A):
                S.wait(V.sem, V.sem.n)
                t_sq = S("activation", sqs[0:64, :], A[0:64, :], AF.Square)
                for ts in range(8):
                    pk, war = pool.get()
                    if war is not None:
                        Tn.wait(war[0], war[1])
                    Tn.wait(S.sem, t_sq)
                    t_mm = mm_group(p, pk[0:1, :], [
                        (ones16[0:64, 0:1], sqs[0:64, ts * 512:(ts + 1) * 512])])
                    V.wait(Tn.sem, t_mm)
                    t_l = V("tensor_copy", nrow[0:1, ts * 512:(ts + 1) * 512],
                            pk[0:1, :])
                    pool.done(p.V.sem, t_l)
                S.wait(V.sem, t_l)
                return S("activation", nrow[0:1, :], nrow[0:1, :], AF.Sqrt)

            t_kn = norm_row(KA)
            V.wait(S.sem, t_kn)
            t_km = V("reduce_max", kmaxc[0:1, 0:1], nrow[0:1, :], axis=AX.X)
            t_qn = norm_row(QA)
            V.wait(S.sem, t_qn)
            t_bnd = V("tensor_scalar", QA[64:65, :], nrow[0:1, :], kmaxc[0:1, 0:1],
                      -1.0, ALU.mult, ALU.mult)
            tn_sees_v(p)
            h = pair * 2 + hh
            attn_core_head(p, KA, QA,
                           lambda kt, hh=hh: Vt[:, kt * 130 + hh * 65:kt * 130 + hh * 65 + 65],
                           o32, pexp, psB, psO, 65, kt_lists)
            if DBG and h == 0:
                full_barrier(p)
                p.dma("gpsimd", dbg_nrow[:, :], nrow[0:1, :], sst)
                p.dma("gpsimd", dbg_kmax[:, :], kmaxc[0:1, :], sst)
                p.dma("gpsimd", dbg_qaug[:, :], QA[0:65, :], sst)
                p.dma("gpsimd", dbg_kaug[:, :], KA[0:65, :], sst)
                p.dma("gpsimd", dbg_o32[:, :], o32[0:65, :], sst)
                p.dma("gpsimd", dbg_vt[:, :], Vt[:, :], sst)
            V.wait(sst, sst.n)
            t_d = attn_divide_head(p, o32, o16h, dance, lrow, lrec, lcols, pool,
                                   ident32, ones32, 0)
            p.qwait("gpsimd", V.sem, t_d)
            p.dma("gpsimd", o_fm_d[h * 64:(h + 1) * 64, :], o16h[0:64, :], sst)

    full_barrier(p)
    es1.close()

    p.qwait("sync", sst, sst.n)
    attn_out_proj(p, io, o_fm_d, weights["l3a_wo"], weights["l3a_bout"], sld, sst, tag="l3aop")
    io["sub_done"] = (sst, sst.n)


def host_prep_L3_attn(inputs):
    w = {}
    w["l3a_wq"] = np.ascontiguousarray(inputs["L3_attn_wq"].reshape(D, D))
    w["l3a_wk"] = np.ascontiguousarray(inputs["L3_attn_wk"].reshape(D, D))
    w["l3a_wv"] = np.ascontiguousarray(inputs["L3_attn_wv"].reshape(D, D))
    w["l3a_wo"] = np.ascontiguousarray(inputs["L3_attn_w_out"].reshape(D, D))
    w["l3a_ns"] = np.ascontiguousarray(inputs["L3_attn_norm_scale"].reshape(8, 128).T)
    w["l3a_bout"] = np.ascontiguousarray(inputs["L3_attn_b_out"].reshape(1, D))
    return w


L3A_WEIGHT_SPECS = {
    "l3a_wq": ([D, D], F32),
    "l3a_wk": ([D, D], F32),
    "l3a_wv": ([D, D], F32),
    "l3a_wo": ([D, D], F32),
    "l3a_ns": ([128, 8], F32),
    "l3a_bout": ([1, D], F32),
}


def build_stage_L3_attn():
    return build_stage(emit_L3_attn, L3A_WEIGHT_SPECS, rms2=True)



def emit_L0_attn(p, io, weights, consts):
    nc = p.nc
    V, S, G, Tn = p.V, p.S, p.G, p.Tn
    ident16 = consts["ident16_sb"]
    ident32 = consts["ident32_sb"]
    ones16 = consts["ones16_sb"]
    ones32 = consts["ones32_sb"]
    p._ones16_row = ones16
    CW = PAD + T

    o_fm_d = p.dram("l0a_ofm", [D, T], BF16)
    sld = p.sem("l0a_sld")
    sst = p.sem("l0a_sst")
    sfm = p.sem("l0a_sfm")
    sw = p.sem("l0a_sw")

    es1 = contextlib.ExitStack()
    ln_fm = [p.sb(f"l0a_lnfm{c}", [128, CW], BF16, es1) for c in range(8)]
    w16 = p.sb("l0a_w16", [128, 3 * 8 * 128], BF16, es1)
    qp = p.sb("l0a_qp", [128, T], BF16, es1)
    Qaug = [p.sb(f"l0a_qaug{i}", [66, T], BF16, es1) for i in range(2)]
    Kaug = [p.sb(f"l0a_kaug{i}", [66, T], BF16, es1) for i in range(2)]
    Vt = p.sb("l0a_vt", [128, NT * 130], BF16, es1)
    o32 = p.sb("l0a_o32", [65, T], F32, es1)
    o16h = p.sb("l0a_o16h", [64, T], BF16, es1)
    lrec = p.sb("l0a_lrec", [1, T], F32, es1)
    lcols = p.sb("l0a_lcols", [128, 32], F32, es1)
    pexp = [p.sb(f"l0a_pexp{i}", [128, 512], BF16, es1) for i in range(3)]
    mcols = p.sb("l0a_mcols", [128, NT], F32, es1)
    mtmp = p.sb("l0a_mtmp", [128, 1], F32, es1)
    mrow = lrow = p.sb("l0a_lrow2", [1, T], F32, es1)
    dirs16 = p.sb("l0a_dirs16", [8, 16], BF16, es1)
    hs8 = p.sb("l0a_hs8", [1, 16], F32, es1)
    maskAw = p.sb("l0a_maskA_sb", [128, 896], F32, es1)
    maskBw = p.sb("l0a_maskB_sb", [128, 896], F32, es1)
    row16 = p.sb("l0a_row16", [1, T], BF16, es1)
    srow = p.sem("l0a_srow")

    def maskA(idx):
        return maskAw[:, 384 - 128 * idx:896 - 128 * idx]

    def maskB(idx):
        return maskBw[:, 384 - 128 * idx:896 - 128 * idx]

    pool = PsumPool(p, "l0a_pp", 2, [128, 512], es=es1)
    trp = PsumPool(p, "l0a_tr", 2, [128, 128], BF16, es=es1)
    psB = PsumPool(p, "l0a_psB", 3, [128, 512], es=es1)
    psO = pool
    dance = Dance(p, "l0a_dance", es=es1)

    p.qwait("sync", io["pre_done"][0], io["pre_done"][1])
    t_fm = emit_fm_load(p, io["ln16_tm"], ln_fm, sfm)
    t_c = p.dma("sync", maskAw[:, :], weights["l0a_maskA"][:, :], sld)
    t_c = p.dma("sync", maskBw[:, :], weights["l0a_maskB"][:, :], sld)
    t_c = p.dma("gpsimd", dirs16[:, :], weights["l0a_dirsT"][:, :], sld)
    t_c = p.dma("sync", hs8[:, :], weights["l0a_hs"][:, :], sld)
    V.wait(sld, t_c)
    t_hs8 = V("tensor_scalar_mul", hs8[0:1, :], hs8[0:1, :], 8.0)

    t_r16 = V("memset", row16[0:1, :], 1.0)
    p.qwait("gpsimd", V.sem, t_r16)
    p.dma("gpsimd", Kaug[0][65:66, :], row16[0:1, :], srow)
    p.dma("gpsimd", Kaug[1][65:66, :], row16[0:1, :], srow)

    kt_lists = {}
    for qc in range(8):
        lst = []
        for kt in range(4 * (qc + 1)):
            if kt >= 4 * qc:
                lst.append((kt, maskB(kt - 4 * qc)))
            else:
                lst.append((kt, None))
        kt_lists[qc] = lst

    for pair in range(8):
        p.qwait("gpsimd", Tn.sem, Tn.sem.n)
        wch = {}
        for wi in range(3):
            for c in range(8):
                ap = w16[:, (wi * 8 + c) * 128:(wi * 8 + c + 1) * 128]
                p.dma("gpsimd", ap,
                      weights["l0a_wqkv"][c * 128:(c + 1) * 128,
                                          wi * 1024 + pair * 128:wi * 1024 + (pair + 1) * 128],
                      sw)
            wch[wi] = [w16[:, (wi * 8 + c) * 128:(wi * 8 + c + 1) * 128]
                       for c in range(8)]
        t_w = sw.n
        V.wait(sw, t_w)
        t_scale = 0
        for c in range(8):
            t_scale = V("tensor_scalar_mul", wch[1][c], wch[1][c], 0.125)

        def pair_proj(wi, split_dests, wait_t):
            t_last = 0
            for ts in range(8):
                pk, war = pool.get()
                if war is not None:
                    Tn.wait(war[0], war[1])
                Tn.wait(V.sem, wait_t)
                Tn.wait(sfm, t_fm)
                t_mm = mm_group(p, pk[:, :], [
                    (wch[wi][ic], ln_fm[ic][:, PAD + ts * 512:PAD + (ts + 1) * 512])
                    for ic in range(8)])
                V.wait(Tn.sem, t_mm)
                if split_dests[1] is None:
                    t_last = V("tensor_copy", split_dests[0][:, ts * 512:(ts + 1) * 512],
                               pk[:, :])
                else:
                    V("tensor_copy", split_dests[0][0:64, ts * 512:(ts + 1) * 512],
                      pk[0:64, :])
                    t_last = V("tensor_copy",
                               split_dests[1][0:64, ts * 512:(ts + 1) * 512],
                               pk[64:128, :])
                pool.done(p.V.sem, t_last)
            return t_last

        pair_proj(0, (Qaug[0], Qaug[1]), t_scale)
        pair_proj(1, (Kaug[0], Kaug[1]), t_scale)
        t_v = pair_proj(2, (qp, None), t_scale)

        if pair == 0:
            V("memset", Vt[:, :], 1.0)
        for j in range(NT):
            pk, war = trp.get()
            if war is not None:
                Tn.wait(war[0], war[1])
            Tn.wait(V.sem, t_v)
            t_tr = Tn.sem.inc()
            nc.tensor.transpose(pk[:, :], qp[:, j * 128:(j + 1) * 128],
                                ident16).then_inc(Tn.sem.h, 1)
            V.wait(Tn.sem, t_tr)
            V("tensor_copy", Vt[:, j * 130:j * 130 + 64], pk[:, 0:64])
            t_e = V("tensor_copy", Vt[:, j * 130 + 65:j * 130 + 129], pk[:, 64:128])
            trp.done(p.V.sem, t_e)

        for hh in range(2):
            h = 2 * pair + hh
            QA, KA = Qaug[hh], Kaug[hh]
            for (A, dst_mul) in ((QA, None), (KA, None)):
                pass
            for ts in range(8):
                pk, war = pool.get()
                if war is not None:
                    Tn.wait(war[0], war[1])
                Tn.wait(V.sem, max(t_hs8, p.V.sem.n if False else t_hs8))
                t_mm = mm_group(p, pk[0:1, :], [
                    (dirs16[0:8, h:h + 1], QA[0:8, ts * 512:(ts + 1) * 512])])
                V.wait(Tn.sem, t_mm)
                t_q64 = V("tensor_scalar_mul", QA[64:65, ts * 512:(ts + 1) * 512],
                          pk[0:1, :], hs8[0:1, h:h + 1])
                pool.done(p.V.sem, t_q64)
            for ts in range(8):
                pk, war = pool.get()
                if war is not None:
                    Tn.wait(war[0], war[1])
                t_mm = mm_group(p, pk[0:1, :], [
                    (dirs16[0:8, h:h + 1], KA[0:8, ts * 512:(ts + 1) * 512])])
                V.wait(Tn.sem, t_mm)
                t_k64 = V("tensor_copy", KA[64:65, ts * 512:(ts + 1) * 512], pk[0:1, :])
                pool.done(p.V.sem, t_k64)

            v_sees_tn(p)
            t_mc = V("memset", mcols[:, :], NEG)
            tn_sees_v(p)
            for j in range(NT):
                for c in range(j // 4 + 1):
                    pb, war = psB.get()
                    if war is not None:
                        Tn.wait(war[0], war[1])
                    last = nc.tensor.matmul(pb[:, :], QA[0:65, j * 128:(j + 1) * 128],
                                            KA[0:65, c * 512:(c + 1) * 512],
                                            start=True, stop=True)
                    t_b = Tn.sem.inc()
                    last.then_inc(Tn.sem.h, 1)
                    V.wait(Tn.sem, t_b)
                    if c == j // 4:
                        V("tensor_add", pb[:, :], pb[:, :], maskA(j % 4))
                    V("reduce_max", mtmp[:, :], pb[:, :], axis=AX.X)
                    t_m = V("tensor_max", mcols[:, j:j + 1], mcols[:, j:j + 1],
                            mtmp[:, :])
                    psB.done(p.V.sem, t_m)
            dance.rows_from_cols(mcols[:, :], mrow[0:1, :], 32, ident32,
                                 waits=[(V.sem, p.V.sem.n)])
            V.wait(dance.sem, dance.sem.n)
            V.wait(srow, srow.n)
            t_m65 = V("tensor_scalar_mul", row16[0:1, :], mrow[0:1, :], -1.0)
            p.qwait("gpsimd", V.sem, t_m65)
            p.dma("gpsimd", QA[65:66, :], row16[0:1, :], srow)

            V.wait(sst, sst.n)
            tn_sees_v(p)
            Tn.wait(srow, srow.n)
            attn_core_head(p, KA, QA,
                           lambda kt, hh=hh: Vt[:, kt * 130 + hh * 65:kt * 130 + hh * 65 + 65],
                           o32, pexp, psB, psO, 66, kt_lists)
            t_d = attn_divide_head(p, o32, o16h, dance, lrow, lrec, lcols, pool,
                                   ident32, ones32, 0)
            p.qwait("gpsimd", V.sem, t_d)
            p.dma("gpsimd", o_fm_d[h * 64:(h + 1) * 64, :], o16h[0:64, :], sst)

    full_barrier(p)
    es1.close()
    p.qwait("sync", sst, sst.n)
    attn_out_proj(p, io, o_fm_d, weights["l0a_wo"], None, sld, sst, tag="l0aop")
    io["sub_done"] = (sst, sst.n)


def host_prep_L0_attn(inputs):
    w = {}
    w["l0a_wqkv"] = np.ascontiguousarray(inputs["L0_attn_w_qkv"])
    w["l0a_wo"] = np.ascontiguousarray(inputs["L0_attn_w_out"])
    w["l0a_dirsT"] = np.ascontiguousarray(inputs["L0_attn_head_dirs"].T)
    w["l0a_hs"] = np.ascontiguousarray(inputs["L0_attn_head_scales"].reshape(1, 16))
    qp = np.arange(128)[:, None]
    wv = np.arange(896)[None, :]
    w["l0a_maskA"] = np.where(wv > qp + 384, NEG, 0.0).astype(np.float32)
    kp = np.arange(128)[:, None]
    w["l0a_maskB"] = np.where(wv < kp + 384, NEG, 0.0).astype(np.float32)
    return w


L0A_WEIGHT_SPECS = {
    "l0a_wqkv": ([D, 3 * D], F32),
    "l0a_wo": ([D, D], F32),
    "l0a_dirsT": ([8, 16], F32),
    "l0a_hs": ([1, 16], F32),
    "l0a_maskA": ([128, 896], F32),
    "l0a_maskB": ([128, 896], F32),
}


def build_stage_L0_attn():
    return build_stage(emit_L0_attn, L0A_WEIGHT_SPECS)



def make_causal_masks():
    qp = np.arange(128)[:, None]
    wv = np.arange(896)[None, :]
    mA = np.where(wv > qp + 384, NEG, 0.0).astype(np.float32)
    kp = np.arange(128)[:, None]
    mB = np.where(wv < kp + 384, NEG, 0.0).astype(np.float32)
    return mA, mB


def causal_kt_lists(maskB):
    kt_lists = {}
    for qc in range(8):
        lst = []
        for kt in range(4 * (qc + 1)):
            lst.append((kt, maskB(kt - 4 * qc) if kt >= 4 * qc else None))
        kt_lists[qc] = lst
    return kt_lists


def emit_L2_attn(p, io, weights, consts):
    nc = p.nc
    V, S, G, Tn = p.V, p.S, p.G, p.Tn
    ident16 = consts["ident16_sb"]
    ident32 = consts["ident32_sb"]
    ones16 = consts["ones16_sb"]
    ones32 = consts["ones32_sb"]
    p._ones16_row = ones16
    CW = PAD + T
    SC96 = 1.0 / math.sqrt(96.0)

    o_fm_d = (p.out if DBG else p.dram)("l2a_ofm", [D, T], BF16)
    if DBG:
        dbg_qaug = p.out("dbg_qaug", [96, T], BF16)
        dbg_kaug = p.out("dbg_kaug", [96, T], BF16)
        dbg_o32 = p.out("dbg_o32", [65, T], F32)
    sld = p.sem("l2a_sld")
    sst = p.sem("l2a_sst")
    sfm = p.sem("l2a_sfm")
    sw = p.sem("l2a_sw")

    esA = contextlib.ExitStack()
    cq = [p.sb(f"l2a_cq{c}", [128, T], BF16, esA) for c in range(4)]
    ckv = [p.sb(f"l2a_ckv{c}", [128, T], BF16, esA) for c in range(4)]
    kr = p.sb("l2a_kr_sb", [32, T], BF16, esA)
    ctab = p.sb("l2a_ctab_sb", [128, T], BF16, esA)
    stab = p.sb("l2a_stab_sb", [128, T], BF16, esA)
    rotscr = p.sb("l2a_rot", [128, 512], BF16, esA)
    qp = p.sb("l2a_qp", [128, T], BF16, esA)
    maskBw = p.sb("l2a_maskB_sb", [128, 896], F32, esA)
    perm4 = p.sb("l2a_perm4_sb", [128, 128], BF16, esA)
    w16 = p.sb("l2a_w16", [128, 4096], BF16, esA)

    def maskB(idx):
        return maskBw[:, 384 - 128 * idx:896 - 128 * idx]

    pool = PsumPool(p, "l2a_pp", 2, [128, 512], es=esA)
    trp = PsumPool(p, "l2a_tr", 2, [128, 128], BF16, es=esA)
    psB = PsumPool(p, "l2a_psB", 3, [128, 512], es=esA)
    psO = pool
    dance = Dance(p, "l2a_dance", es=esA)

    bias_ap, t_bias = p.const_col(-80.0 * SC96)

    es0 = contextlib.ExitStack()
    ln_fm = [p.sb(f"l2a_lnfm{c}", [128, CW], BF16, es0) for c in range(8)]
    w0 = p.sb("l2a_w0", [128, 8 * 512], BF16, es0)

    p.qwait("sync", io["pre_done"][0], io["pre_done"][1])
    t_fm = emit_fm_load(p, io["ln16_tm"], ln_fm, sfm)
    t_c = p.dma("sync", maskBw[:, :], weights["l2a_maskB"][:, :], sld)
    t_c = p.dma("sync", ctab[:, :], weights["l2a_ctab"][:, :], sld)
    t_c = p.dma("sync", stab[:, :], weights["l2a_stab"][:, :], sld)
    t_c = p.dma("sync", perm4[:, :], weights["l2a_perm4"][:, :], sld)
    p.qwait("sync", sld, t_c)

    def rope32(dst_ap, src_ap, nrows, tab_off):
        t_last = 0
        for ts in range(8):
            sl = slice(ts * 512, (ts + 1) * 512)
            pk, war = psB.get()
            if war is not None:
                Tn.wait(war[0], war[1])
            Tn.wait(V.sem, V.sem.n)
            t_mm = mm_group(p, pk[0:nrows, :], [
                (perm4[0:nrows, 0:nrows], src_ap[0:nrows, sl])])
            V.wait(Tn.sem, t_mm)
            V("tensor_mul", rotscr[0:nrows, :], pk[0:nrows, :],
              stab[tab_off:tab_off + nrows, sl])
            t_ps = p.V.sem.n
            psB.done(p.V.sem, t_ps)
            V("tensor_mul", dst_ap[0:nrows, sl], src_ap[0:nrows, sl],
              ctab[tab_off:tab_off + nrows, sl])
            t_last = V("tensor_add", dst_ap[0:nrows, sl], dst_ap[0:nrows, sl],
                       rotscr[0:nrows, 0:512])
        return t_last

    for wname, dst in (("l2a_wdqT", cq), ("l2a_wdkvT", ckv)):
        wch, t_w = load_weight16(p, weights[wname], w0, sw, D, 512)
        proj_fm(p, wch, ln_fm, dst, pool,
                waits=[(sw, t_w), (sfm, t_fm)], col0=0)
    wch, t_w = load_weight16(p, weights["l2a_wkrT"], w16, sw, D, 32)
    t_krp = 0
    for ts in range(8):
        pk, war = pool.get()
        if war is not None:
            Tn.wait(war[0], war[1])
        Tn.wait(sw, t_w)
        t_mm = mm_group(p, pk[0:32, :], [
            (wch[ic], ln_fm[ic][:, PAD + ts * 512:PAD + (ts + 1) * 512])
            for ic in range(8)])
        V.wait(Tn.sem, t_mm)
        t_krp = V("tensor_copy", qp[0:32, ts * 512:(ts + 1) * 512], pk[0:32, :])
        pool.done(p.V.sem, t_krp)
    rope32(kr, qp, 32, 0)

    t_latents = p.V.sem.n
    full_barrier(p)
    es0.close()

    qr16 = p.sb("l2a_qr16_sb", [128, T], BF16, esA)
    Qaug = [p.sb(f"l2a_qaug{i}", [96, T], BF16, esA) for i in range(2)]
    Kaug = [p.sb(f"l2a_kaug{i}", [96, T], BF16, esA) for i in range(2)]
    Vt = p.sb("l2a_vt", [128, NT * 130], BF16, esA)
    o32 = p.sb("l2a_o32", [65, T], F32, esA)
    o16h = p.sb("l2a_o16h", [64, T], BF16, esA)
    lrow = p.sb("l2a_lrow", [1, T], F32, esA)
    lrec = lrow
    lcols = p.sb("l2a_lcols", [128, 32], F32, esA)
    pexp = [p.sb(f"l2a_pexp{i}", [128, 512], BF16, esA) for i in range(3)]

    kt_lists = causal_kt_lists(maskB)
    for pair in range(8):
        p.qwait("gpsimd", Tn.sem, Tn.sem.n)
        wch = {}
        for wi, wname in enumerate(("l2a_wuqT", "l2a_wukT", "l2a_wuvT")):
            for c in range(4):
                ap = w16[:, (wi * 4 + c) * 128:(wi * 4 + c + 1) * 128]
                p.dma("gpsimd", ap,
                      weights[wname][c * 128:(c + 1) * 128,
                                     pair * 128:(pair + 1) * 128], sw)
            wch[wi] = [w16[:, (wi * 4 + c) * 128:(wi * 4 + c + 1) * 128]
                       for c in range(4)]
        t_w = sw.n

        def pair_proj(wi, src, split_dests):
            t_last = 0
            for ts in range(8):
                pk, war = pool.get()
                if war is not None:
                    Tn.wait(war[0], war[1])
                Tn.wait(sw, t_w)
                Tn.wait(V.sem, t_latents)
                t_mm = mm_group(p, pk[:, :], [
                    (wch[wi][ic], src[ic][:, ts * 512:(ts + 1) * 512])
                    for ic in range(4)])
                V.wait(Tn.sem, t_mm)
                if split_dests[1] is None:
                    t_last = V("tensor_copy", split_dests[0][:, ts * 512:(ts + 1) * 512],
                               pk[:, :])
                else:
                    V("tensor_copy", split_dests[0][0:64, ts * 512:(ts + 1) * 512],
                      pk[0:64, :])
                    t_last = V("tensor_copy",
                               split_dests[1][0:64, ts * 512:(ts + 1) * 512],
                               pk[64:128, :])
                pool.done(p.V.sem, t_last)
            return t_last

        pair_proj(0, cq, (Qaug[0], Qaug[1]))
        pair_proj(1, ckv, (Kaug[0], Kaug[1]))
        t_v = pair_proj(2, ckv, (qp, None))

        if pair == 0:
            V("memset", Vt[:, :], 1.0)
        for j in range(NT):
            pk, war = trp.get()
            if war is not None:
                Tn.wait(war[0], war[1])
            Tn.wait(V.sem, t_v)
            t_tr = Tn.sem.inc()
            nc.tensor.transpose(pk[:, :], qp[:, j * 128:(j + 1) * 128],
                                ident16).then_inc(Tn.sem.h, 1)
            V.wait(Tn.sem, t_tr)
            V("tensor_copy", Vt[:, j * 130:j * 130 + 64], pk[:, 0:64])
            t_e = V("tensor_copy", Vt[:, j * 130 + 65:j * 130 + 129], pk[:, 64:128])
            trp.done(p.V.sem, t_e)

        if pair % 2 == 0:
            ch = pair // 2
            wchq = load_weight16(p, weights["l2a_wqrT"][:, ch * 128:(ch + 1) * 128]
                                 if False else weights["l2a_wqrT"], None, None, 0, 0) \
                if False else None
            for c in range(4):
                ap = w16[:, (12 + c) * 128:(13 + c) * 128]
                p.dma("gpsimd", ap,
                      weights["l2a_wqrT"][c * 128:(c + 1) * 128,
                                          ch * 128:(ch + 1) * 128], sw)
            t_wq = sw.n
            wq = [w16[:, (12 + c) * 128:(13 + c) * 128] for c in range(4)]
            t_last = 0
            for ts in range(8):
                pk, war = pool.get()
                if war is not None:
                    Tn.wait(war[0], war[1])
                Tn.wait(sw, t_wq)
                t_mm = mm_group(p, pk[:, :], [
                    (wq[ic], cq[ic][:, ts * 512:(ts + 1) * 512]) for ic in range(4)])
                V.wait(Tn.sem, t_mm)
                t_last = V("tensor_copy", qp[:, ts * 512:(ts + 1) * 512], pk[:, :])
                pool.done(p.V.sem, t_last)
            rope32(qr16, qp, 128, 0)

        for hh in range(2):
            h = 2 * pair + hh
            r0 = 32 * (h % 4)
            V("tensor_copy", Qaug[hh][64:96, :], qr16[r0:r0 + 32, :])
            V("tensor_copy", Kaug[hh][64:96, :], kr[0:32, :])

        for hh in range(2):
            h = 2 * pair + hh
            QA, KA = Qaug[hh], Kaug[hh]
            V.wait(sst, sst.n)
            tn_sees_v(p)

            def clip(pb, qc, kt):
                return V("tensor_scalar", pb[:, :], pb[:, :], 80.0, -80.0,
                         ALU.min, ALU.max)

            attn_core_head(p, KA, QA,
                           lambda kt, hh=hh: Vt[:, kt * 130 + hh * 65:kt * 130 + hh * 65 + 65],
                           o32, pexp, psB, psO, 96, kt_lists, pre_exp=clip,
                           exp_scale=SC96, exp_bias_ap=bias_ap)
            if DBG and h == 0:
                full_barrier(p)
                p.dma("gpsimd", dbg_qaug[:, :], QA[0:96, :], sst)
                p.dma("gpsimd", dbg_kaug[:, :], KA[0:96, :], sst)
                p.dma("gpsimd", dbg_o32[:, :], o32[0:65, :], sst)
            t_d = attn_divide_head(p, o32, o16h, dance, lrow, lrec, lcols, pool,
                                   ident32, ones32, 0)
            p.qwait("gpsimd", V.sem, t_d)
            p.dma("gpsimd", o_fm_d[h * 64:(h + 1) * 64, :], o16h[0:64, :], sst)

    full_barrier(p)
    esA.close()
    p.qwait("sync", sst, sst.n)
    attn_out_proj(p, io, o_fm_d, weights["l2a_woT"], None, sld, sst, tag="l2aop")
    io["sub_done"] = (sst, sst.n)


def host_prep_L2_attn(inputs):
    w = {}
    w["l2a_wdqT"] = np.ascontiguousarray(inputs["L2_attn_W_DQ"].T)
    w["l2a_wuqT"] = np.ascontiguousarray(inputs["L2_attn_W_UQ"].T)
    w["l2a_wqrT"] = np.ascontiguousarray(inputs["L2_attn_W_QR"].T)
    w["l2a_wdkvT"] = np.ascontiguousarray(inputs["L2_attn_W_DKV"].T)
    w["l2a_wukT"] = np.ascontiguousarray(inputs["L2_attn_W_UK"].T)
    w["l2a_wuvT"] = np.ascontiguousarray(inputs["L2_attn_W_UV"].T)
    w["l2a_wkrT"] = np.ascontiguousarray(inputs["L2_attn_W_KR"].T)
    w["l2a_woT"] = np.ascontiguousarray(inputs["L2_attn_W_O"].T)
    _, mB = make_causal_masks()
    w["l2a_maskB"] = mB
    d_hr = 32
    inv = 10000.0 ** (-np.arange(0, d_hr, 2, dtype=np.float64) / d_hr)
    ang = np.arange(T, dtype=np.float64)[:, None] * inv[None, :]
    C = np.cos(ang)[:, np.repeat(np.arange(16), 2)].T
    Sn = np.sin(ang)[:, np.repeat(np.arange(16), 2)].T
    w["l2a_ctab"] = np.tile(C, (4, 1)).astype(BF)
    w["l2a_stab"] = np.tile(Sn, (4, 1)).astype(BF)
    P32 = np.zeros((32, 32), np.float32)
    for i in range(16):
        P32[2 * i + 1, 2 * i] = -1.0
        P32[2 * i, 2 * i + 1] = 1.0
    w["l2a_perm4"] = np.kron(np.eye(4, dtype=np.float32), P32).astype(BF)
    return w


L2A_WEIGHT_SPECS = {
    "l2a_wdqT": ([D, 512], F32),
    "l2a_wuqT": ([512, D], F32),
    "l2a_wqrT": ([512, 512], F32),
    "l2a_wdkvT": ([D, 512], F32),
    "l2a_wukT": ([512, D], F32),
    "l2a_wuvT": ([512, D], F32),
    "l2a_wkrT": ([D, 32], F32),
    "l2a_woT": ([D, D], F32),
    "l2a_maskB": ([128, 896], F32),
    "l2a_ctab": ([128, T], BF16),
    "l2a_stab": ([128, T], BF16),
    "l2a_perm4": ([128, 128], BF16),
}


def build_stage_L2_attn():
    return build_stage(emit_L2_attn, L2A_WEIGHT_SPECS)



def emit_L2_ffn(p, io, weights, consts):
    nc = p.nc
    V, S, G, Tn = p.V, p.S, p.G, p.Tn
    ones16 = consts["ones16_sb"]
    CW = PAD + T
    NE = 16

    sld = p.sem("l2f_sld")
    sst = p.sem("l2f_sst")
    sfm = p.sem("l2f_sfm")
    sw = p.sem("l2f_sw")

    esA = contextlib.ExitStack()
    ln_fm = [p.sb(f"l2f_lnfm{c}", [128, CW], BF16, esA) for c in range(8)]
    acc = p.sb("l2f_acc", [128, NT * D], BF16, esA)
    Hfm = [p.sb(f"l2f_h{c}", [128, T], BF16, esA) for c in range(4)]
    hg = [p.sb(f"l2f_hg{i}", [128, 512], BF16, esA) for i in range(2)]
    wbuf = p.sb("l2f_wbuf", [128, 8 * 512], BF16, esA)
    wdbuf = p.sb("l2f_wdbuf", [128, 4 * D], BF16, esA)
    rwT = p.sb("l2f_rwT", [128, 8 * 16], BF16, esA)
    rb16 = p.sb("l2f_rb16", [1, 16], BF16, esA)
    rbf = p.sb("l2f_rbf", [1, 16], F32, esA)
    comb = p.sb("l2f_comb", [128, NT * 16], F32, esA)
    bi = p.sb("l2f_bi", [128, 16], F32, esA)
    m1 = p.sb("l2f_m1", [128, 16], F32, esA)
    bi2 = p.sb("l2f_bi2", [128, 16], F32, esA)
    t1c = p.sb("l2f_t1", [128, 1], F32, esA)
    t2c = p.sb("l2f_t2", [128, 1], F32, esA)
    sgc = p.sb("l2f_sg", [128, 2], F32, esA)
    sub = [p.sb(f"l2f_sub{i}", [128, D], F32, esA) for i in range(2)]
    acc32 = p.sb("l2f_acc32", [128, 512], F32, esA)

    pool = PsumPool(p, "l2f_pp", 2, [128, 512], es=esA)
    pup = PsumPool(p, "l2f_pu", 2, [128, 512], es=esA)
    pr = PsumPool(p, "l2f_pr", 2, [128, 16], es=esA)

    p.qwait("sync", io["pre_done"][0], io["pre_done"][1])
    t_fm = emit_fm_load(p, io["ln16_tm"], ln_fm, sfm)

    rch, t_rw = load_weight16(p, weights["l2f_routerT"], rwT, sw, D, 16)
    t_rb = p.dma("sync", rbf[0:1, :], weights["l2f_rbias"][:, :], sld)
    V.wait(sld, t_rb)
    t_rb16 = V("tensor_copy", rb16[0:1, :], rbf[0:1, :])
    t_acc0 = V("memset", acc[:, :], 0.0)
    for j in range(NT):
        pk, war = pr.get()
        if war is not None:
            Tn.wait(war[0], war[1])
        Tn.wait(sw, t_rw)
        Tn.wait(sfm, t_fm)
        Tn.wait(V.sem, t_rb16)
        pairs = [(ln_fm[ic][:, PAD + j * 128:PAD + (j + 1) * 128], rch[ic])
                 for ic in range(8)]
        pairs.append((ones16[0:1, 0:128], rb16[0:1, 0:16]))
        t_mm = mm_group(p, pk[:, :], pairs)
        V.wait(Tn.sem, t_mm)
        V("tensor_copy", bi[:, :], pk[:, 0:16])
        pr.done(p.V.sem, p.V.sem.n)
        V("reduce_max", t1c[:, :], bi[:, :], axis=AX.X)
        V("tensor_scalar", m1[:, :], bi[:, :], t1c[:, 0:1], None, ALU.is_ge)
        V("scalar_tensor_tensor", bi2[:, :], m1[:, :], -1e30, bi[:, :],
          ALU.mult, ALU.add)
        V("reduce_max", t2c[:, :], bi2[:, :], axis=AX.X)
        t_d = V("tensor_sub", sgc[:, 0:1], t1c[:, :], t2c[:, :])
        S.wait(V.sem, t_d)
        t_sig = S("activation", sgc[:, 0:1], sgc[:, 0:1], AF.Sigmoid)
        V.wait(S.sem, t_sig)
        V("tensor_scalar", sgc[:, 1:2], sgc[:, 0:1], -1.0, 1.0, ALU.mult, ALU.add)
        V("tensor_scalar", sgc[:, 0:1], sgc[:, 0:1], 2.0, -1.0, ALU.mult, ALU.add)
        V("tensor_scalar", bi2[:, :], bi[:, :], t2c[:, 0:1], None, ALU.is_ge)
        V("tensor_scalar", m1[:, :], m1[:, :], sgc[:, 0:1], None, ALU.mult)
        V("scalar_tensor_tensor", comb[:, j * 16:(j + 1) * 16], bi2[:, :],
          sgc[:, 1:2], m1[:, :], ALU.mult, ALU.add)
    t_comb = p.V.sem.n

    for e in range(NE):
        gch, t_g = load_weight16(p, weights["l2f_gate"][e], wbuf, sw, D, 512)
        uch, t_u = load_weight16(p, weights["l2f_up"][e], wdbuf, sw, D, 512)
        V.wait(Tn.sem, Tn.sem.n)
        hg_rd = {}
        k = 0
        for oc in range(4):
            for ts in range(8):
                pg, warg = pool.get()
                if warg is not None:
                    Tn.wait(warg[0], warg[1])
                Tn.wait(sw, max(t_g, t_u))
                t_mm = mm_group(p, pg[:, :], [
                    (gch[ic][:, oc * 128:(oc + 1) * 128],
                     ln_fm[ic][:, PAD + ts * 512:PAD + (ts + 1) * 512])
                    for ic in range(8)])
                S.wait(Tn.sem, t_mm)
                if k - 2 in hg_rd:
                    S.wait(V.sem, hg_rd[k - 2])
                t_sil = S("activation", hg[k % 2][:, :], pg[:, :], AF.Silu)
                pool.done(p.S.sem, t_sil)
                pu, waru = pup.get()
                if waru is not None:
                    Tn.wait(waru[0], waru[1])
                t_mm2 = mm_group(p, pu[:, :], [
                    (uch[ic][:, oc * 128:(oc + 1) * 128],
                     ln_fm[ic][:, PAD + ts * 512:PAD + (ts + 1) * 512])
                    for ic in range(8)])
                V.wait(Tn.sem, t_mm2)
                V.wait(S.sem, t_sil)
                t_h = V("tensor_mul", Hfm[oc][:, ts * 512:(ts + 1) * 512],
                        pu[:, :], hg[k % 2][:, :])
                hg_rd[k] = t_h
                pup.done(p.V.sem, t_h)
                k += 1
        t_hdone = p.V.sem.n
        dch, t_d = load_weight16(p, weights["l2f_down"][e], wdbuf, sw, 512, D)
        for j in range(NT):
            for ns in range(2):
                pk, war = pool.get()
                if war is not None:
                    Tn.wait(war[0], war[1])
                Tn.wait(sw, t_d)
                Tn.wait(V.sem, t_hdone)
                t_mm = mm_group(p, pk[:, :], [
                    (Hfm[c][:, j * 128:(j + 1) * 128],
                     dch[c][:, ns * 512:(ns + 1) * 512]) for c in range(4)])
                V.wait(Tn.sem, t_mm)
                V.wait(V.sem, t_comb)
                t_e2 = V("scalar_tensor_tensor",
                         acc[:, j * D + ns * 512:j * D + (ns + 1) * 512],
                         pk[:, :], comb[:, j * 16 + e:j * 16 + e + 1],
                         acc[:, j * D + ns * 512:j * D + (ns + 1) * 512],
                         ALU.mult, ALU.add)
                pool.done(p.V.sem, t_e2)

    sub_done = {}
    for half in range(2):
        gch, t_g = load_weight16(
            p, weights["l2f_shgT"][:, half * 512:(half + 1) * 512], wbuf, sw, D, 512)
        uch, t_u = load_weight16(
            p, weights["l2f_shuT"][:, half * 512:(half + 1) * 512], wdbuf, sw, D, 512)
        V.wait(Tn.sem, Tn.sem.n)
        hg_rd = {}
        k = 0
        for oc in range(4):
            for ts in range(8):
                pg, warg = pool.get()
                if warg is not None:
                    Tn.wait(warg[0], warg[1])
                Tn.wait(sw, max(t_g, t_u))
                t_mm = mm_group(p, pg[:, :], [
                    (gch[ic][:, oc * 128:(oc + 1) * 128],
                     ln_fm[ic][:, PAD + ts * 512:PAD + (ts + 1) * 512])
                    for ic in range(8)])
                S.wait(Tn.sem, t_mm)
                if k - 2 in hg_rd:
                    S.wait(V.sem, hg_rd[k - 2])
                t_sil = S("activation", hg[k % 2][:, :], pg[:, :], AF.Silu)
                pool.done(p.S.sem, t_sil)
                pu, waru = pup.get()
                if waru is not None:
                    Tn.wait(waru[0], waru[1])
                t_mm2 = mm_group(p, pu[:, :], [
                    (uch[ic][:, oc * 128:(oc + 1) * 128],
                     ln_fm[ic][:, PAD + ts * 512:PAD + (ts + 1) * 512])
                    for ic in range(8)])
                V.wait(Tn.sem, t_mm2)
                V.wait(S.sem, t_sil)
                t_h = V("tensor_mul", Hfm[oc][:, ts * 512:(ts + 1) * 512],
                        pu[:, :], hg[k % 2][:, :])
                hg_rd[k] = t_h
                pup.done(p.V.sem, t_h)
                k += 1
        t_hs = p.V.sem.n
        dch, t_d = load_weight16(
            p, weights["l2f_shdT"][half * 512:(half + 1) * 512, :], wdbuf, sw, 512, D)
        for j in range(NT):
            i = j % 2
            if half == 1 and j - 2 in sub_done:
                V.wait(sst, sub_done[j - 2])
            t_e2 = None
            for ns in range(2):
                pk, war = pool.get()
                if war is not None:
                    Tn.wait(war[0], war[1])
                Tn.wait(sw, t_d)
                Tn.wait(V.sem, t_hs)
                t_mm = mm_group(p, pk[:, :], [
                    (Hfm[c][:, j * 128:(j + 1) * 128],
                     dch[c][:, ns * 512:(ns + 1) * 512]) for c in range(4)])
                V.wait(Tn.sem, t_mm)
                if half == 0:
                    t_e2 = V("scalar_tensor_tensor",
                             acc[:, j * D + ns * 512:j * D + (ns + 1) * 512],
                             pk[:, :], 1.0,
                             acc[:, j * D + ns * 512:j * D + (ns + 1) * 512],
                             ALU.mult, ALU.add)
                else:
                    V("tensor_copy", acc32[:, :],
                      acc[:, j * D + ns * 512:j * D + (ns + 1) * 512])
                    t_e2 = V("tensor_add", sub[i][:, ns * 512:(ns + 1) * 512],
                             pk[:, :], acc32[:, :])
                pool.done(p.V.sem, t_e2)
            if half == 1:
                p.qwait("gpsimd", V.sem, t_e2)
                p.dma("gpsimd", io["sub_tm"][j * 128:(j + 1) * 128, :], sub[i][:, :], sst)
                sub_done[j] = sst.n
    full_barrier(p)
    esA.close()
    io["sub_done"] = (sst, sst.n)


def host_prep_L2_ffn(inputs):
    w = {}
    w["l2f_routerT"] = np.ascontiguousarray(inputs["L2_ffn_router_w"].T)
    w["l2f_rbias"] = np.ascontiguousarray(inputs["L2_ffn_router_bias"].reshape(1, 16))
    w["l2f_gate"] = np.ascontiguousarray(inputs["L2_ffn_gate_proj"])
    w["l2f_up"] = np.ascontiguousarray(inputs["L2_ffn_up_proj"])
    w["l2f_down"] = np.ascontiguousarray(inputs["L2_ffn_down_proj"])
    w["l2f_shgT"] = np.ascontiguousarray(inputs["L2_ffn_sh_gate"].T)
    w["l2f_shuT"] = np.ascontiguousarray(inputs["L2_ffn_sh_up"].T)
    w["l2f_shdT"] = np.ascontiguousarray(inputs["L2_ffn_sh_down"].T)
    return w


L2F_WEIGHT_SPECS = {
    "l2f_routerT": ([D, 16], F32),
    "l2f_rbias": ([1, 16], F32),
    "l2f_gate": ([16, D, 512], F32),
    "l2f_up": ([16, D, 512], F32),
    "l2f_down": ([16, 512, D], F32),
    "l2f_shgT": ([D, D], F32),
    "l2f_shuT": ([D, D], F32),
    "l2f_shdT": ([D, D], F32),
}


def build_stage_L2_ffn():
    return build_stage(emit_L2_ffn, L2F_WEIGHT_SPECS)



def emit_L1_ffn(p, io, weights, consts):
    nc = p.nc
    V, S, G, Tn = p.V, p.S, p.G, p.Tn
    ones16 = consts["ones16_sb"]
    ones32 = consts["ones32_sb"]
    CW = PAD + T
    NE = 8
    NH = 11

    sld = p.sem("l1f_sld")
    sst = p.sem("l1f_sst")
    sfm = p.sem("l1f_sfm")
    sw = p.sem("l1f_sw")

    esA = contextlib.ExitStack()
    ln_fm = [p.sb(f"l1f_lnfm{c}", [128, CW], BF16, esA) for c in range(8)]
    acc = p.sb("l1f_acc", [128, NT * D], BF16, esA)
    Hfm = [p.sb(f"l1f_h{c}", [128, T], BF16, esA) for c in range(4)]
    hrel = [p.sb(f"l1f_hrel{i}", [128, 512], BF16, esA) for i in range(2)]
    wbuf = p.sb("l1f_wbuf", [128, 8 * 512], BF16, esA)
    wdbuf = p.sb("l1f_wdbuf", [128, 4 * D], BF16, esA)
    gwT = p.sb("l1f_gwT", [128, 8 * 8], BF16, esA)
    ebrow = p.sb("l1f_ebrow", [1, 8], F32, esA)
    ebb = p.sb("l1f_ebb", [128, 8], F32, esA)
    comb = p.sb("l1f_comb", [128, NT * 8], F32, esA)
    sc = p.sb("l1f_sc", [128, 8], F32, esA)
    bi = p.sb("l1f_bi", [128, 8], F32, esA)
    m1 = p.sb("l1f_m1", [128, 8], F32, esA)
    bi2 = p.sb("l1f_bi2", [128, 8], F32, esA)
    t1c = p.sb("l1f_t1", [128, 1], F32, esA)
    t2c = p.sb("l1f_t2", [128, 1], F32, esA)
    sub = [p.sb(f"l1f_sub{i}", [128, D], F32, esA) for i in range(2)]
    acc32 = p.sb("l1f_acc32", [128, 512], F32, esA)

    pool = PsumPool(p, "l1f_pp", 2, [128, 512], es=esA)
    pr = PsumPool(p, "l1f_pr", 2, [128, 8], es=esA)

    if DBG:
        dbg_comb = p.out("dbg_comb", [128, NT * 8], F32)
        dbg_acc = p.out("dbg_acc", [128, NT * D], BF16)
        dbg_sc = p.out("dbg_sc", [128, 8], F32)
        dbg_bi = p.out("dbg_bi", [128, 8], F32)
        dbg_ebb = p.out("dbg_ebb", [128, 8], F32)
    p.qwait("sync", io["pre_done"][0], io["pre_done"][1])
    t_fm = emit_fm_load(p, io["ln16_tm"], ln_fm, sfm)

    gch, t_rw = load_weight16(p, weights["l1f_gateT"], gwT, sw, D, 8)
    t_rb = p.dma("sync", ebrow[0:1, :], weights["l1f_ebias"][:, :], sld)
    pk, war = pr.get()
    Tn.wait(sld, t_rb)
    t_mm = mm_group(p, pk[:, :], [(ones32[0:1, 0:128], ebrow[0:1, 0:8])])
    V.wait(Tn.sem, t_mm)
    t_eb = V("tensor_copy", ebb[:, :], pk[:, 0:8])
    pr.done(p.V.sem, t_eb)
    t_acc0 = V("memset", acc[:, :], 0.0)
    t_scrd = 0
    for j in range(NT):
        pk, war = pr.get()
        if war is not None:
            Tn.wait(war[0], war[1])
        Tn.wait(sw, t_rw)
        Tn.wait(sfm, t_fm)
        pairs = [(ln_fm[ic][:, PAD + j * 128:PAD + (j + 1) * 128], gch[ic])
                 for ic in range(8)]
        t_mm = mm_group(p, pk[:, :], pairs)
        S.wait(Tn.sem, t_mm)
        S.wait(V.sem, t_scrd)
        t_sig = S("activation", sc[:, :], pk[:, 0:8], AF.Sigmoid)
        pr.done(p.S.sem, t_sig)
        V.wait(S.sem, t_sig)
        t_scrd = V("tensor_add", bi[:, :], sc[:, :], ebb[:, :])
        V("reduce_max", t1c[:, :], bi[:, :], axis=AX.X)
        V("tensor_scalar", m1[:, :], bi[:, :], t1c[:, 0:1], None, ALU.is_ge)
        V("scalar_tensor_tensor", bi2[:, :], m1[:, :], -1e30, bi[:, :],
          ALU.mult, ALU.add)
        V("reduce_max", t2c[:, :], bi2[:, :], axis=AX.X)
        V("tensor_scalar", bi2[:, :], bi[:, :], t2c[:, 0:1], None, ALU.is_ge)
        t_scrd = V("tensor_mul", m1[:, :], sc[:, :], sc[:, :])
        t_cj = V("tensor_mul", comb[:, j * 8:(j + 1) * 8], bi2[:, :], m1[:, :])
        if DBG and j == 0:
            p.qwait("gpsimd", V.sem, t_cj)
            p.dma("gpsimd", dbg_sc[:, :], sc[:, :], sst)
            p.dma("gpsimd", dbg_bi[:, :], bi[:, :], sst)
            p.dma("gpsimd", dbg_ebb[:, :], ebb[:, :], sst)
            p.qwait("sync", sst, sst.n)
            V.wait(sst, sst.n)
    t_comb = p.V.sem.n

    def up_relu2(upT_dram, n_h):
        uch, t_u = load_weight16(p, upT_dram, wbuf, sw, D, n_h * 128)
        assert n_h <= 4
        S.wait(Tn.sem, Tn.sem.n)
        sq_done = {}
        k = 0
        for oc in range(n_h):
            for ts in range(8):
                pg, warg = pool.get()
                if warg is not None:
                    Tn.wait(warg[0], warg[1])
                Tn.wait(sw, t_u)
                t_mm = mm_group(p, pg[:, :], [
                    (uch[ic][:, oc * 128:(oc + 1) * 128],
                     ln_fm[ic][:, PAD + ts * 512:PAD + (ts + 1) * 512])
                    for ic in range(8)])
                V.wait(Tn.sem, t_mm)
                if k - 2 in sq_done:
                    V.wait(S.sem, sq_done[k - 2])
                t_r = V("tensor_scalar_max", hrel[k % 2][:, :], pg[:, :], 0.0)
                pool.done(p.V.sem, t_r)
                S.wait(V.sem, t_r)
                sq_done[k] = S("activation", Hfm[oc][:, ts * 512:(ts + 1) * 512],
                               hrel[k % 2][:, :], AF.Square)
                k += 1
        return p.S.sem.n

    def down_acc(downT_dram, n_h, comb_col_fn, to_sub=False):
        dch, t_d = load_weight16(p, downT_dram, wdbuf, sw, n_h * 128, D)
        t_hdone = p.S.sem.n
        sub_done = {}
        for j in range(NT):
            i = j % 2
            if to_sub and j - 2 in sub_done:
                V.wait(sst, sub_done[j - 2])
            t_e2 = None
            for ns in range(2):
                pk, war = pool.get()
                if war is not None:
                    Tn.wait(war[0], war[1])
                Tn.wait(sw, t_d)
                Tn.wait(S.sem, t_hdone)
                t_mm = mm_group(p, pk[:, :], [
                    (Hfm[c][:, j * 128:(j + 1) * 128],
                     dch[c][:, ns * 512:(ns + 1) * 512]) for c in range(n_h)])
                V.wait(Tn.sem, t_mm)
                ccol = comb_col_fn(j)
                if to_sub:
                    V("tensor_copy", acc32[:, :],
                      acc[:, j * D + ns * 512:j * D + (ns + 1) * 512])
                    t_e2 = V("tensor_add", sub[i][:, ns * 512:(ns + 1) * 512],
                             pk[:, :], acc32[:, :])
                elif ccol is None:
                    t_e2 = V("scalar_tensor_tensor",
                             acc[:, j * D + ns * 512:j * D + (ns + 1) * 512],
                             pk[:, :], 1.0,
                             acc[:, j * D + ns * 512:j * D + (ns + 1) * 512],
                             ALU.mult, ALU.add)
                else:
                    V.wait(V.sem, t_comb)
                    t_e2 = V("scalar_tensor_tensor",
                             acc[:, j * D + ns * 512:j * D + (ns + 1) * 512],
                             pk[:, :], ccol,
                             acc[:, j * D + ns * 512:j * D + (ns + 1) * 512],
                             ALU.mult, ALU.add)
                pool.done(p.V.sem, t_e2)
            if to_sub:
                p.qwait("gpsimd", V.sem, t_e2)
                p.dma("gpsimd", io["sub_tm"][j * 128:(j + 1) * 128, :], sub[i][:, :], sst)
                sub_done[j] = sst.n

    HALVES = ((0, 4), (4, 4), (8, 3))
    for e in range(NE):
        for (h0, nh) in HALVES:
            up_relu2(weights["l1f_upT"][e][:, h0 * 128:(h0 + nh) * 128], nh)
            down_acc(weights["l1f_downT"][e][h0 * 128:(h0 + nh) * 128, :], nh,
                     lambda j, e=e: comb[:, j * 8 + e:j * 8 + e + 1])

    if DBG:
        full_barrier(p)
        p.dma("gpsimd", dbg_comb[:, :], comb[:, :], sst)
        p.dma("gpsimd", dbg_acc[:, :], acc[:, :], sst)
    for (h0, nh) in HALVES[:-1]:
        up_relu2(weights["l1f_shupT"][:, h0 * 128:(h0 + nh) * 128], nh)
        down_acc(weights["l1f_shdownT"][h0 * 128:(h0 + nh) * 128, :], nh, lambda j: None)
    h0, nh = HALVES[-1]
    up_relu2(weights["l1f_shupT"][:, h0 * 128:(h0 + nh) * 128], nh)
    down_acc(weights["l1f_shdownT"][h0 * 128:(h0 + nh) * 128, :], nh, lambda j: None,
             to_sub=True)

    full_barrier(p)
    esA.close()
    io["sub_done"] = (sst, sst.n)


def host_prep_L1_ffn(inputs):
    w = {}
    w["l1f_gateT"] = np.ascontiguousarray(inputs["L1_ffn_gate_w"].T)
    w["l1f_ebias"] = np.ascontiguousarray(inputs["L1_ffn_expert_bias"].reshape(1, 8))
    w["l1f_upT"] = np.ascontiguousarray(inputs["L1_ffn_w_up"].transpose(0, 2, 1))
    w["l1f_downT"] = np.ascontiguousarray(inputs["L1_ffn_w_down"].transpose(0, 2, 1))
    w["l1f_shupT"] = np.ascontiguousarray(inputs["L1_ffn_shared_w_up"].T)
    w["l1f_shdownT"] = np.ascontiguousarray(inputs["L1_ffn_shared_w_down"].T)
    return w


L1F_WEIGHT_SPECS = {
    "l1f_gateT": ([D, 8], F32),
    "l1f_ebias": ([1, 8], F32),
    "l1f_upT": ([8, D, 1408], F32),
    "l1f_downT": ([8, 1408, D], F32),
    "l1f_shupT": ([D, 1408], F32),
    "l1f_shdownT": ([1408, D], F32),
}


def build_stage_L1_ffn():
    return build_stage(emit_L1_ffn, L1F_WEIGHT_SPECS)



def emit_local_stack(p, io, weights, consts, cfg):
    nc = p.nc
    V, S, G, Tn = p.V, p.S, p.G, p.Tn
    ident16 = consts["ident16_sb"]
    ident32 = consts["ident32_sb"]
    ones16 = consts["ones16_sb"]
    ones32 = consts["ones32_sb"]
    tag = cfg["tag"]
    DEPTH, H, W, INNER = cfg["depth"], cfg["H"], cfg["W"], cfg["inner"]
    NP = H // 2
    NIC = INNER // 128
    NHG = 22
    CW = PAD + T
    KW = W + T

    x_f = [p.dram(f"{tag}_xf{i}", [T, D], F32) for i in range(2)]
    x_h = [p.dram(f"{tag}_xh{i}", [T, D], BF16) for i in range(2)]
    o_fm_d = (p.out if DBG else p.dram)(f"{tag}_ofm", [INNER, T], BF16)
    if DBG:
        dbg_q = p.out("dbg_q", [128, T], BF16)
        dbg_k = p.out("dbg_k", [128, W + T], BF16)
        dbg_o32 = p.out("dbg_o32", [65, T], F32)
        dbg_nm = p.out("dbg_nm", [1, T], BF16)
    xr_d = p.dram(f"{tag}_xrd", [D, T], BF16)

    sld = p.sem(f"{tag}_sld")
    sst = p.sem(f"{tag}_sst")
    sfm = p.sem(f"{tag}_sfm")
    sw = p.sem(f"{tag}_sw")
    sxr = p.sem(f"{tag}_sxr")

    ctabS = p.sb(f"{tag}_ctabSs", [128, T], BF16)
    stabS = p.sb(f"{tag}_stabSs", [128, T], BF16)
    ctabI = p.sb(f"{tag}_ctabIs", [128, T], BF16)
    stabI = p.sb(f"{tag}_stabIs", [128, T], BF16)
    blkS = p.sb(f"{tag}_blkSs", [128, 128], BF16)
    perm = p.sb(f"{tag}_perms", [128, 128], BF16)
    qkSf = p.sb(f"{tag}_qkSf", [128, 2 * DEPTH], F32)

    maskA = p.sb(f"{tag}_maskAs", [128, W + 128], F32)
    maskB = p.sb(f"{tag}_maskBs", [128, W + 128], F32)
    lnbf = p.sb(f"{tag}_lnbf", [128, 16 * DEPTH], F32)
    lnb16 = p.sb(f"{tag}_lnb16", [128, 16 * DEPTH], BF16)

    rowf = p.sb(f"{tag}_rowf", [97, T], F32)
    rowh = p.sb(f"{tag}_rowh", [97, T], BF16)

    mcols = p.sb(f"{tag}_mcols", [128, NT], F32)
    mtmp = p.sb(f"{tag}_mtmp", [128, 1], F32)
    lcols = p.sb(f"{tag}_lcols", [128, 32], F32)
    rows = rowf[0:1, :]
    srow = rowf[32:33, :]
    ssqrow = rowf[64:65, :]
    lrow = rowf[0:1, :]
    nmrow = rowh[0:1, :]
    murrow = rowh[0:1, :]



    t_c = 0
    for nm, buf in (("ctabS", ctabS), ("stabS", stabS), ("ctabI", ctabI),
                    ("stabI", stabI), ("blkS", blkS), ("perm", perm),
                    ("qkS", qkSf), ("maskA", maskA), ("maskB", maskB),
                    ("lnb", lnbf)):
        t_c = p.dma("sync", buf[:, :], weights[f"{tag}_{nm}"][:, :], sld)
    V.wait(sld, t_c)
    V("tensor_scalar_mul", qkSf[:, 0:DEPTH], qkSf[:, 0:DEPTH], 8.0)
    V("tensor_copy", lnb16[:, :], lnbf[:, :])

    dance = Dance(p, f"{tag}_dance")
    eps24_ap, _ = p.const_col(1e-24)
    eps5_ap, _ = p.const_col(1e-5)

    def sh(fmc, c, t0, n):
        off = PAD - 1 if c >= 4 else PAD
        return fmc[c][:, off + t0:off + t0 + n]

    blists = {}
    for qc in range(8):
        q0, q1 = qc * 512, qc * 512 + 512
        lst = []
        for kt in range(NT):
            klo = kt * 128
            lo = max(q0, klo)
            hi = min(q1, klo + 128 + W)
            if lo < hi:
                lst.append((kt, lo, hi - lo, lo - klo))
        blists[qc] = lst

    for layer in range(DEPTH):
        lsrc_f = io["ln32_tm"] if layer == 0 else x_f[0]
        lsrc_h = io["ln16_tm"] if layer == 0 else x_h[0]
        qs_col = qkSf[:, layer:layer + 1]
        ks_col = qkSf[:, DEPTH + layer:DEPTH + layer + 1]
        g_cols = [lnbf[:, 16 * layer + c:16 * layer + c + 1] for c in range(8)]
        b_cols = [lnb16[:, 16 * layer + 8 + c:16 * layer + 8 + c + 1] for c in range(8)]

        esA = contextlib.ExitStack()
        hfm = [p.sb(f"{tag}_hfm{layer}_{c}", [128, CW], BF16, esA) for c in range(8)]
        qrp = p.sb(f"{tag}_qrp{layer}", [128, T], BF16, esA)
        krp = p.sb(f"{tag}_krp{layer}", [128, KW], BF16, esA)
        vsl = [p.sb(f"{tag}_vsl{layer}_{i}", [128, 512], BF16, esA) for i in range(2)]
        rsl = [p.sb(f"{tag}_rsl{layer}_{i}", [128, 512], BF16, esA) for i in range(2)]
        sq2 = [p.sb(f"{tag}_sq2_{layer}_{i}", [128, 512], BF16, esA) for i in range(2)]
        rot = p.sb(f"{tag}_rot{layer}", [128, 512], BF16, esA)
        Vt = p.sb(f"{tag}_vt{layer}", [128, NT * 130], BF16, esA)
        o32 = p.sb(f"{tag}_o32_{layer}", [65, T], F32, esA)
        o16h = p.sb(f"{tag}_o16h{layer}", [64, T], BF16, esA)
        pexp = [p.sb(f"{tag}_pexp{layer}_{i}", [128, 512], BF16, esA) for i in range(3)]
        w16 = p.sb(f"{tag}_w16a{layer}", [128, 3 * 8 * 128], BF16, esA)
        pool = PsumPool(p, f"{tag}_pp{layer}", 2, [128, 512], es=esA)
        trp = PsumPool(p, f"{tag}_tr{layer}", 2, [128, 128], BF16, es=esA)
        psB = PsumPool(p, f"{tag}_psB{layer}", 3, [128, 512], es=esA)
        psO = pool

        for c in range(8):
            t_hfmpad = V("memset", hfm[c][:, 0:PAD], 0.0)
        if layer == 0:
            p.qwait("sync", io["pre_done"][0], io["pre_done"][1])
        else:
            p.qwait("sync", sst, sst.n)
        t_fm = emit_fm_load(p, lsrc_h, hfm, sfm)

        for pair in range(NP):
            p.qwait("gpsimd", Tn.sem, Tn.sem.n)
            vsl_rd = {}
            wch = {}
            for wi in range(3):
                for c in range(8):
                    ap = w16[:, (wi * 8 + c) * 128:(wi * 8 + c + 1) * 128]
                    p.dma("gpsimd", ap,
                          weights[f"{tag}_wqkv"][layer][c * 128:(c + 1) * 128,
                                                        wi * INNER + pair * 128:
                                                        wi * INNER + (pair + 1) * 128],
                          sw)
                wch[wi] = [w16[:, (wi * 8 + c) * 128:(wi * 8 + c + 1) * 128]
                           for c in range(8)]
            t_w = sw.n

            def pproj(wi, dst, col0):
                t_last = 0
                for ts in range(8):
                    pk, war = pool.get()
                    if war is not None:
                        Tn.wait(war[0], war[1])
                    Tn.wait(sw, t_w)
                    Tn.wait(sfm, t_fm)
                    Tn.wait(V.sem, t_hfmpad)
                    t_mm = mm_group(p, pk[:, :], [
                        (wch[wi][ic], sh(hfm, ic, ts * 512, 512))
                        for ic in range(8)])
                    V.wait(Tn.sem, t_mm)
                    t_last = V("tensor_copy",
                               dst[:, col0 + ts * 512:col0 + (ts + 1) * 512], pk[:, :])
                    pool.done(p.V.sem, t_last)
                return t_last

            def l2norm_rot(buf, col0, scol, ctab, stab):
                sq_rd = {}
                rsl_rd = {}
                for ts in range(8):
                    b = ts % 2
                    bslc = slice(col0 + ts * 512, col0 + (ts + 1) * 512)
                    if ts - 2 in sq_rd:
                        S.wait(Tn.sem, sq_rd[ts - 2])
                    S.wait(V.sem, V.sem.n)
                    t_sq = S("activation", sq2[b][:, :], buf[:, bslc], AF.Square)
                    pk, war = pool.get()
                    if war is not None:
                        Tn.wait(war[0], war[1])
                    Tn.wait(S.sem, t_sq)
                    last = nc.tensor.matmul(pk[:, :], blkS[:, :], sq2[b][:, :],
                                            start=True, stop=True)
                    t_mm = Tn.sem.inc()
                    last.then_inc(Tn.sem.h, 1)
                    sq_rd[ts] = t_mm
                    S.wait(Tn.sem, t_mm)
                    if ts - 2 in rsl_rd:
                        S.wait(V.sem, rsl_rd[ts - 2])
                    S("activation", rsl[b][:, :], pk[:, :], AF.Ln, bias=eps24_ap)
                    t_ex = S("activation", rsl[b][:, :], rsl[b][:, :], AF.Exp,
                             scale=-0.5)
                    pool.done(p.S.sem, t_ex)
                    V.wait(S.sem, t_ex)
                    t_n = V("scalar_tensor_tensor", buf[:, bslc], buf[:, bslc],
                            scol, rsl[b][:, :], ALU.mult, ALU.mult)
                    pk2, war2 = psB.get()
                    if war2 is not None:
                        Tn.wait(war2[0], war2[1])
                    Tn.wait(V.sem, t_n)
                    t_mm2 = mm_group(p, pk2[:, :], [(perm[:, :], buf[:, bslc])])
                    V.wait(Tn.sem, t_mm2)
                    V("tensor_mul", rot[:, :], pk2[:, :],
                      stab[:, ts * 512:(ts + 1) * 512])
                    psB.done(p.V.sem, p.V.sem.n)
                    V("tensor_mul", buf[:, bslc], buf[:, bslc],
                      ctab[:, ts * 512:(ts + 1) * 512])
                    rsl_rd[ts] = V("tensor_add", buf[:, bslc], buf[:, bslc],
                                   rot[:, 0:512])
                return p.V.sem.n

            pproj(0, qrp, 0)
            l2norm_rot(qrp, 0, qs_col, ctabS, stabS)
            V("memset", krp[:, 0:W], 0.0)
            pproj(1, krp, W)
            l2norm_rot(krp, W, ks_col, ctabI, stabI)

            if pair == 0:
                V("memset", Vt[:, :], 1.0)
            for ts in range(8):
                b = ts % 2
                pk, war = pool.get()
                if war is not None:
                    Tn.wait(war[0], war[1])
                Tn.wait(sw, t_w)
                Tn.wait(sfm, t_fm)
                t_mm = mm_group(p, pk[:, :], [
                    (wch[2][ic], sh(hfm, ic, ts * 512, 512)) for ic in range(8)])
                V.wait(Tn.sem, t_mm)
                if ts >= 2:
                    V.wait(Tn.sem, vsl_rd[ts - 2])
                t_v = V("tensor_copy", vsl[b][:, :], pk[:, :])
                pool.done(p.V.sem, t_v)
                for sub in range(4):
                    j = ts * 4 + sub
                    pk2, war2 = trp.get()
                    if war2 is not None:
                        Tn.wait(war2[0], war2[1])
                    Tn.wait(V.sem, t_v)
                    t_tr = Tn.sem.inc()
                    nc.tensor.transpose(pk2[:, :], vsl[b][:, sub * 128:(sub + 1) * 128],
                                        ident16).then_inc(Tn.sem.h, 1)
                    if sub == 3:
                        vsl_rd[ts] = t_tr
                    V.wait(Tn.sem, t_tr)
                    V("tensor_copy", Vt[:, j * 130:j * 130 + 64], pk2[:, 0:64])
                    t_e = V("tensor_copy", Vt[:, j * 130 + 65:j * 130 + 129],
                            pk2[:, 64:128])
                    trp.done(p.V.sem, t_e)

            for hh in range(2):
                h = 2 * pair + hh
                rb0 = hh * 64
                v_sees_tn(p)
                V("memset", mcols[:, :], NEG)
                tn_sees_v(p)
                for j in range(NT):
                    kst = max(0, j * 128 - W)
                    kn = j * 128 + 128 - kst
                    moff = kst - (j * 128 - W)
                    nsub = [kn] if kn <= 512 else [512, kn - 512]
                    off = 0
                    for nn2 in nsub:
                        pb, war = psB.get()
                        if war is not None:
                            Tn.wait(war[0], war[1])
                        last = nc.tensor.matmul(
                            pb[:, 0:nn2],
                            qrp[rb0:rb0 + 64, j * 128:(j + 1) * 128],
                            krp[rb0:rb0 + 64, W + kst + off:W + kst + off + nn2],
                            start=True, stop=True)
                        t_b = Tn.sem.inc()
                        last.then_inc(Tn.sem.h, 1)
                        V.wait(Tn.sem, t_b)
                        V("tensor_add", pb[:, 0:nn2], pb[:, 0:nn2],
                          maskA[:, moff + off:moff + off + nn2])
                        V("reduce_max", mtmp[:, :], pb[:, 0:nn2], axis=AX.X)
                        t_m = V("tensor_max", mcols[:, j:j + 1], mcols[:, j:j + 1],
                                mtmp[:, :])
                        psB.done(p.V.sem, t_m)
                        off += nn2
                dance.rows_from_cols(mcols[:, :], rows, 32, ident32,
                                     waits=[(V.sem, p.V.sem.n)])
                V.wait(dance.sem, dance.sem.n)
                t_nm = V("tensor_scalar_mul", nmrow, rows, -1.0)

                entries = []
                for qc in range(8):
                    lst = blists[qc]
                    for idx, e in enumerate(lst):
                        entries.append((qc, e, idx == 0, idx == len(lst) - 1))
                nE = len(entries)
                exp_done = {}
                av_of_pexp = {}
                po = None
                for i2 in range(nE + 1):
                    if i2 < nE:
                        qc, (kt, qlo, nn2, moff), first, lastk = entries[i2]
                        pb, warB = psB.get()
                        if warB is not None:
                            Tn.wait(warB[0], warB[1])
                        Tn.wait(V.sem, t_nm)
                        nc.tensor.matmul(
                            pb[:, 0:nn2],
                            krp[rb0:rb0 + 64, W + kt * 128:W + kt * 128 + 128],
                            qrp[rb0:rb0 + 64, qlo:qlo + nn2],
                            start=True, stop=False)
                        last = nc.tensor.matmul(
                            pb[:, 0:nn2], ones16[0:1, 0:128],
                            nmrow[0:1, qlo:qlo + nn2], start=False, stop=True)
                        t_b = Tn.sem.inc()
                        last.then_inc(Tn.sem.h, 1)
                        V.wait(Tn.sem, t_b)
                        t_fix = V("tensor_add", pb[:, 0:nn2], pb[:, 0:nn2],
                                  maskB[:, moff:moff + nn2])
                        b = i2 % 3
                        S.wait(Tn.sem, t_b)
                        S.wait(V.sem, t_fix)
                        if b in av_of_pexp:
                            S.wait(Tn.sem, av_of_pexp[b])
                        t_e = S("activation", pexp[b][:, 0:nn2], pb[:, 0:nn2], AF.Exp)
                        psB.done(p.S.sem, t_e)
                        exp_done[i2] = t_e
                    if i2 >= 1:
                        qc, (kt, qlo, nn2, moff), first, lastk = entries[i2 - 1]
                        if first:
                            po, warO = psO.get()
                            if warO is not None:
                                Tn.wait(warO[0], warO[1])
                        b1 = (i2 - 1) % 3
                        Tn.wait(S.sem, exp_done[i2 - 1])
                        last = nc.tensor.matmul(
                            po[0:65, qlo - qc * 512:qlo - qc * 512 + nn2],
                            Vt[:, kt * 130 + hh * 65:kt * 130 + hh * 65 + 65],
                            pexp[b1][:, 0:nn2],
                            start=first, stop=lastk,
                            skip_group_check=True)
                        t_av = Tn.sem.inc()
                        last.then_inc(Tn.sem.h, 1)
                        av_of_pexp[b1] = t_av
                        if lastk:
                            V.wait(Tn.sem, t_av)
                            t_o = V("tensor_copy",
                                    o32[0:65, qc * 512:(qc + 1) * 512], po[0:65, :])
                            psO.done(p.V.sem, t_o)

                if DBG and h == 0 and layer == 0:
                    full_barrier(p)
                    p.dma("gpsimd", dbg_q[:, :], qrp[:, :], sst)
                    p.dma("gpsimd", dbg_k[:, :], krp[:, :], sst)
                    p.dma("gpsimd", dbg_o32[:, :], o32[0:65, :], sst)
                    p.dma("gpsimd", dbg_nm[:, :], nmrow, sst)
                V.wait(sst, sst.n)
                t_d = attn_divide_head(p, o32, o16h, dance, lrow, lrow, lcols,
                                       pool, ident32, ones32, 0)
                p.qwait("gpsimd", V.sem, t_d)
                p.dma("gpsimd", o_fm_d[h * 64:(h + 1) * 64, :], o16h[0:64, :], sst)

        full_barrier(p)
        esA.close()

        stop_attn = cfg.get("stop") == "attn" and layer == DEPTH - 1
        dst_f = io["sub_tm"] if stop_attn else x_f[1]
        dst_h = x_h[1]
        esP = contextlib.ExitStack()
        ofm = [p.sb(f"{tag}_pofm{layer}_{c}", [128, T], BF16, esP) for c in range(NIC)]
        wo16 = p.sb(f"{tag}_pwo{layer}", [128, NIC * D], BF16, esP)
        padd = [p.sb(f"{tag}_padd{layer}_{i}", [128, D], F32, esP) for i in range(2)]
        psub = [p.sb(f"{tag}_psub{layer}_{i}", [128, D], F32, esP) for i in range(2)]
        opool = PsumPool(p, f"{tag}_pop{layer}", 4, [128, 512], es=esP)
        p.qwait("sync", sst, sst.n)
        for c in range(NIC):
            p.dma("sync", ofm[c][:, :], o_fm_d[c * 128:(c + 1) * 128, :], sld)
        t_of = sld.n
        wch, t_w = load_weight16(p, weights[f"{tag}_wo"][layer], wo16, sw, INNER, D)
        st_done = {}
        padd_rd = {}
        for j in range(NT):
            i = j % 2
            if j - 2 in st_done:
                V.wait(sst, st_done[j - 2])
            if j - 2 in padd_rd:
                p.qwait("sync", V.sem, padd_rd[j - 2])
            t_a = p.dma("sync", padd[i][:, :], lsrc_f[j * 128:(j + 1) * 128, :], sld)
            t_e = None
            for ns in range(2):
                pk, war = opool.get()
                if war is not None:
                    Tn.wait(war[0], war[1])
                Tn.wait(sld, t_of)
                Tn.wait(sw, t_w)
                pairs = [(ofm[c][:, j * 128:(j + 1) * 128],
                          wo16[:, (c * 2 + ns) * 512:(c * 2 + ns + 1) * 512])
                         for c in range(NIC)]
                t_mm = mm_group(p, pk[:, :], pairs)
                V.wait(Tn.sem, t_mm)
                V.wait(sld, t_a)
                t_e = V("tensor_add", psub[i][:, ns * 512:(ns + 1) * 512], pk[:, :],
                        padd[i][:, ns * 512:(ns + 1) * 512])
                opool.done(p.V.sem, t_e)
            padd_rd[j] = t_e
            p.qwait("gpsimd", V.sem, t_e)
            p.dma("gpsimd", dst_f[j * 128:(j + 1) * 128, :], psub[i][:, :], sst)
            p.dma("gpsimd", dst_h[j * 128:(j + 1) * 128, :], psub[i][:, :], sst)
            st_done[j] = sst.n
        full_barrier(p)
        esP.close()
        if stop_attn:
            io["sub_done"] = (sst, sst.n)
            continue

        esB = contextlib.ExitStack()
        rowb1 = p.sb(f"{tag}_rowb1_{layer}", [1, 5460], BF16, esB)
        rowu = p.sb(f"{tag}_rowu_{layer}", [1, 5460], BF16, esB)
        rowb2 = p.sb(f"{tag}_rowb2_{layer}", [1, D], BF16, esB)
        b1row = rowb1[0:1, :]
        urow = rowu[0:1, :]
        b2row = rowb2[0:1, :]
        esF1 = contextlib.ExitStack()
        rowfB = p.sb(f"{tag}_rowfB{layer}", [33, T], F32, esF1)
        hfm2 = [p.sb(f"{tag}_h2fm{layer}_{c}", [128, CW], BF16, esF1) for c in range(8)]
        rb = p.sb(f"{tag}_rbs{layer}", [128, T], BF16, esF1)
        xrb = [p.sb(f"{tag}_xrb{layer}_{i}", [128, T], BF16, esF1) for i in range(2)]
        sqf = p.sb(f"{tag}_sqf{layer}", [128, 512], BF16, esF1)
        strow = PsumPool(p, f"{tag}_strow{layer}", 2, [1, 512], es=esF1)
        spool = PsumPool(p, f"{tag}_spool{layer}", 2, [128, 512], es=esF1)
        for c in range(8):
            V("memset", hfm2[c][:, 0:PAD], 0.0)
        p.qwait("sync", sst, sst.n)
        t_fm2 = emit_fm_load(p, dst_h, hfm2, sfm)
        p.qwait("gpsimd", V.sem, V.sem.n)
        t_b1raw = p.dma("gpsimd", b1row, weights[f"{tag}_b1"][layer][:, :], sw)
        t_b2 = p.dma("sync", rowf[96:97, 0:D], weights[f"{tag}_b2"][layer][:, :], sld)
        V.wait(sld, t_b2)
        t_b2r = V("tensor_copy", b2row, rowf[96:97, 0:D])

        for which in (0, 1):
            outrow = rows if which == 0 else rowfB[0:1, :]
            for ts in range(8):
                pk, war = strow.get()
                if war is not None:
                    Tn.wait(war[0], war[1])
                if which == 1:
                    last = None
                    for ic in range(8):
                        S.wait(Tn.sem, Tn.sem.n)
                        S.wait(sfm, t_fm2)
                        t_sq = S("activation", sqf[:, :],
                                 sh(hfm2, ic, ts * 512, 512), AF.Square)
                        Tn.wait(S.sem, t_sq)
                        last = nc.tensor.matmul(pk[0:1, :], ones16[:, 0:1],
                                                sqf[:, :], start=(ic == 0),
                                                stop=(ic == 7),
                                                skip_group_check=True)
                    t_mm = Tn.sem.inc()
                    last.then_inc(Tn.sem.h, 1)
                else:
                    Tn.wait(sfm, t_fm2)
                    t_mm = mm_group(p, pk[0:1, :], [
                        (ones16[:, 0:1], sh(hfm2, ic, ts * 512, 512))
                        for ic in range(8)])
                V.wait(Tn.sem, t_mm)
                t_r = V("tensor_copy", outrow[0:1, ts * 512:(ts + 1) * 512],
                        pk[0:1, :])
                strow.done(p.V.sem, t_r)
        B0 = rowfB[0:1, :]
        B32 = rowfB[32:33, :]
        V("tensor_scalar_mul", rows, rows, 1.0 / D)
        V("tensor_scalar_mul", srow, rows, -1.0)
        V("tensor_mul", rows, rows, rows)
        V("tensor_scalar_mul", B0, B0, 1.0 / D)
        t_var = V("tensor_sub", B0, B0, rows)
        S.wait(V.sem, t_var)
        S("activation", B0, B0, AF.Ln, bias=eps5_ap[0:1, 0:1])
        t_rst = S("activation", B0, B0, AF.Exp, scale=-0.5)
        V.wait(S.sem, t_rst)
        V("tensor_copy", B32, B0)
        t_mur = V("tensor_mul", murrow, srow, B32)
        for ts in range(8):
            pk, war = spool.get()
            if war is not None:
                Tn.wait(war[0], war[1])
            Tn.wait(V.sem, t_mur)
            t_mm = mm_group(p, pk[:, :], [(ones32[0:1, 0:128],
                                           B0[0:1, ts * 512:(ts + 1) * 512])])
            V.wait(Tn.sem, t_mm)
            t_rb = V("tensor_copy", rb[:, ts * 512:(ts + 1) * 512], pk[:, :])
            spool.done(p.V.sem, t_rb)
        xst = {}
        for c in range(8):
            if c - 2 in xst:
                V.wait(sxr, xst[c - 2])
            t_x = V("tensor_mul", xrb[c % 2][:, :], sh(hfm2, c, 0, T), rb[:, :])
            p.qwait("gpsimd", V.sem, t_x)
            p.dma("gpsimd", xr_d[c * 128:(c + 1) * 128, :], xrb[c % 2][:, :], sxr)
            xst[c] = sxr.n
        full_barrier(p)
        esF1.close()
        if cfg.get("stop") == "xr" and layer == DEPTH - 1:
            scp = p.sem(f"{tag}_scp{layer}")
            p.qwait("gpsimd", sst, sst.n)
            for j in range(NT):
                p.dma("gpsimd", io["sub_tm"][j * 128:(j + 1) * 128, :],
                      dst_f[j * 128:(j + 1) * 128, :], scp)
            io["sub_done"] = (scp, scp.n)
            continue

        out_f = io["sub_tm"] if layer == DEPTH - 1 else x_f[0]
        out_h = x_h[0]
        esF2 = contextlib.ExitStack()
        Hp = [p.sb(f"{tag}_Hp{layer}_{c}", [128, T], BF16, esF2) for c in range(2)]
        acc = p.sb(f"{tag}_facc{layer}", [128, NT * D], BF16, esF2)
        xsl = p.sb(f"{tag}_xsl{layer}", [128, 8 * 512], BF16, esF2)
        wA = p.sb(f"{tag}_wA{layer}", [128, 8 * 256], BF16, esF2)
        wG = p.sb(f"{tag}_wG{layer}", [128, 8 * 256], BF16, esF2)
        w2b = p.sb(f"{tag}_w2b{layer}", [128, 2 * D], BF16, esF2)
        gg = [p.sb(f"{tag}_gg{layer}_{i}", [128, 512], BF16, esF2) for i in range(2)]
        fadd = [p.sb(f"{tag}_fadd{layer}", [128, D], F32, esF2)]
        fsub = [p.sb(f"{tag}_fsub{layer}", [128, D], F32, esF2)]
        gpool = PsumPool(p, f"{tag}_gp{layer}", 2, [128, 512], es=esF2)
        apool = PsumPool(p, f"{tag}_ap{layer}", 2, [128, 512], es=esF2)
        dpool = PsumPool(p, f"{tag}_dp{layer}", 2, [128, 512], es=esF2)
        rpool = PsumPool(p, f"{tag}_rp{layer}", 1, [1, 512], es=esF2)

        PARTS = [(k * 2, min(2, NHG - k * 2)) for k in range(11)]
        for pi, (c0, nch) in enumerate(PARTS):
            col_lo = c0 * 128
            width = min(2730, col_lo + nch * 128) - col_lo
            p.qwait("gpsimd", Tn.sem, Tn.sem.n)
            for blk, wdst, base in ((0, wA, col_lo), (1, wG, 2730 + col_lo)):
                for c in range(8):
                    p.dma("gpsimd", wdst[:, c * 256:c * 256 + width],
                          weights[f"{tag}_w1"][layer][c * 128:(c + 1) * 128,
                                                      base:base + width], sw)
            t_w1 = sw.n
            V.wait(sw, t_b1raw)
            for blk, wdst, base in ((0, wA, col_lo), (1, wG, 2730 + col_lo)):
                pk, war = rpool.get()
                if war is not None:
                    Tn.wait(war[0], war[1])
                Tn.wait(sw, t_w1)
                Tn.wait(V.sem, V.sem.n)
                last = None
                for c in range(8):
                    last = nc.tensor.matmul(pk[0:1, 0:width], b_cols[c],
                                            wdst[:, c * 256:c * 256 + width],
                                            start=(c == 0), stop=(c == 7))
                t_mm = Tn.sem.inc()
                last.then_inc(Tn.sem.h, 1)
                V.wait(Tn.sem, t_mm)
                t_b1 = V("scalar_tensor_tensor", b1row[0:1, base:base + width],
                         pk[0:1, 0:width], 1.0, b1row[0:1, base:base + width],
                         ALU.mult, ALU.add)
                rpool.done(p.V.sem, t_b1)
            t_gs = 0
            for blk, wdst in ((0, wA), (1, wG)):
                for c in range(8):
                    t_gs = V("tensor_scalar_mul", wdst[:, c * 256:c * 256 + width],
                             wdst[:, c * 256:c * 256 + width], g_cols[c])
            for blk, wdst, base in ((0, wA, col_lo), (1, wG, 2730 + col_lo)):
                pk, war = rpool.get()
                if war is not None:
                    Tn.wait(war[0], war[1])
                Tn.wait(V.sem, t_gs)
                last = None
                for c in range(8):
                    last = nc.tensor.matmul(pk[0:1, 0:width], ones16[:, 0:1],
                                            wdst[:, c * 256:c * 256 + width],
                                            start=(c == 0), stop=(c == 7))
                t_mm = Tn.sem.inc()
                last.then_inc(Tn.sem.h, 1)
                V.wait(Tn.sem, t_mm)
                t_u = V("tensor_copy", urow[0:1, base:base + width],
                        pk[0:1, 0:width])
                rpool.done(p.V.sem, t_u)
            t_prep = p.V.sem.n

            gg_rd = {}
            k = 0
            for ts in range(8):
                p.qwait("sync", Tn.sem, Tn.sem.n)
                for ic in range(8):
                    p.dma("sync", xsl[:, ic * 512:(ic + 1) * 512],
                          xr_d[ic * 128:(ic + 1) * 128, ts * 512:(ts + 1) * 512],
                          sld)
                t_xs = sld.n
                for cc in range(nch):
                    cw = min(128, 2730 - (c0 + cc) * 128)
                    gbase = 2730 + (c0 + cc) * 128
                    abase = (c0 + cc) * 128
                    pg, warg = gpool.get()
                    if warg is not None:
                        Tn.wait(warg[0], warg[1])
                    Tn.wait(V.sem, max(t_prep, t_b2r))
                    Tn.wait(sld, t_xs)
                    pairs = [(wG[:, ic * 256 + cc * 128:ic * 256 + cc * 128 + cw],
                              xsl[:, ic * 512:(ic + 1) * 512]) for ic in range(8)]
                    if "norank1" not in BISECT:
                        pairs.append((b1row[0:1, gbase:gbase + cw], ones16[0:1, 0:512]))
                        pairs.append((urow[0:1, gbase:gbase + cw],
                                      murrow[0:1, ts * 512:(ts + 1) * 512]))
                    t_mm = mm_group(p, pg[0:cw, :], pairs)
                    b = k % 2
                    S.wait(Tn.sem, t_mm)
                    if k - 2 in gg_rd:
                        S.wait(V.sem, gg_rd[k - 2])
                    t_gel = S("activation", gg[b][0:cw, :], pg[0:cw, :],
                              AF.Relu if "nogelu" in BISECT else AF.Gelu)
                    gpool.done(p.S.sem, t_gel)
                    pa, wara = apool.get()
                    if wara is not None:
                        Tn.wait(wara[0], wara[1])
                    pairs = [(wA[:, ic * 256 + cc * 128:ic * 256 + cc * 128 + cw],
                              xsl[:, ic * 512:(ic + 1) * 512]) for ic in range(8)]
                    if "norank1" not in BISECT:
                        pairs.append((b1row[0:1, abase:abase + cw], ones16[0:1, 0:512]))
                        pairs.append((urow[0:1, abase:abase + cw],
                                      murrow[0:1, ts * 512:(ts + 1) * 512]))
                    t_mm2 = mm_group(p, pa[0:cw, :], pairs)
                    V.wait(Tn.sem, t_mm2)
                    V.wait(S.sem, t_gel)
                    t_h = V("tensor_mul", Hp[cc][0:cw, ts * 512:(ts + 1) * 512],
                            pa[0:cw, :], gg[b][0:cw, :])
                    gg_rd[k] = t_h
                    apool.done(p.V.sem, t_h)
                    k += 1
            t_hp = p.V.sem.n
            if cfg.get("stop") == "h":
                continue

            p.qwait("gpsimd", Tn.sem, Tn.sem.n)
            for cc in range(nch):
                cw = min(128, 2730 - (c0 + cc) * 128)
                p.dma("gpsimd", w2b[0:cw, cc * D:(cc + 1) * D],
                      weights[f"{tag}_w2"][layer][(c0 + cc) * 128:(c0 + cc) * 128 + cw, :],
                      sw)
            t_w2 = sw.n
            st_done = {}
            final = (pi == len(PARTS) - 1)
            for j in range(NT):
                i = 0
                if final and j - 1 in st_done:
                    V.wait(sst, st_done[j - 1])
                if final:
                    p.qwait("sync", V.sem, V.sem.n)
                    t_a2 = p.dma("sync", fadd[i][:, :],
                                 dst_f[j * 128:(j + 1) * 128, :], sld)
                t_e2 = None
                for ns in range(2):
                    pk, war = dpool.get()
                    if war is not None:
                        Tn.wait(war[0], war[1])
                    Tn.wait(sw, t_w2)
                    Tn.wait(V.sem, t_hp)
                    pairs = [(Hp[cc][0:min(128, 2730 - (c0 + cc) * 128),
                                     j * 128:(j + 1) * 128],
                              w2b[0:min(128, 2730 - (c0 + cc) * 128),
                                  cc * D + ns * 512:cc * D + ns * 512 + 512])
                             for cc in range(nch)]
                    if final:
                        pairs.append((ones16[0:1, 0:128],
                                      b2row[0:1, ns * 512:(ns + 1) * 512]))
                    t_mm = mm_group(p, pk[:, :], pairs)
                    V.wait(Tn.sem, t_mm)
                    if pi == 0:
                        t_e2 = V("tensor_copy",
                                 acc[:, j * D + ns * 512:j * D + (ns + 1) * 512],
                                 pk[:, :])
                    elif not final:
                        t_e2 = V("scalar_tensor_tensor",
                                 acc[:, j * D + ns * 512:j * D + (ns + 1) * 512],
                                 pk[:, :], 1.0,
                                 acc[:, j * D + ns * 512:j * D + (ns + 1) * 512],
                                 ALU.mult, ALU.add)
                    else:
                        fslc = fsub[i][:, ns * 512:(ns + 1) * 512]
                        V("tensor_copy", fslc,
                          acc[:, j * D + ns * 512:j * D + (ns + 1) * 512])
                        V("tensor_add", fslc, fslc, pk[:, :])
                        V.wait(sld, t_a2)
                        t_e2 = V("tensor_add", fslc, fslc,
                                 fadd[i][:, ns * 512:(ns + 1) * 512])
                    dpool.done(p.V.sem, t_e2)
                if final:
                    p.qwait("gpsimd", V.sem, t_e2)
                    p.dma("gpsimd", out_f[j * 128:(j + 1) * 128, :], fsub[i][:, :], sst)
                    if layer < DEPTH - 1:
                        p.dma("gpsimd", out_h[j * 128:(j + 1) * 128, :],
                              fsub[i][:, :], sst)
                    st_done[j] = sst.n
        full_barrier(p)
        esF2.close()
        esB.close()
        if cfg.get("stop") == "h" and layer == DEPTH - 1:
            scp = p.sem(f"{tag}_scph{layer}")
            p.qwait("gpsimd", sst, sst.n)
            for j in range(NT):
                p.dma("gpsimd", io["sub_tm"][j * 128:(j + 1) * 128, :],
                      dst_f[j * 128:(j + 1) * 128, :], scp)
            io["sub_done"] = (scp, scp.n)

    if cfg.get("stop") == "h":
        return
    io["sub_done"] = (sst, sst.n)


def _xpos_tables(Wl):
    half = np.arange(0, 64, 2, dtype=np.float64)
    inv_freq = 1.0 / 10000.0 ** (half / 64)
    t = np.arange(T, dtype=np.float64)
    freqs = t[:, None] * inv_freq[None, :]
    freqs = np.concatenate([freqs, freqs], -1)
    base = (half + 0.4 * 64) / (1.4 * 64)
    power = (t - T // 2) / (Wl // 2)
    scale = base[None, :] ** power[:, None]
    scale = np.concatenate([scale, scale], -1)
    cos, sin = np.cos(freqs), np.sin(freqs)
    cS = (cos * scale).T
    sS = (sin * scale).T
    cI = (cos / scale).T
    sI = (sin / scale).T
    tile2 = lambda a: np.tile(a, (2, 1)).astype(BF)
    return tile2(cS), tile2(sS), tile2(cI), tile2(sI)


def _local_masks(Wl):
    qp = np.arange(128)[:, None]
    kp = np.arange(Wl + 128)[None, :]
    mA = np.where((kp < qp) | (kp > qp + Wl), NEG, 0.0).astype(np.float32)
    kp2 = np.arange(128)[:, None]
    qp2 = np.arange(Wl + 128)[None, :]
    mB = np.where((qp2 < kp2) | (qp2 > kp2 + Wl), NEG, 0.0).astype(np.float32)
    return mA, mB


def _rot_perm128():
    P = np.zeros((64, 64), np.float32)
    for i in range(32):
        P[i + 32, i] = -1.0
        P[i, i + 32] = 1.0
    return np.kron(np.eye(2, dtype=np.float32), P).astype(BF)


def _local_host_prep(inputs, tag, pre, depth, Wl):
    w = {}
    w[f"{tag}_wqkv"] = np.ascontiguousarray(inputs[f"{pre}_wqkv" if tag == "l3f" else f"{pre}_Wqkv"])
    w[f"{tag}_wo"] = np.ascontiguousarray(inputs[f"{pre}_wo" if tag == "l3f" else f"{pre}_Wo"])
    w[f"{tag}_w1"] = np.ascontiguousarray(inputs[f"{pre}_w1" if tag == "l3f" else f"{pre}_W1"])
    w[f"{tag}_w2"] = np.ascontiguousarray(inputs[f"{pre}_w2" if tag == "l3f" else f"{pre}_W2"])
    w[f"{tag}_b1"] = np.ascontiguousarray(inputs[f"{pre}_b1"].reshape(depth, 1, 5460))
    w[f"{tag}_b2"] = np.ascontiguousarray(inputs[f"{pre}_b2"].reshape(depth, 1, D))
    qs = inputs[f"{pre}_q_scale"]
    ks = inputs[f"{pre}_k_scale"]
    qk = np.zeros((128, 2 * depth), np.float32)
    for l in range(depth):
        qk[:, l] = np.tile(qs[l], 2)
        qk[:, depth + l] = np.tile(ks[l], 2)
    w[f"{tag}_qkS"] = qk
    lg = inputs[f"{pre}_ln_g"]
    lb = inputs[f"{pre}_ln_b"]
    lnb = np.zeros((128, 16 * depth), np.float32)
    for l in range(depth):
        lnb[:, 16 * l:16 * l + 8] = lg[l].reshape(8, 128).T
        lnb[:, 16 * l + 8:16 * l + 16] = lb[l].reshape(8, 128).T
    w[f"{tag}_lnb"] = lnb
    cS, sS, cI, sI = _xpos_tables(Wl)
    w[f"{tag}_ctabS"] = cS
    w[f"{tag}_stabS"] = sS
    w[f"{tag}_ctabI"] = cI
    w[f"{tag}_stabI"] = sI
    mA, mB = _local_masks(Wl)
    w[f"{tag}_maskA"] = mA
    w[f"{tag}_maskB"] = mB
    w[f"{tag}_blkS"] = np.kron(np.eye(2, dtype=np.float32),
                               np.ones((64, 64), np.float32)).astype(BF)
    w[f"{tag}_perm"] = _rot_perm128()
    return w


def host_prep_L3_ffn(inputs):
    return _local_host_prep(inputs, "l3f", "L3_ffn", 2, 512)


def host_prep_L0_ffn(inputs):
    return _local_host_prep(inputs, "l0f", "L0_ffn", 6, 256)


def _local_specs(tag, depth, Wl, inner):
    return {
        f"{tag}_wqkv": ([depth, D, 3 * inner], F32),
        f"{tag}_wo": ([depth, inner, D], F32),
        f"{tag}_w1": ([depth, D, 5460], F32),
        f"{tag}_w2": ([depth, 2730, D], F32),
        f"{tag}_b1": ([depth, 1, 5460], F32),
        f"{tag}_b2": ([depth, 1, D], F32),
        f"{tag}_qkS": ([128, 2 * depth], F32),
        f"{tag}_lnb": ([128, 16 * depth], F32),
        f"{tag}_ctabS": ([128, T], BF16),
        f"{tag}_stabS": ([128, T], BF16),
        f"{tag}_ctabI": ([128, T], BF16),
        f"{tag}_stabI": ([128, T], BF16),
        f"{tag}_maskA": ([128, Wl + 128], F32),
        f"{tag}_maskB": ([128, Wl + 128], F32),
        f"{tag}_blkS": ([128, 128], BF16),
        f"{tag}_perm": ([128, 128], BF16),
    }


L3F_WEIGHT_SPECS = _local_specs("l3f", 2, 512, 512)
L0F_WEIGHT_SPECS = _local_specs("l0f", 6, 256, 1024)


def emit_L3_ffn(p, io, weights, consts):
    emit_local_stack(p, io, weights, consts,
                     dict(tag="l3f", depth=2, H=8, W=512, inner=512))


def emit_L0_ffn(p, io, weights, consts):
    emit_local_stack(p, io, weights, consts,
                     dict(tag="l0f", depth=6, H=16, W=256, inner=1024))


def build_stage_L3_ffn():
    return build_stage(emit_L3_ffn, L3F_WEIGHT_SPECS, need_ln32=True)


def build_stage_L0_ffn():
    return build_stage(emit_L0_ffn, L0F_WEIGHT_SPECS, need_ln32=True)



FUSED_STAGES = [
    ("L0_attn", emit_L0_attn, L0A_WEIGHT_SPECS, host_prep_L0_attn, False, False),
    ("L0_ffn", emit_L0_ffn, L0F_WEIGHT_SPECS, host_prep_L0_ffn, False, True),
    ("L1_attn", emit_L1_attn, L1A_WEIGHT_SPECS, host_prep_L1_attn, False, True),
    ("L1_ffn", emit_L1_ffn, L1F_WEIGHT_SPECS, host_prep_L1_ffn, False, False),
    ("L2_attn", emit_L2_attn, L2A_WEIGHT_SPECS, host_prep_L2_attn, False, False),
    ("L2_ffn", emit_L2_ffn, L2F_WEIGHT_SPECS, host_prep_L2_ffn, False, False),
    ("L3_attn", emit_L3_attn, L3A_WEIGHT_SPECS, host_prep_L3_attn, True, False),
    ("L3_ffn", emit_L3_ffn, L3F_WEIGHT_SPECS, host_prep_L3_ffn, False, True),
]


def build_fused():
    p = Prog()
    x_in = p.inp("x", [T, D], F32)
    x_out = p.out("x_out", [T, D], F32)
    weights = {}
    for _, _, spec, _, _, _ in FUSED_STAGES:
        for n, (sh, dt) in spec.items():
            weights[n] = p.inp(n, sh, dt)
    p.prealloc_cbuf()
    consts = {}
    for n, (sh, dt) in CONST_SPECS.items():
        ap = p.inp(n, sh, dt)
        sbuf = p.sb(n + "_sb", sh, dt, es=p.es)
        consts[n + "_dram"] = ap
        consts[n.replace("c_", "") + "_sb"] = sbuf[:, :]
    sc = p.sem("consts")
    for n in CONST_SPECS:
        p.dma("sync", consts[n.replace("c_", "") + "_sb"][:, :],
              consts[n + "_dram"][:, :], sc)
    p.qwait("sync", sc, sc.n)

    xa = p.dram("x_ping", [T, D], F32)
    xb = p.dram("x_pong", [T, D], F32)
    io = {}
    io["ln16_tm"] = p.dram("ln16_tm", [T, D], BF16)
    io["ln32_tm"] = p.dram("ln32_tm", [T, D], F32)
    io["sub_tm"] = p.dram("sub_tm", [T, D], F32)
    ldx, sg = p.sem("ldx"), p.sem("sg")

    for i, (name, emit_fn, spec, _, rms2, need32) in enumerate(FUSED_STAGES):
        src = x_in if i == 0 else (xa if i % 2 == 1 else xb)
        dst = x_out if i == len(FUSED_STAGES) - 1 else (xa if i % 2 == 0 else xb)
        es_stage = contextlib.ExitStack()
        p.es_stack.append(es_stage)

        es_pre = contextlib.ExitStack()
        p.es_stack.append(es_pre)
        p.qwait("sync", sg, sg.n)
        full_barrier(p)
        ln = LnPass(p, f"pre{i}", needadd=False, need32=need32)
        t_pre = ln.emit(ldx, sg, src, out16=io["ln16_tm"],
                        out32=(io["ln32_tm"] if need32 else None), rms2=rms2)
        io["pre_done"] = (sg, t_pre)
        barrier(p, io["pre_done"])
        p.es_stack.pop()
        es_pre.close()

        emit_fn(p, io, weights, consts)

        barrier(p, io["sub_done"])
        es_post = contextlib.ExitStack()
        p.es_stack.append(es_post)
        ln2 = LnPass(p, f"post{i}", need16=False)
        t_post = ln2.emit(ldx, sg, io["sub_tm"], out32=dst, add=src)
        p.qwait("sync", sg, t_post)
        p.es_stack.pop()
        es_post.close()
        full_barrier(p)
        p.es_stack.pop()
        es_stage.close()
    return p


STAGES = [
    ("L0_attn", build_stage_L0_attn, host_prep_L0_attn),
    ("L0_ffn", build_stage_L0_ffn, host_prep_L0_ffn),
    ("L1_attn", build_stage_L1_attn, host_prep_L1_attn),
    ("L1_ffn", build_stage_L1_ffn, host_prep_L1_ffn),
    ("L2_attn", build_stage_L2_attn, host_prep_L2_attn),
    ("L2_ffn", build_stage_L2_ffn, host_prep_L2_ffn),
    ("L3_attn", build_stage_L3_attn, host_prep_L3_attn),
    ("L3_ffn", build_stage_L3_ffn, host_prep_L3_ffn),
]

_prog_cache = {}


def kernel(**inputs):
    consts = build_constants()
    x = np.ascontiguousarray(inputs["L0_attn_x"], dtype=np.float32)
    if "fused" not in _prog_cache:
        _prog_cache["fused"] = build_fused()
    p = _prog_cache["fused"]
    w = {}
    for _, _, _, prep_fn, _, _ in FUSED_STAGES:
        w.update(prep_fn(inputs))
    w = {k: np.ascontiguousarray(v) for k, v in w.items()}
    in_maps = []
    for c in range(8):
        m = dict(x=np.ascontiguousarray(x[c]))
        m.update(w)
        m.update(consts)
        in_maps.append(m)
    res = run_bass_kernel_spmd(p.nc, in_maps, core_ids=list(range(8)))
    return np.stack([np.asarray(res.results[c]["x_out"]) for c in range(8)])
